# version 05
# speedup vs baseline: 1.7321x; 1.0168x over previous
import numpy as np
import ml_dtypes
from contextlib import ExitStack

import concourse.bass as bass
import concourse.mybir as mybir
from concourse.bass_isa import ReduceOp
from concourse.bass_utils import run_bass_kernel_spmd

F32 = mybir.dt.float32
BF16 = mybir.dt.bfloat16
AF = mybir.ActivationFunctionType
ALU = mybir.AluOpType
AX = mybir.AxisListType

NB, SEQ, DM = 2, 8192, 1024
TOK = 2048
NT = 16
NTH = 18
GROUPS = [[0, 1, 2, 3], [4, 5, 6, 7]]
FUSED = True
DBG = False
ALL_BLOCKS = ("l0a", "l0f", "l1a", "l1f")

ENGS = ("tensor", "vector", "scalar", "gpsimd", "sync")
_OWN = {"tensor": "pe", "vector": "dve", "scalar": "act", "gpsimd": "pool", "sync": None}


class P:
    def __init__(self, nc, st):
        self.nc = nc
        self.ops = {e: [] for e in ENGS}
        self.sems = {}
        self.cnt = {}
        for s in ("pe", "dve", "act", "pool", "qs", "qg", "cc"):
            self.sems[s] = st.enter_context(nc.semaphore(s))
            self.cnt[s] = 0

    def _op(self, e, fn, waits, sig):
        self.ops[e].append((fn, tuple(waits or ()), sig))
        if sig:
            self.cnt[sig[0]] += sig[1]
            return (sig[0], self.cnt[sig[0]])
        return None

    def t(self, fn, waits=()):  return self._op("tensor", fn, waits, ("pe", 1))
    def v(self, fn, waits=()):  return self._op("vector", fn, waits, ("dve", 1))
    def s(self, fn, waits=()):  return self._op("scalar", fn, waits, ("act", 1))
    def g(self, fn, waits=()):  return self._op("gpsimd", fn, waits, ("pool", 1))
    def ds(self, fn, waits=()): return self._op("sync", fn, waits, ("qs", 16))
    def dg(self, fn, waits=()): return self._op("gpsimd", fn, waits, ("qg", 16))
    def ccop(self, fn, waits=()): return self._op("gpsimd", fn, waits, ("cc", 1))

    def tok(self, *sems):
        return tuple((s, self.cnt[s]) for s in sems)

    def barrier(self):
        snap = tuple((s, self.cnt[s]) for s in self.cnt)
        for e in ENGS:
            self._op(e, None, snap, None)

    def emit(self, block):
        def mk(e):
            own = _OWN[e]
            serialize = e in ("vector", "scalar", "gpsimd")
            def run(eng):
                waited = {}
                cum = {}
                for fn, waits, sig in self.ops[e]:
                    allw = list(waits)
                    if serialize and sig and sig[0] == own and cum.get(own, 0) > 0:
                        allw.append((own, cum[own]))
                    for (s, v) in allw:
                        if v <= 0:
                            continue
                        if waited.get(s, -1) >= v:
                            continue
                        eng.wait_ge(self.sems[s], v)
                        waited[s] = v
                    if fn is not None:
                        inst = fn(eng)
                        if sig:
                            inst.then_inc(self.sems[sig[0]], sig[1])
                    if sig:
                        cum[sig[0]] = cum.get(sig[0], 0) + sig[1]
            return run
        block.tensor(mk("tensor"))
        block.vector(mk("vector"))
        block.scalar(mk("scalar"))
        block.gpsimd(mk("gpsimd"))
        block.sync(mk("sync"))


def _win_masks():
    ms = np.zeros((6, 128, 512), np.float32)
    k = np.arange(128)[:, None]
    q = np.arange(512)[None, :]
    for kt in range(6):
        ms[kt] = (np.abs(k - q + 128 * (kt - 1)) <= 128).astype(np.float32)
    return ms.reshape(768, 512)


def _causal_masks():
    ms = np.zeros((4, 128, 512), np.float32)
    k = np.arange(128)[:, None]
    q = np.arange(512)[None, :]
    for d in range(4):
        ms[d] = ((k + 128 * d) <= q).astype(np.float32)
    return ms.reshape(512, 512)


def _prep(inp):
    inp = {k: np.asarray(v) for k, v in inp.items()}
    x = inp["L0_attn_x"].astype(np.float32, copy=False)
    C = np.ascontiguousarray
    shared = {
        "eye": np.eye(128, dtype=np.float32),
        "l0_masks": _win_masks(),
        "l1_masks": _causal_masks(),
        "l0_wq": C(inp["L0_attn_wq"]), "l0_wk": C(inp["L0_attn_wk"]),
        "l0_wv": C(inp["L0_attn_wv"]), "l0_wo": C(inp["L0_attn_wo"]),
        "l0_brow": C(np.stack([inp["L0_attn_bq"], inp["L0_attn_bk"],
                               inp["L0_attn_bv"], inp["L0_attn_bo"]])),
        "mod_w1": C(inp["L0_ffn_W1"]), "mod_w2": C(inp["L0_ffn_W2"]),
        "mod_b1c": C(np.asarray(inp["L0_ffn_b1"]).reshape(32, 128).T), "mod_b2c": C(np.asarray(inp["L0_ffn_b2"]).reshape(8, 128).T),
        "mod_wr8": C(inp["L0_ffn_w_router"].reshape(8, 128)),
        "f_sIT": C(inp["L1_ffn_slot_init"].T), "f_sI": C(inp["L1_ffn_slot_init"]),
        "f_w1t": C(inp["L1_ffn_w1"][0:1024]), "f_w1b": C(inp["L1_ffn_w1"][1024:2048]),
        "f_b1": C(inp["L1_ffn_b1"][None, :]), "f_w2": C(inp["L1_ffn_w2"]),
        "f_b2": C(inp["L1_ffn_b2"][None, :]), "f_tau": C(inp["L1_ffn_tau"].reshape(1, 1)),
        "f_wihT": C(inp["L1_ffn_Wih"].T), "f_whhT": C(inp["L1_ffn_Whh"].T),
        "f_bih": C(inp["L1_ffn_bih"][None, :]), "f_bhh": C(inp["L1_ffn_bhh"][None, :]),
        "f_wq": C(inp["L1_ffn_wq"]), "f_bq": C(inp["L1_ffn_bq"][None, :]),
        "f_wk": C(inp["L1_ffn_wk"]), "f_bk": C(inp["L1_ffn_bk"][None, :]),
        "f_wv": C(inp["L1_ffn_wv"]), "f_bv": C(inp["L1_ffn_bv"][None, :]),
        "f_lng8": C(inp["L1_ffn_ln_g"].reshape(8, 128)),
        "f_lnb8": C(inp["L1_ffn_ln_b"].reshape(8, 128)),
        "f_wvp": C(inp["L1_ffn_wvp"]), "f_bvp": C(inp["L1_ffn_bvp"][None, :]),
        "f_wo": C(inp["L1_ffn_wo"]), "f_bo": C(inp["L1_ffn_bo"][None, :]),
    }
    qp, kp, vp, op = (inp["L1_attn_q_proj"], inp["L1_attn_k_proj"],
                      inp["L1_attn_v_proj"], inp["L1_attn_o_proj"])
    in_maps = []
    for c in range(8):
        b, j = c // 4, c % 4
        xh = np.zeros((2304, DM), np.float32)
        xh[128:2176] = x[b, j * TOK:(j + 1) * TOK]
        if j > 0:
            xh[0:128] = x[b, j * TOK - 128:j * TOK]
        if j < 3:
            xh[2176:2304] = x[b, (j + 1) * TOK:(j + 1) * TOK + 128]
        padneg = np.zeros((1, 2304), np.float32)
        if j == 0:
            padneg[0, 0:128] = -1.0
        if j == 3:
            padneg[0, 2176:2304] = -1.0
        m = dict(shared)
        m["x0h"] = xh
        m["padneg"] = padneg
        hs = slice(256 * j, 256 * j + 256)
        m["l1_wq"] = C(qp[hs, :].T)
        m["l1_wk"] = C(kp[hs, :].T)
        m["l1_wv"] = C(vp[hs, :].T)
        m["l1_wo"] = C(op[:, hs].T)
        in_maps.append(m)
    return in_maps


def setup_consts(p, st, nc, dr):
    C = {}
    C["eyef"] = st.enter_context(nc.sbuf_tensor("eyef", [128, 128], F32))
    C["eyeb"] = st.enter_context(nc.sbuf_tensor("eyeb", [128, 128], BF16))
    C["onesb"] = st.enter_context(nc.sbuf_tensor("onesb", [128, 2304], BF16))
    C["onesf"] = st.enter_context(nc.sbuf_tensor("onesf", [128, 128], F32))
    C["tiny"] = st.enter_context(nc.sbuf_tensor("tiny", [128, 64], F32))
    C["stats"] = st.enter_context(nc.sbuf_tensor("stats", [128, 14], F32))
    C["mrs"] = st.enter_context(nc.sbuf_tensor("mrs", [128, 4 * NTH], F32))
    p.ds(lambda e: e.dma_start(C["eyef"][:], dr["eye"]))
    p.dg(lambda e: e.dma_start(C["eyeb"][:], dr["eye"]))
    p.v(lambda e: e.memset(C["onesb"][:], 1.0))
    p.v(lambda e: e.memset(C["onesf"][:], 1.0))
    p.barrier()
    return C


def tile_ln(p, C, xin_ap, t, u_dst, wait=()):
    st_, mrs, tmp = C["stats"], C["mrs"], C["tiny"]
    tk = p.v(lambda e: e.bn_stats(st_[:, 0:6], xin_ap[:, 0:512]), tuple(wait) + p.tok("act"))
    tk = p.v(lambda e: e.bn_stats(st_[:, 6:12], xin_ap[:, 512:1024]), [tk])
    tk = p.v(lambda e: e.bn_aggr(mrs[:, 4 * t:4 * t + 2], st_[:, 0:12]), [tk])
    tk = p.v(lambda e, t=t: e.tensor_scalar_add(tmp[:, 0:1], mrs[:, 4 * t + 1:4 * t + 2], 1e-5),
             [tk])
    tk = p.s(lambda e: e.activation(tmp[:, 1:2], tmp[:, 0:1], AF.Ln), [tk])
    tk = p.s(lambda e, t=t: e.activation(mrs[:, 4 * t + 2:4 * t + 3], tmp[:, 1:2], AF.Exp,
                                         scale=-0.5), [tk])
    tk = p.v(lambda e, t=t: e.tensor_tensor(tmp[:, 2:3], mrs[:, 4 * t:4 * t + 1],
                                            mrs[:, 4 * t + 2:4 * t + 3], ALU.mult), [tk])
    tk = p.v(lambda e, t=t: e.tensor_scalar_mul(mrs[:, 4 * t + 3:4 * t + 4], tmp[:, 2:3], -1.0),
             [tk])
    return p.s(lambda e, t=t, xin_ap=xin_ap, u_dst=u_dst: e.activation(
        u_dst, xin_ap, AF.Identity, bias=mrs[:, 4 * t + 3:4 * t + 4],
        scale=mrs[:, 4 * t + 2:4 * t + 3]), [tk])


def ln_resid_store(p, C, src_kind, src0, src1, xin_ap, yb, x_dst_tile, extra_wait=()):
    st_, tmp = C["stats"], C["tiny"]
    if src_kind == "psum2":
        a0, a1 = src0, src1
    else:
        a0, a1 = src0[:, 0:512], src0[:, 512:1024]
    p.v(lambda e, a0=a0: e.bn_stats(st_[:, 0:6], a0), tuple(extra_wait) + p.tok("act"))
    p.v(lambda e, a1=a1: e.bn_stats(st_[:, 6:12], a1))
    p.v(lambda e: e.bn_aggr(tmp[:, 8:10], st_[:, 0:12]))
    p.v(lambda e: e.tensor_scalar_add(tmp[:, 0:1], tmp[:, 9:10], 1e-5))
    p.s(lambda e: e.activation(tmp[:, 1:2], tmp[:, 0:1], AF.Ln), p.tok("dve"))
    p.s(lambda e: e.activation(tmp[:, 2:3], tmp[:, 1:2], AF.Exp, scale=-0.5))
    p.v(lambda e: e.tensor_tensor(tmp[:, 3:4], tmp[:, 8:9], tmp[:, 2:3], ALU.mult), p.tok("act"))
    p.v(lambda e: e.tensor_scalar_mul(tmp[:, 4:5], tmp[:, 3:4], -1.0))
    p.s(lambda e, a0=a0, yb=yb: e.activation(yb[:, 0:512], a0, AF.Identity,
                                             bias=tmp[:, 4:5], scale=tmp[:, 2:3]), p.tok("dve", "qs"))
    lntok = p.s(lambda e, a1=a1, yb=yb: e.activation(yb[:, 512:1024], a1, AF.Identity,
                                                     bias=tmp[:, 4:5], scale=tmp[:, 2:3]))
    p.v(lambda e, yb=yb, xin_ap=xin_ap: e.tensor_add(yb[:], yb[:], xin_ap), [lntok])
    return p.ds(lambda e, x_dst_tile=x_dst_tile, yb=yb: e.dma_start(x_dst_tile, yb[:]),
                p.tok("dve"))


def ln_resid_pipe(p, C, nc, st, prefix, n_tiles, make_src, x_src, x_dst, nslot=3):
    stt = st.enter_context(nc.sbuf_tensor(prefix + "_pst", [128, 12 * nslot], F32))
    tmp = st.enter_context(nc.sbuf_tensor(prefix + "_ptp", [128, 8 * nslot], F32))
    xts = st.enter_context(nc.sbuf_tensor(prefix + "_pxt", [128, 1024 * nslot], F32))
    ybs = st.enter_context(nc.sbuf_tensor(prefix + "_pyb", [128, 1024 * nslot], F32))
    slot_free = {}
    for t in range(n_tiles):
        s = t % nslot
        wfree = [slot_free[s]] if s in slot_free else []
        xin = xts[:, 1024 * s:1024 * (s + 1)]
        ybx = ybs[:, 1024 * s:1024 * (s + 1)]
        xdt = p.ds(lambda e, t=t, xin=xin: e.dma_start(
            xin, x_src[128 * t:128 * (t + 1), :]), wfree)
        src, stok = make_src(t, s, xin, xdt, wfree)
        v = p.v(lambda e, s=s, src=src: e.bn_stats(
            stt[:, 12 * s:12 * s + 6], src[:, 0:512]), [stok])
        v = p.v(lambda e, s=s, src=src: e.bn_stats(
            stt[:, 12 * s + 6:12 * s + 12], src[:, 512:1024]), [v])
        v = p.v(lambda e, s=s: e.bn_aggr(tmp[:, 8 * s + 4:8 * s + 6],
                                         stt[:, 12 * s:12 * s + 12]), [v])
        v = p.v(lambda e, s=s: e.tensor_scalar_add(
            tmp[:, 8 * s:8 * s + 1], tmp[:, 8 * s + 5:8 * s + 6], 1e-5), [v])
        a = p.s(lambda e, s=s: e.activation(tmp[:, 8 * s + 1:8 * s + 2],
                                            tmp[:, 8 * s:8 * s + 1], AF.Ln), [v])
        a = p.s(lambda e, s=s: e.activation(tmp[:, 8 * s + 2:8 * s + 3],
                                            tmp[:, 8 * s + 1:8 * s + 2], AF.Exp,
                                            scale=-0.5), [a])
        v = p.v(lambda e, s=s: e.tensor_tensor(
            tmp[:, 8 * s + 3:8 * s + 4], tmp[:, 8 * s + 4:8 * s + 5],
            tmp[:, 8 * s + 2:8 * s + 3], ALU.mult), [a])
        v = p.v(lambda e, s=s: e.tensor_scalar_mul(
            tmp[:, 8 * s + 6:8 * s + 7], tmp[:, 8 * s + 3:8 * s + 4], -1.0), [v])
        a = p.s(lambda e, s=s, src=src, ybx=ybx: e.activation(
            ybx, src, AF.Identity,
            bias=tmp[:, 8 * s + 6:8 * s + 7], scale=tmp[:, 8 * s + 2:8 * s + 3]), [v])
        v = p.v(lambda e, ybx=ybx, xin=xin: e.tensor_add(ybx, ybx, xin), [a, xdt])
        slot_free[s] = p.dg(lambda e, t=t, ybx=ybx: e.dma_start(
            x_dst[128 * t:128 * (t + 1), :], ybx), [v])


def norm_bound(p, C, src_view, ncols, sq, psbanks, outcol, crow, nch=8):
    groups = []
    off = 0
    while off < ncols:
        groups.append((off, min(2048, ncols - off)))
        off += 2048
    gi = 0
    for (goff, gw) in groups:
        nwin = (gw + 511) // 512
        for c in range(nch):
            p.s(lambda e, c=c, goff=goff, gw=gw, src_view=src_view, sq=sq: e.activation(
                sq[:, 0:gw], src_view[:, c, goff:goff + gw], AF.Square), p.tok("pe"))
            for n in range(nwin):
                w = min(512, gw - 512 * n)
                p.t(lambda e, c=c, n=n, w=w, sq=sq, bk=psbanks[n], nch=nch: e.matmul(
                    bk[0:1, 0:w], C["onesf"][:, 0:1], sq[:, 512 * n:512 * n + w],
                    start=(c == 0), stop=(c == nch - 1)),
                    p.tok("act", "dve") if n == 0 else ())
        for n in range(nwin):
            w = min(512, gw - 512 * n)
            p.v(lambda e, n=n, w=w, gi=gi, bk=psbanks[n]: e.reduce_max(
                C["tiny"][0:1, gi + n:gi + n + 1], bk[0:1, 0:w], axis=AX.X), p.tok("pe"))
        gi += nwin
    p.v(lambda e, gi=gi, outcol=outcol, crow=crow: e.reduce_max(
        crow[0:1, outcol:outcol + 1], C["tiny"][0:1, 0:gi], axis=AX.X))


def bias_col_from(p, C, crow, src_col, dst_col_ap, psbank, scale):
    p.t(lambda e: e.matmul(psbank[:, 0:1], C["onesf"][0:1, :], crow[0:1, src_col:src_col + 1],
                           start=True, stop=True), p.tok("dve"))
    return p.v(lambda e: e.tensor_copy(dst_col_ap, psbank[:, 0:1]), p.tok("pe"))


def phase_l0a(p, C, nc, dr, x_src, x_dst, dbg=None):
    with ExitStack() as st:
        sb = lambda n, shp, dt: st.enter_context(nc.sbuf_tensor(n, shp, dt))
        ps = lambda n, shp, dt=F32: st.enter_context(nc.psum_tensor(n, shp, dt))
        stA = ExitStack()
        sba = lambda n, shp, dt: stA.enter_context(nc.sbuf_tensor(n, shp, dt))
        brow = sb("a_brow", [128, 4 * 1024], BF16)
        padZ = sb("a_padZ", [128, 2304], BF16)
        QT = sb("a_QT", [128, 8 * 2048], BF16)
        KT = sb("a_KT", [128, 8 * 2304], BF16)
        VP = sb("a_VP", [128, 18 * 1040], BF16)
        xt = sb("a_xt", [128, 2 * 1024], F32)
        ub = sb("a_ub", [128, 2 * 1024], BF16)
        uT = sb("a_uT", [128, 2 * 1024], BF16)
        crow = sb("a_crow", [1, 16], F32)
        bias_c = sb("a_biasc", [128, 1], F32)
        rsumZ = sb("a_rsumZ", [128, 2 * 512], BF16)
        rsumf = sb("a_rsumf", [1, 2 * 512], F32)
        bcastS = sb("a_bcastS", [64, 2 * 512], BF16)
        wq = sba("a_wq", [128, 8 * 1024], BF16)
        wk = sba("a_wk", [128, 8 * 1024], BF16)
        wv = sba("a_wv", [128, 8 * 1024], BF16)
        stP = ExitStack()
        pbt = stP.enter_context(nc.psum_tensor("a_pbt", [128, 2 * 1024], BF16))
        psA = stP.enter_context(nc.psum_tensor("a_psA", [128, 512], F32))
        psB = stP.enter_context(nc.psum_tensor("a_psB", [128, 512], F32))
        psS = stP.enter_context(nc.psum_tensor("a_psS", [128, 2 * 512], F32))

        QTv = QT.rearrange("p (c n) -> p c n", n=2048)
        KTv = KT.rearrange("p (c n) -> p c n", n=2304)
        VPv = VP.rearrange("p (tt h n) -> p tt h n", h=16, n=65)

        for wsb, nm in ((wq, "l0_wq"), (wk, "l0_wk"), (wv, "l0_wv")):
            for c in range(8):
                p.dg(lambda e, wsb=wsb, nm=nm, c=c: e.dma_start(
                    wsb[:, 1024 * c:1024 * (c + 1)], dr[nm][128 * c:128 * (c + 1), :]))
        mz1 = p.v(lambda e: e.memset(brow[:], 0.0))
        mz2 = p.v(lambda e: e.memset(padZ[:], 0.0))
        p.dg(lambda e: e.dma_start(brow[0:1, :], dr["l0_brow"].rearrange("a b -> (a b)").unsqueeze(0)),
             [mz1])
        p.dg(lambda e: e.dma_start(padZ[0:1, :], dr["padneg"]), [mz2])
        p.v(lambda e: e.memset(VP[:], 1.0))
        p.v(lambda e: e.memset(rsumZ[:], 0.0))
        p.barrier()

        dma_toks = {}
        last_use_ub = {0: None, 1: None}
        last_evT = {0: None, 1: None}
        for t in (0, 1):
            b = t % 2
            dma_toks[t] = p.ds(lambda e, t=t, b=b: e.dma_start(
                xt[:, 1024 * b:1024 * (b + 1)], x_src[128 * t:128 * (t + 1), :]))

        def project_fmaj(bank, wsb, uTt, brw, f4, dst3, t, padfix, wait=()):
            grpw = list(p.tok("act")) + list(wait)
            first = True
            for fc in range(4):
                f = f4 + fc
                last_grp = (fc == 3)
                for c in range(8):
                    p.t(lambda e, f=f, c=c, fc=fc, bank=bank, wsb=wsb, uTt=uTt: e.matmul(
                        bank[:, 128 * fc:128 * (fc + 1)],
                        wsb[:, 1024 * c + 128 * f:1024 * c + 128 * (f + 1)],
                        uTt[:, 128 * c:128 * (c + 1)],
                        start=(c == 0 and fc == 0), stop=False,
                        skip_group_check=True), grpw if first else ())
                    first = False
                p.t(lambda e, f=f, fc=fc, bank=bank, brw=brw, last_grp=last_grp: e.matmul(
                    bank[:, 128 * fc:128 * (fc + 1)],
                    brw[:, 128 * f:128 * (f + 1)], C["onesb"][:, 0:128],
                    start=False, stop=(last_grp and not padfix),
                    skip_group_check=True))
                if padfix:
                    p.t(lambda e, f=f, fc=fc, bank=bank, brw=brw, t=t, last_grp=last_grp: e.matmul(
                        bank[:, 128 * fc:128 * (fc + 1)],
                        brw[:, 128 * f:128 * (f + 1)],
                        padZ[:, 128 * t:128 * (t + 1)],
                        start=False, stop=last_grp, skip_group_check=True))
            p.s(lambda e, bank=bank, dst3=dst3: e.activation(
                dst3, bank.rearrange("p (c n) -> p c n", n=128), AF.Identity), p.tok("pe"))

        for t in range(NTH):
            b = t % 2
            xin = xt[:, 1024 * b:1024 * (b + 1)]
            uin = ub[:, 1024 * b:1024 * (b + 1)]
            uTt = uT[:, 1024 * b:1024 * (b + 1)]
            w = [dma_toks[t]]
            if last_use_ub[b]:
                w.append(last_use_ub[b])
            ntok = tile_ln(p, C, xin, t, uin, wait=w)
            if t + 2 < NTH:
                dma_toks[t + 2] = p.ds(lambda e, t=t, b=b: e.dma_start(
                    xt[:, 1024 * b:1024 * (b + 1)],
                    x_src[128 * (t + 2):128 * (t + 3), :]), [ntok])
            trw = [ntok]
            if last_evT[b]:
                trw.append(last_evT[b])
            for c in range(8):
                p.t(lambda e, c=c, b=b, uin=uin: e.transpose(
                    pbt[:, 1024 * b + 128 * c:1024 * b + 128 * (c + 1)],
                    uin[:, 128 * c:128 * (c + 1)], C["eyeb"][:]), trw if c == 0 else ())
            last_use_ub[b] = p.tok("pe")[0]
            evw = list(p.tok("pe"))
            evtok = p.v(lambda e, uTt=uTt, b=b: e.tensor_copy(
                uTt, pbt[:, 1024 * b:1024 * (b + 1)]), evw)
            last_evT[b] = evtok
            pad = t in (0, 17)
            if 1 <= t <= 16:
                tq = t - 1
                project_fmaj(psA, wq, uTt, brow[:, 0:1024], 0,
                             QTv[:, 0:4, 128 * tq:128 * (tq + 1)], t, False, wait=[evtok])
                project_fmaj(psB, wq, uTt, brow[:, 0:1024], 4,
                             QTv[:, 4:8, 128 * tq:128 * (tq + 1)], t, False)
            project_fmaj(psA, wk, uTt, brow[:, 1024:2048], 0,
                         KTv[:, 0:4, 128 * t:128 * (t + 1)], t, pad, wait=[evtok])
            project_fmaj(psB, wk, uTt, brow[:, 1024:2048], 4,
                         KTv[:, 4:8, 128 * t:128 * (t + 1)], t, pad)
            for half, bank in ((0, psA), (1, psB)):
                vw = list(p.tok("act")) + [evtok]
                for c in range(8):
                    p.t(lambda e, c=c, half=half, bank=bank, uTt=uTt: e.matmul(
                        bank[:], uTt[:, 128 * c:128 * (c + 1)],
                        wv[:, 1024 * c + 512 * half:1024 * c + 512 * (half + 1)],
                        start=(c == 0), stop=False), vw if c == 0 else ())
                p.t(lambda e, half=half, bank=bank, pad=pad: e.matmul(
                    bank[:], C["onesb"][:, 0:128],
                    brow[:, 2048 + 512 * half:2048 + 512 * (half + 1)],
                    start=False, stop=not pad))
                if pad:
                    p.t(lambda e, half=half, bank=bank, t=t: e.matmul(
                        bank[:], padZ[:, 128 * t:128 * (t + 1)],
                        brow[:, 2048 + 512 * half:2048 + 512 * (half + 1)],
                        start=False, stop=True))
                p.s(lambda e, half=half, bank=bank, t=t: e.activation(
                    VPv[:, t, 8 * half:8 * (half + 1), 0:64],
                    bank.rearrange("p (h n) -> p h n", n=64), AF.Identity), p.tok("pe"))
            last_use_uT = p.tok("pe")
        p.barrier()
        stA.close()
        wo = sb("a_wo", [128, 8 * 1024], BF16)
        masks = sb("a_masks", [128, 6 * 512], BF16)
        for c in range(8):
            p.dg(lambda e, c=c: e.dma_start(
                wo[:, 1024 * c:1024 * (c + 1)], dr["l0_wo"][128 * c:128 * (c + 1), :]))
        for i in range(6):
            p.dg(lambda e, i=i: e.dma_start(masks[:, 512 * i:512 * (i + 1)],
                                            dr["l0_masks"][128 * i:128 * (i + 1), :]))
        p.barrier()
        sq = sb("a_sq", [128, 2048], F32)
        Pb = sb("a_Pb", [128, 2 * 1024], BF16)
        attn = sb("a_attn", [128, 8 * 512], BF16)
        yb = sb("a_yb", [128, 1024], F32)
        xr = sb("a_xr", [128, 1024], F32)

        norm_bound(p, C, QTv, 2048, sq, [psA, psB, psS[:, 0:512], psS[:, 512:1024]], 8, crow)
        norm_bound(p, C, KTv, 2304, sq, [psA, psB, psS[:, 0:512], psS[:, 512:1024]], 9, crow)
        p.v(lambda e: e.tensor_tensor(crow[0:1, 10:11], crow[0:1, 8:9], crow[0:1, 9:10], ALU.mult))
        p.s(lambda e: e.activation(crow[0:1, 11:12], crow[0:1, 10:11], AF.Ln), p.tok("dve"))
        p.s(lambda e: e.activation(crow[0:1, 12:13], crow[0:1, 11:12], AF.Exp, scale=0.5))
        p.v(lambda e: e.tensor_scalar_mul(crow[0:1, 13:14], crow[0:1, 12:13], -0.125), p.tok("act"))
        bias_col_from(p, C, crow, 13, bias_c[:, 0:1], psA, None)
        p.barrier()
        stP.close()
        psS4 = st.enter_context(nc.psum_tensor("a_psS4", [128, 2048], F32))
        psAcc2 = st.enter_context(nc.psum_tensor("a_psAcc2", [128, 1024], F32))
        psBb = st.enter_context(nc.psum_tensor("a_psBb", [128, 512], F32))

        qk_tok = {}
        exp_tok = {}
        av_tok = {}
        h_done = {}
        gg_off = [0]
        oproj_tok = [None]
        for qt in range(4):
            items = [(h, kt) for h in range(16) for kt in range(6)]
            groups = [items[i:i + 2] for i in range(0, len(items), 2)]
            G = len(groups)
            pending = []

            def issue_qk(g, qt=qt, gg_off=gg_off):
                gg = gg_off[0] + g
                for i, (h, kt) in enumerate(groups[g]):
                    hp, hb = h % 2, h // 2
                    w = []
                    if i == 0 and gg >= 2:
                        w.append(exp_tok[gg - 2])
                    if i == 0 and g <= 1 and oproj_tok[0] is not None:
                        w.append(oproj_tok[0])
                    qk_tok[gg] = p.t(lambda e, kt=kt, qt=qt, gg=gg, i=i, hp=hp, hb=hb: e.matmul(
                        psS4[:, 1024 * (gg % 2) + 512 * i:1024 * (gg % 2) + 512 * (i + 1)],
                        KT[64 * hp:64 * (hp + 1),
                           2304 * hb + 512 * qt + 128 * kt:2304 * hb + 512 * qt + 128 * (kt + 1)],
                        QT[64 * hp:64 * (hp + 1),
                           2048 * hb + 512 * qt:2048 * hb + 512 * (qt + 1)],
                        start=True, stop=True), w)

            def issue_exp(g, gg_off=gg_off):
                gg = gg_off[0] + g
                w = [qk_tok[gg]]
                if gg >= 2:
                    w.append(av_tok[gg - 2])
                tok = p.s(lambda e, gg=gg: e.activation(
                    Pb[:, 1024 * (gg % 2):1024 * (gg % 2) + 1024],
                    psS4[:, 1024 * (gg % 2):1024 * (gg % 2) + 1024],
                    AF.Exp, scale=0.125, bias=bias_c[:, 0:1]), w)
                kt0 = groups[g][0][1]
                tok = p.v(lambda e, gg=gg, kt0=kt0: e.tensor_tensor(
                    Pb[:, 1024 * (gg % 2):1024 * (gg % 2) + 1024],
                    Pb[:, 1024 * (gg % 2):1024 * (gg % 2) + 1024],
                    masks[:, 512 * kt0:512 * kt0 + 1024], ALU.mult), [tok])
                exp_tok[gg] = tok

            def issue_av(g, it, qt=qt, gg_off=gg_off):
                gg = gg_off[0] + g
                for i, (h, kt) in enumerate(groups[g]):
                    w = [exp_tok[gg]] if i == 0 else []
                    if kt == 0 and (qt, h - 2) in h_done:
                        w.append(h_done[(qt, h - 2)])
                    if kt == 0 and h <= 1 and (qt - 1, 14 + h % 2) in h_done:
                        w.append(h_done[(qt - 1, 14 + h % 2)])
                    av_tok[gg] = p.t(lambda e, kt=kt, qt=qt, gg=gg, i=i, h=h: e.matmul(
                        psAcc2[0:65, 512 * (h % 2):512 * (h % 2) + 512],
                        VP[:, 1040 * (4 * qt + kt) + 65 * h:1040 * (4 * qt + kt) + 65 * h + 65],
                        Pb[:, 1024 * (gg % 2) + 512 * i:1024 * (gg % 2) + 512 * (i + 1)],
                        start=(kt == 0), stop=(kt == 5),
                        skip_group_check=True), w)
                    if kt == 5:
                        rt = p.v(lambda e, h=h: e.reciprocal(
                            rsumf[0:1, 512 * (h % 2):512 * (h % 2) + 512],
                            psAcc2[64:65, 512 * (h % 2):512 * (h % 2) + 512]),
                            [av_tok[gg]])
                        rt = p.v(lambda e, h=h: e.tensor_copy(
                            rsumZ[0:1, 512 * (h % 2):512 * (h % 2) + 512],
                            rsumf[0:1, 512 * (h % 2):512 * (h % 2) + 512]), [rt])
                        pending.append((h, rt, it))

            def flush_tail(it, qt=qt):
                while pending and pending[0][2] <= it - 2:
                    h, rt, _ = pending.pop(0)
                    bw = [rt]
                    if (qt, h - 1) in h_done:
                        bw.append(h_done[(qt, h - 1)])
                    elif h == 0 and (qt - 1, 15) in h_done:
                        bw.append(h_done[(qt - 1, 15)])
                    bt = p.t(lambda e, h=h: e.matmul(
                        psBb[0:64, :], C["onesb"][:, 0:64],
                        rsumZ[:, 512 * (h % 2):512 * (h % 2) + 512],
                        start=True, stop=True), bw)
                    bt = p.v(lambda e, h=h: e.tensor_copy(
                        bcastS[:, 512 * (h % 2):512 * (h % 2) + 512],
                        psBb[0:64, :]), [bt])
                    h_done[(qt, h)] = p.v(lambda e, h=h, qt=qt: e.tensor_tensor(
                        attn[64 * (h % 2):64 * (h % 2) + 64,
                             512 * (h // 2):512 * (h // 2) + 512],
                        psAcc2[0:64, 512 * (h % 2):512 * (h % 2) + 512],
                        bcastS[:, 512 * (h % 2):512 * (h % 2) + 512],
                        ALU.mult), [bt])

            for g in range(G):
                issue_qk(g)
                issue_exp(g)
                if g >= 1:
                    issue_av(g - 1, g)
                flush_tail(g)
            issue_av(G - 1, G)
            flush_tail(G + 2)
            gg_off[0] += G

            prev_store = None
            attn_w = [h_done[(qt, 14)], h_done[(qt, 15)]]
            for ts in range(4):
                tglob = 4 * qt + ts
                mw = attn_w + list(p.tok("act"))
                for nh in range(2):
                    for pr in range(8):
                        p.t(lambda e, ts=ts, nh=nh, pr=pr: e.matmul(
                            psS4[:, 512 * nh:512 * (nh + 1)],
                            attn[:, 512 * pr + 128 * ts:512 * pr + 128 * (ts + 1)],
                            wo[:, 1024 * pr + 512 * nh:1024 * pr + 512 * (nh + 1)],
                            start=(pr == 0), stop=False), mw if (nh == 0 and pr == 0) else ())
                    p.t(lambda e, nh=nh: e.matmul(
                        psS4[:, 512 * nh:512 * (nh + 1)], C["onesb"][:, 0:128],
                        brow[:, 3072 + 512 * nh:3072 + 512 * (nh + 1)], start=False, stop=True))
                xw = list(p.tok("dve"))
                if prev_store:
                    xw.append(prev_store)
                xrtok = p.ds(lambda e, tglob=tglob: e.dma_start(
                    xr[:], x_src[128 * (tglob + 1):128 * (tglob + 2), :]), xw)
                prev_store = ln_resid_store(
                    p, C, "psum2", psS4[:, 0:512], psS4[:, 512:1024], xr[:], yb,
                    x_dst[128 * tglob:128 * (tglob + 1), :],
                    extra_wait=list(p.tok("pe")) + [xrtok])
            oproj_tok[0] = prev_store
        p.barrier()


def phase_l0f(p, C, nc, dr, x_src, x_dst, cio):
    o2d = cio["o2_dram"]
    with ExitStack() as st:
        sb = lambda n, shp, dt: st.enter_context(nc.sbuf_tensor(n, shp, dt))
        ps = lambda n, shp, dt=F32: st.enter_context(nc.psum_tensor(n, shp, dt))
        stA = ExitStack()
        sba = lambda n, shp, dt: stA.enter_context(nc.sbuf_tensor(n, shp, dt))
        xt = sb("f_xt", [128, 2 * 1024], F32)
        ub = sb("f_ub", [128, 2 * 1024], BF16)
        scfull = sb("f_scfull", [128, 64], F32)
        scloc = sb("f_scloc", [128, 16], F32)
        bis = sb("f_bis", [128, 8], F32)
        gates = sb("f_gates", [128, 16], F32)
        o2tok = sb("f_o2tok", [128, 2 * 1024], BF16)
        W1 = sba("f_W1", [128, 8 * 4096], BF16)
        W2 = sba("f_W2", [128, 32 * 1024], BF16)
        b1c = sba("f_b1c", [128, 32], F32)
        b2c = sba("f_b2c", [128, 8], F32)
        wr8 = sba("f_wr8", [8, 128], F32)
        wrc = sba("f_wrc", [128, 8], BF16)
        u2T = sba("f_u2T", [128, 8 * 2048], BF16)
        scrow = sba("f_scrow", [1, 2048], F32)
        scl16 = sba("f_scl16", [16, 128], F32)
        hbuf = sba("f_hb", [128, 2 * 256], BF16)
        o2f = sba("f_o2f", [128, 256], BF16)
        pbt = ps("f_pbt", [128, 1024], BF16)
        psH = ps("f_psH", [128, 2 * 512])
        psO0 = ps("f_psO0", [128, 2 * 256])
        psO1 = ps("f_psO1", [128, 2 * 256])
        psO2 = ps("f_psO2", [128, 2 * 256])
        psO3 = ps("f_psO3", [128, 2 * 256])
        psC = ps("f_psC", [128, 512])
        psOm = [psO0[:, 0:256], psO0[:, 256:512], psO1[:, 0:256], psO1[:, 256:512],
                psO2[:, 0:256], psO2[:, 256:512], psO3[:, 0:256], psO3[:, 256:512]]

        for c in range(8):
            p.dg(lambda e, c=c: e.dma_start(W1[:, 4096 * c:4096 * (c + 1)],
                                            dr["mod_w1"][128 * c:128 * (c + 1), :]))
        for c in range(32):
            p.dg(lambda e, c=c: e.dma_start(W2[:, 1024 * c:1024 * (c + 1)],
                                            dr["mod_w2"][128 * c:128 * (c + 1), :]))
        p.dg(lambda e: e.dma_start(b1c[:], dr["mod_b1c"]))
        p.dg(lambda e: e.dma_start(b2c[:], dr["mod_b2c"]))
        p.ds(lambda e: e.dma_start(wr8[:], dr["mod_wr8"]))
        p.barrier()
        p.t(lambda e: e.matmul(psC[:, 0:8], wr8[:], C["eyef"][0:8, 0:8], start=True, stop=True))
        p.v(lambda e: e.tensor_copy(wrc[:], psC[:, 0:8]), p.tok("pe"))
        p.barrier()

        dma_toks = {}
        last_use_ub = {0: None, 1: None}
        last_evT = {0: None, 1: None}
        for t in (0, 1):
            b = t % 2
            dma_toks[t] = p.ds(lambda e, t=t, b=b: e.dma_start(
                xt[:, 1024 * b:1024 * (b + 1)], x_src[128 * t:128 * (t + 1), :]))
        for t in range(NT):
            b = t % 2
            xin = xt[:, 1024 * b:1024 * (b + 1)]
            uin = ub[:, 1024 * b:1024 * (b + 1)]
            w = [dma_toks[t]]
            if last_use_ub[b]:
                w.append(last_use_ub[b])
            ntok = tile_ln(p, C, xin, t, uin, wait=w)
            if t + 2 < NT:
                dma_toks[t + 2] = p.ds(lambda e, t=t, b=b: e.dma_start(
                    xt[:, 1024 * b:1024 * (b + 1)],
                    x_src[128 * (t + 2):128 * (t + 3), :]), [ntok])
            trw = [ntok]
            if last_evT[0]:
                trw.append(last_evT[0])
            for c in range(8):
                p.t(lambda e, c=c, uin=uin: e.transpose(
                    pbt[:, 128 * c:128 * (c + 1)],
                    uin[:, 128 * c:128 * (c + 1)], C["eyeb"][:]), trw if c == 0 else ())
            last_use_ub[b] = p.tok("pe")[0]
            last_evT[0] = p.v(lambda e, t=t: e.tensor_copy(
                u2T.rearrange("p (c n) -> p c n", n=2048)[:, :, 128 * t:128 * (t + 1)],
                pbt[:, 0:1024].rearrange("p (c n) -> p c n", n=128)),
                p.tok("pe"))
        p.barrier()

        for n in range(4):
            for c in range(8):
                p.t(lambda e, n=n, c=c: e.matmul(
                    psC[0:1, 0:512], wrc[:, c:c + 1],
                    u2T[:, 2048 * c + 512 * n:2048 * c + 512 * (n + 1)],
                    start=(c == 0), stop=(c == 7)), p.tok("dve") if c == 0 else ())
            p.v(lambda e, n=n: e.tensor_copy(scrow[0:1, 512 * n:512 * (n + 1)], psC[0:1, 0:512]),
                p.tok("pe"))
        p.ds(lambda e: e.dma_start(cio["sc_in"], scrow[0:1, :]), p.tok("dve"))
        p.ccop(lambda e: e.collective_compute(
            "AllGather", ALU.bypass, GROUPS, ins=[cio["sc_in"].opt()],
            outs=[cio["sc_out"].opt()]), p.tok("qs"))
        p.ds(lambda e: e.dma_start(
            scfull[:], cio["sc_out"].rearrange("a (c g) -> (a c) g", g=64)), p.tok("cc"))
        p.ds(lambda e: e.dma_start(
            scl16[:], cio["sc_in"].rearrange("a (b c) -> (a b) c", c=128)), p.tok("cc"))
        p.t(lambda e: e.transpose(psC[:, 0:16], scl16[:], C["eyef"][0:16, 0:16]), p.tok("qs"))
        p.v(lambda e: e.tensor_copy(scloc[:], psC[:, 0:16]), p.tok("pe"))
        p.barrier()

        K_SEL = 4096.0
        N_BIS = 42
        p.v(lambda e: e.memset(bis[:, 0:1], -10000.0))
        p.v(lambda e: e.memset(bis[:, 1:2], 10000.0))
        p.v(lambda e: e.memset(bis[:, 3:4], 3.0e38))

        def bis_pre(it):
            p.v(lambda e: e.tensor_add(bis[:, 6:7], bis[:, 0:1], bis[:, 1:2]))
            p.v(lambda e: e.tensor_scalar_mul(bis[:, 2:3], bis[:, 6:7], 0.5))
            p.v(lambda e: e.tensor_single_scalar(C["tiny"][:], scfull[:], bis[:, 2:3],
                                                 ALU.is_gt))
            return p.v(lambda e: e.reduce_sum(bis[:, 6:7], C["tiny"][:], axis=AX.X))

        def bis_step(it, red_tok):
            cnt = p.t(lambda e: e.matmul(psC[:, 0:1], C["onesf"][:], bis[:, 6:7],
                                         start=True, stop=True), [red_tok])
            p.v(lambda e: e.tensor_single_scalar(bis[:, 4:5], psC[:, 0:1], K_SEL,
                                                 ALU.is_gt), [cnt])
            p.v(lambda e: e.tensor_single_scalar(bis[:, 5:6], psC[:, 0:1], K_SEL,
                                                 ALU.is_equal))
            p.v(lambda e: e.tensor_sub(bis[:, 7:8], bis[:, 2:3], bis[:, 3:4]))
            p.v(lambda e: e.scalar_tensor_tensor(bis[:, 3:4], bis[:, 7:8], bis[:, 5:6],
                                                 bis[:, 3:4], ALU.mult, ALU.add))
            p.v(lambda e: e.tensor_sub(bis[:, 7:8], bis[:, 2:3], bis[:, 0:1]))
            p.v(lambda e: e.scalar_tensor_tensor(bis[:, 0:1], bis[:, 7:8], bis[:, 4:5],
                                                 bis[:, 0:1], ALU.mult, ALU.add))
            p.v(lambda e: e.tensor_scalar(bis[:, 5:6], bis[:, 4:5], -1.0, 1.0,
                                          ALU.mult, ALU.add))
            p.v(lambda e: e.tensor_sub(bis[:, 7:8], bis[:, 2:3], bis[:, 1:2]))
            p.v(lambda e: e.scalar_tensor_tensor(bis[:, 1:2], bis[:, 7:8], bis[:, 5:6],
                                                 bis[:, 1:2], ALU.mult, ALU.add))
            if it + 1 < N_BIS:
                return bis_pre(it + 1)
            return None

        lastgelu = {0: None, 1: None}
        w2done = {0: None, 1: None}
        lasto2 = None
        last_etok = None
        seq = [(tq, f) for tq in range(8) for f in range(32)]

        def issue_w1(tq, f):
            hb = f % 2
            hw = [lastgelu[hb]] if lastgelu[hb] else []
            first = True
            qk_last = None
            for c in range(8):
                qk_last = p.t(lambda e, tq=tq, f=f, c=c, hb=hb: e.matmul(
                    psH[:, 512 * hb:512 * hb + 256],
                    W1[:, 4096 * c + 128 * f:4096 * c + 128 * (f + 1)],
                    u2T[:, 2048 * c + 256 * tq:2048 * c + 256 * (tq + 1)],
                    start=(c == 0), stop=(c == 7), skip_group_check=True),
                    hw if first else ())
                first = False
            gw = [qk_last]
            if w2done[hb]:
                gw.append(w2done[hb])
            lastgelu[hb] = p.s(lambda e, f=f, hb=hb: e.activation(
                hbuf[:, 256 * hb:256 * (hb + 1)], psH[:, 512 * hb:512 * hb + 256],
                AF.Gelu_apprx_tanh, bias=b1c[:, f:f + 1]), gw)

        def issue_w2(tq, f):
            nonlocal lasto2, last_etok
            hb = f % 2
            w2w = [lastgelu[hb]]
            if f == 0 and last_etok is not None:
                w2w.append(last_etok)
            for m in range(8):
                p.t(lambda e, f=f, m=m, hb=hb: e.matmul(
                    psOm[m], W2[:, 1024 * f + 128 * m:1024 * f + 128 * (m + 1)],
                    hbuf[:, 256 * hb:256 * (hb + 1)],
                    start=(f == 0 and m % 2 == 0), stop=(f == 31 and m % 2 == 1),
                    skip_group_check=True),
                    w2w if m == 0 else ())
            w2done[hb] = p.tok("pe")[0]
            if f != 31:
                return
            w2fin = w2done[hb]
            for m in range(8):
                evw = [w2fin]
                if lasto2:
                    evw.append(lasto2)
                etok = p.v(lambda e, m=m: e.tensor_scalar(
                    o2f[:], psOm[m], b2c[:, m:m + 1], None, ALU.add), evw)
                last_etok = etok
                for half in range(2):
                    p.t(lambda e, half=half: e.transpose(
                        pbt[:, 128 * half:128 * (half + 1)],
                        o2f[:, 128 * half:128 * (half + 1)], C["eyeb"][:]),
                        [etok] if half == 0 else ())
                lasto2 = p.v(lambda e, m=m: e.tensor_copy(
                    o2tok.rearrange("p (tt n) -> p tt n", n=1024)[:, 0:2,
                                                                 128 * m:128 * (m + 1)],
                    pbt[:, 0:256].rearrange("p (b n) -> p b n", n=128)),
                    list(p.tok("pe")) + list(p.tok("qs")))
            for tt in range(2):
                p.ds(lambda e, tq=tq, tt=tt: e.dma_start(
                    o2d[128 * (2 * tq + tt):128 * (2 * tq + tt + 1), :],
                    o2tok[:, 1024 * tt:1024 * (tt + 1)]), p.tok("dve"))

        red_tok = bis_pre(0)
        bis_it = 0
        for idx, (tq, f) in enumerate(seq):
            issue_w1(tq, f)
            if idx >= 1:
                issue_w2(*seq[idx - 1])
            if idx >= 1 and bis_it < N_BIS:
                red_tok = bis_step(bis_it, red_tok)
                bis_it += 1
        issue_w2(*seq[-1])
        p.v(lambda e: e.tensor_single_scalar(bis[:, 4:5], bis[:, 3:4], 1.0e38, ALU.is_lt))
        p.v(lambda e: e.tensor_sub(bis[:, 7:8], bis[:, 3:4], bis[:, 2:3]))
        p.v(lambda e: e.scalar_tensor_tensor(bis[:, 6:7], bis[:, 7:8], bis[:, 4:5],
                                             bis[:, 2:3], ALU.mult, ALU.add))
        p.v(lambda e: e.tensor_copy(bis[:, 3:4], bis[:, 6:7]))
        p.v(lambda e: e.tensor_single_scalar(C["tiny"][:, 16:32], scloc[:], bis[:, 3:4],
                                             ALU.is_gt))
        bis_tok = p.tok("dve")
        p.barrier()
        stA.close()

        p.s(lambda e: e.activation(gates[:], scloc[:], AF.Sigmoid), bis_tok)
        gtok = p.v(lambda e: e.tensor_tensor(gates[:], gates[:], C["tiny"][:, 16:32],
                                             ALU.mult), p.tok("act"))

        ub3 = st.enter_context(nc.sbuf_tensor("f_ub3", [128, 3 * 1024], BF16))
        o23 = st.enter_context(nc.sbuf_tensor("f_o23", [128, 3 * 1024], BF16))
        cb3 = st.enter_context(nc.sbuf_tensor("f_cb3", [128, 3 * 1024], F32))

        def mk_src_l0f(t, s, xin, xdt, wfree):
            utok = p.s(lambda e, t=t, s=s, xin=xin: e.activation(
                ub3[:, 1024 * s:1024 * (s + 1)], xin, AF.Identity,
                bias=C["mrs"][:, 4 * t + 3:4 * t + 4],
                scale=C["mrs"][:, 4 * t + 2:4 * t + 3]), [xdt])
            otok = p.ds(lambda e, t=t, s=s: e.dma_start(
                o23[:, 1024 * s:1024 * (s + 1)], o2d[128 * t:128 * (t + 1), :]), wfree)
            ctok = p.v(lambda e, t=t, s=s: e.scalar_tensor_tensor(
                cb3[:, 1024 * s:1024 * (s + 1)], o23[:, 1024 * s:1024 * (s + 1)],
                gates[:, t:t + 1], ub3[:, 1024 * s:1024 * (s + 1)],
                ALU.mult, ALU.add), [utok, otok, gtok])
            return cb3[:, 1024 * s:1024 * (s + 1)], ctok

        ln_resid_pipe(p, C, nc, st, "f", NT, mk_src_l0f, x_src, x_dst)
        p.barrier()


def phase_l1a(p, C, nc, dr, x_src, x_dst, cio):
    with ExitStack() as st:
        sb = lambda n, shp, dt: st.enter_context(nc.sbuf_tensor(n, shp, dt))
        ps = lambda n, shp, dt=F32: st.enter_context(nc.psum_tensor(n, shp, dt))
        masks = sb("c_masks", [128, 4 * 512], BF16)
        xt = sb("c_xt", [128, 2 * 1024], F32)
        wqp = sb("c_wqp", [128, 8 * 128], BF16)
        wkp = sb("c_wkp", [128, 8 * 128], BF16)
        wvpp = sb("c_wvp", [128, 8 * 128], BF16)
        woT = sb("c_woT", [128, 2 * 1024], BF16)
        QTp = sb("c_QTp", [128, 8192], BF16)
        KT0 = sb("c_KT0", [128, 8192], BF16)
        KT1 = sb("c_KT1", [128, 8192], BF16)
        VPp = sb("c_VPp", [128, 64 * 130], BF16)
        AT0 = sb("c_AT0", [128, 8192], BF16)
        AT1 = sb("c_AT1", [128, 8192], BF16)
        sq = sb("c_sq", [128, 2048], F32)
        crow = sb("c_crow", [1, 16], F32)
        bias_c = sb("c_biasc", [128, 1], F32)
        Pb = sb("c_Pb", [128, 2 * 1024], BF16)
        rsumZ = sb("c_rsumZ", [128, 2 * 512], BF16)
        rsumf = sb("c_rsumf", [1, 2 * 512], F32)
        bcastS = sb("c_bcastS", [64, 2 * 512], BF16)
        ytile = sb("c_ytile", [128, 1024], BF16)
        stP = ExitStack()
        pbt = stP.enter_context(nc.psum_tensor("c_pbt", [128, 2 * 1024], BF16))
        ub = stP.enter_context(nc.sbuf_tensor("c_ub", [128, 2 * 1024], BF16))
        u3T = stP.enter_context(nc.sbuf_tensor("c_u3T", [128, 8 * 2048], BF16))

        for i in range(4):
            p.dg(lambda e, i=i: e.dma_start(masks[:, 512 * i:512 * (i + 1)],
                                            dr["l1_masks"][128 * i:128 * (i + 1), :]))
        for c in range(2):
            p.dg(lambda e, c=c: e.dma_start(woT[:, 1024 * c:1024 * (c + 1)],
                                            dr["l1_wo"][128 * c:128 * (c + 1), :]))
        p.v(lambda e: e.memset(VPp[:], 1.0))
        p.v(lambda e: e.memset(KT0[64:128, :], 0.0))
        p.v(lambda e: e.memset(KT1[0:64, :], 0.0))
        p.v(lambda e: e.memset(rsumZ[:], 0.0))
        p.barrier()

        dma_toks = {}
        last_use_ub = {0: None, 1: None}
        last_evT = {0: None, 1: None}
        for t in (0, 1):
            b = t % 2
            dma_toks[t] = p.ds(lambda e, t=t, b=b: e.dma_start(
                xt[:, 1024 * b:1024 * (b + 1)], x_src[128 * t:128 * (t + 1), :]))
        for t in range(NT):
            b = t % 2
            xin = xt[:, 1024 * b:1024 * (b + 1)]
            uin = ub[:, 1024 * b:1024 * (b + 1)]
            w = [dma_toks[t]]
            if last_use_ub[b]:
                w.append(last_use_ub[b])
            ntok = tile_ln(p, C, xin, t, uin, wait=w)
            if t + 2 < NT:
                dma_toks[t + 2] = p.ds(lambda e, t=t, b=b: e.dma_start(
                    xt[:, 1024 * b:1024 * (b + 1)],
                    x_src[128 * (t + 2):128 * (t + 3), :]), [ntok])
            trw = [ntok]
            if last_evT[b]:
                trw.append(last_evT[b])
            for c in range(8):
                p.t(lambda e, c=c, b=b, uin=uin: e.transpose(
                    pbt[:, 1024 * b + 128 * c:1024 * b + 128 * (c + 1)],
                    uin[:, 128 * c:128 * (c + 1)], C["eyeb"][:]), trw if c == 0 else ())
            last_use_ub[b] = p.tok("pe")[0]
            last_evT[b] = p.v(lambda e, t=t, b=b: e.tensor_copy(
                u3T.rearrange("p (c n) -> p c n", n=2048)[:, :, 128 * t:128 * (t + 1)],
                pbt[:, 1024 * b:1024 * (b + 1)].rearrange("p (c n) -> p c n", n=128)),
                p.tok("pe"))
        for c in range(8):
            p.ds(lambda e, c=c: e.dma_start(
                cio["ag_in"][128 * c:128 * (c + 1), :], u3T[:, 2048 * c:2048 * (c + 1)]),
                p.tok("dve"))
        p.ccop(lambda e: e.collective_compute(
            "AllGather", ALU.bypass, GROUPS, ins=[cio["ag_in"].opt()],
            outs=[cio["ag_out"].opt()]), p.tok("qs"))
        p.barrier()
        stP.close()
        xb = st.enter_context(nc.sbuf_tensor("c_xb", [128, 8 * 2048], BF16))
        psS4 = st.enter_context(nc.psum_tensor("c_psS4", [128, 2048], F32))
        psAcc2 = st.enter_context(nc.psum_tensor("c_psAcc2", [128, 1024], F32))
        psB = st.enter_context(nc.psum_tensor("c_psB", [128, 512], F32))
        psV = st.enter_context(nc.psum_tensor("c_psV", [128, 256], F32))

        wsafe = [None]
        for hp in range(2):
            AT = AT0 if hp == 0 else AT1
            wdma = [wsafe[0]] if wsafe[0] is not None else []
            for wsb, nm in ((wqp, "l1_wq"), (wkp, "l1_wk"), (wvpp, "l1_wv")):
                for c in range(8):
                    p.dg(lambda e, wsb=wsb, nm=nm, c=c, hp=hp: e.dma_start(
                        wsb[:, 128 * c:128 * (c + 1)],
                        dr[nm][128 * c:128 * (c + 1), 128 * hp:128 * (hp + 1)]),
                        wdma if (wsb is wqp and c == 0) else ())
            sr_tok = {}

            def load_sr(sr, w):
                r, th = divmod(sr, 2)
                b = sr % 2
                for c in range(8):
                    p.dg(lambda e, r=r, th=th, b=b, c=c: e.dma_start(
                        xb[:, 8192 * b + 1024 * c:8192 * b + 1024 * (c + 1)],
                        cio["ag_out"][1024 * r + 128 * c:1024 * r + 128 * (c + 1),
                                      1024 * th:1024 * (th + 1)]),
                        w if c == 0 else ())
                sr_tok[sr] = p.tok("qg")[0]

            load_sr(0, [])
            load_sr(1, [])
            for sr in range(8):
                r, th = divmod(sr, 2)
                b = sr % 2
                ldt = [sr_tok[sr]]
                for dst, wsb in (("q", wqp), ("k", wkp)):
                    for n2 in range(2):
                        gw = list(ldt) + list(p.tok("act"))
                        for c in range(8):
                            p.t(lambda e, wsb=wsb, b=b, n2=n2, c=c: e.matmul(
                                psS4[:, 0:512], wsb[:, 128 * c:128 * (c + 1)],
                                xb[:, 8192 * b + 1024 * c + 512 * n2:
                                   8192 * b + 1024 * c + 512 * (n2 + 1)],
                                start=(c == 0), stop=(c == 7)), gw if c == 0 else ())
                        seg = 2048 * r + 1024 * th + 512 * n2
                        if dst == "q":
                            p.s(lambda e, seg=seg: e.activation(
                                QTp[:, seg:seg + 512],
                                psS4[:, 0:512], AF.Identity), p.tok("pe"))
                        else:
                            p.s(lambda e, seg=seg: e.activation(
                                KT0[0:64, seg:seg + 512],
                                psS4[0:64, 0:512], AF.Identity), p.tok("pe"))
                            p.s(lambda e, seg=seg: e.activation(
                                KT1[64:128, seg:seg + 512],
                                psS4[64:128, 0:512], AF.Identity))
                for ts2 in range(8):
                    tglob = 16 * r + 8 * th + ts2
                    bank = psV[:, 0:128] if ts2 % 2 == 0 else psB[:, 0:128]
                    vw = list(ldt) + list(p.tok("act"))
                    for c in range(8):
                        p.t(lambda e, ts2=ts2, b=b, c=c, bank=bank: e.matmul(
                            bank,
                            xb[:, 8192 * b + 1024 * c + 128 * ts2:
                               8192 * b + 1024 * c + 128 * (ts2 + 1)],
                            wvpp[:, 128 * c:128 * (c + 1)],
                            start=(c == 0), stop=(c == 7)), vw if c == 0 else ())
                    p.s(lambda e, tglob=tglob, bank=bank: e.activation(
                        VPp.rearrange("p (tt h n) -> p tt h n", h=2, n=65)[:, tglob, :, 0:64],
                        bank.rearrange("p (h n) -> p h n", n=64), AF.Identity),
                        p.tok("pe"))
                if sr + 2 < 8:
                    load_sr(sr + 2, list(p.tok("pe")))
            wsafe[0] = p.tok("pe")[0]
            p.barrier()
            QTv = QTp.rearrange("p (c n) -> p c n", c=1)
            norm_bound(p, C, QTv, 8192, sq,
                       [psS4[:, 0:512], psS4[:, 512:1024], psS4[:, 1024:1536],
                        psS4[:, 1536:2048]], 8, crow, nch=1)
            norm_bound(p, C, KT0.rearrange("p (c n) -> p c n", c=1), 8192, sq,
                       [psS4[:, 0:512], psS4[:, 512:1024], psS4[:, 1024:1536],
                        psS4[:, 1536:2048]], 9, crow, nch=1)
            norm_bound(p, C, KT1.rearrange("p (c n) -> p c n", c=1), 8192, sq,
                       [psS4[:, 0:512], psS4[:, 512:1024], psS4[:, 1024:1536],
                        psS4[:, 1536:2048]], 14, crow, nch=1)
            p.v(lambda e: e.tensor_tensor(crow[0:1, 9:10], crow[0:1, 9:10],
                                          crow[0:1, 14:15], ALU.max))
            p.v(lambda e: e.tensor_tensor(crow[0:1, 10:11], crow[0:1, 8:9], crow[0:1, 9:10], ALU.mult))
            p.s(lambda e: e.activation(crow[0:1, 11:12], crow[0:1, 10:11], AF.Ln), p.tok("dve"))
            p.s(lambda e: e.activation(crow[0:1, 12:13], crow[0:1, 11:12], AF.Exp, scale=0.5))
            p.v(lambda e: e.tensor_scalar_mul(crow[0:1, 13:14], crow[0:1, 12:13], -0.125),
                p.tok("act"))
            bias_col_from(p, C, crow, 13, bias_c[:, 0:1], psB, None)
            p.barrier()
            for h2 in range(2):
                items = []
                for qt in range(16):
                    for kt in range(4 * qt + 4):
                        items.append((qt, kt))
                groups = [items[i:i + 2] for i in range(0, len(items), 2)]
                G = len(groups)
                qk_tok = {}
                exp_tok = {}
                av_tok = {}
                qt_done = {}

                KTsel = KT0 if h2 == 0 else KT1

                def issue_qk(g, KTsel=KTsel):
                    for i, (qt, kt) in enumerate(groups[g]):
                        w = []
                        if i == 0 and g >= 2:
                            w.append(exp_tok[g - 2])
                        qk_tok[g] = p.t(lambda e, kt=kt, qt=qt, g=g, i=i, KTsel=KTsel: e.matmul(
                            psS4[:, 1024 * (g % 2) + 512 * i:1024 * (g % 2) + 512 * (i + 1)],
                            KTsel[:, 128 * kt:128 * (kt + 1)],
                            QTp[:, 512 * qt:512 * (qt + 1)],
                            start=True, stop=True), w)

                def issue_exp(g):
                    w = [qk_tok[g]]
                    if g >= 2:
                        w.append(av_tok[g - 2])
                    tok = p.s(lambda e, g=g: e.activation(
                        Pb[:, 1024 * (g % 2):1024 * (g % 2) + 1024],
                        psS4[:, 1024 * (g % 2):1024 * (g % 2) + 1024],
                        AF.Exp, scale=0.125, bias=bias_c[:, 0:1]), w)
                    qt0, kt0 = groups[g][0]
                    dlt0 = kt0 - 4 * qt0
                    if dlt0 >= 0:
                        tok = p.v(lambda e, g=g, dlt0=dlt0: e.tensor_tensor(
                            Pb[:, 1024 * (g % 2):1024 * (g % 2) + 1024],
                            Pb[:, 1024 * (g % 2):1024 * (g % 2) + 1024],
                            masks[:, 512 * dlt0:512 * dlt0 + 1024], ALU.mult), [tok])
                    exp_tok[g] = tok

                pending = []

                def issue_av(g, it):
                    for i, (qt, kt) in enumerate(groups[g]):
                        nkt = 4 * qt + 4
                        w = [exp_tok[g]] if i == 0 else []
                        if kt == 0 and qt >= 2:
                            w.append(qt_done[qt - 2])
                        av_tok[g] = p.t(lambda e, h2=h2, kt=kt, qt=qt, g=g, i=i, nkt=nkt: e.matmul(
                            psAcc2[0:65, 512 * (qt % 2):512 * (qt % 2) + 512],
                            VPp[:, 130 * kt + 65 * h2:130 * kt + 65 * h2 + 65],
                            Pb[:, 1024 * (g % 2) + 512 * i:1024 * (g % 2) + 512 * (i + 1)],
                            start=(kt == 0), stop=(kt == nkt - 1),
                            skip_group_check=True), w)
                        if kt == nkt - 1:
                            rt = p.v(lambda e, qt=qt: e.reciprocal(
                                rsumf[0:1, 512 * (qt % 2):512 * (qt % 2) + 512],
                                psAcc2[64:65, 512 * (qt % 2):512 * (qt % 2) + 512]),
                                [av_tok[g]])
                            rt = p.v(lambda e, qt=qt: e.tensor_copy(
                                rsumZ[0:1, 512 * (qt % 2):512 * (qt % 2) + 512],
                                rsumf[0:1, 512 * (qt % 2):512 * (qt % 2) + 512]), [rt])
                            pending.append((qt, rt, it))

                def flush_tail(it, h2=h2, AT=AT):
                    while pending and pending[0][2] <= it - 2:
                        qt, rt, _ = pending.pop(0)
                        bw = [rt]
                        if qt >= 1:
                            bw.append(qt_done[qt - 1])
                        bt = p.t(lambda e, qt=qt: e.matmul(
                            psB[0:64, :], C["onesb"][:, 0:64],
                            rsumZ[:, 512 * (qt % 2):512 * (qt % 2) + 512],
                            start=True, stop=True), bw)
                        bt = p.v(lambda e, qt=qt: e.tensor_copy(
                            bcastS[:, 512 * (qt % 2):512 * (qt % 2) + 512],
                            psB[0:64, :]), [bt])
                        qt_done[qt] = p.v(lambda e, h2=h2, qt=qt, AT=AT: e.tensor_tensor(
                            AT[64 * h2:64 * (h2 + 1), 512 * qt:512 * (qt + 1)],
                            psAcc2[0:64, 512 * (qt % 2):512 * (qt % 2) + 512],
                            bcastS[:, 512 * (qt % 2):512 * (qt % 2) + 512],
                            ALU.mult), [bt])

                for g in range(G):
                    issue_qk(g)
                    issue_exp(g)
                    if g >= 1:
                        issue_av(g - 1, g)
                    flush_tail(g)
                issue_av(G - 1, G)
                flush_tail(G + 2)
            p.barrier()
        lastst = None
        for ts in range(64):
            mw = list(p.tok("dve", "act"))
            for nh in range(2):
                for pr, AT in ((0, AT0), (1, AT1)):
                    p.t(lambda e, ts=ts, nh=nh, pr=pr, AT=AT: e.matmul(
                        psS4[:, 512 * nh:512 * (nh + 1)],
                        AT[:, 128 * ts:128 * (ts + 1)],
                        woT[:, 1024 * pr + 512 * nh:1024 * pr + 512 * (nh + 1)],
                        start=(pr == 0), stop=(pr == 1)), mw if (nh == 0 and pr == 0) else ())
            evw = list(p.tok("pe"))
            if lastst:
                evw.append(lastst)
            p.s(lambda e: e.activation(ytile[:, 0:512], psS4[:, 0:512], AF.Identity), evw)
            p.s(lambda e: e.activation(ytile[:, 512:1024], psS4[:, 512:1024], AF.Identity))
            lastst = p.ds(lambda e, ts=ts: e.dma_start(
                cio["rs_in"][128 * ts:128 * (ts + 1), :], ytile[:]), p.tok("act"))
        p.ccop(lambda e: e.collective_compute(
            "ReduceScatter", ALU.add, GROUPS, ins=[cio["rs_in"].opt()],
            outs=[cio["rs_out"].opt()]), p.tok("qs"))
        p.barrier()
        cc_tok = [p.tok("cc")[0]]
        ybf3 = st.enter_context(nc.sbuf_tensor("c_ybf3", [128, 3 * 1024], BF16))

        def mk_src_l1a(t, s, xin, xdt, wfree):
            ydt = p.ds(lambda e, t=t, s=s: e.dma_start(
                ybf3[:, 1024 * s:1024 * (s + 1)],
                cio["rs_out"][128 * t:128 * (t + 1), :]), wfree + cc_tok)
            return ybf3[:, 1024 * s:1024 * (s + 1)], ydt

        ln_resid_pipe(p, C, nc, st, "c", NT, mk_src_l1a, x_src, x_dst, nslot=2)
        p.barrier()


def phase_l1f(p, C, nc, dr, x_src, x_dst, cio):
    with ExitStack() as st:
        sb = lambda n, shp, dt: st.enter_context(nc.sbuf_tensor(n, shp, dt))
        ps = lambda n, shp, dt=F32: st.enter_context(nc.psum_tensor(n, shp, dt))
        xt = sb("d_xt", [128, 2 * 1024], F32)
        uf = sb("d_uf", [128, 1024], F32)
        uhi = sb("d_uhi", [128, 1024], BF16)
        ulo = sb("d_ulo", [128, 1024], BF16)
        alpha = sb("d_alpha", [128, 16 * 64], BF16)
        alphaT = sb("d_alphaT", [128, 2048], BF16)
        mcol = sb("d_mcol", [128, 8], BF16)
        taus = sb("d_taus", [128, 4], F32)
        b1p = sb("d_b1p", [128, 512], BF16)
        sIT = sb("d_sIT", [128, 8 * 64], BF16)
        sITf = sb("d_sITf", [128, 8 * 64], F32)
        sIb = sb("d_sI", [64, 1024], BF16)
        si = sb("d_si", [64, 1028], F32)
        sid_b = sb("d_sidb", [128, 1024], BF16)
        siT = sb("d_siT", [128, 8 * 64], BF16)
        SnT = sb("d_SnT", [128, 8 * 64], F32)
        SnTb = sb("d_SnTb", [128, 8 * 64], BF16)
        qT = sb("d_qT", [128, 2 * 64], BF16)
        kT = sb("d_kT", [128, 2 * 64], BF16)
        scb = sb("d_scb", [128, 64], BF16)
        scq = sb("d_scq", [64, 64], F32)
        attnSb = sb("d_attnSb", [128, 64], BF16)
        attnTb = sb("d_attnTb", [128, 64], BF16)
        vs = sb("d_vs", [128, 1024], BF16)
        ySlT = sb("d_ySlT", [128, 8 * 64], F32)
        ySl = sb("d_ySl", [64, 1024], F32)
        yhat = sb("d_yhat", [128, 1024], BF16)
        yhT = sb("d_yhT", [128, 8 * 64], BF16)
        sv = sb("d_sv", [128, 1024], BF16)
        yb = sb("d_yb", [128, 1024], F32)
        pbt = ps("d_pbt", [128, 2 * 1024], BF16)
        pbf = ps("d_pbf", [128, 2 * 512])
        psHM = ps("d_psHM", [128, 512])
        psX = ps("d_psX", [128, 64])
        psSI = ps("d_psSI", [128, 2 * 512])
        psH = psHM
        psL = psX
        psM = psHM
        psD = psX

        for t_ in (alphaT, sid_b, scb, attnSb, attnTb, vs, yhat, sv):
            p.v(lambda e, t_=t_: e.memset(t_[64:128, :], 0.0))
        p.dg(lambda e: e.dma_start(sIb[:], dr["f_sI"]))
        for c in range(8):
            p.dg(lambda e, c=c: e.dma_start(sIT[:, 64 * c:64 * (c + 1)],
                                            dr["f_sIT"][128 * c:128 * (c + 1), :]))
            p.ds(lambda e, c=c: e.dma_start(sITf[:, 64 * c:64 * (c + 1)],
                                            dr["f_sIT"][128 * c:128 * (c + 1), :]))
        p.ds(lambda e: e.dma_start(taus[0:1, 0:1], dr["f_tau"]))
        p.barrier()
        p.s(lambda e: e.activation(taus[0:1, 1:2], taus[0:1, 0:1], AF.Abs))
        p.v(lambda e: e.tensor_scalar_add(taus[0:1, 2:3], taus[0:1, 1:2], 0.1), p.tok("act"))
        p.v(lambda e: e.reciprocal(taus[0:1, 3:4], taus[0:1, 2:3]))
        p.t(lambda e: e.matmul(pbf[:, 0:1], C["onesf"][0:1, :], taus[0:1, 3:4],
                               start=True, stop=True), p.tok("dve"))
        p.v(lambda e: e.tensor_copy(taus[:, 0:1], pbf[:, 0:1]), p.tok("pe"))
        for c in range(8):
            p.t(lambda e, c=c: e.matmul(psM[:, 0:1], sIb[:, 128 * c:128 * (c + 1)],
                                        C["onesb"][0:64, 0:1], start=True, stop=True),
                p.tok("act"))
            p.s(lambda e, c=c: e.activation(mcol[:, c:c + 1], psM[:, 0:1], AF.Identity,
                                            scale=1.0 / 64.0), p.tok("pe"))
        p.barrier()

        stA = ExitStack()
        sba = lambda n, shp, dt: stA.enter_context(nc.sbuf_tensor(n, shp, dt))
        w1hi = sba("d_w1hi", [128, 8 * 512], BF16)
        w1lo = sba("d_w1lo", [128, 8 * 512], BF16)
        w1f = sba("d_w1f", [128, 512], F32)
        w1bot = sba("d_w1bot", [128, 8 * 512], BF16)
        w2f = sba("d_w2f", [128, 4 * 64], F32)
        hT = sba("d_hT", [128, 4 * 2048], F32)
        b1raw = sba("d_b1raw", [1, 512], BF16)
        b2r = sba("d_b2r", [128, 64], F32)
        logit = sba("d_logit", [128, 64], F32)
        ebuf = sba("d_ebuf", [128, 64], F32)
        mbuf = sba("d_mbuf", [128, 64], F32)
        mx8 = sba("d_mx8", [128, 8], F32)
        for c in range(8):
            p.dg(lambda e, c=c: e.dma_start(w1bot[:, 512 * c:512 * (c + 1)],
                                            dr["f_w1b"][128 * c:128 * (c + 1), :]))
        p.dg(lambda e: e.dma_start(b1raw[:], dr["f_b1"]))
        mzb = p.v(lambda e: e.memset(b2r[:], 0.0))
        p.ds(lambda e: e.dma_start(b2r[0:1, :], dr["f_b2"]), [mzb])
        for c in range(4):
            p.ds(lambda e, c=c: e.dma_start(w2f[:, 64 * c:64 * (c + 1)],
                                            dr["f_w2"][128 * c:128 * (c + 1), :]))
        for c in range(8):
            dt = p.ds(lambda e, c=c: e.dma_start(w1f[:], dr["f_w1t"][128 * c:128 * (c + 1), :]),
                      p.tok("dve"))
            p.v(lambda e, c=c: e.tensor_copy(w1hi[:, 512 * c:512 * (c + 1)], w1f[:]), [dt])
            p.v(lambda e, c=c: e.tensor_tensor(w1lo[:, 512 * c:512 * (c + 1)], w1f[:],
                                               w1hi[:, 512 * c:512 * (c + 1)], ALU.subtract))
        p.barrier()
        for c in range(8):
            p.t(lambda e, c=c: e.matmul(psM[0:1, 0:512], mcol[:, c:c + 1],
                                        w1bot[:, 512 * c:512 * (c + 1)],
                                        start=(c == 0), stop=False))
        p.t(lambda e: e.matmul(psM[0:1, 0:512], C["onesb"][0:1, 0:1], b1raw[0:1, :],
                               start=False, stop=True))
        mzp = p.v(lambda e: e.memset(b1p[:], 0.0))
        p.s(lambda e: e.activation(b1p[0:1, :], psM[0:1, 0:512], AF.Identity),
            list(p.tok("pe")) + [mzp])
        p.barrier()

        stA2 = ExitStack()
        uThi = stA2.enter_context(nc.sbuf_tensor("d_uThi", [128, 8 * 2048], BF16))
        uTlo = stA2.enter_context(nc.sbuf_tensor("d_uTlo", [128, 8 * 2048], BF16))
        dma_toks = {}
        last_lo = None
        last_hi_tr = None
        for t in (0, 1):
            b = t % 2
            dma_toks[t] = p.ds(lambda e, t=t, b=b: e.dma_start(
                xt[:, 1024 * b:1024 * (b + 1)], x_src[128 * t:128 * (t + 1), :]))
        for t in range(NT):
            b = t % 2
            xin = xt[:, 1024 * b:1024 * (b + 1)]
            w = [dma_toks[t]]
            if last_lo:
                w.append(last_lo)
            ntok = tile_ln(p, C, xin, t, uf[:], wait=w)
            if t + 2 < NT:
                dma_toks[t + 2] = p.ds(lambda e, t=t, b=b: e.dma_start(
                    xt[:, 1024 * b:1024 * (b + 1)],
                    x_src[128 * (t + 2):128 * (t + 3), :]), [ntok])
            hitok = p.v(lambda e: e.tensor_copy(uhi[:], uf[:]),
                        [ntok] + ([last_hi_tr] if last_hi_tr else []))
            lotok = p.v(lambda e: e.tensor_tensor(ulo[:], uf[:], uhi[:], ALU.subtract))
            for c in range(8):
                p.t(lambda e, c=c: e.transpose(pbt[:, 128 * c:128 * (c + 1)],
                                               uhi[:, 128 * c:128 * (c + 1)], C["eyeb"][:]),
                    [hitok] if c == 0 else ())
            last_hi_tr = p.tok("pe")[0]
            p.v(lambda e, t=t: e.tensor_copy(
                uThi.rearrange("p (c n) -> p c n", n=2048)[:, :, 128 * t:128 * (t + 1)],
                pbt[:, 0:1024].rearrange("p (c n) -> p c n", n=128)), p.tok("pe"))
            for c in range(8):
                p.t(lambda e, c=c: e.transpose(pbt[:, 1024 + 128 * c:1024 + 128 * (c + 1)],
                                               ulo[:, 128 * c:128 * (c + 1)], C["eyeb"][:]),
                    [lotok] if c == 0 else ())
            last_lo = p.v(lambda e, t=t: e.tensor_copy(
                uTlo.rearrange("p (c n) -> p c n", n=2048)[:, :, 128 * t:128 * (t + 1)],
                pbt[:, 1024:2048].rearrange("p (c n) -> p c n", n=128)), p.tok("pe"))
        p.barrier()

        lastg = None
        for fc in range(4):
            for n in range(4):
                hw = [lastg] if lastg else []
                first = True
                for c in range(8):
                    for wsb, usb, st_flag in ((w1hi, uThi, True), (w1lo, uThi, False),
                                              (w1hi, uTlo, False)):
                        p.t(lambda e, fc=fc, n=n, c=c, wsb=wsb, usb=usb, st_flag=st_flag: e.matmul(
                            psH[:], wsb[:, 512 * c + 128 * fc:512 * c + 128 * (fc + 1)],
                            usb[:, 2048 * c + 512 * n:2048 * c + 512 * (n + 1)],
                            start=(c == 0 and st_flag), stop=False), hw if first else ())
                        first = False
                p.t(lambda e, fc=fc: e.matmul(
                    psH[:], b1p[:, 128 * fc:128 * (fc + 1)], C["onesb"][:, 0:512],
                    start=False, stop=True))
                lastg = p.s(lambda e, fc=fc, n=n: e.activation(
                    hT[:, 2048 * fc + 512 * n:2048 * fc + 512 * (n + 1)], psH[:], AF.Gelu),
                    p.tok("pe"))
        p.barrier()
        lastal = None
        for t in range(NT):
            lw = list(p.tok("act", "dve"))
            for fc in range(4):
                p.t(lambda e, t=t, fc=fc: e.matmul(
                    psL[:], hT[:, 2048 * fc + 128 * t:2048 * fc + 128 * (t + 1)],
                    w2f[:, 64 * fc:64 * (fc + 1)],
                    start=(fc == 0), stop=False), lw if fc == 0 else ())
            p.t(lambda e: e.matmul(psL[:], C["onesf"][:, 0:128], b2r[:],
                                   start=False, stop=True))
            ltok = p.s(lambda e: e.activation(logit[:], psL[:], AF.Identity,
                                              scale=taus[:, 0:1]), p.tok("pe", "dve"))
            p.v(lambda e: e.max(out=mx8[:], in_=logit[:]), [ltok])
            p.v(lambda e: e.tensor_scalar_mul(C["tiny"][:, 0:1], mx8[:, 0:1], -1.0))
            etok = p.s(lambda e: e.activation(ebuf[:], logit[:], AF.Exp,
                                              bias=C["tiny"][:, 0:1]), p.tok("dve"))
            p.v(lambda e: e.tensor_single_scalar(mbuf[:], logit[:], mx8[:, 7:8], ALU.is_ge),
                [etok])
            p.v(lambda e: e.tensor_tensor(ebuf[:], ebuf[:], mbuf[:], ALU.mult))
            p.v(lambda e: e.reduce_sum(C["tiny"][:, 1:2], ebuf[:], axis=AX.X))
            p.v(lambda e: e.reciprocal(C["tiny"][:, 2:3], C["tiny"][:, 1:2]))
            altok = p.v(lambda e, t=t: e.tensor_scalar(
                alpha[:, 64 * t:64 * (t + 1)], ebuf[:], C["tiny"][:, 2:3], None, ALU.mult))
            trw = [altok]
            if lastal:
                trw.append(lastal)
            p.t(lambda e, t=t: e.transpose(pbt[0:64, 0:128], alpha[:, 64 * t:64 * (t + 1)],
                                           C["eyeb"][:]), trw)
            lastal = p.v(lambda e, t=t: e.tensor_copy(
                alphaT[0:64, 128 * t:128 * (t + 1)], pbt[0:64, 0:128]), p.tok("pe"))
        p.barrier()
        stA2.close()
        stA.close()

        p.v(lambda e: e.memset(si[:], 0.0))
        lastu = None
        for t in range(NT):
            b = t % 2
            w = list(p.tok("dve"))
            dtok = p.ds(lambda e, t=t, b=b: e.dma_start(
                xt[:, 1024 * b:1024 * (b + 1)], x_src[128 * t:128 * (t + 1), :]), w)
            xin = xt[:, 1024 * b:1024 * (b + 1)]
            uw = [dtok]
            if lastu:
                uw.append(lastu)
            utok = p.s(lambda e, xin=xin, t=t: e.activation(
                uhi[:], xin, AF.Identity, bias=C["mrs"][:, 4 * t + 3:4 * t + 4],
                scale=C["mrs"][:, 4 * t + 2:4 * t + 3]), uw)
            for half in range(2):
                p.t(lambda e, t=t, half=half: e.matmul(
                    psSI[0:64, 512 * half:512 * (half + 1)],
                    alpha[:, 64 * t:64 * (t + 1)],
                    uhi[:, 512 * half:512 * (half + 1)],
                    start=(t == 0), stop=(t == NT - 1), skip_group_check=True),
                    [utok] if half == 0 else ())
            p.t(lambda e, t=t: e.matmul(
                psD[0:64, 0:1], alpha[:, 64 * t:64 * (t + 1)], C["onesb"][:, 0:1],
                start=(t == 0), stop=(t == NT - 1), skip_group_check=True))
            lastu = p.tok("pe")[0]
        p.s(lambda e: e.activation(si[0:64, 0:512], psSI[0:64, 0:512], AF.Identity), p.tok("pe"))
        p.s(lambda e: e.activation(si[0:64, 512:1024], psSI[0:64, 512:1024], AF.Identity))
        p.s(lambda e: e.activation(si[0:64, 1024:1025], psD[0:64, 0:1], AF.Identity))
        p.ds(lambda e: e.dma_start(cio["ar_in"], si[0:64, :]), p.tok("act"))
        p.ccop(lambda e: e.collective_compute(
            "AllReduce", ALU.add, GROUPS, ins=[cio["ar_in"].opt()],
            outs=[cio["ar_out"].opt()]), p.tok("qs"))
        p.ds(lambda e: e.dma_start(si[0:64, :], cio["ar_out"]), p.tok("cc"))
        p.barrier()

        p.v(lambda e: e.tensor_scalar_add(si[0:64, 1024:1025], si[0:64, 1024:1025], 1e-8))
        p.v(lambda e: e.reciprocal(si[0:64, 1025:1026], si[0:64, 1024:1025]))
        p.v(lambda e: e.tensor_scalar(sid_b[0:64, :], si[0:64, 0:1024],
                                      si[0:64, 1025:1026], None, ALU.mult))
        for c in range(8):
            p.t(lambda e, c=c: e.transpose(pbt[:, 128 * c:128 * (c + 1)],
                                           sid_b[:, 128 * c:128 * (c + 1)],
                                           C["eyeb"][:]),
                p.tok("dve") if c == 0 else ())
        p.v(lambda e: e.tensor_copy(
            siT.rearrange("p (c n) -> p c n", n=64),
            pbt[:, 0:1024].rearrange("p (c n) -> p c n", n=128)[:, :, 0:64]), p.tok("pe"))
        giT = sb("d_giT", [128, 24 * 64], F32)
        ghT = sb("d_ghT", [128, 24 * 64], F32)
        rz = sb("d_rz", [128, 16 * 64], F32)
        gn = sb("d_gn", [128, 8 * 64], F32)
        stC1 = ExitStack()
        wih = stC1.enter_context(nc.sbuf_tensor("d_wih", [128, 8 * 3072], BF16))
        bihr = stC1.enter_context(nc.sbuf_tensor("d_bihr", [128, 3072], BF16))
        mzi = p.v(lambda e: e.memset(bihr[:], 0.0))
        for c in range(8):
            p.dg(lambda e, c=c: e.dma_start(wih[:, 3072 * c:3072 * (c + 1)],
                                            dr["f_wihT"][128 * c:128 * (c + 1), :]))
        p.dg(lambda e: e.dma_start(bihr[0:1, :], dr["f_bih"]), [mzi])
        p.barrier()
        for g in range(24):
            gw = list(p.tok("dve", "act"))
            for c in range(8):
                p.t(lambda e, g=g, c=c: e.matmul(
                    psM[:, 0:64], wih[:, 3072 * c + 128 * g:3072 * c + 128 * (g + 1)],
                    siT[:, 64 * c:64 * (c + 1)], start=(c == 0), stop=False),
                    gw if c == 0 else ())
            p.t(lambda e, g=g: e.matmul(psM[:, 0:64], bihr[:, 128 * g:128 * (g + 1)],
                                        C["onesb"][:, 0:64], start=False, stop=True))
            p.s(lambda e, g=g: e.activation(giT[:, 64 * g:64 * (g + 1)], psM[:, 0:64],
                                            AF.Identity), p.tok("pe"))
        p.barrier()
        stC1.close()
        stC2 = ExitStack()
        whh = stC2.enter_context(nc.sbuf_tensor("d_whh", [128, 8 * 3072], BF16))
        bhhr = stC2.enter_context(nc.sbuf_tensor("d_bhhr", [128, 3072], BF16))
        mzh = p.v(lambda e: e.memset(bhhr[:], 0.0))
        for c in range(8):
            p.dg(lambda e, c=c: e.dma_start(whh[:, 3072 * c:3072 * (c + 1)],
                                            dr["f_whhT"][128 * c:128 * (c + 1), :]))
        p.dg(lambda e: e.dma_start(bhhr[0:1, :], dr["f_bhh"]), [mzh])
        p.barrier()
        for g in range(24):
            gw = list(p.tok("act"))
            for c in range(8):
                p.t(lambda e, g=g, c=c: e.matmul(
                    psM[:, 0:64], whh[:, 3072 * c + 128 * g:3072 * c + 128 * (g + 1)],
                    sIT[:, 64 * c:64 * (c + 1)], start=(c == 0), stop=False),
                    gw if c == 0 else ())
            p.t(lambda e, g=g: e.matmul(psM[:, 0:64], bhhr[:, 128 * g:128 * (g + 1)],
                                        C["onesb"][:, 0:64], start=False, stop=True))
            p.s(lambda e, g=g: e.activation(ghT[:, 64 * g:64 * (g + 1)], psM[:, 0:64],
                                            AF.Identity), p.tok("pe"))
        p.barrier()
        stC2.close()
        stC3 = ExitStack()
        sbc = lambda n, shp, dt: stC3.enter_context(nc.sbuf_tensor(n, shp, dt))
        wqs = sbc("d_wqs", [128, 8 * 256], BF16)
        wks = sbc("d_wks", [128, 8 * 256], BF16)
        wvs = sbc("d_wvs", [128, 8 * 1024], BF16)
        bqr = sbc("d_bqr", [128, 256], BF16)
        bkr = sbc("d_bkr", [128, 256], BF16)
        bvr = sbc("d_bvr", [128, 1024], BF16)
        gwvp = sbc("d_gwvp", [128, 8 * 1024], BF16)
        wvpf = sbc("d_wvpf", [128, 1024], F32)
        lngc = sbc("d_lngc", [128, 8], F32)
        lnbc = sbc("d_lnbc", [128, 8], F32)
        lng8 = sbc("d_lng8", [8, 128], F32)
        lnb8 = sbc("d_lnb8", [8, 128], F32)
        cr1 = sbc("d_cr1", [1, 1024], F32)
        crb = sbc("d_crb", [128, 1024], BF16)
        bvpr = sbc("d_bvpr", [128, 1024], F32)
        for c in range(8):
            p.dg(lambda e, c=c: e.dma_start(wqs[:, 256 * c:256 * (c + 1)],
                                            dr["f_wq"][128 * c:128 * (c + 1), :]))
            p.dg(lambda e, c=c: e.dma_start(wks[:, 256 * c:256 * (c + 1)],
                                            dr["f_wk"][128 * c:128 * (c + 1), :]))
            p.dg(lambda e, c=c: e.dma_start(wvs[:, 1024 * c:1024 * (c + 1)],
                                            dr["f_wv"][128 * c:128 * (c + 1), :]))
        mzc = p.v(lambda e: e.memset(bqr[:], 0.0))
        mzc = p.v(lambda e: e.memset(bkr[:], 0.0), [mzc])
        mzc = p.v(lambda e: e.memset(bvr[:], 0.0), [mzc])
        mzc = p.v(lambda e: e.memset(crb[:], 0.0), [mzc])
        mzc = p.v(lambda e: e.memset(bvpr[:], 0.0), [mzc])
        p.dg(lambda e: e.dma_start(bqr[0:1, :], dr["f_bq"]), [mzc])
        p.dg(lambda e: e.dma_start(bkr[0:1, :], dr["f_bk"]))
        p.dg(lambda e: e.dma_start(bvr[0:1, :], dr["f_bv"]))
        p.ds(lambda e: e.dma_start(lng8[:], dr["f_lng8"]))
        p.ds(lambda e: e.dma_start(lnb8[:], dr["f_lnb8"]))
        p.ds(lambda e: e.dma_start(bvpr[0:1, :], dr["f_bvp"]), [mzc])
        p.barrier()
        p.v(lambda e: e.tensor_add(rz[:, 0:1024], giT[:, 0:1024], ghT[:, 0:1024]))
        p.s(lambda e: e.activation(rz[:, 0:1024], rz[:, 0:1024], AF.Sigmoid), p.tok("dve"))
        p.v(lambda e: e.tensor_tensor(gn[:], rz[:, 0:512], ghT[:, 1024:1536], ALU.mult),
            p.tok("act"))
        p.v(lambda e: e.tensor_add(gn[:], gn[:], giT[:, 1024:1536]))
        p.s(lambda e: e.activation(gn[:], gn[:], AF.Tanh), p.tok("dve"))
        p.v(lambda e: e.tensor_tensor(SnT[:], sITf[:], gn[:], ALU.subtract), p.tok("act"))
        p.v(lambda e: e.tensor_tensor(SnT[:], SnT[:], rz[:, 512:1024], ALU.mult))
        p.v(lambda e: e.tensor_add(SnT[:], SnT[:], gn[:]))
        p.v(lambda e: e.tensor_copy(SnTb[:], SnT[:]))
        for dst, wsb, bsb in ((qT, wqs, bqr), (kT, wks, bkr)):
            for f2 in range(2):
                gw = list(p.tok("dve", "act"))
                for c in range(8):
                    p.t(lambda e, f2=f2, c=c, wsb=wsb: e.matmul(
                        psM[:, 0:64], wsb[:, 256 * c + 128 * f2:256 * c + 128 * (f2 + 1)],
                        SnTb[:, 64 * c:64 * (c + 1)], start=(c == 0), stop=False),
                        gw if c == 0 else ())
                p.t(lambda e, f2=f2, bsb=bsb: e.matmul(
                    psM[:, 0:64], bsb[:, 128 * f2:128 * (f2 + 1)], C["onesb"][:, 0:64],
                    start=False, stop=True))
                p.s(lambda e, f2=f2, dst=dst: e.activation(
                    dst[:, 64 * f2:64 * (f2 + 1)], psM[:, 0:64], AF.Identity), p.tok("pe"))
        saw = list(p.tok("act"))
        for f2 in range(2):
            p.t(lambda e, f2=f2: e.matmul(
                psM[0:64, 0:64], kT[:, 64 * f2:64 * (f2 + 1)], qT[:, 64 * f2:64 * (f2 + 1)],
                start=(f2 == 0), stop=(f2 == 1)), saw if f2 == 0 else ())
        p.s(lambda e: e.activation(scb[0:64, :], psM[0:64, 0:64], AF.Identity), p.tok("pe"))
        p.t(lambda e: e.transpose(pbt[0:64, 0:128], scb[:], C["eyeb"][:]), p.tok("act"))
        p.v(lambda e: e.tensor_copy(scq[:], pbt[0:64, 0:64]), p.tok("pe"))
        p.v(lambda e: e.reduce_max(C["tiny"][0:64, 0:1], scq[:], axis=AX.X, negate=True))
        p.v(lambda e: e.tensor_scalar_mul(C["tiny"][0:64, 1:2], C["tiny"][0:64, 0:1], 0.0625))
        p.s(lambda e: e.activation(scq[:], scq[:], AF.Exp, scale=0.0625,
                                   bias=C["tiny"][0:64, 1:2]), p.tok("dve"))
        p.v(lambda e: e.reduce_sum(C["tiny"][0:64, 2:3], scq[:], axis=AX.X), p.tok("act"))
        p.v(lambda e: e.reciprocal(C["tiny"][0:64, 3:4], C["tiny"][0:64, 2:3]))
        p.v(lambda e: e.tensor_scalar(attnSb[0:64, :], scq[:], C["tiny"][0:64, 3:4],
                                      None, ALU.mult))
        p.t(lambda e: e.transpose(pbt[0:64, 0:128], attnSb[:], C["eyeb"][:]),
            p.tok("dve"))
        p.v(lambda e: e.tensor_copy(attnTb[0:64, :], pbt[0:64, 0:64]), p.tok("pe"))
        for n in range(2):
            gw = list(p.tok("dve", "act"))
            for c in range(8):
                p.t(lambda e, n=n, c=c: e.matmul(
                    psM[0:64, 0:512], SnTb[:, 64 * c:64 * (c + 1)],
                    wvs[:, 1024 * c + 512 * n:1024 * c + 512 * (n + 1)],
                    start=(c == 0), stop=False), gw if c == 0 else ())
            p.t(lambda e, n=n: e.matmul(psM[0:64, 0:512], C["onesb"][:, 0:64],
                                        bvr[:, 512 * n:512 * (n + 1)],
                                        start=False, stop=True))
            p.s(lambda e, n=n: e.activation(vs[0:64, 512 * n:512 * (n + 1)],
                                            psM[0:64, 0:512], AF.Identity), p.tok("pe"))
        for c in range(8):
            p.t(lambda e, c=c: e.matmul(
                psM[:, 0:64], vs[:, 128 * c:128 * (c + 1)], attnTb[:, 0:64],
                start=True, stop=True), p.tok("act", "dve"))
            p.v(lambda e, c=c: e.tensor_tensor(ySlT[:, 64 * c:64 * (c + 1)],
                                               SnT[:, 64 * c:64 * (c + 1)], psM[:, 0:64],
                                               ALU.add), p.tok("pe"))
        for c in range(8):
            bk = c % 2
            tw = list(p.tok("dve"))
            p.t(lambda e, c=c, bk=bk: e.transpose(
                pbf[0:64, 512 * bk:512 * bk + 128], ySlT[:, 64 * c:64 * (c + 1)],
                C["eyef"][:]), tw)
            p.v(lambda e, c=c, bk=bk: e.tensor_copy(
                ySl[:, 128 * c:128 * (c + 1)], pbf[0:64, 512 * bk:512 * bk + 128]), p.tok("pe"))
        st_, tmp = C["stats"], C["tiny"]
        p.v(lambda e: e.bn_stats(st_[0:64, 0:6], ySl[:, 0:512]), p.tok("act"))
        p.v(lambda e: e.bn_stats(st_[0:64, 6:12], ySl[:, 512:1024]))
        p.v(lambda e: e.bn_aggr(tmp[0:64, 8:10], st_[0:64, 0:12]))
        p.v(lambda e: e.tensor_scalar_add(tmp[0:64, 0:1], tmp[0:64, 9:10], 1e-5))
        p.s(lambda e: e.activation(tmp[0:64, 1:2], tmp[0:64, 0:1], AF.Ln), p.tok("dve"))
        p.s(lambda e: e.activation(tmp[0:64, 2:3], tmp[0:64, 1:2], AF.Exp, scale=-0.5))
        p.v(lambda e: e.tensor_tensor(tmp[0:64, 3:4], tmp[0:64, 8:9], tmp[0:64, 2:3], ALU.mult),
            p.tok("act"))
        p.v(lambda e: e.tensor_scalar_mul(tmp[0:64, 4:5], tmp[0:64, 3:4], -1.0))
        p.s(lambda e: e.activation(yhat[0:64, :], ySl[:], AF.Identity, bias=tmp[0:64, 4:5],
                                   scale=tmp[0:64, 2:3]), p.tok("dve"))
        p.t(lambda e: e.matmul(pbf[:, 0:8], lng8[:], C["eyef"][0:8, 0:8], start=True, stop=True),
            p.tok("dve"))
        p.v(lambda e: e.tensor_copy(lngc[:], pbf[:, 0:8]), p.tok("pe"))
        p.t(lambda e: e.matmul(pbf[:, 8:16], lnb8[:], C["eyef"][0:8, 0:8], start=True, stop=True),
            p.tok("dve"))
        p.v(lambda e: e.tensor_copy(lnbc[:], pbf[:, 8:16]), p.tok("pe"))
        for c in range(8):
            dt = p.ds(lambda e, c=c: e.dma_start(wvpf[:], dr["f_wvp"][128 * c:128 * (c + 1), :]),
                      p.tok("dve", "pe"))
            p.v(lambda e, c=c: e.tensor_scalar(gwvp[:, 1024 * c:1024 * (c + 1)], wvpf[:],
                                               lngc[:, c:c + 1], None, ALU.mult), [dt])
            for half in range(2):
                p.t(lambda e, c=c, half=half: e.matmul(
                    pbf[0:1, 512 * half:512 * half + 512], lnbc[:, c:c + 1],
                    wvpf[:, 512 * half:512 * (half + 1)],
                    start=(c == 0), stop=False, skip_group_check=True),
                    p.tok("dve") if half == 0 else ())
        for half in range(2):
            p.t(lambda e, half=half: e.matmul(
                pbf[0:1, 512 * half:512 * half + 512], C["onesf"][:, 0:1],
                bvpr[:, 512 * half:512 * (half + 1)], start=False, stop=True,
                skip_group_check=True))
        p.v(lambda e: e.tensor_copy(cr1[0:1, 0:512], pbf[0:1, 0:512]), p.tok("pe"))
        p.v(lambda e: e.tensor_copy(cr1[0:1, 512:1024], pbf[0:1, 512:1024]))
        p.v(lambda e: e.tensor_copy(crb[0:1, :], cr1[:]))
        for c in range(8):
            p.t(lambda e, c=c: e.transpose(pbt[:, 128 * c:128 * (c + 1)],
                                           yhat[:, 128 * c:128 * (c + 1)],
                                           C["eyeb"][:]),
                p.tok("act", "dve") if c == 0 else ())
        p.v(lambda e: e.tensor_copy(
            yhT.rearrange("p (c n) -> p c n", n=64),
            pbt[:, 0:1024].rearrange("p (c n) -> p c n", n=128)[:, :, 0:64]), p.tok("pe"))
        for n in range(2):
            gw = list(p.tok("dve", "act"))
            for c in range(8):
                p.t(lambda e, n=n, c=c: e.matmul(
                    psM[0:64, 0:512], yhT[:, 64 * c:64 * (c + 1)],
                    gwvp[:, 1024 * c + 512 * n:1024 * c + 512 * (n + 1)],
                    start=(c == 0), stop=False), gw if c == 0 else ())
            p.t(lambda e, n=n: e.matmul(psM[0:64, 0:512], C["onesb"][:, 0:64],
                                        crb[:, 512 * n:512 * (n + 1)],
                                        start=False, stop=True))
            p.s(lambda e, n=n: e.activation(sv[0:64, 512 * n:512 * (n + 1)],
                                            psM[0:64, 0:512], AF.Identity), p.tok("pe"))
        p.barrier()
        stC3.close()

        stD = ExitStack()
        outT = stD.enter_context(nc.sbuf_tensor("d_outT", [128, 8 * 2048], BF16))
        wo2 = stD.enter_context(nc.sbuf_tensor("d_wo2", [128, 8 * 1024], BF16))
        bor = stD.enter_context(nc.sbuf_tensor("d_bor", [128, 1024], BF16))
        mzo = p.v(lambda e: e.memset(bor[:], 0.0))
        for c in range(8):
            p.dg(lambda e, c=c: e.dma_start(wo2[:, 1024 * c:1024 * (c + 1)],
                                            dr["f_wo"][128 * c:128 * (c + 1), :]))
        p.dg(lambda e: e.dma_start(bor[0:1, :], dr["f_bo"]), [mzo])
        p.barrier()
        for c in range(8):
            for n in range(4):
                w = list(p.tok("act"))
                p.t(lambda e, c=c, n=n: e.matmul(
                    psSI[:, 0:512], sv[:, 128 * c:128 * (c + 1)],
                    alphaT[:, 512 * n:512 * (n + 1)], start=True, stop=True), w)
                p.s(lambda e, c=c, n=n: e.activation(
                    outT[:, 2048 * c + 512 * n:2048 * c + 512 * (n + 1)], psSI[:, 0:512],
                    AF.Identity), p.tok("pe"))
        p.barrier()
        for t in range(NT):
            b = t % 2
            mw = list(p.tok("act", "dve"))
            for half in range(2):
                for c in range(8):
                    p.t(lambda e, t=t, half=half, c=c: e.matmul(
                        psSI[:, 512 * half:512 * (half + 1)],
                        outT[:, 2048 * c + 128 * t:2048 * c + 128 * (t + 1)],
                        wo2[:, 1024 * c + 512 * half:1024 * c + 512 * (half + 1)],
                        start=(c == 0), stop=False), mw if (half == 0 and c == 0) else ())
                p.t(lambda e, half=half: e.matmul(
                    psSI[:, 512 * half:512 * (half + 1)], C["onesb"][:, 0:128],
                    bor[:, 512 * half:512 * (half + 1)], start=False, stop=True))
            xdt = p.ds(lambda e, t=t, b=b: e.dma_start(
                xt[:, 1024 * b:1024 * (b + 1)], x_src[128 * t:128 * (t + 1), :]),
                p.tok("dve"))
            xin = xt[:, 1024 * b:1024 * (b + 1)]
            ln_resid_store(p, C, "psum2", psSI[:, 0:512], psSI[:, 512:1024], xin, yb,
                           x_dst[128 * t:128 * (t + 1), :],
                           extra_wait=list(p.tok("pe")) + [xdt])
        p.barrier()
        stD.close()


def _build(blocks, fused):
    nc = bass.Bass("TRN2", target_bir_lowering=False, detect_race_conditions=False)
    dr = {}
    specs = {
        "eye": [128, 128], "l0_masks": [768, 512], "l1_masks": [512, 512],
        "x0h": [2304, 1024], "padneg": [1, 2304],
        "l0_wq": [1024, 1024], "l0_wk": [1024, 1024], "l0_wv": [1024, 1024],
        "l0_wo": [1024, 1024], "l0_brow": [4, 1024],
        "mod_w1": [1024, 4096], "mod_w2": [4096, 1024], "mod_b1c": [128, 32],
        "mod_b2c": [128, 8], "mod_wr8": [8, 128],
        "l1_wq": [1024, 256], "l1_wk": [1024, 256], "l1_wv": [1024, 256],
        "l1_wo": [256, 1024],
        "f_sIT": [1024, 64], "f_sI": [64, 1024], "f_w1t": [1024, 512],
        "f_w1b": [1024, 512], "f_b1": [1, 512], "f_w2": [512, 64], "f_b2": [1, 64],
        "f_tau": [1, 1], "f_wihT": [1024, 3072], "f_whhT": [1024, 3072],
        "f_bih": [1, 3072], "f_bhh": [1, 3072], "f_wq": [1024, 256], "f_bq": [1, 256],
        "f_wk": [1024, 256], "f_bk": [1, 256], "f_wv": [1024, 1024], "f_bv": [1, 1024],
        "f_lng8": [8, 128], "f_lnb8": [8, 128], "f_wvp": [1024, 1024],
        "f_bvp": [1, 1024], "f_wo": [1024, 1024], "f_bo": [1, 1024],
    }
    for nm, shp in specs.items():
        dr[nm] = nc.dram_tensor(nm, shp, F32, kind="ExternalInput").ap()
    out = nc.dram_tensor("xout", [2048, 1024], F32, kind="ExternalOutput").ap()
    dbg = nc.dram_tensor("dbg", [128, 8192], F32, kind="ExternalOutput").ap() if DBG else None
    xs = {}
    if fused:
        for nm in ("x1", "x2", "x3"):
            xs[nm] = nc.dram_tensor(nm, [2048, 1024], F32, kind="Internal").ap()
    else:
        if blocks[0] != "l0a":
            xs["xin"] = nc.dram_tensor("xin", [2048, 1024], F32, kind="ExternalInput").ap()
    cio = {
        "sc_in": nc.dram_tensor("sc_in", [1, 2048], F32, kind="Internal").ap(),
        "sc_out": nc.dram_tensor("sc_out", [4, 2048], F32, kind="Internal").ap(),
        "ag_in": nc.dram_tensor("ag_in", [1024, 2048], BF16, kind="Internal").ap(),
        "ag_out": nc.dram_tensor("ag_out", [4096, 2048], BF16, kind="Internal").ap(),
        "rs_in": nc.dram_tensor("rs_in", [8192, 1024], BF16, kind="Internal").ap(),
        "rs_out": nc.dram_tensor("rs_out", [2048, 1024], BF16, kind="Internal").ap(),
        "ar_in": nc.dram_tensor("ar_in", [64, 1028], F32, kind="Internal").ap(),
        "ar_out": nc.dram_tensor("ar_out", [64, 1028], F32, kind="Internal").ap(),
        "o2_dram": nc.dram_tensor("o2_dram", [2048, 1024], BF16, kind="Internal").ap(),
    }
    with ExitStack() as st:
        p = P(nc, st)
        block = st.enter_context(nc.Block())
        C = setup_consts(p, st, nc, dr)
        chain = {"l0a": ("x0h", "x1"), "l0f": ("x1", "x2"),
                 "l1a": ("x2", "x3"), "l1f": ("x3", "out")}
        for i, bname in enumerate(blocks):
            srcn, dstn = chain[bname]
            if fused:
                src = dr["x0h"] if srcn == "x0h" else xs[srcn]
                dst = out if (i == len(blocks) - 1) else xs[dstn]
            else:
                src = dr["x0h"] if bname == "l0a" else xs["xin"]
                dst = out
            if bname == "l0a":
                phase_l0a(p, C, nc, dr, src, dst, dbg=dbg)
            elif bname == "l0f":
                phase_l0f(p, C, nc, dr, src, dst, cio)
            elif bname == "l1a":
                phase_l1a(p, C, nc, dr, src, dst, cio)
            elif bname == "l1f":
                phase_l1f(p, C, nc, dr, src, dst, cio)
        p.barrier()
        p.emit(block)
    mybir.codegen_inst_isa_subclasses(nc)
    return nc


def _run(in_maps, blocks, fused):
    if fused:
        nc = _build(blocks, True)
        res = run_bass_kernel_spmd(nc, in_maps, core_ids=list(range(8)))
        return [np.asarray(r["xout"]) for r in res.results]
    cur = None
    for bname in blocks:
        nc = _build((bname,), False)
        maps = []
        for c in range(8):
            m = dict(in_maps[c])
            if bname != "l0a":
                m["xin"] = cur[c]
            maps.append(m)
        res = run_bass_kernel_spmd(nc, maps, core_ids=list(range(8)))
        cur = [np.asarray(r["xout"]) for r in res.results]
    return cur


def kernel(**inputs):
    in_maps = _prep(inputs)
    shards = _run(in_maps, ALL_BLOCKS, FUSED)
    out = np.zeros((NB, SEQ, DM), np.float32)
    for c in range(8):
        b, j = c // 4, c % 4
        out[b, j * TOK:(j + 1) * TOK] = shards[c]
    return out


# version 06
# speedup vs baseline: 1.7340x; 1.0011x over previous
import numpy as np
import ml_dtypes
from contextlib import ExitStack

import concourse.bass as bass
import concourse.mybir as mybir
from concourse.bass_isa import ReduceOp
from concourse.bass_utils import run_bass_kernel_spmd

F32 = mybir.dt.float32
BF16 = mybir.dt.bfloat16
AF = mybir.ActivationFunctionType
ALU = mybir.AluOpType
AX = mybir.AxisListType

NB, SEQ, DM = 2, 8192, 1024
TOK = 2048
NT = 16
NTH = 18
GROUPS = [[0, 1, 2, 3], [4, 5, 6, 7]]
FUSED = True
DBG = False
ALL_BLOCKS = ("l0a", "l0f", "l1a", "l1f")

ENGS = ("tensor", "vector", "scalar", "gpsimd", "sync")
_OWN = {"tensor": "pe", "vector": "dve", "scalar": "act", "gpsimd": "pool", "sync": None}


class P:
    def __init__(self, nc, st):
        self.nc = nc
        self.ops = {e: [] for e in ENGS}
        self.sems = {}
        self.cnt = {}
        for s in ("pe", "dve", "act", "pool", "qs", "qg", "cc"):
            self.sems[s] = st.enter_context(nc.semaphore(s))
            self.cnt[s] = 0

    def _op(self, e, fn, waits, sig):
        self.ops[e].append((fn, tuple(waits or ()), sig))
        if sig:
            self.cnt[sig[0]] += sig[1]
            return (sig[0], self.cnt[sig[0]])
        return None

    def t(self, fn, waits=()):  return self._op("tensor", fn, waits, ("pe", 1))
    def v(self, fn, waits=()):  return self._op("vector", fn, waits, ("dve", 1))
    def s(self, fn, waits=()):  return self._op("scalar", fn, waits, ("act", 1))
    def g(self, fn, waits=()):  return self._op("gpsimd", fn, waits, ("pool", 1))
    def ds(self, fn, waits=()): return self._op("sync", fn, waits, ("qs", 16))
    def dg(self, fn, waits=()): return self._op("gpsimd", fn, waits, ("qg", 16))
    def ccop(self, fn, waits=()): return self._op("gpsimd", fn, waits, ("cc", 1))

    def tok(self, *sems):
        return tuple((s, self.cnt[s]) for s in sems)

    def barrier(self):
        snap = tuple((s, self.cnt[s]) for s in self.cnt)
        for e in ENGS:
            self._op(e, None, snap, None)

    def emit(self, block):
        def mk(e):
            own = _OWN[e]
            serialize = e in ("vector", "scalar", "gpsimd")
            def run(eng):
                waited = {}
                cum = {}
                for fn, waits, sig in self.ops[e]:
                    allw = list(waits)
                    if serialize and sig and sig[0] == own and cum.get(own, 0) > 0:
                        allw.append((own, cum[own]))
                    for (s, v) in allw:
                        if v <= 0:
                            continue
                        if waited.get(s, -1) >= v:
                            continue
                        eng.wait_ge(self.sems[s], v)
                        waited[s] = v
                    if fn is not None:
                        inst = fn(eng)
                        if sig:
                            inst.then_inc(self.sems[sig[0]], sig[1])
                    if sig:
                        cum[sig[0]] = cum.get(sig[0], 0) + sig[1]
            return run
        block.tensor(mk("tensor"))
        block.vector(mk("vector"))
        block.scalar(mk("scalar"))
        block.gpsimd(mk("gpsimd"))
        block.sync(mk("sync"))


def _win_masks():
    ms = np.zeros((6, 128, 512), np.float32)
    k = np.arange(128)[:, None]
    q = np.arange(512)[None, :]
    for kt in range(6):
        ms[kt] = (np.abs(k - q + 128 * (kt - 1)) <= 128).astype(np.float32)
    return ms.reshape(768, 512)


def _causal_masks():
    ms = np.zeros((4, 128, 512), np.float32)
    k = np.arange(128)[:, None]
    q = np.arange(512)[None, :]
    for d in range(4):
        ms[d] = ((k + 128 * d) <= q).astype(np.float32)
    return ms.reshape(512, 512)


def _prep(inp):
    inp = {k: np.asarray(v) for k, v in inp.items()}
    x = inp["L0_attn_x"].astype(np.float32, copy=False)
    C = np.ascontiguousarray
    shared = {
        "eye": np.eye(128, dtype=np.float32),
        "l0_masks": _win_masks(),
        "l1_masks": _causal_masks(),
        "l0_wq": C(inp["L0_attn_wq"]), "l0_wk": C(inp["L0_attn_wk"]),
        "l0_wv": C(inp["L0_attn_wv"]), "l0_wo": C(inp["L0_attn_wo"]),
        "l0_brow": C(np.stack([inp["L0_attn_bq"], inp["L0_attn_bk"],
                               inp["L0_attn_bv"], inp["L0_attn_bo"]])),
        "mod_w1": C(inp["L0_ffn_W1"]), "mod_w2": C(inp["L0_ffn_W2"]),
        "mod_b1c": C(np.asarray(inp["L0_ffn_b1"]).reshape(32, 128).T), "mod_b2c": C(np.asarray(inp["L0_ffn_b2"]).reshape(8, 128).T),
        "mod_wr8": C(inp["L0_ffn_w_router"].reshape(8, 128)),
        "f_sIT": C(inp["L1_ffn_slot_init"].T), "f_sI": C(inp["L1_ffn_slot_init"]),
        "f_w1t": C(inp["L1_ffn_w1"][0:1024]), "f_w1b": C(inp["L1_ffn_w1"][1024:2048]),
        "f_b1": C(inp["L1_ffn_b1"][None, :]), "f_w2": C(inp["L1_ffn_w2"]),
        "f_b2": C(inp["L1_ffn_b2"][None, :]), "f_tau": C(inp["L1_ffn_tau"].reshape(1, 1)),
        "f_wihT": C(inp["L1_ffn_Wih"].T), "f_whhT": C(inp["L1_ffn_Whh"].T),
        "f_bih": C(inp["L1_ffn_bih"][None, :]), "f_bhh": C(inp["L1_ffn_bhh"][None, :]),
        "f_wq": C(inp["L1_ffn_wq"]), "f_bq": C(inp["L1_ffn_bq"][None, :]),
        "f_wk": C(inp["L1_ffn_wk"]), "f_bk": C(inp["L1_ffn_bk"][None, :]),
        "f_wv": C(inp["L1_ffn_wv"]), "f_bv": C(inp["L1_ffn_bv"][None, :]),
        "f_lng8": C(inp["L1_ffn_ln_g"].reshape(8, 128)),
        "f_lnb8": C(inp["L1_ffn_ln_b"].reshape(8, 128)),
        "f_wvp": C(inp["L1_ffn_wvp"]), "f_bvp": C(inp["L1_ffn_bvp"][None, :]),
        "f_wo": C(inp["L1_ffn_wo"]), "f_bo": C(inp["L1_ffn_bo"][None, :]),
    }
    qp, kp, vp, op = (inp["L1_attn_q_proj"], inp["L1_attn_k_proj"],
                      inp["L1_attn_v_proj"], inp["L1_attn_o_proj"])
    in_maps = []
    for c in range(8):
        b, j = c // 4, c % 4
        xh = np.zeros((2304, DM), np.float32)
        xh[128:2176] = x[b, j * TOK:(j + 1) * TOK]
        if j > 0:
            xh[0:128] = x[b, j * TOK - 128:j * TOK]
        if j < 3:
            xh[2176:2304] = x[b, (j + 1) * TOK:(j + 1) * TOK + 128]
        padneg = np.zeros((1, 2304), np.float32)
        if j == 0:
            padneg[0, 0:128] = -1.0
        if j == 3:
            padneg[0, 2176:2304] = -1.0
        m = dict(shared)
        m["x0h"] = xh
        m["padneg"] = padneg
        hs = slice(256 * j, 256 * j + 256)
        m["l1_wq"] = C(qp[hs, :].T)
        m["l1_wk"] = C(kp[hs, :].T)
        m["l1_wv"] = C(vp[hs, :].T)
        m["l1_wo"] = C(op[:, hs].T)
        in_maps.append(m)
    return in_maps


def setup_consts(p, st, nc, dr):
    C = {}
    C["eyef"] = st.enter_context(nc.sbuf_tensor("eyef", [128, 128], F32))
    C["eyeb"] = st.enter_context(nc.sbuf_tensor("eyeb", [128, 128], BF16))
    C["onesb"] = st.enter_context(nc.sbuf_tensor("onesb", [128, 2304], BF16))
    C["onesf"] = st.enter_context(nc.sbuf_tensor("onesf", [128, 128], F32))
    C["tiny"] = st.enter_context(nc.sbuf_tensor("tiny", [128, 64], F32))
    C["stats"] = st.enter_context(nc.sbuf_tensor("stats", [128, 14], F32))
    C["mrs"] = st.enter_context(nc.sbuf_tensor("mrs", [128, 4 * NTH], F32))
    p.ds(lambda e: e.dma_start(C["eyef"][:], dr["eye"]))
    p.dg(lambda e: e.dma_start(C["eyeb"][:], dr["eye"]))
    p.v(lambda e: e.memset(C["onesb"][:], 1.0))
    p.v(lambda e: e.memset(C["onesf"][:], 1.0))
    p.barrier()
    return C


def tile_ln(p, C, xin_ap, t, u_dst, wait=()):
    st_, mrs, tmp = C["stats"], C["mrs"], C["tiny"]
    tk = p.v(lambda e: e.bn_stats(st_[:, 0:6], xin_ap[:, 0:512]), tuple(wait) + p.tok("act"))
    tk = p.v(lambda e: e.bn_stats(st_[:, 6:12], xin_ap[:, 512:1024]), [tk])
    tk = p.v(lambda e: e.bn_aggr(mrs[:, 4 * t:4 * t + 2], st_[:, 0:12]), [tk])
    tk = p.v(lambda e, t=t: e.tensor_scalar_add(tmp[:, 0:1], mrs[:, 4 * t + 1:4 * t + 2], 1e-5),
             [tk])
    tk = p.s(lambda e: e.activation(tmp[:, 1:2], tmp[:, 0:1], AF.Ln), [tk])
    tk = p.s(lambda e, t=t: e.activation(mrs[:, 4 * t + 2:4 * t + 3], tmp[:, 1:2], AF.Exp,
                                         scale=-0.5), [tk])
    tk = p.v(lambda e, t=t: e.tensor_tensor(tmp[:, 2:3], mrs[:, 4 * t:4 * t + 1],
                                            mrs[:, 4 * t + 2:4 * t + 3], ALU.mult), [tk])
    tk = p.v(lambda e, t=t: e.tensor_scalar_mul(mrs[:, 4 * t + 3:4 * t + 4], tmp[:, 2:3], -1.0),
             [tk])
    return p.s(lambda e, t=t, xin_ap=xin_ap, u_dst=u_dst: e.activation(
        u_dst, xin_ap, AF.Identity, bias=mrs[:, 4 * t + 3:4 * t + 4],
        scale=mrs[:, 4 * t + 2:4 * t + 3]), [tk])


def ln_resid_store(p, C, src_kind, src0, src1, xin_ap, yb, x_dst_tile, extra_wait=()):
    st_, tmp = C["stats"], C["tiny"]
    if src_kind == "psum2":
        a0, a1 = src0, src1
    else:
        a0, a1 = src0[:, 0:512], src0[:, 512:1024]
    p.v(lambda e, a0=a0: e.bn_stats(st_[:, 0:6], a0), tuple(extra_wait) + p.tok("act"))
    p.v(lambda e, a1=a1: e.bn_stats(st_[:, 6:12], a1))
    p.v(lambda e: e.bn_aggr(tmp[:, 8:10], st_[:, 0:12]))
    p.v(lambda e: e.tensor_scalar_add(tmp[:, 0:1], tmp[:, 9:10], 1e-5))
    p.s(lambda e: e.activation(tmp[:, 1:2], tmp[:, 0:1], AF.Ln), p.tok("dve"))
    p.s(lambda e: e.activation(tmp[:, 2:3], tmp[:, 1:2], AF.Exp, scale=-0.5))
    p.v(lambda e: e.tensor_tensor(tmp[:, 3:4], tmp[:, 8:9], tmp[:, 2:3], ALU.mult), p.tok("act"))
    p.v(lambda e: e.tensor_scalar_mul(tmp[:, 4:5], tmp[:, 3:4], -1.0))
    p.s(lambda e, a0=a0, yb=yb: e.activation(yb[:, 0:512], a0, AF.Identity,
                                             bias=tmp[:, 4:5], scale=tmp[:, 2:3]), p.tok("dve", "qs"))
    lntok = p.s(lambda e, a1=a1, yb=yb: e.activation(yb[:, 512:1024], a1, AF.Identity,
                                                     bias=tmp[:, 4:5], scale=tmp[:, 2:3]))
    p.v(lambda e, yb=yb, xin_ap=xin_ap: e.tensor_add(yb[:], yb[:], xin_ap), [lntok])
    return p.ds(lambda e, x_dst_tile=x_dst_tile, yb=yb: e.dma_start(x_dst_tile, yb[:]),
                p.tok("dve"))


def ln_resid_pipe(p, C, nc, st, prefix, n_tiles, make_src, x_src, x_dst, nslot=3):
    stt = st.enter_context(nc.sbuf_tensor(prefix + "_pst", [128, 12 * nslot], F32))
    tmp = st.enter_context(nc.sbuf_tensor(prefix + "_ptp", [128, 8 * nslot], F32))
    xts = st.enter_context(nc.sbuf_tensor(prefix + "_pxt", [128, 1024 * nslot], F32))
    ybs = st.enter_context(nc.sbuf_tensor(prefix + "_pyb", [128, 1024 * nslot], F32))
    slot_free = {}
    for t in range(n_tiles):
        s = t % nslot
        wfree = [slot_free[s]] if s in slot_free else []
        xin = xts[:, 1024 * s:1024 * (s + 1)]
        ybx = ybs[:, 1024 * s:1024 * (s + 1)]
        xdt = p.ds(lambda e, t=t, xin=xin: e.dma_start(
            xin, x_src[128 * t:128 * (t + 1), :]), wfree)
        src, stok = make_src(t, s, xin, xdt, wfree)
        v = p.v(lambda e, s=s, src=src: e.bn_stats(
            stt[:, 12 * s:12 * s + 6], src[:, 0:512]), [stok])
        v = p.v(lambda e, s=s, src=src: e.bn_stats(
            stt[:, 12 * s + 6:12 * s + 12], src[:, 512:1024]), [v])
        v = p.v(lambda e, s=s: e.bn_aggr(tmp[:, 8 * s + 4:8 * s + 6],
                                         stt[:, 12 * s:12 * s + 12]), [v])
        v = p.v(lambda e, s=s: e.tensor_scalar_add(
            tmp[:, 8 * s:8 * s + 1], tmp[:, 8 * s + 5:8 * s + 6], 1e-5), [v])
        a = p.s(lambda e, s=s: e.activation(tmp[:, 8 * s + 1:8 * s + 2],
                                            tmp[:, 8 * s:8 * s + 1], AF.Ln), [v])
        a = p.s(lambda e, s=s: e.activation(tmp[:, 8 * s + 2:8 * s + 3],
                                            tmp[:, 8 * s + 1:8 * s + 2], AF.Exp,
                                            scale=-0.5), [a])
        v = p.v(lambda e, s=s: e.tensor_tensor(
            tmp[:, 8 * s + 3:8 * s + 4], tmp[:, 8 * s + 4:8 * s + 5],
            tmp[:, 8 * s + 2:8 * s + 3], ALU.mult), [a])
        v = p.v(lambda e, s=s: e.tensor_scalar_mul(
            tmp[:, 8 * s + 6:8 * s + 7], tmp[:, 8 * s + 3:8 * s + 4], -1.0), [v])
        a = p.s(lambda e, s=s, src=src, ybx=ybx: e.activation(
            ybx, src, AF.Identity,
            bias=tmp[:, 8 * s + 6:8 * s + 7], scale=tmp[:, 8 * s + 2:8 * s + 3]), [v])
        v = p.v(lambda e, ybx=ybx, xin=xin: e.tensor_add(ybx, ybx, xin), [a, xdt])
        slot_free[s] = p.dg(lambda e, t=t, ybx=ybx: e.dma_start(
            x_dst[128 * t:128 * (t + 1), :], ybx), [v])


def norm_bound(p, C, src_view, ncols, sq, psbanks, outcol, crow, nch=8):
    groups = []
    off = 0
    while off < ncols:
        groups.append((off, min(2048, ncols - off)))
        off += 2048
    gi = 0
    for (goff, gw) in groups:
        nwin = (gw + 511) // 512
        for c in range(nch):
            p.s(lambda e, c=c, goff=goff, gw=gw, src_view=src_view, sq=sq: e.activation(
                sq[:, 0:gw], src_view[:, c, goff:goff + gw], AF.Square), p.tok("pe"))
            for n in range(nwin):
                w = min(512, gw - 512 * n)
                p.t(lambda e, c=c, n=n, w=w, sq=sq, bk=psbanks[n], nch=nch: e.matmul(
                    bk[0:1, 0:w], C["onesf"][:, 0:1], sq[:, 512 * n:512 * n + w],
                    start=(c == 0), stop=(c == nch - 1)),
                    p.tok("act", "dve") if n == 0 else ())
        for n in range(nwin):
            w = min(512, gw - 512 * n)
            p.v(lambda e, n=n, w=w, gi=gi, bk=psbanks[n]: e.reduce_max(
                C["tiny"][0:1, gi + n:gi + n + 1], bk[0:1, 0:w], axis=AX.X), p.tok("pe"))
        gi += nwin
    p.v(lambda e, gi=gi, outcol=outcol, crow=crow: e.reduce_max(
        crow[0:1, outcol:outcol + 1], C["tiny"][0:1, 0:gi], axis=AX.X))


def bias_col_from(p, C, crow, src_col, dst_col_ap, psbank, scale):
    p.t(lambda e: e.matmul(psbank[:, 0:1], C["onesf"][0:1, :], crow[0:1, src_col:src_col + 1],
                           start=True, stop=True), p.tok("dve"))
    return p.v(lambda e: e.tensor_copy(dst_col_ap, psbank[:, 0:1]), p.tok("pe"))


def phase_l0a(p, C, nc, dr, x_src, x_dst, dbg=None):
    with ExitStack() as st:
        sb = lambda n, shp, dt: st.enter_context(nc.sbuf_tensor(n, shp, dt))
        ps = lambda n, shp, dt=F32: st.enter_context(nc.psum_tensor(n, shp, dt))
        stA = ExitStack()
        sba = lambda n, shp, dt: stA.enter_context(nc.sbuf_tensor(n, shp, dt))
        brow = sb("a_brow", [128, 4 * 1024], BF16)
        padZ = sb("a_padZ", [128, 2304], BF16)
        QT = sb("a_QT", [128, 8 * 2048], BF16)
        KT = sb("a_KT", [128, 8 * 2304], BF16)
        VP = sb("a_VP", [128, 18 * 1040], BF16)
        xt = sb("a_xt", [128, 2 * 1024], F32)
        ub = sb("a_ub", [128, 2 * 1024], BF16)
        uT = sb("a_uT", [128, 2 * 1024], BF16)
        crow = sb("a_crow", [1, 16], F32)
        bias_c = sb("a_biasc", [128, 1], F32)
        rsumZ = sb("a_rsumZ", [128, 2 * 512], BF16)
        rsumf = sb("a_rsumf", [1, 2 * 512], F32)
        bcastS = sb("a_bcastS", [64, 2 * 512], BF16)
        wq = sba("a_wq", [128, 8 * 1024], BF16)
        wk = sba("a_wk", [128, 8 * 1024], BF16)
        wv = sba("a_wv", [128, 8 * 1024], BF16)
        stP = ExitStack()
        pbt = stP.enter_context(nc.psum_tensor("a_pbt", [128, 2 * 1024], BF16))
        psA = stP.enter_context(nc.psum_tensor("a_psA", [128, 512], F32))
        psB = stP.enter_context(nc.psum_tensor("a_psB", [128, 512], F32))
        psS = stP.enter_context(nc.psum_tensor("a_psS", [128, 2 * 512], F32))

        QTv = QT.rearrange("p (c n) -> p c n", n=2048)
        KTv = KT.rearrange("p (c n) -> p c n", n=2304)
        VPv = VP.rearrange("p (tt h n) -> p tt h n", h=16, n=65)

        for wsb, nm in ((wq, "l0_wq"), (wk, "l0_wk"), (wv, "l0_wv")):
            for c in range(8):
                p.dg(lambda e, wsb=wsb, nm=nm, c=c: e.dma_start(
                    wsb[:, 1024 * c:1024 * (c + 1)], dr[nm][128 * c:128 * (c + 1), :]))
        mz1 = p.v(lambda e: e.memset(brow[:], 0.0))
        mz2 = p.v(lambda e: e.memset(padZ[:], 0.0))
        p.dg(lambda e: e.dma_start(brow[0:1, :], dr["l0_brow"].rearrange("a b -> (a b)").unsqueeze(0)),
             [mz1])
        p.dg(lambda e: e.dma_start(padZ[0:1, :], dr["padneg"]), [mz2])
        p.v(lambda e: e.memset(VP[:], 1.0))
        p.v(lambda e: e.memset(rsumZ[:], 0.0))
        p.barrier()

        dma_toks = {}
        last_use_ub = {0: None, 1: None}
        last_evT = {0: None, 1: None}
        for t in (0, 1):
            b = t % 2
            dma_toks[t] = p.ds(lambda e, t=t, b=b: e.dma_start(
                xt[:, 1024 * b:1024 * (b + 1)], x_src[128 * t:128 * (t + 1), :]))

        def project_fmaj(bank, wsb, uTt, brw, f4, dst3, t, padfix, wait=()):
            grpw = list(p.tok("act")) + list(wait)
            first = True
            for fc in range(4):
                f = f4 + fc
                last_grp = (fc == 3)
                for c in range(8):
                    p.t(lambda e, f=f, c=c, fc=fc, bank=bank, wsb=wsb, uTt=uTt: e.matmul(
                        bank[:, 128 * fc:128 * (fc + 1)],
                        wsb[:, 1024 * c + 128 * f:1024 * c + 128 * (f + 1)],
                        uTt[:, 128 * c:128 * (c + 1)],
                        start=(c == 0 and fc == 0), stop=False,
                        skip_group_check=True), grpw if first else ())
                    first = False
                p.t(lambda e, f=f, fc=fc, bank=bank, brw=brw, last_grp=last_grp: e.matmul(
                    bank[:, 128 * fc:128 * (fc + 1)],
                    brw[:, 128 * f:128 * (f + 1)], C["onesb"][:, 0:128],
                    start=False, stop=(last_grp and not padfix),
                    skip_group_check=True))
                if padfix:
                    p.t(lambda e, f=f, fc=fc, bank=bank, brw=brw, t=t, last_grp=last_grp: e.matmul(
                        bank[:, 128 * fc:128 * (fc + 1)],
                        brw[:, 128 * f:128 * (f + 1)],
                        padZ[:, 128 * t:128 * (t + 1)],
                        start=False, stop=last_grp, skip_group_check=True))
            p.s(lambda e, bank=bank, dst3=dst3: e.activation(
                dst3, bank.rearrange("p (c n) -> p c n", n=128), AF.Identity), p.tok("pe"))

        evtoks = {}

        def do_proj(t):
            b = t % 2
            uTt = uT[:, 1024 * b:1024 * (b + 1)]
            evtok = evtoks[t]
            pad = t in (0, 17)
            if 1 <= t <= 16:
                tq = t - 1
                project_fmaj(psA, wq, uTt, brow[:, 0:1024], 0,
                             QTv[:, 0:4, 128 * tq:128 * (tq + 1)], t, False, wait=[evtok])
                project_fmaj(psB, wq, uTt, brow[:, 0:1024], 4,
                             QTv[:, 4:8, 128 * tq:128 * (tq + 1)], t, False)
            project_fmaj(psA, wk, uTt, brow[:, 1024:2048], 0,
                         KTv[:, 0:4, 128 * t:128 * (t + 1)], t, pad, wait=[evtok])
            project_fmaj(psB, wk, uTt, brow[:, 1024:2048], 4,
                         KTv[:, 4:8, 128 * t:128 * (t + 1)], t, pad)
            for half, bank in ((0, psA), (1, psB)):
                vw = list(p.tok("act")) + [evtok]
                for c in range(8):
                    p.t(lambda e, c=c, half=half, bank=bank, uTt=uTt: e.matmul(
                        bank[:], uTt[:, 128 * c:128 * (c + 1)],
                        wv[:, 1024 * c + 512 * half:1024 * c + 512 * (half + 1)],
                        start=(c == 0), stop=False), vw if c == 0 else ())
                p.t(lambda e, half=half, bank=bank, pad=pad: e.matmul(
                    bank[:], C["onesb"][:, 0:128],
                    brow[:, 2048 + 512 * half:2048 + 512 * (half + 1)],
                    start=False, stop=not pad))
                if pad:
                    p.t(lambda e, half=half, bank=bank, t=t: e.matmul(
                        bank[:], padZ[:, 128 * t:128 * (t + 1)],
                        brow[:, 2048 + 512 * half:2048 + 512 * (half + 1)],
                        start=False, stop=True))
                p.s(lambda e, half=half, bank=bank, t=t: e.activation(
                    VPv[:, t, 8 * half:8 * (half + 1), 0:64],
                    bank.rearrange("p (h n) -> p h n", n=64), AF.Identity), p.tok("pe"))

        for t in range(NTH):
            b = t % 2
            xin = xt[:, 1024 * b:1024 * (b + 1)]
            uin = ub[:, 1024 * b:1024 * (b + 1)]
            uTt = uT[:, 1024 * b:1024 * (b + 1)]
            w = [dma_toks[t]]
            if last_use_ub[b]:
                w.append(last_use_ub[b])
            ntok = tile_ln(p, C, xin, t, uin, wait=w)
            if t + 2 < NTH:
                dma_toks[t + 2] = p.ds(lambda e, t=t, b=b: e.dma_start(
                    xt[:, 1024 * b:1024 * (b + 1)],
                    x_src[128 * (t + 2):128 * (t + 3), :]), [ntok])
            trw = [ntok]
            if last_evT[b]:
                trw.append(last_evT[b])
            for c in range(8):
                p.t(lambda e, c=c, b=b, uin=uin: e.transpose(
                    pbt[:, 1024 * b + 128 * c:1024 * b + 128 * (c + 1)],
                    uin[:, 128 * c:128 * (c + 1)], C["eyeb"][:]), trw if c == 0 else ())
            last_use_ub[b] = p.tok("pe")[0]
            evw = list(p.tok("pe"))
            evtoks[t] = p.v(lambda e, uTt=uTt, b=b: e.tensor_copy(
                uTt, pbt[:, 1024 * b:1024 * (b + 1)]), evw)
            last_evT[b] = evtoks[t]
            if t >= 1:
                do_proj(t - 1)
        do_proj(NTH - 1)
        p.barrier()
        stA.close()
        wo = sb("a_wo", [128, 8 * 1024], BF16)
        masks = sb("a_masks", [128, 6 * 512], BF16)
        for c in range(8):
            p.dg(lambda e, c=c: e.dma_start(
                wo[:, 1024 * c:1024 * (c + 1)], dr["l0_wo"][128 * c:128 * (c + 1), :]))
        for i in range(6):
            p.dg(lambda e, i=i: e.dma_start(masks[:, 512 * i:512 * (i + 1)],
                                            dr["l0_masks"][128 * i:128 * (i + 1), :]))
        p.barrier()
        sq = sb("a_sq", [128, 2048], F32)
        Pb = sb("a_Pb", [128, 2 * 1024], BF16)
        attn = sb("a_attn", [128, 8 * 512], BF16)
        yb = sb("a_yb", [128, 1024], F32)
        xr = sb("a_xr", [128, 1024], F32)

        norm_bound(p, C, QTv, 2048, sq, [psA, psB, psS[:, 0:512], psS[:, 512:1024]], 8, crow)
        norm_bound(p, C, KTv, 2304, sq, [psA, psB, psS[:, 0:512], psS[:, 512:1024]], 9, crow)
        p.v(lambda e: e.tensor_tensor(crow[0:1, 10:11], crow[0:1, 8:9], crow[0:1, 9:10], ALU.mult))
        p.s(lambda e: e.activation(crow[0:1, 11:12], crow[0:1, 10:11], AF.Ln), p.tok("dve"))
        p.s(lambda e: e.activation(crow[0:1, 12:13], crow[0:1, 11:12], AF.Exp, scale=0.5))
        p.v(lambda e: e.tensor_scalar_mul(crow[0:1, 13:14], crow[0:1, 12:13], -0.125), p.tok("act"))
        bias_col_from(p, C, crow, 13, bias_c[:, 0:1], psA, None)
        p.barrier()
        stP.close()
        psS4 = st.enter_context(nc.psum_tensor("a_psS4", [128, 2048], F32))
        psAcc2 = st.enter_context(nc.psum_tensor("a_psAcc2", [128, 1024], F32))
        psBb = st.enter_context(nc.psum_tensor("a_psBb", [128, 512], F32))

        qk_tok = {}
        exp_tok = {}
        av_tok = {}
        h_done = {}
        gg_off = [0]
        oproj_tok = [None]
        for qt in range(4):
            items = [(h, kt) for h in range(16) for kt in range(6)]
            groups = [items[i:i + 2] for i in range(0, len(items), 2)]
            G = len(groups)
            pending = []

            def issue_qk(g, qt=qt, gg_off=gg_off):
                gg = gg_off[0] + g
                for i, (h, kt) in enumerate(groups[g]):
                    hp, hb = h % 2, h // 2
                    w = []
                    if i == 0 and gg >= 2:
                        w.append(exp_tok[gg - 2])
                    if i == 0 and g <= 1 and oproj_tok[0] is not None:
                        w.append(oproj_tok[0])
                    qk_tok[gg] = p.t(lambda e, kt=kt, qt=qt, gg=gg, i=i, hp=hp, hb=hb: e.matmul(
                        psS4[:, 1024 * (gg % 2) + 512 * i:1024 * (gg % 2) + 512 * (i + 1)],
                        KT[64 * hp:64 * (hp + 1),
                           2304 * hb + 512 * qt + 128 * kt:2304 * hb + 512 * qt + 128 * (kt + 1)],
                        QT[64 * hp:64 * (hp + 1),
                           2048 * hb + 512 * qt:2048 * hb + 512 * (qt + 1)],
                        start=True, stop=True), w)

            def issue_exp(g, gg_off=gg_off):
                gg = gg_off[0] + g
                w = [qk_tok[gg]]
                if gg >= 2:
                    w.append(av_tok[gg - 2])
                tok = p.s(lambda e, gg=gg: e.activation(
                    Pb[:, 1024 * (gg % 2):1024 * (gg % 2) + 1024],
                    psS4[:, 1024 * (gg % 2):1024 * (gg % 2) + 1024],
                    AF.Exp, scale=0.125, bias=bias_c[:, 0:1]), w)
                kt0 = groups[g][0][1]
                tok = p.v(lambda e, gg=gg, kt0=kt0: e.tensor_tensor(
                    Pb[:, 1024 * (gg % 2):1024 * (gg % 2) + 1024],
                    Pb[:, 1024 * (gg % 2):1024 * (gg % 2) + 1024],
                    masks[:, 512 * kt0:512 * kt0 + 1024], ALU.mult), [tok])
                exp_tok[gg] = tok

            def issue_av(g, it, qt=qt, gg_off=gg_off):
                gg = gg_off[0] + g
                for i, (h, kt) in enumerate(groups[g]):
                    w = [exp_tok[gg]] if i == 0 else []
                    if kt == 0 and (qt, h - 2) in h_done:
                        w.append(h_done[(qt, h - 2)])
                    if kt == 0 and h <= 1 and (qt - 1, 14 + h % 2) in h_done:
                        w.append(h_done[(qt - 1, 14 + h % 2)])
                    av_tok[gg] = p.t(lambda e, kt=kt, qt=qt, gg=gg, i=i, h=h: e.matmul(
                        psAcc2[0:65, 512 * (h % 2):512 * (h % 2) + 512],
                        VP[:, 1040 * (4 * qt + kt) + 65 * h:1040 * (4 * qt + kt) + 65 * h + 65],
                        Pb[:, 1024 * (gg % 2) + 512 * i:1024 * (gg % 2) + 512 * (i + 1)],
                        start=(kt == 0), stop=(kt == 5),
                        skip_group_check=True), w)
                    if kt == 5:
                        rt = p.v(lambda e, h=h: e.reciprocal(
                            rsumf[0:1, 512 * (h % 2):512 * (h % 2) + 512],
                            psAcc2[64:65, 512 * (h % 2):512 * (h % 2) + 512]),
                            [av_tok[gg]])
                        rt = p.v(lambda e, h=h: e.tensor_copy(
                            rsumZ[0:1, 512 * (h % 2):512 * (h % 2) + 512],
                            rsumf[0:1, 512 * (h % 2):512 * (h % 2) + 512]), [rt])
                        pending.append((h, rt, it))

            def flush_tail(it, qt=qt):
                while pending and pending[0][2] <= it - 2:
                    h, rt, _ = pending.pop(0)
                    bw = [rt]
                    if (qt, h - 1) in h_done:
                        bw.append(h_done[(qt, h - 1)])
                    elif h == 0 and (qt - 1, 15) in h_done:
                        bw.append(h_done[(qt - 1, 15)])
                    bt = p.t(lambda e, h=h: e.matmul(
                        psBb[0:64, :], C["onesb"][:, 0:64],
                        rsumZ[:, 512 * (h % 2):512 * (h % 2) + 512],
                        start=True, stop=True), bw)
                    bt = p.v(lambda e, h=h: e.tensor_copy(
                        bcastS[:, 512 * (h % 2):512 * (h % 2) + 512],
                        psBb[0:64, :]), [bt])
                    h_done[(qt, h)] = p.v(lambda e, h=h, qt=qt: e.tensor_tensor(
                        attn[64 * (h % 2):64 * (h % 2) + 64,
                             512 * (h // 2):512 * (h // 2) + 512],
                        psAcc2[0:64, 512 * (h % 2):512 * (h % 2) + 512],
                        bcastS[:, 512 * (h % 2):512 * (h % 2) + 512],
                        ALU.mult), [bt])

            for g in range(G):
                issue_qk(g)
                issue_exp(g)
                if g >= 1:
                    issue_av(g - 1, g)
                flush_tail(g)
            issue_av(G - 1, G)
            flush_tail(G + 2)
            gg_off[0] += G

            prev_store = None
            attn_w = [h_done[(qt, 14)], h_done[(qt, 15)]]
            for ts in range(4):
                tglob = 4 * qt + ts
                mw = attn_w + list(p.tok("act"))
                for nh in range(2):
                    for pr in range(8):
                        p.t(lambda e, ts=ts, nh=nh, pr=pr: e.matmul(
                            psS4[:, 512 * nh:512 * (nh + 1)],
                            attn[:, 512 * pr + 128 * ts:512 * pr + 128 * (ts + 1)],
                            wo[:, 1024 * pr + 512 * nh:1024 * pr + 512 * (nh + 1)],
                            start=(pr == 0), stop=False), mw if (nh == 0 and pr == 0) else ())
                    p.t(lambda e, nh=nh: e.matmul(
                        psS4[:, 512 * nh:512 * (nh + 1)], C["onesb"][:, 0:128],
                        brow[:, 3072 + 512 * nh:3072 + 512 * (nh + 1)], start=False, stop=True))
                xw = list(p.tok("dve"))
                if prev_store:
                    xw.append(prev_store)
                xrtok = p.ds(lambda e, tglob=tglob: e.dma_start(
                    xr[:], x_src[128 * (tglob + 1):128 * (tglob + 2), :]), xw)
                prev_store = ln_resid_store(
                    p, C, "psum2", psS4[:, 0:512], psS4[:, 512:1024], xr[:], yb,
                    x_dst[128 * tglob:128 * (tglob + 1), :],
                    extra_wait=list(p.tok("pe")) + [xrtok])
            oproj_tok[0] = prev_store
        p.barrier()


def phase_l0f(p, C, nc, dr, x_src, x_dst, cio):
    o2d = cio["o2_dram"]
    with ExitStack() as st:
        sb = lambda n, shp, dt: st.enter_context(nc.sbuf_tensor(n, shp, dt))
        ps = lambda n, shp, dt=F32: st.enter_context(nc.psum_tensor(n, shp, dt))
        stA = ExitStack()
        sba = lambda n, shp, dt: stA.enter_context(nc.sbuf_tensor(n, shp, dt))
        xt = sb("f_xt", [128, 2 * 1024], F32)
        ub = sb("f_ub", [128, 2 * 1024], BF16)
        scfull = sb("f_scfull", [128, 64], F32)
        scloc = sb("f_scloc", [128, 16], F32)
        bis = sb("f_bis", [128, 8], F32)
        gates = sb("f_gates", [128, 16], F32)
        o2tok = sb("f_o2tok", [128, 2 * 1024], BF16)
        W1 = sba("f_W1", [128, 8 * 4096], BF16)
        W2 = sba("f_W2", [128, 32 * 1024], BF16)
        b1c = sba("f_b1c", [128, 32], F32)
        b2c = sba("f_b2c", [128, 8], F32)
        wr8 = sba("f_wr8", [8, 128], F32)
        wrc = sba("f_wrc", [128, 8], BF16)
        u2T = sba("f_u2T", [128, 8 * 2048], BF16)
        scrow = sba("f_scrow", [1, 2048], F32)
        scl16 = sba("f_scl16", [16, 128], F32)
        hbuf = sba("f_hb", [128, 2 * 256], BF16)
        o2f = sba("f_o2f", [128, 256], BF16)
        pbt = ps("f_pbt", [128, 1024], BF16)
        psH = ps("f_psH", [128, 2 * 512])
        psO0 = ps("f_psO0", [128, 2 * 256])
        psO1 = ps("f_psO1", [128, 2 * 256])
        psO2 = ps("f_psO2", [128, 2 * 256])
        psO3 = ps("f_psO3", [128, 2 * 256])
        psC = ps("f_psC", [128, 512])
        psOm = [psO0[:, 0:256], psO0[:, 256:512], psO1[:, 0:256], psO1[:, 256:512],
                psO2[:, 0:256], psO2[:, 256:512], psO3[:, 0:256], psO3[:, 256:512]]

        for c in range(8):
            p.dg(lambda e, c=c: e.dma_start(W1[:, 4096 * c:4096 * (c + 1)],
                                            dr["mod_w1"][128 * c:128 * (c + 1), :]))
        for c in range(32):
            p.dg(lambda e, c=c: e.dma_start(W2[:, 1024 * c:1024 * (c + 1)],
                                            dr["mod_w2"][128 * c:128 * (c + 1), :]))
        p.dg(lambda e: e.dma_start(b1c[:], dr["mod_b1c"]))
        p.dg(lambda e: e.dma_start(b2c[:], dr["mod_b2c"]))
        p.ds(lambda e: e.dma_start(wr8[:], dr["mod_wr8"]))
        p.barrier()
        p.t(lambda e: e.matmul(psC[:, 0:8], wr8[:], C["eyef"][0:8, 0:8], start=True, stop=True))
        p.v(lambda e: e.tensor_copy(wrc[:], psC[:, 0:8]), p.tok("pe"))
        p.barrier()

        dma_toks = {}
        last_use_ub = {0: None, 1: None}
        last_evT = {0: None, 1: None}
        for t in (0, 1):
            b = t % 2
            dma_toks[t] = p.ds(lambda e, t=t, b=b: e.dma_start(
                xt[:, 1024 * b:1024 * (b + 1)], x_src[128 * t:128 * (t + 1), :]))
        for t in range(NT):
            b = t % 2
            xin = xt[:, 1024 * b:1024 * (b + 1)]
            uin = ub[:, 1024 * b:1024 * (b + 1)]
            w = [dma_toks[t]]
            if last_use_ub[b]:
                w.append(last_use_ub[b])
            ntok = tile_ln(p, C, xin, t, uin, wait=w)
            if t + 2 < NT:
                dma_toks[t + 2] = p.ds(lambda e, t=t, b=b: e.dma_start(
                    xt[:, 1024 * b:1024 * (b + 1)],
                    x_src[128 * (t + 2):128 * (t + 3), :]), [ntok])
            trw = [ntok]
            if last_evT[0]:
                trw.append(last_evT[0])
            for c in range(8):
                p.t(lambda e, c=c, uin=uin: e.transpose(
                    pbt[:, 128 * c:128 * (c + 1)],
                    uin[:, 128 * c:128 * (c + 1)], C["eyeb"][:]), trw if c == 0 else ())
            last_use_ub[b] = p.tok("pe")[0]
            last_evT[0] = p.v(lambda e, t=t: e.tensor_copy(
                u2T.rearrange("p (c n) -> p c n", n=2048)[:, :, 128 * t:128 * (t + 1)],
                pbt[:, 0:1024].rearrange("p (c n) -> p c n", n=128)),
                p.tok("pe"))
        p.barrier()

        for n in range(4):
            for c in range(8):
                p.t(lambda e, n=n, c=c: e.matmul(
                    psC[0:1, 0:512], wrc[:, c:c + 1],
                    u2T[:, 2048 * c + 512 * n:2048 * c + 512 * (n + 1)],
                    start=(c == 0), stop=(c == 7)), p.tok("dve") if c == 0 else ())
            p.v(lambda e, n=n: e.tensor_copy(scrow[0:1, 512 * n:512 * (n + 1)], psC[0:1, 0:512]),
                p.tok("pe"))
        p.ds(lambda e: e.dma_start(cio["sc_in"], scrow[0:1, :]), p.tok("dve"))
        p.ccop(lambda e: e.collective_compute(
            "AllGather", ALU.bypass, GROUPS, ins=[cio["sc_in"].opt()],
            outs=[cio["sc_out"].opt()]), p.tok("qs"))
        p.ds(lambda e: e.dma_start(
            scfull[:], cio["sc_out"].rearrange("a (c g) -> (a c) g", g=64)), p.tok("cc"))
        p.ds(lambda e: e.dma_start(
            scl16[:], cio["sc_in"].rearrange("a (b c) -> (a b) c", c=128)), p.tok("cc"))
        p.t(lambda e: e.transpose(psC[:, 0:16], scl16[:], C["eyef"][0:16, 0:16]), p.tok("qs"))
        p.v(lambda e: e.tensor_copy(scloc[:], psC[:, 0:16]), p.tok("pe"))
        p.barrier()

        K_SEL = 4096.0
        N_BIS = 42
        p.v(lambda e: e.memset(bis[:, 0:1], -10000.0))
        p.v(lambda e: e.memset(bis[:, 1:2], 10000.0))
        p.v(lambda e: e.memset(bis[:, 3:4], 3.0e38))

        def bis_pre(it):
            p.v(lambda e: e.tensor_add(bis[:, 6:7], bis[:, 0:1], bis[:, 1:2]))
            p.v(lambda e: e.tensor_scalar_mul(bis[:, 2:3], bis[:, 6:7], 0.5))
            p.v(lambda e: e.tensor_single_scalar(C["tiny"][:], scfull[:], bis[:, 2:3],
                                                 ALU.is_gt))
            return p.v(lambda e: e.reduce_sum(bis[:, 6:7], C["tiny"][:], axis=AX.X))

        def bis_step(it, red_tok):
            cnt = p.t(lambda e: e.matmul(psC[:, 0:1], C["onesf"][:], bis[:, 6:7],
                                         start=True, stop=True), [red_tok])
            p.v(lambda e: e.tensor_single_scalar(bis[:, 4:5], psC[:, 0:1], K_SEL,
                                                 ALU.is_gt), [cnt])
            p.v(lambda e: e.tensor_single_scalar(bis[:, 5:6], psC[:, 0:1], K_SEL,
                                                 ALU.is_equal))
            p.v(lambda e: e.tensor_sub(bis[:, 7:8], bis[:, 2:3], bis[:, 3:4]))
            p.v(lambda e: e.scalar_tensor_tensor(bis[:, 3:4], bis[:, 7:8], bis[:, 5:6],
                                                 bis[:, 3:4], ALU.mult, ALU.add))
            p.v(lambda e: e.tensor_sub(bis[:, 7:8], bis[:, 2:3], bis[:, 0:1]))
            p.v(lambda e: e.scalar_tensor_tensor(bis[:, 0:1], bis[:, 7:8], bis[:, 4:5],
                                                 bis[:, 0:1], ALU.mult, ALU.add))
            p.v(lambda e: e.tensor_scalar(bis[:, 5:6], bis[:, 4:5], -1.0, 1.0,
                                          ALU.mult, ALU.add))
            p.v(lambda e: e.tensor_sub(bis[:, 7:8], bis[:, 2:3], bis[:, 1:2]))
            p.v(lambda e: e.scalar_tensor_tensor(bis[:, 1:2], bis[:, 7:8], bis[:, 5:6],
                                                 bis[:, 1:2], ALU.mult, ALU.add))
            if it + 1 < N_BIS:
                return bis_pre(it + 1)
            return None

        lastgelu = {0: None, 1: None}
        w2done = {0: None, 1: None}
        lasto2 = None
        last_etok = None
        seq = [(tq, f) for tq in range(8) for f in range(32)]

        def issue_w1(tq, f):
            hb = f % 2
            hw = [lastgelu[hb]] if lastgelu[hb] else []
            first = True
            qk_last = None
            for c in range(8):
                qk_last = p.t(lambda e, tq=tq, f=f, c=c, hb=hb: e.matmul(
                    psH[:, 512 * hb:512 * hb + 256],
                    W1[:, 4096 * c + 128 * f:4096 * c + 128 * (f + 1)],
                    u2T[:, 2048 * c + 256 * tq:2048 * c + 256 * (tq + 1)],
                    start=(c == 0), stop=(c == 7), skip_group_check=True),
                    hw if first else ())
                first = False
            gw = [qk_last]
            if w2done[hb]:
                gw.append(w2done[hb])
            lastgelu[hb] = p.s(lambda e, f=f, hb=hb: e.activation(
                hbuf[:, 256 * hb:256 * (hb + 1)], psH[:, 512 * hb:512 * hb + 256],
                AF.Gelu_apprx_tanh, bias=b1c[:, f:f + 1]), gw)

        def issue_w2(tq, f):
            nonlocal lasto2, last_etok
            hb = f % 2
            w2w = [lastgelu[hb]]
            if f == 0 and last_etok is not None:
                w2w.append(last_etok)
            for m in range(8):
                p.t(lambda e, f=f, m=m, hb=hb: e.matmul(
                    psOm[m], W2[:, 1024 * f + 128 * m:1024 * f + 128 * (m + 1)],
                    hbuf[:, 256 * hb:256 * (hb + 1)],
                    start=(f == 0 and m % 2 == 0), stop=(f == 31 and m % 2 == 1),
                    skip_group_check=True),
                    w2w if m == 0 else ())
            w2done[hb] = p.tok("pe")[0]
            if f != 31:
                return
            w2fin = w2done[hb]
            for m in range(8):
                evw = [w2fin]
                if lasto2:
                    evw.append(lasto2)
                etok = p.v(lambda e, m=m: e.tensor_scalar(
                    o2f[:], psOm[m], b2c[:, m:m + 1], None, ALU.add), evw)
                last_etok = etok
                for half in range(2):
                    p.t(lambda e, half=half: e.transpose(
                        pbt[:, 128 * half:128 * (half + 1)],
                        o2f[:, 128 * half:128 * (half + 1)], C["eyeb"][:]),
                        [etok] if half == 0 else ())
                lasto2 = p.v(lambda e, m=m: e.tensor_copy(
                    o2tok.rearrange("p (tt n) -> p tt n", n=1024)[:, 0:2,
                                                                 128 * m:128 * (m + 1)],
                    pbt[:, 0:256].rearrange("p (b n) -> p b n", n=128)),
                    list(p.tok("pe")) + list(p.tok("qs")))
            for tt in range(2):
                p.ds(lambda e, tq=tq, tt=tt: e.dma_start(
                    o2d[128 * (2 * tq + tt):128 * (2 * tq + tt + 1), :],
                    o2tok[:, 1024 * tt:1024 * (tt + 1)]), p.tok("dve"))

        red_tok = bis_pre(0)
        bis_it = 0
        for idx, (tq, f) in enumerate(seq):
            issue_w1(tq, f)
            if idx >= 1:
                issue_w2(*seq[idx - 1])
            if idx >= 1 and bis_it < N_BIS:
                red_tok = bis_step(bis_it, red_tok)
                bis_it += 1
        issue_w2(*seq[-1])
        p.v(lambda e: e.tensor_single_scalar(bis[:, 4:5], bis[:, 3:4], 1.0e38, ALU.is_lt))
        p.v(lambda e: e.tensor_sub(bis[:, 7:8], bis[:, 3:4], bis[:, 2:3]))
        p.v(lambda e: e.scalar_tensor_tensor(bis[:, 6:7], bis[:, 7:8], bis[:, 4:5],
                                             bis[:, 2:3], ALU.mult, ALU.add))
        p.v(lambda e: e.tensor_copy(bis[:, 3:4], bis[:, 6:7]))
        p.v(lambda e: e.tensor_single_scalar(C["tiny"][:, 16:32], scloc[:], bis[:, 3:4],
                                             ALU.is_gt))
        bis_tok = p.tok("dve")
        p.barrier()
        stA.close()

        p.s(lambda e: e.activation(gates[:], scloc[:], AF.Sigmoid), bis_tok)
        gtok = p.v(lambda e: e.tensor_tensor(gates[:], gates[:], C["tiny"][:, 16:32],
                                             ALU.mult), p.tok("act"))

        ub3 = st.enter_context(nc.sbuf_tensor("f_ub3", [128, 3 * 1024], BF16))
        o23 = st.enter_context(nc.sbuf_tensor("f_o23", [128, 3 * 1024], BF16))
        cb3 = st.enter_context(nc.sbuf_tensor("f_cb3", [128, 3 * 1024], F32))

        def mk_src_l0f(t, s, xin, xdt, wfree):
            utok = p.s(lambda e, t=t, s=s, xin=xin: e.activation(
                ub3[:, 1024 * s:1024 * (s + 1)], xin, AF.Identity,
                bias=C["mrs"][:, 4 * t + 3:4 * t + 4],
                scale=C["mrs"][:, 4 * t + 2:4 * t + 3]), [xdt])
            otok = p.ds(lambda e, t=t, s=s: e.dma_start(
                o23[:, 1024 * s:1024 * (s + 1)], o2d[128 * t:128 * (t + 1), :]), wfree)
            ctok = p.v(lambda e, t=t, s=s: e.scalar_tensor_tensor(
                cb3[:, 1024 * s:1024 * (s + 1)], o23[:, 1024 * s:1024 * (s + 1)],
                gates[:, t:t + 1], ub3[:, 1024 * s:1024 * (s + 1)],
                ALU.mult, ALU.add), [utok, otok, gtok])
            return cb3[:, 1024 * s:1024 * (s + 1)], ctok

        ln_resid_pipe(p, C, nc, st, "f", NT, mk_src_l0f, x_src, x_dst)
        p.barrier()


def phase_l1a(p, C, nc, dr, x_src, x_dst, cio):
    with ExitStack() as st:
        sb = lambda n, shp, dt: st.enter_context(nc.sbuf_tensor(n, shp, dt))
        ps = lambda n, shp, dt=F32: st.enter_context(nc.psum_tensor(n, shp, dt))
        masks = sb("c_masks", [128, 4 * 512], BF16)
        xt = sb("c_xt", [128, 2 * 1024], F32)
        wqp = sb("c_wqp", [128, 8 * 128], BF16)
        wkp = sb("c_wkp", [128, 8 * 128], BF16)
        wvpp = sb("c_wvp", [128, 8 * 128], BF16)
        woT = sb("c_woT", [128, 2 * 1024], BF16)
        QTp = sb("c_QTp", [128, 8192], BF16)
        KT0 = sb("c_KT0", [128, 8192], BF16)
        KT1 = sb("c_KT1", [128, 8192], BF16)
        VPp = sb("c_VPp", [128, 64 * 130], BF16)
        AT0 = sb("c_AT0", [128, 8192], BF16)
        AT1 = sb("c_AT1", [128, 8192], BF16)
        sq = sb("c_sq", [128, 2048], F32)
        crow = sb("c_crow", [1, 16], F32)
        bias_c = sb("c_biasc", [128, 1], F32)
        Pb = sb("c_Pb", [128, 2 * 1024], BF16)
        rsumZ = sb("c_rsumZ", [128, 2 * 512], BF16)
        rsumf = sb("c_rsumf", [1, 2 * 512], F32)
        bcastS = sb("c_bcastS", [64, 2 * 512], BF16)
        ytile = sb("c_ytile", [128, 1024], BF16)
        stP = ExitStack()
        pbt = stP.enter_context(nc.psum_tensor("c_pbt", [128, 2 * 1024], BF16))
        ub = stP.enter_context(nc.sbuf_tensor("c_ub", [128, 2 * 1024], BF16))
        u3T = stP.enter_context(nc.sbuf_tensor("c_u3T", [128, 8 * 2048], BF16))

        for i in range(4):
            p.dg(lambda e, i=i: e.dma_start(masks[:, 512 * i:512 * (i + 1)],
                                            dr["l1_masks"][128 * i:128 * (i + 1), :]))
        for c in range(2):
            p.dg(lambda e, c=c: e.dma_start(woT[:, 1024 * c:1024 * (c + 1)],
                                            dr["l1_wo"][128 * c:128 * (c + 1), :]))
        p.v(lambda e: e.memset(VPp[:], 1.0))
        p.v(lambda e: e.memset(KT0[64:128, :], 0.0))
        p.v(lambda e: e.memset(KT1[0:64, :], 0.0))
        p.v(lambda e: e.memset(rsumZ[:], 0.0))
        p.barrier()

        dma_toks = {}
        last_use_ub = {0: None, 1: None}
        last_evT = {0: None, 1: None}
        for t in (0, 1):
            b = t % 2
            dma_toks[t] = p.ds(lambda e, t=t, b=b: e.dma_start(
                xt[:, 1024 * b:1024 * (b + 1)], x_src[128 * t:128 * (t + 1), :]))
        for t in range(NT):
            b = t % 2
            xin = xt[:, 1024 * b:1024 * (b + 1)]
            uin = ub[:, 1024 * b:1024 * (b + 1)]
            w = [dma_toks[t]]
            if last_use_ub[b]:
                w.append(last_use_ub[b])
            ntok = tile_ln(p, C, xin, t, uin, wait=w)
            if t + 2 < NT:
                dma_toks[t + 2] = p.ds(lambda e, t=t, b=b: e.dma_start(
                    xt[:, 1024 * b:1024 * (b + 1)],
                    x_src[128 * (t + 2):128 * (t + 3), :]), [ntok])
            trw = [ntok]
            if last_evT[b]:
                trw.append(last_evT[b])
            for c in range(8):
                p.t(lambda e, c=c, b=b, uin=uin: e.transpose(
                    pbt[:, 1024 * b + 128 * c:1024 * b + 128 * (c + 1)],
                    uin[:, 128 * c:128 * (c + 1)], C["eyeb"][:]), trw if c == 0 else ())
            last_use_ub[b] = p.tok("pe")[0]
            last_evT[b] = p.v(lambda e, t=t, b=b: e.tensor_copy(
                u3T.rearrange("p (c n) -> p c n", n=2048)[:, :, 128 * t:128 * (t + 1)],
                pbt[:, 1024 * b:1024 * (b + 1)].rearrange("p (c n) -> p c n", n=128)),
                p.tok("pe"))
        for c in range(8):
            p.ds(lambda e, c=c: e.dma_start(
                cio["ag_in"][128 * c:128 * (c + 1), :], u3T[:, 2048 * c:2048 * (c + 1)]),
                p.tok("dve"))
        p.ccop(lambda e: e.collective_compute(
            "AllGather", ALU.bypass, GROUPS, ins=[cio["ag_in"].opt()],
            outs=[cio["ag_out"].opt()]), p.tok("qs"))
        p.barrier()
        stP.close()
        xb = st.enter_context(nc.sbuf_tensor("c_xb", [128, 8 * 2048], BF16))
        psS4 = st.enter_context(nc.psum_tensor("c_psS4", [128, 2048], F32))
        psAcc2 = st.enter_context(nc.psum_tensor("c_psAcc2", [128, 1024], F32))
        psB = st.enter_context(nc.psum_tensor("c_psB", [128, 512], F32))
        psV = st.enter_context(nc.psum_tensor("c_psV", [128, 256], F32))

        wsafe = [None]
        for hp in range(2):
            AT = AT0 if hp == 0 else AT1
            wdma = [wsafe[0]] if wsafe[0] is not None else []
            for wsb, nm in ((wqp, "l1_wq"), (wkp, "l1_wk"), (wvpp, "l1_wv")):
                for c in range(8):
                    p.dg(lambda e, wsb=wsb, nm=nm, c=c, hp=hp: e.dma_start(
                        wsb[:, 128 * c:128 * (c + 1)],
                        dr[nm][128 * c:128 * (c + 1), 128 * hp:128 * (hp + 1)]),
                        wdma if (wsb is wqp and c == 0) else ())
            sr_tok = {}

            def load_sr(sr, w):
                r, th = divmod(sr, 2)
                b = sr % 2
                for c in range(8):
                    p.dg(lambda e, r=r, th=th, b=b, c=c: e.dma_start(
                        xb[:, 8192 * b + 1024 * c:8192 * b + 1024 * (c + 1)],
                        cio["ag_out"][1024 * r + 128 * c:1024 * r + 128 * (c + 1),
                                      1024 * th:1024 * (th + 1)]),
                        w if c == 0 else ())
                sr_tok[sr] = p.tok("qg")[0]

            load_sr(0, [])
            load_sr(1, [])
            for sr in range(8):
                r, th = divmod(sr, 2)
                b = sr % 2
                ldt = [sr_tok[sr]]
                for dst, wsb in (("q", wqp), ("k", wkp)):
                    for n2 in range(2):
                        gw = list(ldt) + list(p.tok("act"))
                        for c in range(8):
                            p.t(lambda e, wsb=wsb, b=b, n2=n2, c=c: e.matmul(
                                psS4[:, 0:512], wsb[:, 128 * c:128 * (c + 1)],
                                xb[:, 8192 * b + 1024 * c + 512 * n2:
                                   8192 * b + 1024 * c + 512 * (n2 + 1)],
                                start=(c == 0), stop=(c == 7)), gw if c == 0 else ())
                        seg = 2048 * r + 1024 * th + 512 * n2
                        if dst == "q":
                            p.s(lambda e, seg=seg: e.activation(
                                QTp[:, seg:seg + 512],
                                psS4[:, 0:512], AF.Identity), p.tok("pe"))
                        else:
                            p.s(lambda e, seg=seg: e.activation(
                                KT0[0:64, seg:seg + 512],
                                psS4[0:64, 0:512], AF.Identity), p.tok("pe"))
                            p.s(lambda e, seg=seg: e.activation(
                                KT1[64:128, seg:seg + 512],
                                psS4[64:128, 0:512], AF.Identity))
                for ts2 in range(8):
                    tglob = 16 * r + 8 * th + ts2
                    bank = psV[:, 0:128] if ts2 % 2 == 0 else psB[:, 0:128]
                    vw = list(ldt) + list(p.tok("act"))
                    for c in range(8):
                        p.t(lambda e, ts2=ts2, b=b, c=c, bank=bank: e.matmul(
                            bank,
                            xb[:, 8192 * b + 1024 * c + 128 * ts2:
                               8192 * b + 1024 * c + 128 * (ts2 + 1)],
                            wvpp[:, 128 * c:128 * (c + 1)],
                            start=(c == 0), stop=(c == 7)), vw if c == 0 else ())
                    p.s(lambda e, tglob=tglob, bank=bank: e.activation(
                        VPp.rearrange("p (tt h n) -> p tt h n", h=2, n=65)[:, tglob, :, 0:64],
                        bank.rearrange("p (h n) -> p h n", n=64), AF.Identity),
                        p.tok("pe"))
                if sr + 2 < 8:
                    load_sr(sr + 2, list(p.tok("pe")))
            wsafe[0] = p.tok("pe")[0]
            p.barrier()
            QTv = QTp.rearrange("p (c n) -> p c n", c=1)
            norm_bound(p, C, QTv, 8192, sq,
                       [psS4[:, 0:512], psS4[:, 512:1024], psS4[:, 1024:1536],
                        psS4[:, 1536:2048]], 8, crow, nch=1)
            norm_bound(p, C, KT0.rearrange("p (c n) -> p c n", c=1), 8192, sq,
                       [psS4[:, 0:512], psS4[:, 512:1024], psS4[:, 1024:1536],
                        psS4[:, 1536:2048]], 9, crow, nch=1)
            norm_bound(p, C, KT1.rearrange("p (c n) -> p c n", c=1), 8192, sq,
                       [psS4[:, 0:512], psS4[:, 512:1024], psS4[:, 1024:1536],
                        psS4[:, 1536:2048]], 14, crow, nch=1)
            p.v(lambda e: e.tensor_tensor(crow[0:1, 9:10], crow[0:1, 9:10],
                                          crow[0:1, 14:15], ALU.max))
            p.v(lambda e: e.tensor_tensor(crow[0:1, 10:11], crow[0:1, 8:9], crow[0:1, 9:10], ALU.mult))
            p.s(lambda e: e.activation(crow[0:1, 11:12], crow[0:1, 10:11], AF.Ln), p.tok("dve"))
            p.s(lambda e: e.activation(crow[0:1, 12:13], crow[0:1, 11:12], AF.Exp, scale=0.5))
            p.v(lambda e: e.tensor_scalar_mul(crow[0:1, 13:14], crow[0:1, 12:13], -0.125),
                p.tok("act"))
            bias_col_from(p, C, crow, 13, bias_c[:, 0:1], psB, None)
            p.barrier()
            for h2 in range(2):
                items = []
                for qt in range(16):
                    for kt in range(4 * qt + 4):
                        items.append((qt, kt))
                groups = [items[i:i + 2] for i in range(0, len(items), 2)]
                G = len(groups)
                qk_tok = {}
                exp_tok = {}
                av_tok = {}
                qt_done = {}

                KTsel = KT0 if h2 == 0 else KT1

                def issue_qk(g, KTsel=KTsel):
                    for i, (qt, kt) in enumerate(groups[g]):
                        w = []
                        if i == 0 and g >= 2:
                            w.append(exp_tok[g - 2])
                        qk_tok[g] = p.t(lambda e, kt=kt, qt=qt, g=g, i=i, KTsel=KTsel: e.matmul(
                            psS4[:, 1024 * (g % 2) + 512 * i:1024 * (g % 2) + 512 * (i + 1)],
                            KTsel[:, 128 * kt:128 * (kt + 1)],
                            QTp[:, 512 * qt:512 * (qt + 1)],
                            start=True, stop=True), w)

                def issue_exp(g):
                    w = [qk_tok[g]]
                    if g >= 2:
                        w.append(av_tok[g - 2])
                    tok = p.s(lambda e, g=g: e.activation(
                        Pb[:, 1024 * (g % 2):1024 * (g % 2) + 1024],
                        psS4[:, 1024 * (g % 2):1024 * (g % 2) + 1024],
                        AF.Exp, scale=0.125, bias=bias_c[:, 0:1]), w)
                    qt0, kt0 = groups[g][0]
                    dlt0 = kt0 - 4 * qt0
                    if dlt0 >= 0:
                        tok = p.v(lambda e, g=g, dlt0=dlt0: e.tensor_tensor(
                            Pb[:, 1024 * (g % 2):1024 * (g % 2) + 1024],
                            Pb[:, 1024 * (g % 2):1024 * (g % 2) + 1024],
                            masks[:, 512 * dlt0:512 * dlt0 + 1024], ALU.mult), [tok])
                    exp_tok[g] = tok

                pending = []

                def issue_av(g, it):
                    for i, (qt, kt) in enumerate(groups[g]):
                        nkt = 4 * qt + 4
                        w = [exp_tok[g]] if i == 0 else []
                        if kt == 0 and qt >= 2:
                            w.append(qt_done[qt - 2])
                        av_tok[g] = p.t(lambda e, h2=h2, kt=kt, qt=qt, g=g, i=i, nkt=nkt: e.matmul(
                            psAcc2[0:65, 512 * (qt % 2):512 * (qt % 2) + 512],
                            VPp[:, 130 * kt + 65 * h2:130 * kt + 65 * h2 + 65],
                            Pb[:, 1024 * (g % 2) + 512 * i:1024 * (g % 2) + 512 * (i + 1)],
                            start=(kt == 0), stop=(kt == nkt - 1),
                            skip_group_check=True), w)
                        if kt == nkt - 1:
                            rt = p.v(lambda e, qt=qt: e.reciprocal(
                                rsumf[0:1, 512 * (qt % 2):512 * (qt % 2) + 512],
                                psAcc2[64:65, 512 * (qt % 2):512 * (qt % 2) + 512]),
                                [av_tok[g]])
                            rt = p.v(lambda e, qt=qt: e.tensor_copy(
                                rsumZ[0:1, 512 * (qt % 2):512 * (qt % 2) + 512],
                                rsumf[0:1, 512 * (qt % 2):512 * (qt % 2) + 512]), [rt])
                            pending.append((qt, rt, it))

                def flush_tail(it, h2=h2, AT=AT):
                    while pending and pending[0][2] <= it - 2:
                        qt, rt, _ = pending.pop(0)
                        bw = [rt]
                        if qt >= 1:
                            bw.append(qt_done[qt - 1])
                        bt = p.t(lambda e, qt=qt: e.matmul(
                            psB[0:64, :], C["onesb"][:, 0:64],
                            rsumZ[:, 512 * (qt % 2):512 * (qt % 2) + 512],
                            start=True, stop=True), bw)
                        bt = p.v(lambda e, qt=qt: e.tensor_copy(
                            bcastS[:, 512 * (qt % 2):512 * (qt % 2) + 512],
                            psB[0:64, :]), [bt])
                        qt_done[qt] = p.v(lambda e, h2=h2, qt=qt, AT=AT: e.tensor_tensor(
                            AT[64 * h2:64 * (h2 + 1), 512 * qt:512 * (qt + 1)],
                            psAcc2[0:64, 512 * (qt % 2):512 * (qt % 2) + 512],
                            bcastS[:, 512 * (qt % 2):512 * (qt % 2) + 512],
                            ALU.mult), [bt])

                for g in range(G):
                    issue_qk(g)
                    issue_exp(g)
                    if g >= 1:
                        issue_av(g - 1, g)
                    flush_tail(g)
                issue_av(G - 1, G)
                flush_tail(G + 2)
            p.barrier()
        lastst = None
        for ts in range(64):
            mw = list(p.tok("dve", "act"))
            for nh in range(2):
                for pr, AT in ((0, AT0), (1, AT1)):
                    p.t(lambda e, ts=ts, nh=nh, pr=pr, AT=AT: e.matmul(
                        psS4[:, 512 * nh:512 * (nh + 1)],
                        AT[:, 128 * ts:128 * (ts + 1)],
                        woT[:, 1024 * pr + 512 * nh:1024 * pr + 512 * (nh + 1)],
                        start=(pr == 0), stop=(pr == 1)), mw if (nh == 0 and pr == 0) else ())
            evw = list(p.tok("pe"))
            if lastst:
                evw.append(lastst)
            p.s(lambda e: e.activation(ytile[:, 0:512], psS4[:, 0:512], AF.Identity), evw)
            p.s(lambda e: e.activation(ytile[:, 512:1024], psS4[:, 512:1024], AF.Identity))
            lastst = p.ds(lambda e, ts=ts: e.dma_start(
                cio["rs_in"][128 * ts:128 * (ts + 1), :], ytile[:]), p.tok("act"))
        p.ccop(lambda e: e.collective_compute(
            "ReduceScatter", ALU.add, GROUPS, ins=[cio["rs_in"].opt()],
            outs=[cio["rs_out"].opt()]), p.tok("qs"))
        p.barrier()
        cc_tok = [p.tok("cc")[0]]
        ybf3 = st.enter_context(nc.sbuf_tensor("c_ybf3", [128, 3 * 1024], BF16))

        def mk_src_l1a(t, s, xin, xdt, wfree):
            ydt = p.ds(lambda e, t=t, s=s: e.dma_start(
                ybf3[:, 1024 * s:1024 * (s + 1)],
                cio["rs_out"][128 * t:128 * (t + 1), :]), wfree + cc_tok)
            return ybf3[:, 1024 * s:1024 * (s + 1)], ydt

        ln_resid_pipe(p, C, nc, st, "c", NT, mk_src_l1a, x_src, x_dst, nslot=2)
        p.barrier()


def phase_l1f(p, C, nc, dr, x_src, x_dst, cio):
    with ExitStack() as st:
        sb = lambda n, shp, dt: st.enter_context(nc.sbuf_tensor(n, shp, dt))
        ps = lambda n, shp, dt=F32: st.enter_context(nc.psum_tensor(n, shp, dt))
        xt = sb("d_xt", [128, 2 * 1024], F32)
        uf = sb("d_uf", [128, 1024], F32)
        uhi = sb("d_uhi", [128, 1024], BF16)
        ulo = sb("d_ulo", [128, 1024], BF16)
        alpha = sb("d_alpha", [128, 16 * 64], BF16)
        alphaT = sb("d_alphaT", [128, 2048], BF16)
        mcol = sb("d_mcol", [128, 8], BF16)
        taus = sb("d_taus", [128, 4], F32)
        b1p = sb("d_b1p", [128, 512], BF16)
        sIT = sb("d_sIT", [128, 8 * 64], BF16)
        sITf = sb("d_sITf", [128, 8 * 64], F32)
        sIb = sb("d_sI", [64, 1024], BF16)
        si = sb("d_si", [64, 1028], F32)
        sid_b = sb("d_sidb", [128, 1024], BF16)
        siT = sb("d_siT", [128, 8 * 64], BF16)
        SnT = sb("d_SnT", [128, 8 * 64], F32)
        SnTb = sb("d_SnTb", [128, 8 * 64], BF16)
        qT = sb("d_qT", [128, 2 * 64], BF16)
        kT = sb("d_kT", [128, 2 * 64], BF16)
        scb = sb("d_scb", [128, 64], BF16)
        scq = sb("d_scq", [64, 64], F32)
        attnSb = sb("d_attnSb", [128, 64], BF16)
        attnTb = sb("d_attnTb", [128, 64], BF16)
        vs = sb("d_vs", [128, 1024], BF16)
        ySlT = sb("d_ySlT", [128, 8 * 64], F32)
        ySl = sb("d_ySl", [64, 1024], F32)
        yhat = sb("d_yhat", [128, 1024], BF16)
        yhT = sb("d_yhT", [128, 8 * 64], BF16)
        sv = sb("d_sv", [128, 1024], BF16)
        yb = sb("d_yb", [128, 1024], F32)
        pbt = ps("d_pbt", [128, 2 * 1024], BF16)
        pbf = ps("d_pbf", [128, 2 * 512])
        psHM = ps("d_psHM", [128, 512])
        psX = ps("d_psX", [128, 64])
        psSI = ps("d_psSI", [128, 2 * 512])
        psH = psHM
        psL = psX
        psM = psHM
        psD = psX

        for t_ in (alphaT, sid_b, scb, attnSb, attnTb, vs, yhat, sv):
            p.v(lambda e, t_=t_: e.memset(t_[64:128, :], 0.0))
        p.dg(lambda e: e.dma_start(sIb[:], dr["f_sI"]))
        for c in range(8):
            p.dg(lambda e, c=c: e.dma_start(sIT[:, 64 * c:64 * (c + 1)],
                                            dr["f_sIT"][128 * c:128 * (c + 1), :]))
            p.ds(lambda e, c=c: e.dma_start(sITf[:, 64 * c:64 * (c + 1)],
                                            dr["f_sIT"][128 * c:128 * (c + 1), :]))
        p.ds(lambda e: e.dma_start(taus[0:1, 0:1], dr["f_tau"]))
        p.barrier()
        p.s(lambda e: e.activation(taus[0:1, 1:2], taus[0:1, 0:1], AF.Abs))
        p.v(lambda e: e.tensor_scalar_add(taus[0:1, 2:3], taus[0:1, 1:2], 0.1), p.tok("act"))
        p.v(lambda e: e.reciprocal(taus[0:1, 3:4], taus[0:1, 2:3]))
        p.t(lambda e: e.matmul(pbf[:, 0:1], C["onesf"][0:1, :], taus[0:1, 3:4],
                               start=True, stop=True), p.tok("dve"))
        p.v(lambda e: e.tensor_copy(taus[:, 0:1], pbf[:, 0:1]), p.tok("pe"))
        for c in range(8):
            p.t(lambda e, c=c: e.matmul(psM[:, 0:1], sIb[:, 128 * c:128 * (c + 1)],
                                        C["onesb"][0:64, 0:1], start=True, stop=True),
                p.tok("act"))
            p.s(lambda e, c=c: e.activation(mcol[:, c:c + 1], psM[:, 0:1], AF.Identity,
                                            scale=1.0 / 64.0), p.tok("pe"))
        p.barrier()

        stA = ExitStack()
        sba = lambda n, shp, dt: stA.enter_context(nc.sbuf_tensor(n, shp, dt))
        w1hi = sba("d_w1hi", [128, 8 * 512], BF16)
        w1lo = sba("d_w1lo", [128, 8 * 512], BF16)
        w1f = sba("d_w1f", [128, 512], F32)
        w1bot = sba("d_w1bot", [128, 8 * 512], BF16)
        w2f = sba("d_w2f", [128, 4 * 64], F32)
        hT = sba("d_hT", [128, 4 * 2048], F32)
        b1raw = sba("d_b1raw", [1, 512], BF16)
        b2r = sba("d_b2r", [128, 64], F32)
        logit = sba("d_logit", [128, 64], F32)
        ebuf = sba("d_ebuf", [128, 64], F32)
        mbuf = sba("d_mbuf", [128, 64], F32)
        mx8 = sba("d_mx8", [128, 8], F32)
        for c in range(8):
            p.dg(lambda e, c=c: e.dma_start(w1bot[:, 512 * c:512 * (c + 1)],
                                            dr["f_w1b"][128 * c:128 * (c + 1), :]))
        p.dg(lambda e: e.dma_start(b1raw[:], dr["f_b1"]))
        mzb = p.v(lambda e: e.memset(b2r[:], 0.0))
        p.ds(lambda e: e.dma_start(b2r[0:1, :], dr["f_b2"]), [mzb])
        for c in range(4):
            p.ds(lambda e, c=c: e.dma_start(w2f[:, 64 * c:64 * (c + 1)],
                                            dr["f_w2"][128 * c:128 * (c + 1), :]))
        for c in range(8):
            dt = p.ds(lambda e, c=c: e.dma_start(w1f[:], dr["f_w1t"][128 * c:128 * (c + 1), :]),
                      p.tok("dve"))
            p.v(lambda e, c=c: e.tensor_copy(w1hi[:, 512 * c:512 * (c + 1)], w1f[:]), [dt])
            p.v(lambda e, c=c: e.tensor_tensor(w1lo[:, 512 * c:512 * (c + 1)], w1f[:],
                                               w1hi[:, 512 * c:512 * (c + 1)], ALU.subtract))
        p.barrier()
        for c in range(8):
            p.t(lambda e, c=c: e.matmul(psM[0:1, 0:512], mcol[:, c:c + 1],
                                        w1bot[:, 512 * c:512 * (c + 1)],
                                        start=(c == 0), stop=False))
        p.t(lambda e: e.matmul(psM[0:1, 0:512], C["onesb"][0:1, 0:1], b1raw[0:1, :],
                               start=False, stop=True))
        mzp = p.v(lambda e: e.memset(b1p[:], 0.0))
        p.s(lambda e: e.activation(b1p[0:1, :], psM[0:1, 0:512], AF.Identity),
            list(p.tok("pe")) + [mzp])
        p.barrier()

        stA2 = ExitStack()
        uThi = stA2.enter_context(nc.sbuf_tensor("d_uThi", [128, 8 * 2048], BF16))
        uTlo = stA2.enter_context(nc.sbuf_tensor("d_uTlo", [128, 8 * 2048], BF16))
        dma_toks = {}
        last_lo = None
        last_hi_tr = None
        for t in (0, 1):
            b = t % 2
            dma_toks[t] = p.ds(lambda e, t=t, b=b: e.dma_start(
                xt[:, 1024 * b:1024 * (b + 1)], x_src[128 * t:128 * (t + 1), :]))
        for t in range(NT):
            b = t % 2
            xin = xt[:, 1024 * b:1024 * (b + 1)]
            w = [dma_toks[t]]
            if last_lo:
                w.append(last_lo)
            ntok = tile_ln(p, C, xin, t, uf[:], wait=w)
            if t + 2 < NT:
                dma_toks[t + 2] = p.ds(lambda e, t=t, b=b: e.dma_start(
                    xt[:, 1024 * b:1024 * (b + 1)],
                    x_src[128 * (t + 2):128 * (t + 3), :]), [ntok])
            hitok = p.v(lambda e: e.tensor_copy(uhi[:], uf[:]),
                        [ntok] + ([last_hi_tr] if last_hi_tr else []))
            lotok = p.v(lambda e: e.tensor_tensor(ulo[:], uf[:], uhi[:], ALU.subtract))
            for c in range(8):
                p.t(lambda e, c=c: e.transpose(pbt[:, 128 * c:128 * (c + 1)],
                                               uhi[:, 128 * c:128 * (c + 1)], C["eyeb"][:]),
                    [hitok] if c == 0 else ())
            last_hi_tr = p.tok("pe")[0]
            p.v(lambda e, t=t: e.tensor_copy(
                uThi.rearrange("p (c n) -> p c n", n=2048)[:, :, 128 * t:128 * (t + 1)],
                pbt[:, 0:1024].rearrange("p (c n) -> p c n", n=128)), p.tok("pe"))
            for c in range(8):
                p.t(lambda e, c=c: e.transpose(pbt[:, 1024 + 128 * c:1024 + 128 * (c + 1)],
                                               ulo[:, 128 * c:128 * (c + 1)], C["eyeb"][:]),
                    [lotok] if c == 0 else ())
            last_lo = p.v(lambda e, t=t: e.tensor_copy(
                uTlo.rearrange("p (c n) -> p c n", n=2048)[:, :, 128 * t:128 * (t + 1)],
                pbt[:, 1024:2048].rearrange("p (c n) -> p c n", n=128)), p.tok("pe"))
        p.barrier()

        lastg = None
        for fc in range(4):
            for n in range(4):
                hw = [lastg] if lastg else []
                first = True
                for c in range(8):
                    for wsb, usb, st_flag in ((w1hi, uThi, True), (w1lo, uThi, False),
                                              (w1hi, uTlo, False)):
                        p.t(lambda e, fc=fc, n=n, c=c, wsb=wsb, usb=usb, st_flag=st_flag: e.matmul(
                            psH[:], wsb[:, 512 * c + 128 * fc:512 * c + 128 * (fc + 1)],
                            usb[:, 2048 * c + 512 * n:2048 * c + 512 * (n + 1)],
                            start=(c == 0 and st_flag), stop=False), hw if first else ())
                        first = False
                p.t(lambda e, fc=fc: e.matmul(
                    psH[:], b1p[:, 128 * fc:128 * (fc + 1)], C["onesb"][:, 0:512],
                    start=False, stop=True))
                lastg = p.s(lambda e, fc=fc, n=n: e.activation(
                    hT[:, 2048 * fc + 512 * n:2048 * fc + 512 * (n + 1)], psH[:], AF.Gelu),
                    p.tok("pe"))
        p.barrier()
        lastal = None
        for t in range(NT):
            lw = list(p.tok("act", "dve"))
            for fc in range(4):
                p.t(lambda e, t=t, fc=fc: e.matmul(
                    psL[:], hT[:, 2048 * fc + 128 * t:2048 * fc + 128 * (t + 1)],
                    w2f[:, 64 * fc:64 * (fc + 1)],
                    start=(fc == 0), stop=False), lw if fc == 0 else ())
            p.t(lambda e: e.matmul(psL[:], C["onesf"][:, 0:128], b2r[:],
                                   start=False, stop=True))
            ltok = p.s(lambda e: e.activation(logit[:], psL[:], AF.Identity,
                                              scale=taus[:, 0:1]), p.tok("pe", "dve"))
            p.v(lambda e: e.max(out=mx8[:], in_=logit[:]), [ltok])
            p.v(lambda e: e.tensor_scalar_mul(C["tiny"][:, 0:1], mx8[:, 0:1], -1.0))
            etok = p.s(lambda e: e.activation(ebuf[:], logit[:], AF.Exp,
                                              bias=C["tiny"][:, 0:1]), p.tok("dve"))
            p.v(lambda e: e.tensor_single_scalar(mbuf[:], logit[:], mx8[:, 7:8], ALU.is_ge),
                [etok])
            p.v(lambda e: e.tensor_tensor(ebuf[:], ebuf[:], mbuf[:], ALU.mult))
            p.v(lambda e: e.reduce_sum(C["tiny"][:, 1:2], ebuf[:], axis=AX.X))
            p.v(lambda e: e.reciprocal(C["tiny"][:, 2:3], C["tiny"][:, 1:2]))
            altok = p.v(lambda e, t=t: e.tensor_scalar(
                alpha[:, 64 * t:64 * (t + 1)], ebuf[:], C["tiny"][:, 2:3], None, ALU.mult))
            trw = [altok]
            if lastal:
                trw.append(lastal)
            p.t(lambda e, t=t: e.transpose(pbt[0:64, 0:128], alpha[:, 64 * t:64 * (t + 1)],
                                           C["eyeb"][:]), trw)
            lastal = p.v(lambda e, t=t: e.tensor_copy(
                alphaT[0:64, 128 * t:128 * (t + 1)], pbt[0:64, 0:128]), p.tok("pe"))
        p.barrier()
        stA2.close()
        stA.close()

        p.v(lambda e: e.memset(si[:], 0.0))
        lastu = None
        for t in range(NT):
            b = t % 2
            w = list(p.tok("dve"))
            dtok = p.ds(lambda e, t=t, b=b: e.dma_start(
                xt[:, 1024 * b:1024 * (b + 1)], x_src[128 * t:128 * (t + 1), :]), w)
            xin = xt[:, 1024 * b:1024 * (b + 1)]
            uw = [dtok]
            if lastu:
                uw.append(lastu)
            utok = p.s(lambda e, xin=xin, t=t: e.activation(
                uhi[:], xin, AF.Identity, bias=C["mrs"][:, 4 * t + 3:4 * t + 4],
                scale=C["mrs"][:, 4 * t + 2:4 * t + 3]), uw)
            for half in range(2):
                p.t(lambda e, t=t, half=half: e.matmul(
                    psSI[0:64, 512 * half:512 * (half + 1)],
                    alpha[:, 64 * t:64 * (t + 1)],
                    uhi[:, 512 * half:512 * (half + 1)],
                    start=(t == 0), stop=(t == NT - 1), skip_group_check=True),
                    [utok] if half == 0 else ())
            p.t(lambda e, t=t: e.matmul(
                psD[0:64, 0:1], alpha[:, 64 * t:64 * (t + 1)], C["onesb"][:, 0:1],
                start=(t == 0), stop=(t == NT - 1), skip_group_check=True))
            lastu = p.tok("pe")[0]
        p.s(lambda e: e.activation(si[0:64, 0:512], psSI[0:64, 0:512], AF.Identity), p.tok("pe"))
        p.s(lambda e: e.activation(si[0:64, 512:1024], psSI[0:64, 512:1024], AF.Identity))
        p.s(lambda e: e.activation(si[0:64, 1024:1025], psD[0:64, 0:1], AF.Identity))
        p.ds(lambda e: e.dma_start(cio["ar_in"], si[0:64, :]), p.tok("act"))
        p.ccop(lambda e: e.collective_compute(
            "AllReduce", ALU.add, GROUPS, ins=[cio["ar_in"].opt()],
            outs=[cio["ar_out"].opt()]), p.tok("qs"))
        p.ds(lambda e: e.dma_start(si[0:64, :], cio["ar_out"]), p.tok("cc"))
        p.barrier()

        p.v(lambda e: e.tensor_scalar_add(si[0:64, 1024:1025], si[0:64, 1024:1025], 1e-8))
        p.v(lambda e: e.reciprocal(si[0:64, 1025:1026], si[0:64, 1024:1025]))
        p.v(lambda e: e.tensor_scalar(sid_b[0:64, :], si[0:64, 0:1024],
                                      si[0:64, 1025:1026], None, ALU.mult))
        for c in range(8):
            p.t(lambda e, c=c: e.transpose(pbt[:, 128 * c:128 * (c + 1)],
                                           sid_b[:, 128 * c:128 * (c + 1)],
                                           C["eyeb"][:]),
                p.tok("dve") if c == 0 else ())
        p.v(lambda e: e.tensor_copy(
            siT.rearrange("p (c n) -> p c n", n=64),
            pbt[:, 0:1024].rearrange("p (c n) -> p c n", n=128)[:, :, 0:64]), p.tok("pe"))
        giT = sb("d_giT", [128, 24 * 64], F32)
        ghT = sb("d_ghT", [128, 24 * 64], F32)
        rz = sb("d_rz", [128, 16 * 64], F32)
        gn = sb("d_gn", [128, 8 * 64], F32)
        stC1 = ExitStack()
        wih = stC1.enter_context(nc.sbuf_tensor("d_wih", [128, 8 * 3072], BF16))
        bihr = stC1.enter_context(nc.sbuf_tensor("d_bihr", [128, 3072], BF16))
        mzi = p.v(lambda e: e.memset(bihr[:], 0.0))
        for c in range(8):
            p.dg(lambda e, c=c: e.dma_start(wih[:, 3072 * c:3072 * (c + 1)],
                                            dr["f_wihT"][128 * c:128 * (c + 1), :]))
        p.dg(lambda e: e.dma_start(bihr[0:1, :], dr["f_bih"]), [mzi])
        p.barrier()
        for g in range(24):
            gw = list(p.tok("dve", "act"))
            for c in range(8):
                p.t(lambda e, g=g, c=c: e.matmul(
                    psM[:, 0:64], wih[:, 3072 * c + 128 * g:3072 * c + 128 * (g + 1)],
                    siT[:, 64 * c:64 * (c + 1)], start=(c == 0), stop=False),
                    gw if c == 0 else ())
            p.t(lambda e, g=g: e.matmul(psM[:, 0:64], bihr[:, 128 * g:128 * (g + 1)],
                                        C["onesb"][:, 0:64], start=False, stop=True))
            p.s(lambda e, g=g: e.activation(giT[:, 64 * g:64 * (g + 1)], psM[:, 0:64],
                                            AF.Identity), p.tok("pe"))
        p.barrier()
        stC1.close()
        stC2 = ExitStack()
        whh = stC2.enter_context(nc.sbuf_tensor("d_whh", [128, 8 * 3072], BF16))
        bhhr = stC2.enter_context(nc.sbuf_tensor("d_bhhr", [128, 3072], BF16))
        mzh = p.v(lambda e: e.memset(bhhr[:], 0.0))
        for c in range(8):
            p.dg(lambda e, c=c: e.dma_start(whh[:, 3072 * c:3072 * (c + 1)],
                                            dr["f_whhT"][128 * c:128 * (c + 1), :]))
        p.dg(lambda e: e.dma_start(bhhr[0:1, :], dr["f_bhh"]), [mzh])
        p.barrier()
        for g in range(24):
            gw = list(p.tok("act"))
            for c in range(8):
                p.t(lambda e, g=g, c=c: e.matmul(
                    psM[:, 0:64], whh[:, 3072 * c + 128 * g:3072 * c + 128 * (g + 1)],
                    sIT[:, 64 * c:64 * (c + 1)], start=(c == 0), stop=False),
                    gw if c == 0 else ())
            p.t(lambda e, g=g: e.matmul(psM[:, 0:64], bhhr[:, 128 * g:128 * (g + 1)],
                                        C["onesb"][:, 0:64], start=False, stop=True))
            p.s(lambda e, g=g: e.activation(ghT[:, 64 * g:64 * (g + 1)], psM[:, 0:64],
                                            AF.Identity), p.tok("pe"))
        p.barrier()
        stC2.close()
        stC3 = ExitStack()
        sbc = lambda n, shp, dt: stC3.enter_context(nc.sbuf_tensor(n, shp, dt))
        wqs = sbc("d_wqs", [128, 8 * 256], BF16)
        wks = sbc("d_wks", [128, 8 * 256], BF16)
        wvs = sbc("d_wvs", [128, 8 * 1024], BF16)
        bqr = sbc("d_bqr", [128, 256], BF16)
        bkr = sbc("d_bkr", [128, 256], BF16)
        bvr = sbc("d_bvr", [128, 1024], BF16)
        gwvp = sbc("d_gwvp", [128, 8 * 1024], BF16)
        wvpf = sbc("d_wvpf", [128, 1024], F32)
        lngc = sbc("d_lngc", [128, 8], F32)
        lnbc = sbc("d_lnbc", [128, 8], F32)
        lng8 = sbc("d_lng8", [8, 128], F32)
        lnb8 = sbc("d_lnb8", [8, 128], F32)
        cr1 = sbc("d_cr1", [1, 1024], F32)
        crb = sbc("d_crb", [128, 1024], BF16)
        bvpr = sbc("d_bvpr", [128, 1024], F32)
        for c in range(8):
            p.dg(lambda e, c=c: e.dma_start(wqs[:, 256 * c:256 * (c + 1)],
                                            dr["f_wq"][128 * c:128 * (c + 1), :]))
            p.dg(lambda e, c=c: e.dma_start(wks[:, 256 * c:256 * (c + 1)],
                                            dr["f_wk"][128 * c:128 * (c + 1), :]))
            p.dg(lambda e, c=c: e.dma_start(wvs[:, 1024 * c:1024 * (c + 1)],
                                            dr["f_wv"][128 * c:128 * (c + 1), :]))
        mzc = p.v(lambda e: e.memset(bqr[:], 0.0))
        mzc = p.v(lambda e: e.memset(bkr[:], 0.0), [mzc])
        mzc = p.v(lambda e: e.memset(bvr[:], 0.0), [mzc])
        mzc = p.v(lambda e: e.memset(crb[:], 0.0), [mzc])
        mzc = p.v(lambda e: e.memset(bvpr[:], 0.0), [mzc])
        p.dg(lambda e: e.dma_start(bqr[0:1, :], dr["f_bq"]), [mzc])
        p.dg(lambda e: e.dma_start(bkr[0:1, :], dr["f_bk"]))
        p.dg(lambda e: e.dma_start(bvr[0:1, :], dr["f_bv"]))
        p.ds(lambda e: e.dma_start(lng8[:], dr["f_lng8"]))
        p.ds(lambda e: e.dma_start(lnb8[:], dr["f_lnb8"]))
        p.ds(lambda e: e.dma_start(bvpr[0:1, :], dr["f_bvp"]), [mzc])
        p.barrier()
        p.v(lambda e: e.tensor_add(rz[:, 0:1024], giT[:, 0:1024], ghT[:, 0:1024]))
        p.s(lambda e: e.activation(rz[:, 0:1024], rz[:, 0:1024], AF.Sigmoid), p.tok("dve"))
        p.v(lambda e: e.tensor_tensor(gn[:], rz[:, 0:512], ghT[:, 1024:1536], ALU.mult),
            p.tok("act"))
        p.v(lambda e: e.tensor_add(gn[:], gn[:], giT[:, 1024:1536]))
        p.s(lambda e: e.activation(gn[:], gn[:], AF.Tanh), p.tok("dve"))
        p.v(lambda e: e.tensor_tensor(SnT[:], sITf[:], gn[:], ALU.subtract), p.tok("act"))
        p.v(lambda e: e.tensor_tensor(SnT[:], SnT[:], rz[:, 512:1024], ALU.mult))
        p.v(lambda e: e.tensor_add(SnT[:], SnT[:], gn[:]))
        p.v(lambda e: e.tensor_copy(SnTb[:], SnT[:]))
        for dst, wsb, bsb in ((qT, wqs, bqr), (kT, wks, bkr)):
            for f2 in range(2):
                gw = list(p.tok("dve", "act"))
                for c in range(8):
                    p.t(lambda e, f2=f2, c=c, wsb=wsb: e.matmul(
                        psM[:, 0:64], wsb[:, 256 * c + 128 * f2:256 * c + 128 * (f2 + 1)],
                        SnTb[:, 64 * c:64 * (c + 1)], start=(c == 0), stop=False),
                        gw if c == 0 else ())
                p.t(lambda e, f2=f2, bsb=bsb: e.matmul(
                    psM[:, 0:64], bsb[:, 128 * f2:128 * (f2 + 1)], C["onesb"][:, 0:64],
                    start=False, stop=True))
                p.s(lambda e, f2=f2, dst=dst: e.activation(
                    dst[:, 64 * f2:64 * (f2 + 1)], psM[:, 0:64], AF.Identity), p.tok("pe"))
        saw = list(p.tok("act"))
        for f2 in range(2):
            p.t(lambda e, f2=f2: e.matmul(
                psM[0:64, 0:64], kT[:, 64 * f2:64 * (f2 + 1)], qT[:, 64 * f2:64 * (f2 + 1)],
                start=(f2 == 0), stop=(f2 == 1)), saw if f2 == 0 else ())
        p.s(lambda e: e.activation(scb[0:64, :], psM[0:64, 0:64], AF.Identity), p.tok("pe"))
        p.t(lambda e: e.transpose(pbt[0:64, 0:128], scb[:], C["eyeb"][:]), p.tok("act"))
        p.v(lambda e: e.tensor_copy(scq[:], pbt[0:64, 0:64]), p.tok("pe"))
        p.v(lambda e: e.reduce_max(C["tiny"][0:64, 0:1], scq[:], axis=AX.X, negate=True))
        p.v(lambda e: e.tensor_scalar_mul(C["tiny"][0:64, 1:2], C["tiny"][0:64, 0:1], 0.0625))
        p.s(lambda e: e.activation(scq[:], scq[:], AF.Exp, scale=0.0625,
                                   bias=C["tiny"][0:64, 1:2]), p.tok("dve"))
        p.v(lambda e: e.reduce_sum(C["tiny"][0:64, 2:3], scq[:], axis=AX.X), p.tok("act"))
        p.v(lambda e: e.reciprocal(C["tiny"][0:64, 3:4], C["tiny"][0:64, 2:3]))
        p.v(lambda e: e.tensor_scalar(attnSb[0:64, :], scq[:], C["tiny"][0:64, 3:4],
                                      None, ALU.mult))
        p.t(lambda e: e.transpose(pbt[0:64, 0:128], attnSb[:], C["eyeb"][:]),
            p.tok("dve"))
        p.v(lambda e: e.tensor_copy(attnTb[0:64, :], pbt[0:64, 0:64]), p.tok("pe"))
        for n in range(2):
            gw = list(p.tok("dve", "act"))
            for c in range(8):
                p.t(lambda e, n=n, c=c: e.matmul(
                    psM[0:64, 0:512], SnTb[:, 64 * c:64 * (c + 1)],
                    wvs[:, 1024 * c + 512 * n:1024 * c + 512 * (n + 1)],
                    start=(c == 0), stop=False), gw if c == 0 else ())
            p.t(lambda e, n=n: e.matmul(psM[0:64, 0:512], C["onesb"][:, 0:64],
                                        bvr[:, 512 * n:512 * (n + 1)],
                                        start=False, stop=True))
            p.s(lambda e, n=n: e.activation(vs[0:64, 512 * n:512 * (n + 1)],
                                            psM[0:64, 0:512], AF.Identity), p.tok("pe"))
        for c in range(8):
            p.t(lambda e, c=c: e.matmul(
                psM[:, 0:64], vs[:, 128 * c:128 * (c + 1)], attnTb[:, 0:64],
                start=True, stop=True), p.tok("act", "dve"))
            p.v(lambda e, c=c: e.tensor_tensor(ySlT[:, 64 * c:64 * (c + 1)],
                                               SnT[:, 64 * c:64 * (c + 1)], psM[:, 0:64],
                                               ALU.add), p.tok("pe"))
        for c in range(8):
            bk = c % 2
            tw = list(p.tok("dve"))
            p.t(lambda e, c=c, bk=bk: e.transpose(
                pbf[0:64, 512 * bk:512 * bk + 128], ySlT[:, 64 * c:64 * (c + 1)],
                C["eyef"][:]), tw)
            p.v(lambda e, c=c, bk=bk: e.tensor_copy(
                ySl[:, 128 * c:128 * (c + 1)], pbf[0:64, 512 * bk:512 * bk + 128]), p.tok("pe"))
        st_, tmp = C["stats"], C["tiny"]
        p.v(lambda e: e.bn_stats(st_[0:64, 0:6], ySl[:, 0:512]), p.tok("act"))
        p.v(lambda e: e.bn_stats(st_[0:64, 6:12], ySl[:, 512:1024]))
        p.v(lambda e: e.bn_aggr(tmp[0:64, 8:10], st_[0:64, 0:12]))
        p.v(lambda e: e.tensor_scalar_add(tmp[0:64, 0:1], tmp[0:64, 9:10], 1e-5))
        p.s(lambda e: e.activation(tmp[0:64, 1:2], tmp[0:64, 0:1], AF.Ln), p.tok("dve"))
        p.s(lambda e: e.activation(tmp[0:64, 2:3], tmp[0:64, 1:2], AF.Exp, scale=-0.5))
        p.v(lambda e: e.tensor_tensor(tmp[0:64, 3:4], tmp[0:64, 8:9], tmp[0:64, 2:3], ALU.mult),
            p.tok("act"))
        p.v(lambda e: e.tensor_scalar_mul(tmp[0:64, 4:5], tmp[0:64, 3:4], -1.0))
        p.s(lambda e: e.activation(yhat[0:64, :], ySl[:], AF.Identity, bias=tmp[0:64, 4:5],
                                   scale=tmp[0:64, 2:3]), p.tok("dve"))
        p.t(lambda e: e.matmul(pbf[:, 0:8], lng8[:], C["eyef"][0:8, 0:8], start=True, stop=True),
            p.tok("dve"))
        p.v(lambda e: e.tensor_copy(lngc[:], pbf[:, 0:8]), p.tok("pe"))
        p.t(lambda e: e.matmul(pbf[:, 8:16], lnb8[:], C["eyef"][0:8, 0:8], start=True, stop=True),
            p.tok("dve"))
        p.v(lambda e: e.tensor_copy(lnbc[:], pbf[:, 8:16]), p.tok("pe"))
        for c in range(8):
            dt = p.ds(lambda e, c=c: e.dma_start(wvpf[:], dr["f_wvp"][128 * c:128 * (c + 1), :]),
                      p.tok("dve", "pe"))
            p.v(lambda e, c=c: e.tensor_scalar(gwvp[:, 1024 * c:1024 * (c + 1)], wvpf[:],
                                               lngc[:, c:c + 1], None, ALU.mult), [dt])
            for half in range(2):
                p.t(lambda e, c=c, half=half: e.matmul(
                    pbf[0:1, 512 * half:512 * half + 512], lnbc[:, c:c + 1],
                    wvpf[:, 512 * half:512 * (half + 1)],
                    start=(c == 0), stop=False, skip_group_check=True),
                    p.tok("dve") if half == 0 else ())
        for half in range(2):
            p.t(lambda e, half=half: e.matmul(
                pbf[0:1, 512 * half:512 * half + 512], C["onesf"][:, 0:1],
                bvpr[:, 512 * half:512 * (half + 1)], start=False, stop=True,
                skip_group_check=True))
        p.v(lambda e: e.tensor_copy(cr1[0:1, 0:512], pbf[0:1, 0:512]), p.tok("pe"))
        p.v(lambda e: e.tensor_copy(cr1[0:1, 512:1024], pbf[0:1, 512:1024]))
        p.v(lambda e: e.tensor_copy(crb[0:1, :], cr1[:]))
        for c in range(8):
            p.t(lambda e, c=c: e.transpose(pbt[:, 128 * c:128 * (c + 1)],
                                           yhat[:, 128 * c:128 * (c + 1)],
                                           C["eyeb"][:]),
                p.tok("act", "dve") if c == 0 else ())
        p.v(lambda e: e.tensor_copy(
            yhT.rearrange("p (c n) -> p c n", n=64),
            pbt[:, 0:1024].rearrange("p (c n) -> p c n", n=128)[:, :, 0:64]), p.tok("pe"))
        for n in range(2):
            gw = list(p.tok("dve", "act"))
            for c in range(8):
                p.t(lambda e, n=n, c=c: e.matmul(
                    psM[0:64, 0:512], yhT[:, 64 * c:64 * (c + 1)],
                    gwvp[:, 1024 * c + 512 * n:1024 * c + 512 * (n + 1)],
                    start=(c == 0), stop=False), gw if c == 0 else ())
            p.t(lambda e, n=n: e.matmul(psM[0:64, 0:512], C["onesb"][:, 0:64],
                                        crb[:, 512 * n:512 * (n + 1)],
                                        start=False, stop=True))
            p.s(lambda e, n=n: e.activation(sv[0:64, 512 * n:512 * (n + 1)],
                                            psM[0:64, 0:512], AF.Identity), p.tok("pe"))
        p.barrier()
        stC3.close()

        stD = ExitStack()
        outT = stD.enter_context(nc.sbuf_tensor("d_outT", [128, 8 * 2048], BF16))
        wo2 = stD.enter_context(nc.sbuf_tensor("d_wo2", [128, 8 * 1024], BF16))
        bor = stD.enter_context(nc.sbuf_tensor("d_bor", [128, 1024], BF16))
        mzo = p.v(lambda e: e.memset(bor[:], 0.0))
        for c in range(8):
            p.dg(lambda e, c=c: e.dma_start(wo2[:, 1024 * c:1024 * (c + 1)],
                                            dr["f_wo"][128 * c:128 * (c + 1), :]))
        p.dg(lambda e: e.dma_start(bor[0:1, :], dr["f_bo"]), [mzo])
        p.barrier()
        for c in range(8):
            for n in range(4):
                w = list(p.tok("act"))
                p.t(lambda e, c=c, n=n: e.matmul(
                    psSI[:, 0:512], sv[:, 128 * c:128 * (c + 1)],
                    alphaT[:, 512 * n:512 * (n + 1)], start=True, stop=True), w)
                p.s(lambda e, c=c, n=n: e.activation(
                    outT[:, 2048 * c + 512 * n:2048 * c + 512 * (n + 1)], psSI[:, 0:512],
                    AF.Identity), p.tok("pe"))
        p.barrier()
        for t in range(NT):
            b = t % 2
            mw = list(p.tok("act", "dve"))
            for half in range(2):
                for c in range(8):
                    p.t(lambda e, t=t, half=half, c=c: e.matmul(
                        psSI[:, 512 * half:512 * (half + 1)],
                        outT[:, 2048 * c + 128 * t:2048 * c + 128 * (t + 1)],
                        wo2[:, 1024 * c + 512 * half:1024 * c + 512 * (half + 1)],
                        start=(c == 0), stop=False), mw if (half == 0 and c == 0) else ())
                p.t(lambda e, half=half: e.matmul(
                    psSI[:, 512 * half:512 * (half + 1)], C["onesb"][:, 0:128],
                    bor[:, 512 * half:512 * (half + 1)], start=False, stop=True))
            xdt = p.ds(lambda e, t=t, b=b: e.dma_start(
                xt[:, 1024 * b:1024 * (b + 1)], x_src[128 * t:128 * (t + 1), :]),
                p.tok("dve"))
            xin = xt[:, 1024 * b:1024 * (b + 1)]
            ln_resid_store(p, C, "psum2", psSI[:, 0:512], psSI[:, 512:1024], xin, yb,
                           x_dst[128 * t:128 * (t + 1), :],
                           extra_wait=list(p.tok("pe")) + [xdt])
        p.barrier()
        stD.close()


def _build(blocks, fused):
    nc = bass.Bass("TRN2", target_bir_lowering=False, detect_race_conditions=False)
    dr = {}
    specs = {
        "eye": [128, 128], "l0_masks": [768, 512], "l1_masks": [512, 512],
        "x0h": [2304, 1024], "padneg": [1, 2304],
        "l0_wq": [1024, 1024], "l0_wk": [1024, 1024], "l0_wv": [1024, 1024],
        "l0_wo": [1024, 1024], "l0_brow": [4, 1024],
        "mod_w1": [1024, 4096], "mod_w2": [4096, 1024], "mod_b1c": [128, 32],
        "mod_b2c": [128, 8], "mod_wr8": [8, 128],
        "l1_wq": [1024, 256], "l1_wk": [1024, 256], "l1_wv": [1024, 256],
        "l1_wo": [256, 1024],
        "f_sIT": [1024, 64], "f_sI": [64, 1024], "f_w1t": [1024, 512],
        "f_w1b": [1024, 512], "f_b1": [1, 512], "f_w2": [512, 64], "f_b2": [1, 64],
        "f_tau": [1, 1], "f_wihT": [1024, 3072], "f_whhT": [1024, 3072],
        "f_bih": [1, 3072], "f_bhh": [1, 3072], "f_wq": [1024, 256], "f_bq": [1, 256],
        "f_wk": [1024, 256], "f_bk": [1, 256], "f_wv": [1024, 1024], "f_bv": [1, 1024],
        "f_lng8": [8, 128], "f_lnb8": [8, 128], "f_wvp": [1024, 1024],
        "f_bvp": [1, 1024], "f_wo": [1024, 1024], "f_bo": [1, 1024],
    }
    for nm, shp in specs.items():
        dr[nm] = nc.dram_tensor(nm, shp, F32, kind="ExternalInput").ap()
    out = nc.dram_tensor("xout", [2048, 1024], F32, kind="ExternalOutput").ap()
    dbg = nc.dram_tensor("dbg", [128, 8192], F32, kind="ExternalOutput").ap() if DBG else None
    xs = {}
    if fused:
        for nm in ("x1", "x2", "x3"):
            xs[nm] = nc.dram_tensor(nm, [2048, 1024], F32, kind="Internal").ap()
    else:
        if blocks[0] != "l0a":
            xs["xin"] = nc.dram_tensor("xin", [2048, 1024], F32, kind="ExternalInput").ap()
    cio = {
        "sc_in": nc.dram_tensor("sc_in", [1, 2048], F32, kind="Internal").ap(),
        "sc_out": nc.dram_tensor("sc_out", [4, 2048], F32, kind="Internal").ap(),
        "ag_in": nc.dram_tensor("ag_in", [1024, 2048], BF16, kind="Internal").ap(),
        "ag_out": nc.dram_tensor("ag_out", [4096, 2048], BF16, kind="Internal").ap(),
        "rs_in": nc.dram_tensor("rs_in", [8192, 1024], BF16, kind="Internal").ap(),
        "rs_out": nc.dram_tensor("rs_out", [2048, 1024], BF16, kind="Internal").ap(),
        "ar_in": nc.dram_tensor("ar_in", [64, 1028], F32, kind="Internal").ap(),
        "ar_out": nc.dram_tensor("ar_out", [64, 1028], F32, kind="Internal").ap(),
        "o2_dram": nc.dram_tensor("o2_dram", [2048, 1024], BF16, kind="Internal").ap(),
    }
    with ExitStack() as st:
        p = P(nc, st)
        block = st.enter_context(nc.Block())
        C = setup_consts(p, st, nc, dr)
        chain = {"l0a": ("x0h", "x1"), "l0f": ("x1", "x2"),
                 "l1a": ("x2", "x3"), "l1f": ("x3", "out")}
        for i, bname in enumerate(blocks):
            srcn, dstn = chain[bname]
            if fused:
                src = dr["x0h"] if srcn == "x0h" else xs[srcn]
                dst = out if (i == len(blocks) - 1) else xs[dstn]
            else:
                src = dr["x0h"] if bname == "l0a" else xs["xin"]
                dst = out
            if bname == "l0a":
                phase_l0a(p, C, nc, dr, src, dst, dbg=dbg)
            elif bname == "l0f":
                phase_l0f(p, C, nc, dr, src, dst, cio)
            elif bname == "l1a":
                phase_l1a(p, C, nc, dr, src, dst, cio)
            elif bname == "l1f":
                phase_l1f(p, C, nc, dr, src, dst, cio)
        p.barrier()
        p.emit(block)
    mybir.codegen_inst_isa_subclasses(nc)
    return nc


def _run(in_maps, blocks, fused):
    if fused:
        nc = _build(blocks, True)
        res = run_bass_kernel_spmd(nc, in_maps, core_ids=list(range(8)))
        return [np.asarray(r["xout"]) for r in res.results]
    cur = None
    for bname in blocks:
        nc = _build((bname,), False)
        maps = []
        for c in range(8):
            m = dict(in_maps[c])
            if bname != "l0a":
                m["xin"] = cur[c]
            maps.append(m)
        res = run_bass_kernel_spmd(nc, maps, core_ids=list(range(8)))
        cur = [np.asarray(r["xout"]) for r in res.results]
    return cur


def kernel(**inputs):
    in_maps = _prep(inputs)
    shards = _run(in_maps, ALL_BLOCKS, FUSED)
    out = np.zeros((NB, SEQ, DM), np.float32)
    for c in range(8):
        b, j = c // 4, c % 4
        out[b, j * TOK:(j + 1) * TOK] = shards[c]
    return out


# version 07
# speedup vs baseline: 1.7703x; 1.0210x over previous
import numpy as np
import ml_dtypes
from contextlib import ExitStack

import concourse.bass as bass
import concourse.mybir as mybir
from concourse.bass_isa import ReduceOp
from concourse.bass_utils import run_bass_kernel_spmd

F32 = mybir.dt.float32
BF16 = mybir.dt.bfloat16
AF = mybir.ActivationFunctionType
ALU = mybir.AluOpType
AX = mybir.AxisListType

NB, SEQ, DM = 2, 8192, 1024
TOK = 2048
NT = 16
NTH = 18
GROUPS = [[0, 1, 2, 3], [4, 5, 6, 7]]
FUSED = True
DBG = False
ALL_BLOCKS = ("l0a", "l0f", "l1a", "l1f")

ENGS = ("tensor", "vector", "scalar", "gpsimd", "sync")
_OWN = {"tensor": "pe", "vector": "dve", "scalar": "act", "gpsimd": "pool", "sync": None}


class P:
    def __init__(self, nc, st):
        self.nc = nc
        self.ops = {e: [] for e in ENGS}
        self.sems = {}
        self.cnt = {}
        for s in ("pe", "dve", "act", "pool", "qs", "qg", "cc"):
            self.sems[s] = st.enter_context(nc.semaphore(s))
            self.cnt[s] = 0

    def _op(self, e, fn, waits, sig):
        self.ops[e].append((fn, tuple(waits or ()), sig))
        if sig:
            self.cnt[sig[0]] += sig[1]
            return (sig[0], self.cnt[sig[0]])
        return None

    def t(self, fn, waits=()):  return self._op("tensor", fn, waits, ("pe", 1))
    def v(self, fn, waits=()):  return self._op("vector", fn, waits, ("dve", 1))
    def s(self, fn, waits=()):  return self._op("scalar", fn, waits, ("act", 1))
    def g(self, fn, waits=()):  return self._op("gpsimd", fn, waits, ("pool", 1))
    def ds(self, fn, waits=()): return self._op("sync", fn, waits, ("qs", 16))
    def dg(self, fn, waits=()): return self._op("gpsimd", fn, waits, ("qg", 16))
    def ccop(self, fn, waits=()): return self._op("gpsimd", fn, waits, ("cc", 1))

    def tok(self, *sems):
        return tuple((s, self.cnt[s]) for s in sems)

    def barrier(self):
        snap = tuple((s, self.cnt[s]) for s in self.cnt)
        for e in ENGS:
            self._op(e, None, snap, None)

    def emit(self, block):
        def mk(e):
            own = _OWN[e]
            serialize = e in ("vector", "scalar", "gpsimd")
            def run(eng):
                waited = {}
                cum = {}
                for fn, waits, sig in self.ops[e]:
                    allw = list(waits)
                    if serialize and sig and sig[0] == own and cum.get(own, 0) > 0:
                        allw.append((own, cum[own]))
                    for (s, v) in allw:
                        if v <= 0:
                            continue
                        if waited.get(s, -1) >= v:
                            continue
                        eng.wait_ge(self.sems[s], v)
                        waited[s] = v
                    if fn is not None:
                        inst = fn(eng)
                        if sig:
                            inst.then_inc(self.sems[sig[0]], sig[1])
                    if sig:
                        cum[sig[0]] = cum.get(sig[0], 0) + sig[1]
            return run
        block.tensor(mk("tensor"))
        block.vector(mk("vector"))
        block.scalar(mk("scalar"))
        block.gpsimd(mk("gpsimd"))
        block.sync(mk("sync"))


def _win_masks():
    ms = np.zeros((6, 128, 512), np.float32)
    k = np.arange(128)[:, None]
    q = np.arange(512)[None, :]
    for kt in range(6):
        ms[kt] = (np.abs(k - q + 128 * (kt - 1)) <= 128).astype(np.float32)
    return ms.reshape(768, 512)


def _causal_masks():
    ms = np.zeros((4, 128, 512), np.float32)
    k = np.arange(128)[:, None]
    q = np.arange(512)[None, :]
    for d in range(4):
        ms[d] = ((k + 128 * d) <= q).astype(np.float32)
    return ms.reshape(512, 512)


def _prep(inp):
    inp = {k: np.asarray(v) for k, v in inp.items()}
    x = inp["L0_attn_x"].astype(np.float32, copy=False)
    C = np.ascontiguousarray
    shared = {
        "eye": np.eye(128, dtype=np.float32),
        "l0_masks": _win_masks(),
        "l1_masks": _causal_masks(),
        "l0_wq": C(inp["L0_attn_wq"]), "l0_wk": C(inp["L0_attn_wk"]),
        "l0_wv": C(inp["L0_attn_wv"]), "l0_wo": C(inp["L0_attn_wo"]),
        "l0_brow": C(np.stack([inp["L0_attn_bq"], inp["L0_attn_bk"],
                               inp["L0_attn_bv"], inp["L0_attn_bo"]])),
        "mod_w1": C(inp["L0_ffn_W1"]), "mod_w2": C(inp["L0_ffn_W2"]),
        "mod_b1c": C(np.asarray(inp["L0_ffn_b1"]).reshape(32, 128).T), "mod_b2c": C(np.asarray(inp["L0_ffn_b2"]).reshape(8, 128).T),
        "mod_wr8": C(inp["L0_ffn_w_router"].reshape(8, 128)),
        "f_sIT": C(inp["L1_ffn_slot_init"].T), "f_sI": C(inp["L1_ffn_slot_init"]),
        "f_w1t": C(inp["L1_ffn_w1"][0:1024]), "f_w1b": C(inp["L1_ffn_w1"][1024:2048]),
        "f_b1": C(inp["L1_ffn_b1"][None, :]), "f_w2": C(inp["L1_ffn_w2"]),
        "f_b2": C(inp["L1_ffn_b2"][None, :]), "f_tau": C(inp["L1_ffn_tau"].reshape(1, 1)),
        "f_wihT": C(inp["L1_ffn_Wih"].T), "f_whhT": C(inp["L1_ffn_Whh"].T),
        "f_bih": C(inp["L1_ffn_bih"][None, :]), "f_bhh": C(inp["L1_ffn_bhh"][None, :]),
        "f_wq": C(inp["L1_ffn_wq"]), "f_bq": C(inp["L1_ffn_bq"][None, :]),
        "f_wk": C(inp["L1_ffn_wk"]), "f_bk": C(inp["L1_ffn_bk"][None, :]),
        "f_wv": C(inp["L1_ffn_wv"]), "f_bv": C(inp["L1_ffn_bv"][None, :]),
        "f_lng8": C(inp["L1_ffn_ln_g"].reshape(8, 128)),
        "f_lnb8": C(inp["L1_ffn_ln_b"].reshape(8, 128)),
        "f_wvp": C(inp["L1_ffn_wvp"]), "f_bvp": C(inp["L1_ffn_bvp"][None, :]),
        "f_wo": C(inp["L1_ffn_wo"]), "f_bo": C(inp["L1_ffn_bo"][None, :]),
    }
    qp, kp, vp, op = (inp["L1_attn_q_proj"], inp["L1_attn_k_proj"],
                      inp["L1_attn_v_proj"], inp["L1_attn_o_proj"])
    in_maps = []
    for c in range(8):
        b, j = c // 4, c % 4
        xh = np.zeros((2304, DM), np.float32)
        xh[128:2176] = x[b, j * TOK:(j + 1) * TOK]
        if j > 0:
            xh[0:128] = x[b, j * TOK - 128:j * TOK]
        if j < 3:
            xh[2176:2304] = x[b, (j + 1) * TOK:(j + 1) * TOK + 128]
        padneg = np.zeros((1, 2304), np.float32)
        if j == 0:
            padneg[0, 0:128] = -1.0
        if j == 3:
            padneg[0, 2176:2304] = -1.0
        m = dict(shared)
        m["x0h"] = xh
        m["padneg"] = padneg
        hs = slice(256 * j, 256 * j + 256)
        m["l1_wq"] = C(qp[hs, :].T)
        m["l1_wk"] = C(kp[hs, :].T)
        m["l1_wv"] = C(vp[hs, :].T)
        m["l1_wo"] = C(op[:, hs].T)
        in_maps.append(m)
    return in_maps


def setup_consts(p, st, nc, dr):
    C = {}
    C["eyef"] = st.enter_context(nc.sbuf_tensor("eyef", [128, 128], F32))
    C["eyeb"] = st.enter_context(nc.sbuf_tensor("eyeb", [128, 128], BF16))
    C["onesb"] = st.enter_context(nc.sbuf_tensor("onesb", [128, 2304], BF16))
    C["onesf"] = st.enter_context(nc.sbuf_tensor("onesf", [128, 128], F32))
    C["tiny"] = st.enter_context(nc.sbuf_tensor("tiny", [128, 64], F32))
    C["stats"] = st.enter_context(nc.sbuf_tensor("stats", [128, 14], F32))
    C["mrs"] = st.enter_context(nc.sbuf_tensor("mrs", [128, 4 * NTH], F32))
    p.ds(lambda e: e.dma_start(C["eyef"][:], dr["eye"]))
    p.dg(lambda e: e.dma_start(C["eyeb"][:], dr["eye"]))
    p.v(lambda e: e.memset(C["onesb"][:], 1.0))
    p.v(lambda e: e.memset(C["onesf"][:], 1.0))
    p.barrier()
    return C


def tile_ln(p, C, xin_ap, t, u_dst, wait=()):
    st_, mrs, tmp = C["stats"], C["mrs"], C["tiny"]
    tk = p.v(lambda e: e.bn_stats(st_[:, 0:6], xin_ap[:, 0:512]), tuple(wait) + p.tok("act"))
    tk = p.v(lambda e: e.bn_stats(st_[:, 6:12], xin_ap[:, 512:1024]), [tk])
    tk = p.v(lambda e: e.bn_aggr(mrs[:, 4 * t:4 * t + 2], st_[:, 0:12]), [tk])
    tk = p.v(lambda e, t=t: e.tensor_scalar_add(tmp[:, 0:1], mrs[:, 4 * t + 1:4 * t + 2], 1e-5),
             [tk])
    tk = p.s(lambda e: e.activation(tmp[:, 1:2], tmp[:, 0:1], AF.Ln), [tk])
    tk = p.s(lambda e, t=t: e.activation(mrs[:, 4 * t + 2:4 * t + 3], tmp[:, 1:2], AF.Exp,
                                         scale=-0.5), [tk])
    tk = p.v(lambda e, t=t: e.tensor_tensor(tmp[:, 2:3], mrs[:, 4 * t:4 * t + 1],
                                            mrs[:, 4 * t + 2:4 * t + 3], ALU.mult), [tk])
    tk = p.v(lambda e, t=t: e.tensor_scalar_mul(mrs[:, 4 * t + 3:4 * t + 4], tmp[:, 2:3], -1.0),
             [tk])
    return p.s(lambda e, t=t, xin_ap=xin_ap, u_dst=u_dst: e.activation(
        u_dst, xin_ap, AF.Identity, bias=mrs[:, 4 * t + 3:4 * t + 4],
        scale=mrs[:, 4 * t + 2:4 * t + 3]), [tk])


def ln_resid_store(p, C, src_kind, src0, src1, xin_ap, yb, x_dst_tile, extra_wait=()):
    st_, tmp = C["stats"], C["tiny"]
    if src_kind == "psum2":
        a0, a1 = src0, src1
    else:
        a0, a1 = src0[:, 0:512], src0[:, 512:1024]
    p.v(lambda e, a0=a0: e.bn_stats(st_[:, 0:6], a0), tuple(extra_wait) + p.tok("act"))
    p.v(lambda e, a1=a1: e.bn_stats(st_[:, 6:12], a1))
    p.v(lambda e: e.bn_aggr(tmp[:, 8:10], st_[:, 0:12]))
    p.v(lambda e: e.tensor_scalar_add(tmp[:, 0:1], tmp[:, 9:10], 1e-5))
    p.s(lambda e: e.activation(tmp[:, 1:2], tmp[:, 0:1], AF.Ln), p.tok("dve"))
    p.s(lambda e: e.activation(tmp[:, 2:3], tmp[:, 1:2], AF.Exp, scale=-0.5))
    p.v(lambda e: e.tensor_tensor(tmp[:, 3:4], tmp[:, 8:9], tmp[:, 2:3], ALU.mult), p.tok("act"))
    p.v(lambda e: e.tensor_scalar_mul(tmp[:, 4:5], tmp[:, 3:4], -1.0))
    p.s(lambda e, a0=a0, yb=yb: e.activation(yb[:, 0:512], a0, AF.Identity,
                                             bias=tmp[:, 4:5], scale=tmp[:, 2:3]), p.tok("dve", "qs"))
    lntok = p.s(lambda e, a1=a1, yb=yb: e.activation(yb[:, 512:1024], a1, AF.Identity,
                                                     bias=tmp[:, 4:5], scale=tmp[:, 2:3]))
    p.v(lambda e, yb=yb, xin_ap=xin_ap: e.tensor_add(yb[:], yb[:], xin_ap), [lntok])
    return p.ds(lambda e, x_dst_tile=x_dst_tile, yb=yb: e.dma_start(x_dst_tile, yb[:]),
                p.tok("dve"))


def ln_resid_pipe(p, C, nc, st, prefix, n_tiles, make_src, x_src, x_dst, nslot=3):
    stt = st.enter_context(nc.sbuf_tensor(prefix + "_pst", [128, 12 * nslot], F32))
    tmp = st.enter_context(nc.sbuf_tensor(prefix + "_ptp", [128, 8 * nslot], F32))
    xts = st.enter_context(nc.sbuf_tensor(prefix + "_pxt", [128, 1024 * nslot], F32))
    ybs = st.enter_context(nc.sbuf_tensor(prefix + "_pyb", [128, 1024 * nslot], F32))
    slot_free = {}
    for t in range(n_tiles):
        s = t % nslot
        wfree = [slot_free[s]] if s in slot_free else []
        xin = xts[:, 1024 * s:1024 * (s + 1)]
        ybx = ybs[:, 1024 * s:1024 * (s + 1)]
        xdt = p.ds(lambda e, t=t, xin=xin: e.dma_start(
            xin, x_src[128 * t:128 * (t + 1), :]), wfree)
        src, stok = make_src(t, s, xin, xdt, wfree)
        v = p.v(lambda e, s=s, src=src: e.bn_stats(
            stt[:, 12 * s:12 * s + 6], src[:, 0:512]), [stok])
        v = p.v(lambda e, s=s, src=src: e.bn_stats(
            stt[:, 12 * s + 6:12 * s + 12], src[:, 512:1024]), [v])
        v = p.v(lambda e, s=s: e.bn_aggr(tmp[:, 8 * s + 4:8 * s + 6],
                                         stt[:, 12 * s:12 * s + 12]), [v])
        v = p.v(lambda e, s=s: e.tensor_scalar_add(
            tmp[:, 8 * s:8 * s + 1], tmp[:, 8 * s + 5:8 * s + 6], 1e-5), [v])
        a = p.s(lambda e, s=s: e.activation(tmp[:, 8 * s + 1:8 * s + 2],
                                            tmp[:, 8 * s:8 * s + 1], AF.Ln), [v])
        a = p.s(lambda e, s=s: e.activation(tmp[:, 8 * s + 2:8 * s + 3],
                                            tmp[:, 8 * s + 1:8 * s + 2], AF.Exp,
                                            scale=-0.5), [a])
        v = p.v(lambda e, s=s: e.tensor_tensor(
            tmp[:, 8 * s + 3:8 * s + 4], tmp[:, 8 * s + 4:8 * s + 5],
            tmp[:, 8 * s + 2:8 * s + 3], ALU.mult), [a])
        v = p.v(lambda e, s=s: e.tensor_scalar_mul(
            tmp[:, 8 * s + 6:8 * s + 7], tmp[:, 8 * s + 3:8 * s + 4], -1.0), [v])
        a = p.s(lambda e, s=s, src=src, ybx=ybx: e.activation(
            ybx, src, AF.Identity,
            bias=tmp[:, 8 * s + 6:8 * s + 7], scale=tmp[:, 8 * s + 2:8 * s + 3]), [v])
        v = p.v(lambda e, ybx=ybx, xin=xin: e.tensor_add(ybx, ybx, xin), [a, xdt])
        slot_free[s] = p.dg(lambda e, t=t, ybx=ybx: e.dma_start(
            x_dst[128 * t:128 * (t + 1), :], ybx), [v])


def norm_bound(p, C, src_view, ncols, sq, psbanks, outcol, crow, nch=8):
    groups = []
    off = 0
    while off < ncols:
        groups.append((off, min(2048, ncols - off)))
        off += 2048
    gi = 0
    for (goff, gw) in groups:
        nwin = (gw + 511) // 512
        for c in range(nch):
            p.s(lambda e, c=c, goff=goff, gw=gw, src_view=src_view, sq=sq: e.activation(
                sq[:, 0:gw], src_view[:, c, goff:goff + gw], AF.Square), p.tok("pe"))
            for n in range(nwin):
                w = min(512, gw - 512 * n)
                p.t(lambda e, c=c, n=n, w=w, sq=sq, bk=psbanks[n], nch=nch: e.matmul(
                    bk[0:1, 0:w], C["onesf"][:, 0:1], sq[:, 512 * n:512 * n + w],
                    start=(c == 0), stop=(c == nch - 1)),
                    p.tok("act", "dve") if n == 0 else ())
        for n in range(nwin):
            w = min(512, gw - 512 * n)
            p.v(lambda e, n=n, w=w, gi=gi, bk=psbanks[n]: e.reduce_max(
                C["tiny"][0:1, gi + n:gi + n + 1], bk[0:1, 0:w], axis=AX.X), p.tok("pe"))
        gi += nwin
    p.v(lambda e, gi=gi, outcol=outcol, crow=crow: e.reduce_max(
        crow[0:1, outcol:outcol + 1], C["tiny"][0:1, 0:gi], axis=AX.X))


def bias_col_from(p, C, crow, src_col, dst_col_ap, psbank, scale):
    p.t(lambda e: e.matmul(psbank[:, 0:1], C["onesf"][0:1, :], crow[0:1, src_col:src_col + 1],
                           start=True, stop=True), p.tok("dve"))
    return p.v(lambda e: e.tensor_copy(dst_col_ap, psbank[:, 0:1]), p.tok("pe"))


def phase_l0a(p, C, nc, dr, x_src, x_dst, dbg=None):
    with ExitStack() as st:
        sb = lambda n, shp, dt: st.enter_context(nc.sbuf_tensor(n, shp, dt))
        ps = lambda n, shp, dt=F32: st.enter_context(nc.psum_tensor(n, shp, dt))
        stA = ExitStack()
        sba = lambda n, shp, dt: stA.enter_context(nc.sbuf_tensor(n, shp, dt))
        brow = sb("a_brow", [128, 4 * 1024], BF16)
        padZ = sb("a_padZ", [128, 2304], BF16)
        QT = sb("a_QT", [128, 8 * 2048], BF16)
        KT = sb("a_KT", [128, 8 * 2304], BF16)
        VP = sb("a_VP", [128, 18 * 1040], BF16)
        xt = sb("a_xt", [128, 2 * 1024], F32)
        ub = sb("a_ub", [128, 2 * 1024], BF16)
        uT = sb("a_uT", [128, 2 * 1024], BF16)
        crow = sb("a_crow", [1, 16], F32)
        bias_c = sb("a_biasc", [128, 1], F32)
        rsumZ = sb("a_rsumZ", [128, 2 * 512], BF16)
        rsumf = sb("a_rsumf", [1, 2 * 512], F32)
        bcastS = sb("a_bcastS", [64, 2 * 512], BF16)
        wq = sba("a_wq", [128, 8 * 1024], BF16)
        wk = sba("a_wk", [128, 8 * 1024], BF16)
        wv = sba("a_wv", [128, 8 * 1024], BF16)
        stP = ExitStack()
        pbt = stP.enter_context(nc.psum_tensor("a_pbt", [128, 2 * 1024], BF16))
        psA = stP.enter_context(nc.psum_tensor("a_psA", [128, 512], F32))
        psB = stP.enter_context(nc.psum_tensor("a_psB", [128, 512], F32))
        psS = stP.enter_context(nc.psum_tensor("a_psS", [128, 2 * 512], F32))

        QTv = QT.rearrange("p (c n) -> p c n", n=2048)
        KTv = KT.rearrange("p (c n) -> p c n", n=2304)
        VPv = VP.rearrange("p (tt h n) -> p tt h n", h=16, n=65)

        for wsb, nm in ((wq, "l0_wq"), (wk, "l0_wk"), (wv, "l0_wv")):
            for c in range(8):
                p.dg(lambda e, wsb=wsb, nm=nm, c=c: e.dma_start(
                    wsb[:, 1024 * c:1024 * (c + 1)], dr[nm][128 * c:128 * (c + 1), :]))
        mz1 = p.v(lambda e: e.memset(brow[:], 0.0))
        mz2 = p.v(lambda e: e.memset(padZ[:], 0.0))
        p.dg(lambda e: e.dma_start(brow[0:1, :], dr["l0_brow"].rearrange("a b -> (a b)").unsqueeze(0)),
             [mz1])
        p.dg(lambda e: e.dma_start(padZ[0:1, :], dr["padneg"]), [mz2])
        p.v(lambda e: e.memset(VP[:], 1.0))
        p.v(lambda e: e.memset(rsumZ[:], 0.0))
        p.barrier()

        dma_toks = {}
        last_use_ub = {0: None, 1: None}
        last_evT = {0: None, 1: None}
        for t in (0, 1):
            b = t % 2
            dma_toks[t] = p.ds(lambda e, t=t, b=b: e.dma_start(
                xt[:, 1024 * b:1024 * (b + 1)], x_src[128 * t:128 * (t + 1), :]))

        def project_fmaj(bank, wsb, uTt, brw, f4, dst3, t, padfix, wait=()):
            grpw = list(p.tok("act")) + list(wait)
            first = True
            for fc in range(4):
                f = f4 + fc
                last_grp = (fc == 3)
                for c in range(8):
                    p.t(lambda e, f=f, c=c, fc=fc, bank=bank, wsb=wsb, uTt=uTt: e.matmul(
                        bank[:, 128 * fc:128 * (fc + 1)],
                        wsb[:, 1024 * c + 128 * f:1024 * c + 128 * (f + 1)],
                        uTt[:, 128 * c:128 * (c + 1)],
                        start=(c == 0 and fc == 0), stop=False,
                        skip_group_check=True), grpw if first else ())
                    first = False
                p.t(lambda e, f=f, fc=fc, bank=bank, brw=brw, last_grp=last_grp: e.matmul(
                    bank[:, 128 * fc:128 * (fc + 1)],
                    brw[:, 128 * f:128 * (f + 1)], C["onesb"][:, 0:128],
                    start=False, stop=(last_grp and not padfix),
                    skip_group_check=True))
                if padfix:
                    p.t(lambda e, f=f, fc=fc, bank=bank, brw=brw, t=t, last_grp=last_grp: e.matmul(
                        bank[:, 128 * fc:128 * (fc + 1)],
                        brw[:, 128 * f:128 * (f + 1)],
                        padZ[:, 128 * t:128 * (t + 1)],
                        start=False, stop=last_grp, skip_group_check=True))
            p.s(lambda e, bank=bank, dst3=dst3: e.activation(
                dst3, bank.rearrange("p (c n) -> p c n", n=128), AF.Identity), p.tok("pe"))

        evtoks = {}

        def do_proj(t):
            b = t % 2
            uTt = uT[:, 1024 * b:1024 * (b + 1)]
            evtok = evtoks[t]
            pad = t in (0, 17)
            if 1 <= t <= 16:
                tq = t - 1
                project_fmaj(psA, wq, uTt, brow[:, 0:1024], 0,
                             QTv[:, 0:4, 128 * tq:128 * (tq + 1)], t, False, wait=[evtok])
                project_fmaj(psB, wq, uTt, brow[:, 0:1024], 4,
                             QTv[:, 4:8, 128 * tq:128 * (tq + 1)], t, False)
            project_fmaj(psA, wk, uTt, brow[:, 1024:2048], 0,
                         KTv[:, 0:4, 128 * t:128 * (t + 1)], t, pad, wait=[evtok])
            project_fmaj(psB, wk, uTt, brow[:, 1024:2048], 4,
                         KTv[:, 4:8, 128 * t:128 * (t + 1)], t, pad)
            for half, bank in ((0, psA), (1, psB)):
                vw = list(p.tok("act")) + [evtok]
                for c in range(8):
                    p.t(lambda e, c=c, half=half, bank=bank, uTt=uTt: e.matmul(
                        bank[:], uTt[:, 128 * c:128 * (c + 1)],
                        wv[:, 1024 * c + 512 * half:1024 * c + 512 * (half + 1)],
                        start=(c == 0), stop=False), vw if c == 0 else ())
                p.t(lambda e, half=half, bank=bank, pad=pad: e.matmul(
                    bank[:], C["onesb"][:, 0:128],
                    brow[:, 2048 + 512 * half:2048 + 512 * (half + 1)],
                    start=False, stop=not pad))
                if pad:
                    p.t(lambda e, half=half, bank=bank, t=t: e.matmul(
                        bank[:], padZ[:, 128 * t:128 * (t + 1)],
                        brow[:, 2048 + 512 * half:2048 + 512 * (half + 1)],
                        start=False, stop=True))
                p.s(lambda e, half=half, bank=bank, t=t: e.activation(
                    VPv[:, t, 8 * half:8 * (half + 1), 0:64],
                    bank.rearrange("p (h n) -> p h n", n=64), AF.Identity), p.tok("pe"))

        for t in range(NTH):
            b = t % 2
            xin = xt[:, 1024 * b:1024 * (b + 1)]
            uin = ub[:, 1024 * b:1024 * (b + 1)]
            uTt = uT[:, 1024 * b:1024 * (b + 1)]
            w = [dma_toks[t]]
            if last_use_ub[b]:
                w.append(last_use_ub[b])
            ntok = tile_ln(p, C, xin, t, uin, wait=w)
            if t + 2 < NTH:
                dma_toks[t + 2] = p.ds(lambda e, t=t, b=b: e.dma_start(
                    xt[:, 1024 * b:1024 * (b + 1)],
                    x_src[128 * (t + 2):128 * (t + 3), :]), [ntok])
            trw = [ntok]
            if last_evT[b]:
                trw.append(last_evT[b])
            for c in range(8):
                p.t(lambda e, c=c, b=b, uin=uin: e.transpose(
                    pbt[:, 1024 * b + 128 * c:1024 * b + 128 * (c + 1)],
                    uin[:, 128 * c:128 * (c + 1)], C["eyeb"][:]), trw if c == 0 else ())
            last_use_ub[b] = p.tok("pe")[0]
            evw = list(p.tok("pe"))
            evtoks[t] = p.v(lambda e, uTt=uTt, b=b: e.tensor_copy(
                uTt, pbt[:, 1024 * b:1024 * (b + 1)]), evw)
            last_evT[b] = evtoks[t]
            if t >= 1:
                do_proj(t - 1)
        do_proj(NTH - 1)
        p.barrier()
        stA.close()
        wo = sb("a_wo", [128, 8 * 1024], BF16)
        masks = sb("a_masks", [128, 6 * 512], BF16)
        for c in range(8):
            p.dg(lambda e, c=c: e.dma_start(
                wo[:, 1024 * c:1024 * (c + 1)], dr["l0_wo"][128 * c:128 * (c + 1), :]))
        for i in range(6):
            p.dg(lambda e, i=i: e.dma_start(masks[:, 512 * i:512 * (i + 1)],
                                            dr["l0_masks"][128 * i:128 * (i + 1), :]))
        p.barrier()
        sq = sb("a_sq", [128, 2048], F32)
        Pb = sb("a_Pb", [128, 2 * 1024], BF16)
        attn = sb("a_attn", [128, 8 * 512], BF16)
        yb = sb("a_yb", [128, 1024], F32)
        xr = sb("a_xr", [128, 1024], F32)

        norm_bound(p, C, QTv, 2048, sq, [psA, psB, psS[:, 0:512], psS[:, 512:1024]], 8, crow)
        norm_bound(p, C, KTv, 2304, sq, [psA, psB, psS[:, 0:512], psS[:, 512:1024]], 9, crow)
        p.v(lambda e: e.tensor_tensor(crow[0:1, 10:11], crow[0:1, 8:9], crow[0:1, 9:10], ALU.mult))
        p.s(lambda e: e.activation(crow[0:1, 11:12], crow[0:1, 10:11], AF.Ln), p.tok("dve"))
        p.s(lambda e: e.activation(crow[0:1, 12:13], crow[0:1, 11:12], AF.Exp, scale=0.5))
        p.v(lambda e: e.tensor_scalar_mul(crow[0:1, 13:14], crow[0:1, 12:13], -0.125), p.tok("act"))
        bias_col_from(p, C, crow, 13, bias_c[:, 0:1], psA, None)
        p.barrier()
        stP.close()
        psS4 = st.enter_context(nc.psum_tensor("a_psS4", [128, 2048], F32))
        psAcc2 = st.enter_context(nc.psum_tensor("a_psAcc2", [128, 1024], F32))
        psBb = st.enter_context(nc.psum_tensor("a_psBb", [128, 512], F32))

        qk_tok = {}
        exp_tok = {}
        av_tok = {}
        h_done = {}
        gg_off = [0]
        oproj_tok = [None]
        for qt in range(4):
            items = [(h, kt) for h in range(16) for kt in range(6)]
            groups = [items[i:i + 2] for i in range(0, len(items), 2)]
            G = len(groups)
            pending = []

            def issue_qk(g, qt=qt, gg_off=gg_off):
                gg = gg_off[0] + g
                for i, (h, kt) in enumerate(groups[g]):
                    hp, hb = h % 2, h // 2
                    w = []
                    if i == 0 and gg >= 2:
                        w.append(exp_tok[gg - 2])
                    if i == 0 and g <= 1 and oproj_tok[0] is not None:
                        w.append(oproj_tok[0])
                    qk_tok[gg] = p.t(lambda e, kt=kt, qt=qt, gg=gg, i=i, hp=hp, hb=hb: e.matmul(
                        psS4[:, 1024 * (gg % 2) + 512 * i:1024 * (gg % 2) + 512 * (i + 1)],
                        KT[64 * hp:64 * (hp + 1),
                           2304 * hb + 512 * qt + 128 * kt:2304 * hb + 512 * qt + 128 * (kt + 1)],
                        QT[64 * hp:64 * (hp + 1),
                           2048 * hb + 512 * qt:2048 * hb + 512 * (qt + 1)],
                        start=True, stop=True), w)

            def issue_exp(g, gg_off=gg_off):
                gg = gg_off[0] + g
                w = [qk_tok[gg]]
                if gg >= 2:
                    w.append(av_tok[gg - 2])
                tok = p.s(lambda e, gg=gg: e.activation(
                    Pb[:, 1024 * (gg % 2):1024 * (gg % 2) + 1024],
                    psS4[:, 1024 * (gg % 2):1024 * (gg % 2) + 1024],
                    AF.Exp, scale=0.125, bias=bias_c[:, 0:1]), w)
                kt0 = groups[g][0][1]
                tok = p.v(lambda e, gg=gg, kt0=kt0: e.tensor_tensor(
                    Pb[:, 1024 * (gg % 2):1024 * (gg % 2) + 1024],
                    Pb[:, 1024 * (gg % 2):1024 * (gg % 2) + 1024],
                    masks[:, 512 * kt0:512 * kt0 + 1024], ALU.mult), [tok])
                exp_tok[gg] = tok

            def issue_av(g, it, qt=qt, gg_off=gg_off):
                gg = gg_off[0] + g
                for i, (h, kt) in enumerate(groups[g]):
                    w = [exp_tok[gg]] if i == 0 else []
                    if kt == 0 and (qt, h - 2) in h_done:
                        w.append(h_done[(qt, h - 2)])
                    if kt == 0 and h <= 1 and (qt - 1, 14 + h % 2) in h_done:
                        w.append(h_done[(qt - 1, 14 + h % 2)])
                    av_tok[gg] = p.t(lambda e, kt=kt, qt=qt, gg=gg, i=i, h=h: e.matmul(
                        psAcc2[0:65, 512 * (h % 2):512 * (h % 2) + 512],
                        VP[:, 1040 * (4 * qt + kt) + 65 * h:1040 * (4 * qt + kt) + 65 * h + 65],
                        Pb[:, 1024 * (gg % 2) + 512 * i:1024 * (gg % 2) + 512 * (i + 1)],
                        start=(kt == 0), stop=(kt == 5),
                        skip_group_check=True), w)
                    if kt == 5:
                        rt = p.v(lambda e, h=h: e.reciprocal(
                            rsumf[0:1, 512 * (h % 2):512 * (h % 2) + 512],
                            psAcc2[64:65, 512 * (h % 2):512 * (h % 2) + 512]),
                            [av_tok[gg]])
                        rt = p.v(lambda e, h=h: e.tensor_copy(
                            rsumZ[0:1, 512 * (h % 2):512 * (h % 2) + 512],
                            rsumf[0:1, 512 * (h % 2):512 * (h % 2) + 512]), [rt])
                        pending.append((h, rt, it))

            def flush_tail(it, qt=qt):
                while pending and pending[0][2] <= it - 2:
                    h, rt, _ = pending.pop(0)
                    bw = [rt]
                    if (qt, h - 1) in h_done:
                        bw.append(h_done[(qt, h - 1)])
                    elif h == 0 and (qt - 1, 15) in h_done:
                        bw.append(h_done[(qt - 1, 15)])
                    bt = p.t(lambda e, h=h: e.matmul(
                        psBb[0:64, :], C["onesb"][:, 0:64],
                        rsumZ[:, 512 * (h % 2):512 * (h % 2) + 512],
                        start=True, stop=True), bw)
                    bt = p.v(lambda e, h=h: e.tensor_copy(
                        bcastS[:, 512 * (h % 2):512 * (h % 2) + 512],
                        psBb[0:64, :]), [bt])
                    h_done[(qt, h)] = p.v(lambda e, h=h, qt=qt: e.tensor_tensor(
                        attn[64 * (h % 2):64 * (h % 2) + 64,
                             512 * (h // 2):512 * (h // 2) + 512],
                        psAcc2[0:64, 512 * (h % 2):512 * (h % 2) + 512],
                        bcastS[:, 512 * (h % 2):512 * (h % 2) + 512],
                        ALU.mult), [bt])

            for g in range(G):
                issue_qk(g)
                issue_exp(g)
                if g >= 1:
                    issue_av(g - 1, g)
                flush_tail(g)
            issue_av(G - 1, G)
            flush_tail(G + 2)
            gg_off[0] += G

            prev_store = None
            attn_w = [h_done[(qt, 14)], h_done[(qt, 15)]]
            for ts in range(4):
                tglob = 4 * qt + ts
                mw = attn_w + list(p.tok("act"))
                for nh in range(2):
                    for pr in range(8):
                        p.t(lambda e, ts=ts, nh=nh, pr=pr: e.matmul(
                            psS4[:, 512 * nh:512 * (nh + 1)],
                            attn[:, 512 * pr + 128 * ts:512 * pr + 128 * (ts + 1)],
                            wo[:, 1024 * pr + 512 * nh:1024 * pr + 512 * (nh + 1)],
                            start=(pr == 0), stop=False), mw if (nh == 0 and pr == 0) else ())
                    p.t(lambda e, nh=nh: e.matmul(
                        psS4[:, 512 * nh:512 * (nh + 1)], C["onesb"][:, 0:128],
                        brow[:, 3072 + 512 * nh:3072 + 512 * (nh + 1)], start=False, stop=True))
                xw = list(p.tok("dve"))
                if prev_store:
                    xw.append(prev_store)
                xrtok = p.ds(lambda e, tglob=tglob: e.dma_start(
                    xr[:], x_src[128 * (tglob + 1):128 * (tglob + 2), :]), xw)
                prev_store = ln_resid_store(
                    p, C, "psum2", psS4[:, 0:512], psS4[:, 512:1024], xr[:], yb,
                    x_dst[128 * tglob:128 * (tglob + 1), :],
                    extra_wait=list(p.tok("pe")) + [xrtok])
            oproj_tok[0] = prev_store
        p.barrier()


def phase_l0f(p, C, nc, dr, x_src, x_dst, cio):
    o2d = cio["o2_dram"]
    with ExitStack() as st:
        sb = lambda n, shp, dt: st.enter_context(nc.sbuf_tensor(n, shp, dt))
        ps = lambda n, shp, dt=F32: st.enter_context(nc.psum_tensor(n, shp, dt))
        stA = ExitStack()
        sba = lambda n, shp, dt: stA.enter_context(nc.sbuf_tensor(n, shp, dt))
        xt = sb("f_xt", [128, 2 * 1024], F32)
        ub = sb("f_ub", [128, 2 * 1024], BF16)
        scfull = sb("f_scfull", [128, 64], F32)
        scloc = sb("f_scloc", [128, 16], F32)
        bis = sb("f_bis", [128, 8], F32)
        gates = sb("f_gates", [128, 16], F32)
        o2tok = sb("f_o2tok", [128, 2 * 1024], BF16)
        W1 = sba("f_W1", [128, 8 * 4096], BF16)
        W2 = sba("f_W2", [128, 32 * 1024], BF16)
        b1c = sba("f_b1c", [128, 32], F32)
        b2c = sba("f_b2c", [128, 8], F32)
        wr8 = sba("f_wr8", [8, 128], F32)
        wrc = sba("f_wrc", [128, 8], BF16)
        u2T = sba("f_u2T", [128, 8 * 2048], BF16)
        scrow = sba("f_scrow", [1, 2048], F32)
        scl16 = sba("f_scl16", [16, 128], F32)
        hbuf = sba("f_hb", [128, 2 * 256], BF16)
        o2f = sba("f_o2f", [128, 256], BF16)
        pbt = ps("f_pbt", [128, 1024], BF16)
        psH = ps("f_psH", [128, 2 * 512])
        psO0 = ps("f_psO0", [128, 2 * 256])
        psO1 = ps("f_psO1", [128, 2 * 256])
        psO2 = ps("f_psO2", [128, 2 * 256])
        psO3 = ps("f_psO3", [128, 2 * 256])
        psC = ps("f_psC", [128, 512])
        psOm = [psO0[:, 0:256], psO0[:, 256:512], psO1[:, 0:256], psO1[:, 256:512],
                psO2[:, 0:256], psO2[:, 256:512], psO3[:, 0:256], psO3[:, 256:512]]

        for c in range(8):
            p.dg(lambda e, c=c: e.dma_start(W1[:, 4096 * c:4096 * (c + 1)],
                                            dr["mod_w1"][128 * c:128 * (c + 1), :]))
        for c in range(32):
            p.dg(lambda e, c=c: e.dma_start(W2[:, 1024 * c:1024 * (c + 1)],
                                            dr["mod_w2"][128 * c:128 * (c + 1), :]))
        p.dg(lambda e: e.dma_start(b1c[:], dr["mod_b1c"]))
        p.dg(lambda e: e.dma_start(b2c[:], dr["mod_b2c"]))
        p.ds(lambda e: e.dma_start(wr8[:], dr["mod_wr8"]))
        p.barrier()
        p.t(lambda e: e.matmul(psC[:, 0:8], wr8[:], C["eyef"][0:8, 0:8], start=True, stop=True))
        p.v(lambda e: e.tensor_copy(wrc[:], psC[:, 0:8]), p.tok("pe"))
        p.barrier()

        dma_toks = {}
        last_use_ub = {0: None, 1: None}
        last_evT = {0: None, 1: None}
        for t in (0, 1):
            b = t % 2
            dma_toks[t] = p.ds(lambda e, t=t, b=b: e.dma_start(
                xt[:, 1024 * b:1024 * (b + 1)], x_src[128 * t:128 * (t + 1), :]))
        for t in range(NT):
            b = t % 2
            xin = xt[:, 1024 * b:1024 * (b + 1)]
            uin = ub[:, 1024 * b:1024 * (b + 1)]
            w = [dma_toks[t]]
            if last_use_ub[b]:
                w.append(last_use_ub[b])
            ntok = tile_ln(p, C, xin, t, uin, wait=w)
            if t + 2 < NT:
                dma_toks[t + 2] = p.ds(lambda e, t=t, b=b: e.dma_start(
                    xt[:, 1024 * b:1024 * (b + 1)],
                    x_src[128 * (t + 2):128 * (t + 3), :]), [ntok])
            trw = [ntok]
            if last_evT[0]:
                trw.append(last_evT[0])
            for c in range(8):
                p.t(lambda e, c=c, uin=uin: e.transpose(
                    pbt[:, 128 * c:128 * (c + 1)],
                    uin[:, 128 * c:128 * (c + 1)], C["eyeb"][:]), trw if c == 0 else ())
            last_use_ub[b] = p.tok("pe")[0]
            last_evT[0] = p.v(lambda e, t=t: e.tensor_copy(
                u2T.rearrange("p (c n) -> p c n", n=2048)[:, :, 128 * t:128 * (t + 1)],
                pbt[:, 0:1024].rearrange("p (c n) -> p c n", n=128)),
                p.tok("pe"))
        p.barrier()

        for n in range(4):
            for c in range(8):
                p.t(lambda e, n=n, c=c: e.matmul(
                    psC[0:1, 0:512], wrc[:, c:c + 1],
                    u2T[:, 2048 * c + 512 * n:2048 * c + 512 * (n + 1)],
                    start=(c == 0), stop=(c == 7)), p.tok("dve") if c == 0 else ())
            p.v(lambda e, n=n: e.tensor_copy(scrow[0:1, 512 * n:512 * (n + 1)], psC[0:1, 0:512]),
                p.tok("pe"))
        p.ds(lambda e: e.dma_start(cio["sc_in"], scrow[0:1, :]), p.tok("dve"))
        p.ccop(lambda e: e.collective_compute(
            "AllGather", ALU.bypass, GROUPS, ins=[cio["sc_in"].opt()],
            outs=[cio["sc_out"].opt()]), p.tok("qs"))
        p.ds(lambda e: e.dma_start(
            scfull[:], cio["sc_out"].rearrange("a (c g) -> (a c) g", g=64)), p.tok("cc"))
        p.ds(lambda e: e.dma_start(
            scl16[:], cio["sc_in"].rearrange("a (b c) -> (a b) c", c=128)), p.tok("cc"))
        p.t(lambda e: e.transpose(psC[:, 0:16], scl16[:], C["eyef"][0:16, 0:16]), p.tok("qs"))
        p.v(lambda e: e.tensor_copy(scloc[:], psC[:, 0:16]), p.tok("pe"))
        p.barrier()

        K_SEL = 4096.0
        N_BIS = 42
        p.v(lambda e: e.memset(bis[:, 0:1], -10000.0))
        p.v(lambda e: e.memset(bis[:, 1:2], 10000.0))
        p.v(lambda e: e.memset(bis[:, 3:4], 3.0e38))

        def bis_pre(it):
            p.v(lambda e: e.tensor_add(bis[:, 6:7], bis[:, 0:1], bis[:, 1:2]))
            p.v(lambda e: e.tensor_scalar_mul(bis[:, 2:3], bis[:, 6:7], 0.5))
            p.v(lambda e: e.tensor_single_scalar(C["tiny"][:], scfull[:], bis[:, 2:3],
                                                 ALU.is_gt))
            return p.v(lambda e: e.reduce_sum(bis[:, 6:7], C["tiny"][:], axis=AX.X))

        def bis_step(it, red_tok):
            cnt = p.t(lambda e: e.matmul(psC[:, 0:1], C["onesf"][:], bis[:, 6:7],
                                         start=True, stop=True), [red_tok])
            p.v(lambda e: e.tensor_single_scalar(bis[:, 4:5], psC[:, 0:1], K_SEL,
                                                 ALU.is_gt), [cnt])
            p.v(lambda e: e.tensor_single_scalar(bis[:, 5:6], psC[:, 0:1], K_SEL,
                                                 ALU.is_equal))
            p.v(lambda e: e.tensor_sub(bis[:, 7:8], bis[:, 2:3], bis[:, 3:4]))
            p.v(lambda e: e.scalar_tensor_tensor(bis[:, 3:4], bis[:, 7:8], bis[:, 5:6],
                                                 bis[:, 3:4], ALU.mult, ALU.add))
            p.v(lambda e: e.tensor_sub(bis[:, 7:8], bis[:, 2:3], bis[:, 0:1]))
            p.v(lambda e: e.scalar_tensor_tensor(bis[:, 0:1], bis[:, 7:8], bis[:, 4:5],
                                                 bis[:, 0:1], ALU.mult, ALU.add))
            p.v(lambda e: e.tensor_scalar(bis[:, 5:6], bis[:, 4:5], -1.0, 1.0,
                                          ALU.mult, ALU.add))
            p.v(lambda e: e.tensor_sub(bis[:, 7:8], bis[:, 2:3], bis[:, 1:2]))
            p.v(lambda e: e.scalar_tensor_tensor(bis[:, 1:2], bis[:, 7:8], bis[:, 5:6],
                                                 bis[:, 1:2], ALU.mult, ALU.add))
            if it + 1 < N_BIS:
                return bis_pre(it + 1)
            return None

        lastgelu = {0: None, 1: None}
        w2done = {0: None, 1: None}
        lasto2 = None
        last_etok = None
        seq = [(tq, f) for tq in range(8) for f in range(32)]

        def issue_w1(tq, f):
            hb = f % 2
            hw = [lastgelu[hb]] if lastgelu[hb] else []
            first = True
            qk_last = None
            for c in range(8):
                qk_last = p.t(lambda e, tq=tq, f=f, c=c, hb=hb: e.matmul(
                    psH[:, 512 * hb:512 * hb + 256],
                    W1[:, 4096 * c + 128 * f:4096 * c + 128 * (f + 1)],
                    u2T[:, 2048 * c + 256 * tq:2048 * c + 256 * (tq + 1)],
                    start=(c == 0), stop=(c == 7), skip_group_check=True),
                    hw if first else ())
                first = False
            gw = [qk_last]
            if w2done[hb]:
                gw.append(w2done[hb])
            lastgelu[hb] = p.s(lambda e, f=f, hb=hb: e.activation(
                hbuf[:, 256 * hb:256 * (hb + 1)], psH[:, 512 * hb:512 * hb + 256],
                AF.Gelu_apprx_tanh, bias=b1c[:, f:f + 1]), gw)

        def issue_w2(tq, f):
            nonlocal lasto2, last_etok
            hb = f % 2
            w2w = [lastgelu[hb]]
            if f == 0 and last_etok is not None:
                w2w.append(last_etok)
            for m in range(8):
                p.t(lambda e, f=f, m=m, hb=hb: e.matmul(
                    psOm[m], W2[:, 1024 * f + 128 * m:1024 * f + 128 * (m + 1)],
                    hbuf[:, 256 * hb:256 * (hb + 1)],
                    start=(f == 0 and m % 2 == 0), stop=(f == 31 and m % 2 == 1),
                    skip_group_check=True),
                    w2w if m == 0 else ())
            w2done[hb] = p.tok("pe")[0]
            if f != 31:
                return
            w2fin = w2done[hb]
            for m in range(8):
                evw = [w2fin]
                if lasto2:
                    evw.append(lasto2)
                etok = p.v(lambda e, m=m: e.tensor_scalar(
                    o2f[:], psOm[m], b2c[:, m:m + 1], None, ALU.add), evw)
                last_etok = etok
                for half in range(2):
                    p.t(lambda e, half=half: e.transpose(
                        pbt[:, 128 * half:128 * (half + 1)],
                        o2f[:, 128 * half:128 * (half + 1)], C["eyeb"][:]),
                        [etok] if half == 0 else ())
                lasto2 = p.v(lambda e, m=m: e.tensor_copy(
                    o2tok.rearrange("p (tt n) -> p tt n", n=1024)[:, 0:2,
                                                                 128 * m:128 * (m + 1)],
                    pbt[:, 0:256].rearrange("p (b n) -> p b n", n=128)),
                    list(p.tok("pe")) + list(p.tok("qs")))
            for tt in range(2):
                p.ds(lambda e, tq=tq, tt=tt: e.dma_start(
                    o2d[128 * (2 * tq + tt):128 * (2 * tq + tt + 1), :],
                    o2tok[:, 1024 * tt:1024 * (tt + 1)]), p.tok("dve"))

        red_tok = bis_pre(0)
        bis_it = 0
        for idx, (tq, f) in enumerate(seq):
            issue_w1(tq, f)
            if idx >= 1:
                issue_w2(*seq[idx - 1])
            if idx >= 1 and bis_it < N_BIS:
                red_tok = bis_step(bis_it, red_tok)
                bis_it += 1
        issue_w2(*seq[-1])
        p.v(lambda e: e.tensor_single_scalar(bis[:, 4:5], bis[:, 3:4], 1.0e38, ALU.is_lt))
        p.v(lambda e: e.tensor_sub(bis[:, 7:8], bis[:, 3:4], bis[:, 2:3]))
        p.v(lambda e: e.scalar_tensor_tensor(bis[:, 6:7], bis[:, 7:8], bis[:, 4:5],
                                             bis[:, 2:3], ALU.mult, ALU.add))
        p.v(lambda e: e.tensor_copy(bis[:, 3:4], bis[:, 6:7]))
        p.v(lambda e: e.tensor_single_scalar(C["tiny"][:, 16:32], scloc[:], bis[:, 3:4],
                                             ALU.is_gt))
        bis_tok = p.tok("dve")
        p.barrier()
        stA.close()

        p.s(lambda e: e.activation(gates[:], scloc[:], AF.Sigmoid), bis_tok)
        gtok = p.v(lambda e: e.tensor_tensor(gates[:], gates[:], C["tiny"][:, 16:32],
                                             ALU.mult), p.tok("act"))

        ub3 = st.enter_context(nc.sbuf_tensor("f_ub3", [128, 3 * 1024], BF16))
        o23 = st.enter_context(nc.sbuf_tensor("f_o23", [128, 3 * 1024], BF16))
        cb3 = st.enter_context(nc.sbuf_tensor("f_cb3", [128, 3 * 1024], F32))

        def mk_src_l0f(t, s, xin, xdt, wfree):
            utok = p.s(lambda e, t=t, s=s, xin=xin: e.activation(
                ub3[:, 1024 * s:1024 * (s + 1)], xin, AF.Identity,
                bias=C["mrs"][:, 4 * t + 3:4 * t + 4],
                scale=C["mrs"][:, 4 * t + 2:4 * t + 3]), [xdt])
            otok = p.ds(lambda e, t=t, s=s: e.dma_start(
                o23[:, 1024 * s:1024 * (s + 1)], o2d[128 * t:128 * (t + 1), :]), wfree)
            ctok = p.v(lambda e, t=t, s=s: e.scalar_tensor_tensor(
                cb3[:, 1024 * s:1024 * (s + 1)], o23[:, 1024 * s:1024 * (s + 1)],
                gates[:, t:t + 1], ub3[:, 1024 * s:1024 * (s + 1)],
                ALU.mult, ALU.add), [utok, otok, gtok])
            return cb3[:, 1024 * s:1024 * (s + 1)], ctok

        ln_resid_pipe(p, C, nc, st, "f", NT, mk_src_l0f, x_src, x_dst)
        p.barrier()


def phase_l1a(p, C, nc, dr, x_src, x_dst, cio):
    with ExitStack() as st:
        sb = lambda n, shp, dt: st.enter_context(nc.sbuf_tensor(n, shp, dt))
        ps = lambda n, shp, dt=F32: st.enter_context(nc.psum_tensor(n, shp, dt))
        masks = sb("c_masks", [128, 4 * 512], BF16)
        xt = sb("c_xt", [128, 2 * 1024], F32)
        wqp = sb("c_wqp", [128, 8 * 128], BF16)
        wkp = sb("c_wkp", [128, 8 * 128], BF16)
        wvpp = sb("c_wvp", [128, 8 * 128], BF16)
        woT = sb("c_woT", [128, 2 * 1024], BF16)
        QTp = sb("c_QTp", [128, 8192], BF16)
        KT0 = sb("c_KT0", [128, 8192], BF16)
        KT1 = sb("c_KT1", [128, 8192], BF16)
        VPp = sb("c_VPp", [128, 64 * 130], BF16)
        AT0 = sb("c_AT0", [128, 8192], BF16)
        AT1 = sb("c_AT1", [128, 8192], BF16)
        sq = sb("c_sq", [128, 2048], F32)
        crow = sb("c_crow", [1, 16], F32)
        bias_c = sb("c_biasc", [128, 1], F32)
        Pb = sb("c_Pb", [128, 2 * 1024], BF16)
        rsumZ = sb("c_rsumZ", [128, 2 * 512], BF16)
        rsumf = sb("c_rsumf", [1, 2 * 512], F32)
        bcastS = sb("c_bcastS", [64, 2 * 512], BF16)
        ytile = sb("c_ytile", [128, 2 * 1024], BF16)
        stP = ExitStack()
        pbt = stP.enter_context(nc.psum_tensor("c_pbt", [128, 2 * 1024], BF16))
        ub = stP.enter_context(nc.sbuf_tensor("c_ub", [128, 2 * 1024], BF16))
        u3T = stP.enter_context(nc.sbuf_tensor("c_u3T", [128, 8 * 2048], BF16))

        for i in range(4):
            p.dg(lambda e, i=i: e.dma_start(masks[:, 512 * i:512 * (i + 1)],
                                            dr["l1_masks"][128 * i:128 * (i + 1), :]))
        for c in range(2):
            p.dg(lambda e, c=c: e.dma_start(woT[:, 1024 * c:1024 * (c + 1)],
                                            dr["l1_wo"][128 * c:128 * (c + 1), :]))
        p.v(lambda e: e.memset(VPp[:], 1.0))
        p.v(lambda e: e.memset(KT0[64:128, :], 0.0))
        p.v(lambda e: e.memset(KT1[0:64, :], 0.0))
        p.v(lambda e: e.memset(rsumZ[:], 0.0))
        p.barrier()

        dma_toks = {}
        last_use_ub = {0: None, 1: None}
        last_evT = {0: None, 1: None}
        for t in (0, 1):
            b = t % 2
            dma_toks[t] = p.ds(lambda e, t=t, b=b: e.dma_start(
                xt[:, 1024 * b:1024 * (b + 1)], x_src[128 * t:128 * (t + 1), :]))
        for t in range(NT):
            b = t % 2
            xin = xt[:, 1024 * b:1024 * (b + 1)]
            uin = ub[:, 1024 * b:1024 * (b + 1)]
            w = [dma_toks[t]]
            if last_use_ub[b]:
                w.append(last_use_ub[b])
            ntok = tile_ln(p, C, xin, t, uin, wait=w)
            if t + 2 < NT:
                dma_toks[t + 2] = p.ds(lambda e, t=t, b=b: e.dma_start(
                    xt[:, 1024 * b:1024 * (b + 1)],
                    x_src[128 * (t + 2):128 * (t + 3), :]), [ntok])
            trw = [ntok]
            if last_evT[b]:
                trw.append(last_evT[b])
            for c in range(8):
                p.t(lambda e, c=c, b=b, uin=uin: e.transpose(
                    pbt[:, 1024 * b + 128 * c:1024 * b + 128 * (c + 1)],
                    uin[:, 128 * c:128 * (c + 1)], C["eyeb"][:]), trw if c == 0 else ())
            last_use_ub[b] = p.tok("pe")[0]
            last_evT[b] = p.v(lambda e, t=t, b=b: e.tensor_copy(
                u3T.rearrange("p (c n) -> p c n", n=2048)[:, :, 128 * t:128 * (t + 1)],
                pbt[:, 1024 * b:1024 * (b + 1)].rearrange("p (c n) -> p c n", n=128)),
                p.tok("pe"))
        for c in range(8):
            p.ds(lambda e, c=c: e.dma_start(
                cio["ag_in"][128 * c:128 * (c + 1), :], u3T[:, 2048 * c:2048 * (c + 1)]),
                p.tok("dve"))
        p.ccop(lambda e: e.collective_compute(
            "AllGather", ALU.bypass, GROUPS, ins=[cio["ag_in"].opt()],
            outs=[cio["ag_out"].opt()]), p.tok("qs"))
        p.barrier()
        stP.close()
        xb = st.enter_context(nc.sbuf_tensor("c_xb", [128, 8 * 2048], BF16))
        psS4 = st.enter_context(nc.psum_tensor("c_psS4", [128, 2048], F32))
        psAcc2 = st.enter_context(nc.psum_tensor("c_psAcc2", [128, 1024], F32))
        psB = st.enter_context(nc.psum_tensor("c_psB", [128, 512], F32))
        psV = st.enter_context(nc.psum_tensor("c_psV", [128, 256], F32))

        wsafe = [None]
        for hp in range(2):
            AT = AT0 if hp == 0 else AT1
            wdma = [wsafe[0]] if wsafe[0] is not None else []
            for wsb, nm in ((wqp, "l1_wq"), (wkp, "l1_wk"), (wvpp, "l1_wv")):
                for c in range(8):
                    p.dg(lambda e, wsb=wsb, nm=nm, c=c, hp=hp: e.dma_start(
                        wsb[:, 128 * c:128 * (c + 1)],
                        dr[nm][128 * c:128 * (c + 1), 128 * hp:128 * (hp + 1)]),
                        wdma if (wsb is wqp and c == 0) else ())
            sr_tok = {}

            def load_sr(sr, w):
                r, th = divmod(sr, 2)
                b = sr % 2
                for c in range(8):
                    p.dg(lambda e, r=r, th=th, b=b, c=c: e.dma_start(
                        xb[:, 8192 * b + 1024 * c:8192 * b + 1024 * (c + 1)],
                        cio["ag_out"][1024 * r + 128 * c:1024 * r + 128 * (c + 1),
                                      1024 * th:1024 * (th + 1)]),
                        w if c == 0 else ())
                sr_tok[sr] = p.tok("qg")[0]

            load_sr(0, [])
            load_sr(1, [])
            for sr in range(8):
                r, th = divmod(sr, 2)
                b = sr % 2
                ldt = [sr_tok[sr]]
                for dst, wsb in (("q", wqp), ("k", wkp)):
                    for n2 in range(2):
                        gw = list(ldt) + list(p.tok("act"))
                        for c in range(8):
                            p.t(lambda e, wsb=wsb, b=b, n2=n2, c=c: e.matmul(
                                psS4[:, 0:512], wsb[:, 128 * c:128 * (c + 1)],
                                xb[:, 8192 * b + 1024 * c + 512 * n2:
                                   8192 * b + 1024 * c + 512 * (n2 + 1)],
                                start=(c == 0), stop=(c == 7)), gw if c == 0 else ())
                        seg = 2048 * r + 1024 * th + 512 * n2
                        if dst == "q":
                            p.s(lambda e, seg=seg: e.activation(
                                QTp[:, seg:seg + 512],
                                psS4[:, 0:512], AF.Identity), p.tok("pe"))
                        else:
                            p.s(lambda e, seg=seg: e.activation(
                                KT0[0:64, seg:seg + 512],
                                psS4[0:64, 0:512], AF.Identity), p.tok("pe"))
                            p.s(lambda e, seg=seg: e.activation(
                                KT1[64:128, seg:seg + 512],
                                psS4[64:128, 0:512], AF.Identity))
                for ts2 in range(8):
                    tglob = 16 * r + 8 * th + ts2
                    bank = psV[:, 0:128] if ts2 % 2 == 0 else psB[:, 0:128]
                    vw = list(ldt) + list(p.tok("act"))
                    for c in range(8):
                        p.t(lambda e, ts2=ts2, b=b, c=c, bank=bank: e.matmul(
                            bank,
                            xb[:, 8192 * b + 1024 * c + 128 * ts2:
                               8192 * b + 1024 * c + 128 * (ts2 + 1)],
                            wvpp[:, 128 * c:128 * (c + 1)],
                            start=(c == 0), stop=(c == 7)), vw if c == 0 else ())
                    p.s(lambda e, tglob=tglob, bank=bank: e.activation(
                        VPp.rearrange("p (tt h n) -> p tt h n", h=2, n=65)[:, tglob, :, 0:64],
                        bank.rearrange("p (h n) -> p h n", n=64), AF.Identity),
                        p.tok("pe"))
                if sr + 2 < 8:
                    load_sr(sr + 2, list(p.tok("pe")))
            wsafe[0] = p.tok("pe")[0]
            p.barrier()
            QTv = QTp.rearrange("p (c n) -> p c n", c=1)
            norm_bound(p, C, QTv, 8192, sq,
                       [psS4[:, 0:512], psS4[:, 512:1024], psS4[:, 1024:1536],
                        psS4[:, 1536:2048]], 8, crow, nch=1)
            norm_bound(p, C, KT0.rearrange("p (c n) -> p c n", c=1), 8192, sq,
                       [psS4[:, 0:512], psS4[:, 512:1024], psS4[:, 1024:1536],
                        psS4[:, 1536:2048]], 9, crow, nch=1)
            norm_bound(p, C, KT1.rearrange("p (c n) -> p c n", c=1), 8192, sq,
                       [psS4[:, 0:512], psS4[:, 512:1024], psS4[:, 1024:1536],
                        psS4[:, 1536:2048]], 14, crow, nch=1)
            p.v(lambda e: e.tensor_tensor(crow[0:1, 9:10], crow[0:1, 9:10],
                                          crow[0:1, 14:15], ALU.max))
            p.v(lambda e: e.tensor_tensor(crow[0:1, 10:11], crow[0:1, 8:9], crow[0:1, 9:10], ALU.mult))
            p.s(lambda e: e.activation(crow[0:1, 11:12], crow[0:1, 10:11], AF.Ln), p.tok("dve"))
            p.s(lambda e: e.activation(crow[0:1, 12:13], crow[0:1, 11:12], AF.Exp, scale=0.5))
            p.v(lambda e: e.tensor_scalar_mul(crow[0:1, 13:14], crow[0:1, 12:13], -0.125),
                p.tok("act"))
            bias_col_from(p, C, crow, 13, bias_c[:, 0:1], psB, None)
            p.barrier()
            for h2 in range(2):
                items = []
                for qt in range(16):
                    for kt in range(4 * qt + 4):
                        items.append((qt, kt))
                groups = [items[i:i + 2] for i in range(0, len(items), 2)]
                G = len(groups)
                qk_tok = {}
                exp_tok = {}
                av_tok = {}
                qt_done = {}

                KTsel = KT0 if h2 == 0 else KT1

                def issue_qk(g, KTsel=KTsel):
                    for i, (qt, kt) in enumerate(groups[g]):
                        w = []
                        if i == 0 and g >= 2:
                            w.append(exp_tok[g - 2])
                        qk_tok[g] = p.t(lambda e, kt=kt, qt=qt, g=g, i=i, KTsel=KTsel: e.matmul(
                            psS4[:, 1024 * (g % 2) + 512 * i:1024 * (g % 2) + 512 * (i + 1)],
                            KTsel[:, 128 * kt:128 * (kt + 1)],
                            QTp[:, 512 * qt:512 * (qt + 1)],
                            start=True, stop=True), w)

                def issue_exp(g):
                    w = [qk_tok[g]]
                    if g >= 2:
                        w.append(av_tok[g - 2])
                    tok = p.s(lambda e, g=g: e.activation(
                        Pb[:, 1024 * (g % 2):1024 * (g % 2) + 1024],
                        psS4[:, 1024 * (g % 2):1024 * (g % 2) + 1024],
                        AF.Exp, scale=0.125, bias=bias_c[:, 0:1]), w)
                    qt0, kt0 = groups[g][0]
                    dlt0 = kt0 - 4 * qt0
                    if dlt0 >= 0:
                        tok = p.v(lambda e, g=g, dlt0=dlt0: e.tensor_tensor(
                            Pb[:, 1024 * (g % 2):1024 * (g % 2) + 1024],
                            Pb[:, 1024 * (g % 2):1024 * (g % 2) + 1024],
                            masks[:, 512 * dlt0:512 * dlt0 + 1024], ALU.mult), [tok])
                    exp_tok[g] = tok

                pending = []

                def issue_av(g, it):
                    for i, (qt, kt) in enumerate(groups[g]):
                        nkt = 4 * qt + 4
                        w = [exp_tok[g]] if i == 0 else []
                        if kt == 0 and qt >= 2:
                            w.append(qt_done[qt - 2])
                        av_tok[g] = p.t(lambda e, h2=h2, kt=kt, qt=qt, g=g, i=i, nkt=nkt: e.matmul(
                            psAcc2[0:65, 512 * (qt % 2):512 * (qt % 2) + 512],
                            VPp[:, 130 * kt + 65 * h2:130 * kt + 65 * h2 + 65],
                            Pb[:, 1024 * (g % 2) + 512 * i:1024 * (g % 2) + 512 * (i + 1)],
                            start=(kt == 0), stop=(kt == nkt - 1),
                            skip_group_check=True), w)
                        if kt == nkt - 1:
                            rt = p.v(lambda e, qt=qt: e.reciprocal(
                                rsumf[0:1, 512 * (qt % 2):512 * (qt % 2) + 512],
                                psAcc2[64:65, 512 * (qt % 2):512 * (qt % 2) + 512]),
                                [av_tok[g]])
                            rt = p.v(lambda e, qt=qt: e.tensor_copy(
                                rsumZ[0:1, 512 * (qt % 2):512 * (qt % 2) + 512],
                                rsumf[0:1, 512 * (qt % 2):512 * (qt % 2) + 512]), [rt])
                            pending.append((qt, rt, it))

                def flush_tail(it, h2=h2, AT=AT):
                    while pending and pending[0][2] <= it - 2:
                        qt, rt, _ = pending.pop(0)
                        bw = [rt]
                        if qt >= 1:
                            bw.append(qt_done[qt - 1])
                        bt = p.t(lambda e, qt=qt: e.matmul(
                            psB[0:64, :], C["onesb"][:, 0:64],
                            rsumZ[:, 512 * (qt % 2):512 * (qt % 2) + 512],
                            start=True, stop=True), bw)
                        bt = p.v(lambda e, qt=qt: e.tensor_copy(
                            bcastS[:, 512 * (qt % 2):512 * (qt % 2) + 512],
                            psB[0:64, :]), [bt])
                        qt_done[qt] = p.v(lambda e, h2=h2, qt=qt, AT=AT: e.tensor_tensor(
                            AT[64 * h2:64 * (h2 + 1), 512 * qt:512 * (qt + 1)],
                            psAcc2[0:64, 512 * (qt % 2):512 * (qt % 2) + 512],
                            bcastS[:, 512 * (qt % 2):512 * (qt % 2) + 512],
                            ALU.mult), [bt])

                for g in range(G):
                    issue_qk(g)
                    issue_exp(g)
                    if g >= 1:
                        issue_av(g - 1, g)
                    flush_tail(g)
                issue_av(G - 1, G)
                flush_tail(G + 2)
            p.barrier()
        lastst = {0: None, 1: None}
        evtok = {0: None, 1: None}
        for ts in range(64):
            pb = ts % 2
            mw = [evtok[pb]] if evtok[pb] else list(p.tok("dve", "act"))
            for nh in range(2):
                for pr, AT in ((0, AT0), (1, AT1)):
                    p.t(lambda e, ts=ts, nh=nh, pr=pr, AT=AT, pb=pb: e.matmul(
                        psS4[:, 1024 * pb + 512 * nh:1024 * pb + 512 * (nh + 1)],
                        AT[:, 128 * ts:128 * (ts + 1)],
                        woT[:, 1024 * pr + 512 * nh:1024 * pr + 512 * (nh + 1)],
                        start=(pr == 0), stop=(pr == 1)), mw if (nh == 0 and pr == 0) else ())
            evw = list(p.tok("pe"))
            if lastst[pb]:
                evw.append(lastst[pb])
            p.s(lambda e, pb=pb: e.activation(
                ytile[:, 1024 * pb:1024 * pb + 512],
                psS4[:, 1024 * pb:1024 * pb + 512], AF.Identity), evw)
            a2 = p.s(lambda e, pb=pb: e.activation(
                ytile[:, 1024 * pb + 512:1024 * pb + 1024],
                psS4[:, 1024 * pb + 512:1024 * pb + 1024], AF.Identity))
            evtok[pb] = a2
            lastst[pb] = p.ds(lambda e, ts=ts, pb=pb: e.dma_start(
                cio["rs_in"][128 * ts:128 * (ts + 1), :],
                ytile[:, 1024 * pb:1024 * pb + 1024]), [a2])
        p.ccop(lambda e: e.collective_compute(
            "ReduceScatter", ALU.add, GROUPS, ins=[cio["rs_in"].opt()],
            outs=[cio["rs_out"].opt()]), p.tok("qs"))
        p.barrier()
        cc_tok = [p.tok("cc")[0]]
        ybf3 = st.enter_context(nc.sbuf_tensor("c_ybf3", [128, 3 * 1024], BF16))

        def mk_src_l1a(t, s, xin, xdt, wfree):
            ydt = p.ds(lambda e, t=t, s=s: e.dma_start(
                ybf3[:, 1024 * s:1024 * (s + 1)],
                cio["rs_out"][128 * t:128 * (t + 1), :]), wfree + cc_tok)
            return ybf3[:, 1024 * s:1024 * (s + 1)], ydt

        ln_resid_pipe(p, C, nc, st, "c", NT, mk_src_l1a, x_src, x_dst, nslot=2)
        p.barrier()


def phase_l1f(p, C, nc, dr, x_src, x_dst, cio):
    with ExitStack() as st:
        sb = lambda n, shp, dt: st.enter_context(nc.sbuf_tensor(n, shp, dt))
        ps = lambda n, shp, dt=F32: st.enter_context(nc.psum_tensor(n, shp, dt))
        xt = sb("d_xt", [128, 2 * 1024], F32)
        uf = sb("d_uf", [128, 1024], F32)
        uhi = sb("d_uhi", [128, 1024], BF16)
        ulo = sb("d_ulo", [128, 1024], BF16)
        alpha = sb("d_alpha", [128, 16 * 64], BF16)
        alphaT = sb("d_alphaT", [128, 2048], BF16)
        mcol = sb("d_mcol", [128, 8], BF16)
        taus = sb("d_taus", [128, 4], F32)
        b1p = sb("d_b1p", [128, 512], BF16)
        sIT = sb("d_sIT", [128, 8 * 64], BF16)
        sITf = sb("d_sITf", [128, 8 * 64], F32)
        sIb = sb("d_sI", [64, 1024], BF16)
        si = sb("d_si", [64, 1028], F32)
        sid_b = sb("d_sidb", [128, 1024], BF16)
        siT = sb("d_siT", [128, 8 * 64], BF16)
        SnT = sb("d_SnT", [128, 8 * 64], F32)
        SnTb = sb("d_SnTb", [128, 8 * 64], BF16)
        qT = sb("d_qT", [128, 2 * 64], BF16)
        kT = sb("d_kT", [128, 2 * 64], BF16)
        scb = sb("d_scb", [128, 64], BF16)
        scq = sb("d_scq", [64, 64], F32)
        attnSb = sb("d_attnSb", [128, 64], BF16)
        attnTb = sb("d_attnTb", [128, 64], BF16)
        vs = sb("d_vs", [128, 1024], BF16)
        ySlT = sb("d_ySlT", [128, 8 * 64], F32)
        ySl = sb("d_ySl", [64, 1024], F32)
        yhat = sb("d_yhat", [128, 1024], BF16)
        yhT = sb("d_yhT", [128, 8 * 64], BF16)
        sv = sb("d_sv", [128, 1024], BF16)
        yb = sb("d_yb", [128, 1024], F32)
        pbt = ps("d_pbt", [128, 2 * 1024], BF16)
        pbf = ps("d_pbf", [128, 2 * 512])
        psHM = ps("d_psHM", [128, 512])
        psX = ps("d_psX", [128, 64])
        psSI = ps("d_psSI", [128, 2 * 512])
        psH = psHM
        psL = psX
        psM = psHM
        psD = psX

        for t_ in (alphaT, sid_b, scb, attnSb, attnTb, vs, yhat, sv):
            p.v(lambda e, t_=t_: e.memset(t_[64:128, :], 0.0))
        p.dg(lambda e: e.dma_start(sIb[:], dr["f_sI"]))
        for c in range(8):
            p.dg(lambda e, c=c: e.dma_start(sIT[:, 64 * c:64 * (c + 1)],
                                            dr["f_sIT"][128 * c:128 * (c + 1), :]))
            p.ds(lambda e, c=c: e.dma_start(sITf[:, 64 * c:64 * (c + 1)],
                                            dr["f_sIT"][128 * c:128 * (c + 1), :]))
        p.ds(lambda e: e.dma_start(taus[0:1, 0:1], dr["f_tau"]))
        p.barrier()
        p.s(lambda e: e.activation(taus[0:1, 1:2], taus[0:1, 0:1], AF.Abs))
        p.v(lambda e: e.tensor_scalar_add(taus[0:1, 2:3], taus[0:1, 1:2], 0.1), p.tok("act"))
        p.v(lambda e: e.reciprocal(taus[0:1, 3:4], taus[0:1, 2:3]))
        p.t(lambda e: e.matmul(pbf[:, 0:1], C["onesf"][0:1, :], taus[0:1, 3:4],
                               start=True, stop=True), p.tok("dve"))
        p.v(lambda e: e.tensor_copy(taus[:, 0:1], pbf[:, 0:1]), p.tok("pe"))
        for c in range(8):
            p.t(lambda e, c=c: e.matmul(psM[:, 0:1], sIb[:, 128 * c:128 * (c + 1)],
                                        C["onesb"][0:64, 0:1], start=True, stop=True),
                p.tok("act"))
            p.s(lambda e, c=c: e.activation(mcol[:, c:c + 1], psM[:, 0:1], AF.Identity,
                                            scale=1.0 / 64.0), p.tok("pe"))
        p.barrier()

        stA = ExitStack()
        sba = lambda n, shp, dt: stA.enter_context(nc.sbuf_tensor(n, shp, dt))
        w1hi = sba("d_w1hi", [128, 8 * 512], BF16)
        w1lo = sba("d_w1lo", [128, 8 * 512], BF16)
        w1f = sba("d_w1f", [128, 512], F32)
        w1bot = sba("d_w1bot", [128, 8 * 512], BF16)
        w2f = sba("d_w2f", [128, 4 * 64], F32)
        hT = sba("d_hT", [128, 4 * 2048], F32)
        b1raw = sba("d_b1raw", [1, 512], BF16)
        b2r = sba("d_b2r", [128, 64], F32)
        logit = sba("d_logit", [128, 64], F32)
        ebuf = sba("d_ebuf", [128, 64], F32)
        mbuf = sba("d_mbuf", [128, 64], F32)
        mx8 = sba("d_mx8", [128, 8], F32)
        for c in range(8):
            p.dg(lambda e, c=c: e.dma_start(w1bot[:, 512 * c:512 * (c + 1)],
                                            dr["f_w1b"][128 * c:128 * (c + 1), :]))
        p.dg(lambda e: e.dma_start(b1raw[:], dr["f_b1"]))
        mzb = p.v(lambda e: e.memset(b2r[:], 0.0))
        p.ds(lambda e: e.dma_start(b2r[0:1, :], dr["f_b2"]), [mzb])
        for c in range(4):
            p.ds(lambda e, c=c: e.dma_start(w2f[:, 64 * c:64 * (c + 1)],
                                            dr["f_w2"][128 * c:128 * (c + 1), :]))
        for c in range(8):
            dt = p.ds(lambda e, c=c: e.dma_start(w1f[:], dr["f_w1t"][128 * c:128 * (c + 1), :]),
                      p.tok("dve"))
            p.v(lambda e, c=c: e.tensor_copy(w1hi[:, 512 * c:512 * (c + 1)], w1f[:]), [dt])
            p.v(lambda e, c=c: e.tensor_tensor(w1lo[:, 512 * c:512 * (c + 1)], w1f[:],
                                               w1hi[:, 512 * c:512 * (c + 1)], ALU.subtract))
        p.barrier()
        for c in range(8):
            p.t(lambda e, c=c: e.matmul(psM[0:1, 0:512], mcol[:, c:c + 1],
                                        w1bot[:, 512 * c:512 * (c + 1)],
                                        start=(c == 0), stop=False))
        p.t(lambda e: e.matmul(psM[0:1, 0:512], C["onesb"][0:1, 0:1], b1raw[0:1, :],
                               start=False, stop=True))
        mzp = p.v(lambda e: e.memset(b1p[:], 0.0))
        p.s(lambda e: e.activation(b1p[0:1, :], psM[0:1, 0:512], AF.Identity),
            list(p.tok("pe")) + [mzp])
        p.barrier()

        stA2 = ExitStack()
        uThi = stA2.enter_context(nc.sbuf_tensor("d_uThi", [128, 8 * 2048], BF16))
        uTlo = stA2.enter_context(nc.sbuf_tensor("d_uTlo", [128, 8 * 2048], BF16))
        dma_toks = {}
        last_lo = None
        last_hi_tr = None
        for t in (0, 1):
            b = t % 2
            dma_toks[t] = p.ds(lambda e, t=t, b=b: e.dma_start(
                xt[:, 1024 * b:1024 * (b + 1)], x_src[128 * t:128 * (t + 1), :]))
        for t in range(NT):
            b = t % 2
            xin = xt[:, 1024 * b:1024 * (b + 1)]
            w = [dma_toks[t]]
            if last_lo:
                w.append(last_lo)
            ntok = tile_ln(p, C, xin, t, uf[:], wait=w)
            if t + 2 < NT:
                dma_toks[t + 2] = p.ds(lambda e, t=t, b=b: e.dma_start(
                    xt[:, 1024 * b:1024 * (b + 1)],
                    x_src[128 * (t + 2):128 * (t + 3), :]), [ntok])
            hitok = p.v(lambda e: e.tensor_copy(uhi[:], uf[:]),
                        [ntok] + ([last_hi_tr] if last_hi_tr else []))
            lotok = p.v(lambda e: e.tensor_tensor(ulo[:], uf[:], uhi[:], ALU.subtract))
            for c in range(8):
                p.t(lambda e, c=c: e.transpose(pbt[:, 128 * c:128 * (c + 1)],
                                               uhi[:, 128 * c:128 * (c + 1)], C["eyeb"][:]),
                    [hitok] if c == 0 else ())
            last_hi_tr = p.tok("pe")[0]
            p.v(lambda e, t=t: e.tensor_copy(
                uThi.rearrange("p (c n) -> p c n", n=2048)[:, :, 128 * t:128 * (t + 1)],
                pbt[:, 0:1024].rearrange("p (c n) -> p c n", n=128)), p.tok("pe"))
            for c in range(8):
                p.t(lambda e, c=c: e.transpose(pbt[:, 1024 + 128 * c:1024 + 128 * (c + 1)],
                                               ulo[:, 128 * c:128 * (c + 1)], C["eyeb"][:]),
                    [lotok] if c == 0 else ())
            last_lo = p.v(lambda e, t=t: e.tensor_copy(
                uTlo.rearrange("p (c n) -> p c n", n=2048)[:, :, 128 * t:128 * (t + 1)],
                pbt[:, 1024:2048].rearrange("p (c n) -> p c n", n=128)), p.tok("pe"))
        p.barrier()

        lastg = None
        for fc in range(4):
            for n in range(4):
                hw = [lastg] if lastg else []
                first = True
                for c in range(8):
                    for wsb, usb, st_flag in ((w1hi, uThi, True), (w1lo, uThi, False),
                                              (w1hi, uTlo, False)):
                        p.t(lambda e, fc=fc, n=n, c=c, wsb=wsb, usb=usb, st_flag=st_flag: e.matmul(
                            psH[:], wsb[:, 512 * c + 128 * fc:512 * c + 128 * (fc + 1)],
                            usb[:, 2048 * c + 512 * n:2048 * c + 512 * (n + 1)],
                            start=(c == 0 and st_flag), stop=False), hw if first else ())
                        first = False
                p.t(lambda e, fc=fc: e.matmul(
                    psH[:], b1p[:, 128 * fc:128 * (fc + 1)], C["onesb"][:, 0:512],
                    start=False, stop=True))
                lastg = p.s(lambda e, fc=fc, n=n: e.activation(
                    hT[:, 2048 * fc + 512 * n:2048 * fc + 512 * (n + 1)], psH[:], AF.Gelu),
                    p.tok("pe"))
        p.barrier()
        lastal = None
        for t in range(NT):
            lw = list(p.tok("act", "dve"))
            for fc in range(4):
                p.t(lambda e, t=t, fc=fc: e.matmul(
                    psL[:], hT[:, 2048 * fc + 128 * t:2048 * fc + 128 * (t + 1)],
                    w2f[:, 64 * fc:64 * (fc + 1)],
                    start=(fc == 0), stop=False), lw if fc == 0 else ())
            p.t(lambda e: e.matmul(psL[:], C["onesf"][:, 0:128], b2r[:],
                                   start=False, stop=True))
            ltok = p.s(lambda e: e.activation(logit[:], psL[:], AF.Identity,
                                              scale=taus[:, 0:1]), p.tok("pe", "dve"))
            p.v(lambda e: e.max(out=mx8[:], in_=logit[:]), [ltok])
            p.v(lambda e: e.tensor_scalar_mul(C["tiny"][:, 0:1], mx8[:, 0:1], -1.0))
            etok = p.s(lambda e: e.activation(ebuf[:], logit[:], AF.Exp,
                                              bias=C["tiny"][:, 0:1]), p.tok("dve"))
            p.v(lambda e: e.tensor_single_scalar(mbuf[:], logit[:], mx8[:, 7:8], ALU.is_ge),
                [etok])
            p.v(lambda e: e.tensor_tensor(ebuf[:], ebuf[:], mbuf[:], ALU.mult))
            p.v(lambda e: e.reduce_sum(C["tiny"][:, 1:2], ebuf[:], axis=AX.X))
            p.v(lambda e: e.reciprocal(C["tiny"][:, 2:3], C["tiny"][:, 1:2]))
            altok = p.v(lambda e, t=t: e.tensor_scalar(
                alpha[:, 64 * t:64 * (t + 1)], ebuf[:], C["tiny"][:, 2:3], None, ALU.mult))
            trw = [altok]
            if lastal:
                trw.append(lastal)
            p.t(lambda e, t=t: e.transpose(pbt[0:64, 0:128], alpha[:, 64 * t:64 * (t + 1)],
                                           C["eyeb"][:]), trw)
            lastal = p.v(lambda e, t=t: e.tensor_copy(
                alphaT[0:64, 128 * t:128 * (t + 1)], pbt[0:64, 0:128]), p.tok("pe"))
        p.barrier()
        stA2.close()
        stA.close()

        p.v(lambda e: e.memset(si[:], 0.0))
        lastu = None
        for t in range(NT):
            b = t % 2
            w = list(p.tok("dve"))
            dtok = p.ds(lambda e, t=t, b=b: e.dma_start(
                xt[:, 1024 * b:1024 * (b + 1)], x_src[128 * t:128 * (t + 1), :]), w)
            xin = xt[:, 1024 * b:1024 * (b + 1)]
            uw = [dtok]
            if lastu:
                uw.append(lastu)
            utok = p.s(lambda e, xin=xin, t=t: e.activation(
                uhi[:], xin, AF.Identity, bias=C["mrs"][:, 4 * t + 3:4 * t + 4],
                scale=C["mrs"][:, 4 * t + 2:4 * t + 3]), uw)
            for half in range(2):
                p.t(lambda e, t=t, half=half: e.matmul(
                    psSI[0:64, 512 * half:512 * (half + 1)],
                    alpha[:, 64 * t:64 * (t + 1)],
                    uhi[:, 512 * half:512 * (half + 1)],
                    start=(t == 0), stop=(t == NT - 1), skip_group_check=True),
                    [utok] if half == 0 else ())
            p.t(lambda e, t=t: e.matmul(
                psD[0:64, 0:1], alpha[:, 64 * t:64 * (t + 1)], C["onesb"][:, 0:1],
                start=(t == 0), stop=(t == NT - 1), skip_group_check=True))
            lastu = p.tok("pe")[0]
        p.s(lambda e: e.activation(si[0:64, 0:512], psSI[0:64, 0:512], AF.Identity), p.tok("pe"))
        p.s(lambda e: e.activation(si[0:64, 512:1024], psSI[0:64, 512:1024], AF.Identity))
        p.s(lambda e: e.activation(si[0:64, 1024:1025], psD[0:64, 0:1], AF.Identity))
        p.ds(lambda e: e.dma_start(cio["ar_in"], si[0:64, :]), p.tok("act"))
        p.ccop(lambda e: e.collective_compute(
            "AllReduce", ALU.add, GROUPS, ins=[cio["ar_in"].opt()],
            outs=[cio["ar_out"].opt()]), p.tok("qs"))
        p.ds(lambda e: e.dma_start(si[0:64, :], cio["ar_out"]), p.tok("cc"))
        p.barrier()

        p.v(lambda e: e.tensor_scalar_add(si[0:64, 1024:1025], si[0:64, 1024:1025], 1e-8))
        p.v(lambda e: e.reciprocal(si[0:64, 1025:1026], si[0:64, 1024:1025]))
        p.v(lambda e: e.tensor_scalar(sid_b[0:64, :], si[0:64, 0:1024],
                                      si[0:64, 1025:1026], None, ALU.mult))
        for c in range(8):
            p.t(lambda e, c=c: e.transpose(pbt[:, 128 * c:128 * (c + 1)],
                                           sid_b[:, 128 * c:128 * (c + 1)],
                                           C["eyeb"][:]),
                p.tok("dve") if c == 0 else ())
        p.v(lambda e: e.tensor_copy(
            siT.rearrange("p (c n) -> p c n", n=64),
            pbt[:, 0:1024].rearrange("p (c n) -> p c n", n=128)[:, :, 0:64]), p.tok("pe"))
        giT = sb("d_giT", [128, 24 * 64], F32)
        ghT = sb("d_ghT", [128, 24 * 64], F32)
        rz = sb("d_rz", [128, 16 * 64], F32)
        gn = sb("d_gn", [128, 8 * 64], F32)
        stC1 = ExitStack()
        wih = stC1.enter_context(nc.sbuf_tensor("d_wih", [128, 8 * 3072], BF16))
        bihr = stC1.enter_context(nc.sbuf_tensor("d_bihr", [128, 3072], BF16))
        mzi = p.v(lambda e: e.memset(bihr[:], 0.0))
        for c in range(8):
            p.dg(lambda e, c=c: e.dma_start(wih[:, 3072 * c:3072 * (c + 1)],
                                            dr["f_wihT"][128 * c:128 * (c + 1), :]))
        p.dg(lambda e: e.dma_start(bihr[0:1, :], dr["f_bih"]), [mzi])
        p.barrier()
        for g in range(24):
            gw = list(p.tok("dve", "act"))
            for c in range(8):
                p.t(lambda e, g=g, c=c: e.matmul(
                    psM[:, 0:64], wih[:, 3072 * c + 128 * g:3072 * c + 128 * (g + 1)],
                    siT[:, 64 * c:64 * (c + 1)], start=(c == 0), stop=False),
                    gw if c == 0 else ())
            p.t(lambda e, g=g: e.matmul(psM[:, 0:64], bihr[:, 128 * g:128 * (g + 1)],
                                        C["onesb"][:, 0:64], start=False, stop=True))
            p.s(lambda e, g=g: e.activation(giT[:, 64 * g:64 * (g + 1)], psM[:, 0:64],
                                            AF.Identity), p.tok("pe"))
        p.barrier()
        stC1.close()
        stC2 = ExitStack()
        whh = stC2.enter_context(nc.sbuf_tensor("d_whh", [128, 8 * 3072], BF16))
        bhhr = stC2.enter_context(nc.sbuf_tensor("d_bhhr", [128, 3072], BF16))
        mzh = p.v(lambda e: e.memset(bhhr[:], 0.0))
        for c in range(8):
            p.dg(lambda e, c=c: e.dma_start(whh[:, 3072 * c:3072 * (c + 1)],
                                            dr["f_whhT"][128 * c:128 * (c + 1), :]))
        p.dg(lambda e: e.dma_start(bhhr[0:1, :], dr["f_bhh"]), [mzh])
        p.barrier()
        for g in range(24):
            gw = list(p.tok("act"))
            for c in range(8):
                p.t(lambda e, g=g, c=c: e.matmul(
                    psM[:, 0:64], whh[:, 3072 * c + 128 * g:3072 * c + 128 * (g + 1)],
                    sIT[:, 64 * c:64 * (c + 1)], start=(c == 0), stop=False),
                    gw if c == 0 else ())
            p.t(lambda e, g=g: e.matmul(psM[:, 0:64], bhhr[:, 128 * g:128 * (g + 1)],
                                        C["onesb"][:, 0:64], start=False, stop=True))
            p.s(lambda e, g=g: e.activation(ghT[:, 64 * g:64 * (g + 1)], psM[:, 0:64],
                                            AF.Identity), p.tok("pe"))
        p.barrier()
        stC2.close()
        stC3 = ExitStack()
        sbc = lambda n, shp, dt: stC3.enter_context(nc.sbuf_tensor(n, shp, dt))
        wqs = sbc("d_wqs", [128, 8 * 256], BF16)
        wks = sbc("d_wks", [128, 8 * 256], BF16)
        wvs = sbc("d_wvs", [128, 8 * 1024], BF16)
        bqr = sbc("d_bqr", [128, 256], BF16)
        bkr = sbc("d_bkr", [128, 256], BF16)
        bvr = sbc("d_bvr", [128, 1024], BF16)
        gwvp = sbc("d_gwvp", [128, 8 * 1024], BF16)
        wvpf = sbc("d_wvpf", [128, 1024], F32)
        lngc = sbc("d_lngc", [128, 8], F32)
        lnbc = sbc("d_lnbc", [128, 8], F32)
        lng8 = sbc("d_lng8", [8, 128], F32)
        lnb8 = sbc("d_lnb8", [8, 128], F32)
        cr1 = sbc("d_cr1", [1, 1024], F32)
        crb = sbc("d_crb", [128, 1024], BF16)
        bvpr = sbc("d_bvpr", [128, 1024], F32)
        for c in range(8):
            p.dg(lambda e, c=c: e.dma_start(wqs[:, 256 * c:256 * (c + 1)],
                                            dr["f_wq"][128 * c:128 * (c + 1), :]))
            p.dg(lambda e, c=c: e.dma_start(wks[:, 256 * c:256 * (c + 1)],
                                            dr["f_wk"][128 * c:128 * (c + 1), :]))
            p.dg(lambda e, c=c: e.dma_start(wvs[:, 1024 * c:1024 * (c + 1)],
                                            dr["f_wv"][128 * c:128 * (c + 1), :]))
        mzc = p.v(lambda e: e.memset(bqr[:], 0.0))
        mzc = p.v(lambda e: e.memset(bkr[:], 0.0), [mzc])
        mzc = p.v(lambda e: e.memset(bvr[:], 0.0), [mzc])
        mzc = p.v(lambda e: e.memset(crb[:], 0.0), [mzc])
        mzc = p.v(lambda e: e.memset(bvpr[:], 0.0), [mzc])
        p.dg(lambda e: e.dma_start(bqr[0:1, :], dr["f_bq"]), [mzc])
        p.dg(lambda e: e.dma_start(bkr[0:1, :], dr["f_bk"]))
        p.dg(lambda e: e.dma_start(bvr[0:1, :], dr["f_bv"]))
        p.ds(lambda e: e.dma_start(lng8[:], dr["f_lng8"]))
        p.ds(lambda e: e.dma_start(lnb8[:], dr["f_lnb8"]))
        p.ds(lambda e: e.dma_start(bvpr[0:1, :], dr["f_bvp"]), [mzc])
        p.barrier()
        p.v(lambda e: e.tensor_add(rz[:, 0:1024], giT[:, 0:1024], ghT[:, 0:1024]))
        p.s(lambda e: e.activation(rz[:, 0:1024], rz[:, 0:1024], AF.Sigmoid), p.tok("dve"))
        p.v(lambda e: e.tensor_tensor(gn[:], rz[:, 0:512], ghT[:, 1024:1536], ALU.mult),
            p.tok("act"))
        p.v(lambda e: e.tensor_add(gn[:], gn[:], giT[:, 1024:1536]))
        p.s(lambda e: e.activation(gn[:], gn[:], AF.Tanh), p.tok("dve"))
        p.v(lambda e: e.tensor_tensor(SnT[:], sITf[:], gn[:], ALU.subtract), p.tok("act"))
        p.v(lambda e: e.tensor_tensor(SnT[:], SnT[:], rz[:, 512:1024], ALU.mult))
        p.v(lambda e: e.tensor_add(SnT[:], SnT[:], gn[:]))
        p.v(lambda e: e.tensor_copy(SnTb[:], SnT[:]))
        for dst, wsb, bsb in ((qT, wqs, bqr), (kT, wks, bkr)):
            for f2 in range(2):
                gw = list(p.tok("dve", "act"))
                for c in range(8):
                    p.t(lambda e, f2=f2, c=c, wsb=wsb: e.matmul(
                        psM[:, 0:64], wsb[:, 256 * c + 128 * f2:256 * c + 128 * (f2 + 1)],
                        SnTb[:, 64 * c:64 * (c + 1)], start=(c == 0), stop=False),
                        gw if c == 0 else ())
                p.t(lambda e, f2=f2, bsb=bsb: e.matmul(
                    psM[:, 0:64], bsb[:, 128 * f2:128 * (f2 + 1)], C["onesb"][:, 0:64],
                    start=False, stop=True))
                p.s(lambda e, f2=f2, dst=dst: e.activation(
                    dst[:, 64 * f2:64 * (f2 + 1)], psM[:, 0:64], AF.Identity), p.tok("pe"))
        saw = list(p.tok("act"))
        for f2 in range(2):
            p.t(lambda e, f2=f2: e.matmul(
                psM[0:64, 0:64], kT[:, 64 * f2:64 * (f2 + 1)], qT[:, 64 * f2:64 * (f2 + 1)],
                start=(f2 == 0), stop=(f2 == 1)), saw if f2 == 0 else ())
        p.s(lambda e: e.activation(scb[0:64, :], psM[0:64, 0:64], AF.Identity), p.tok("pe"))
        p.t(lambda e: e.transpose(pbt[0:64, 0:128], scb[:], C["eyeb"][:]), p.tok("act"))
        p.v(lambda e: e.tensor_copy(scq[:], pbt[0:64, 0:64]), p.tok("pe"))
        p.v(lambda e: e.reduce_max(C["tiny"][0:64, 0:1], scq[:], axis=AX.X, negate=True))
        p.v(lambda e: e.tensor_scalar_mul(C["tiny"][0:64, 1:2], C["tiny"][0:64, 0:1], 0.0625))
        p.s(lambda e: e.activation(scq[:], scq[:], AF.Exp, scale=0.0625,
                                   bias=C["tiny"][0:64, 1:2]), p.tok("dve"))
        p.v(lambda e: e.reduce_sum(C["tiny"][0:64, 2:3], scq[:], axis=AX.X), p.tok("act"))
        p.v(lambda e: e.reciprocal(C["tiny"][0:64, 3:4], C["tiny"][0:64, 2:3]))
        p.v(lambda e: e.tensor_scalar(attnSb[0:64, :], scq[:], C["tiny"][0:64, 3:4],
                                      None, ALU.mult))
        p.t(lambda e: e.transpose(pbt[0:64, 0:128], attnSb[:], C["eyeb"][:]),
            p.tok("dve"))
        p.v(lambda e: e.tensor_copy(attnTb[0:64, :], pbt[0:64, 0:64]), p.tok("pe"))
        for n in range(2):
            gw = list(p.tok("dve", "act"))
            for c in range(8):
                p.t(lambda e, n=n, c=c: e.matmul(
                    psM[0:64, 0:512], SnTb[:, 64 * c:64 * (c + 1)],
                    wvs[:, 1024 * c + 512 * n:1024 * c + 512 * (n + 1)],
                    start=(c == 0), stop=False), gw if c == 0 else ())
            p.t(lambda e, n=n: e.matmul(psM[0:64, 0:512], C["onesb"][:, 0:64],
                                        bvr[:, 512 * n:512 * (n + 1)],
                                        start=False, stop=True))
            p.s(lambda e, n=n: e.activation(vs[0:64, 512 * n:512 * (n + 1)],
                                            psM[0:64, 0:512], AF.Identity), p.tok("pe"))
        for c in range(8):
            p.t(lambda e, c=c: e.matmul(
                psM[:, 0:64], vs[:, 128 * c:128 * (c + 1)], attnTb[:, 0:64],
                start=True, stop=True), p.tok("act", "dve"))
            p.v(lambda e, c=c: e.tensor_tensor(ySlT[:, 64 * c:64 * (c + 1)],
                                               SnT[:, 64 * c:64 * (c + 1)], psM[:, 0:64],
                                               ALU.add), p.tok("pe"))
        for c in range(8):
            bk = c % 2
            tw = list(p.tok("dve"))
            p.t(lambda e, c=c, bk=bk: e.transpose(
                pbf[0:64, 512 * bk:512 * bk + 128], ySlT[:, 64 * c:64 * (c + 1)],
                C["eyef"][:]), tw)
            p.v(lambda e, c=c, bk=bk: e.tensor_copy(
                ySl[:, 128 * c:128 * (c + 1)], pbf[0:64, 512 * bk:512 * bk + 128]), p.tok("pe"))
        st_, tmp = C["stats"], C["tiny"]
        p.v(lambda e: e.bn_stats(st_[0:64, 0:6], ySl[:, 0:512]), p.tok("act"))
        p.v(lambda e: e.bn_stats(st_[0:64, 6:12], ySl[:, 512:1024]))
        p.v(lambda e: e.bn_aggr(tmp[0:64, 8:10], st_[0:64, 0:12]))
        p.v(lambda e: e.tensor_scalar_add(tmp[0:64, 0:1], tmp[0:64, 9:10], 1e-5))
        p.s(lambda e: e.activation(tmp[0:64, 1:2], tmp[0:64, 0:1], AF.Ln), p.tok("dve"))
        p.s(lambda e: e.activation(tmp[0:64, 2:3], tmp[0:64, 1:2], AF.Exp, scale=-0.5))
        p.v(lambda e: e.tensor_tensor(tmp[0:64, 3:4], tmp[0:64, 8:9], tmp[0:64, 2:3], ALU.mult),
            p.tok("act"))
        p.v(lambda e: e.tensor_scalar_mul(tmp[0:64, 4:5], tmp[0:64, 3:4], -1.0))
        p.s(lambda e: e.activation(yhat[0:64, :], ySl[:], AF.Identity, bias=tmp[0:64, 4:5],
                                   scale=tmp[0:64, 2:3]), p.tok("dve"))
        p.t(lambda e: e.matmul(pbf[:, 0:8], lng8[:], C["eyef"][0:8, 0:8], start=True, stop=True),
            p.tok("dve"))
        p.v(lambda e: e.tensor_copy(lngc[:], pbf[:, 0:8]), p.tok("pe"))
        p.t(lambda e: e.matmul(pbf[:, 8:16], lnb8[:], C["eyef"][0:8, 0:8], start=True, stop=True),
            p.tok("dve"))
        p.v(lambda e: e.tensor_copy(lnbc[:], pbf[:, 8:16]), p.tok("pe"))
        for c in range(8):
            dt = p.ds(lambda e, c=c: e.dma_start(wvpf[:], dr["f_wvp"][128 * c:128 * (c + 1), :]),
                      p.tok("dve", "pe"))
            p.v(lambda e, c=c: e.tensor_scalar(gwvp[:, 1024 * c:1024 * (c + 1)], wvpf[:],
                                               lngc[:, c:c + 1], None, ALU.mult), [dt])
            for half in range(2):
                p.t(lambda e, c=c, half=half: e.matmul(
                    pbf[0:1, 512 * half:512 * half + 512], lnbc[:, c:c + 1],
                    wvpf[:, 512 * half:512 * (half + 1)],
                    start=(c == 0), stop=False, skip_group_check=True),
                    p.tok("dve") if half == 0 else ())
        for half in range(2):
            p.t(lambda e, half=half: e.matmul(
                pbf[0:1, 512 * half:512 * half + 512], C["onesf"][:, 0:1],
                bvpr[:, 512 * half:512 * (half + 1)], start=False, stop=True,
                skip_group_check=True))
        p.v(lambda e: e.tensor_copy(cr1[0:1, 0:512], pbf[0:1, 0:512]), p.tok("pe"))
        p.v(lambda e: e.tensor_copy(cr1[0:1, 512:1024], pbf[0:1, 512:1024]))
        p.v(lambda e: e.tensor_copy(crb[0:1, :], cr1[:]))
        for c in range(8):
            p.t(lambda e, c=c: e.transpose(pbt[:, 128 * c:128 * (c + 1)],
                                           yhat[:, 128 * c:128 * (c + 1)],
                                           C["eyeb"][:]),
                p.tok("act", "dve") if c == 0 else ())
        p.v(lambda e: e.tensor_copy(
            yhT.rearrange("p (c n) -> p c n", n=64),
            pbt[:, 0:1024].rearrange("p (c n) -> p c n", n=128)[:, :, 0:64]), p.tok("pe"))
        for n in range(2):
            gw = list(p.tok("dve", "act"))
            for c in range(8):
                p.t(lambda e, n=n, c=c: e.matmul(
                    psM[0:64, 0:512], yhT[:, 64 * c:64 * (c + 1)],
                    gwvp[:, 1024 * c + 512 * n:1024 * c + 512 * (n + 1)],
                    start=(c == 0), stop=False), gw if c == 0 else ())
            p.t(lambda e, n=n: e.matmul(psM[0:64, 0:512], C["onesb"][:, 0:64],
                                        crb[:, 512 * n:512 * (n + 1)],
                                        start=False, stop=True))
            p.s(lambda e, n=n: e.activation(sv[0:64, 512 * n:512 * (n + 1)],
                                            psM[0:64, 0:512], AF.Identity), p.tok("pe"))
        p.barrier()
        stC3.close()

        stD = ExitStack()
        outT = stD.enter_context(nc.sbuf_tensor("d_outT", [128, 8 * 2048], BF16))
        wo2 = stD.enter_context(nc.sbuf_tensor("d_wo2", [128, 8 * 1024], BF16))
        bor = stD.enter_context(nc.sbuf_tensor("d_bor", [128, 1024], BF16))
        mzo = p.v(lambda e: e.memset(bor[:], 0.0))
        for c in range(8):
            p.dg(lambda e, c=c: e.dma_start(wo2[:, 1024 * c:1024 * (c + 1)],
                                            dr["f_wo"][128 * c:128 * (c + 1), :]))
        p.dg(lambda e: e.dma_start(bor[0:1, :], dr["f_bo"]), [mzo])
        p.barrier()
        for c in range(8):
            for n in range(4):
                w = list(p.tok("act"))
                p.t(lambda e, c=c, n=n: e.matmul(
                    psSI[:, 0:512], sv[:, 128 * c:128 * (c + 1)],
                    alphaT[:, 512 * n:512 * (n + 1)], start=True, stop=True), w)
                p.s(lambda e, c=c, n=n: e.activation(
                    outT[:, 2048 * c + 512 * n:2048 * c + 512 * (n + 1)], psSI[:, 0:512],
                    AF.Identity), p.tok("pe"))
        p.barrier()
        for t in range(NT):
            b = t % 2
            mw = list(p.tok("act", "dve"))
            for half in range(2):
                for c in range(8):
                    p.t(lambda e, t=t, half=half, c=c: e.matmul(
                        psSI[:, 512 * half:512 * (half + 1)],
                        outT[:, 2048 * c + 128 * t:2048 * c + 128 * (t + 1)],
                        wo2[:, 1024 * c + 512 * half:1024 * c + 512 * (half + 1)],
                        start=(c == 0), stop=False), mw if (half == 0 and c == 0) else ())
                p.t(lambda e, half=half: e.matmul(
                    psSI[:, 512 * half:512 * (half + 1)], C["onesb"][:, 0:128],
                    bor[:, 512 * half:512 * (half + 1)], start=False, stop=True))
            xdt = p.ds(lambda e, t=t, b=b: e.dma_start(
                xt[:, 1024 * b:1024 * (b + 1)], x_src[128 * t:128 * (t + 1), :]),
                p.tok("dve"))
            xin = xt[:, 1024 * b:1024 * (b + 1)]
            ln_resid_store(p, C, "psum2", psSI[:, 0:512], psSI[:, 512:1024], xin, yb,
                           x_dst[128 * t:128 * (t + 1), :],
                           extra_wait=list(p.tok("pe")) + [xdt])
        p.barrier()
        stD.close()


def _build(blocks, fused):
    nc = bass.Bass("TRN2", target_bir_lowering=False, detect_race_conditions=False)
    dr = {}
    specs = {
        "eye": [128, 128], "l0_masks": [768, 512], "l1_masks": [512, 512],
        "x0h": [2304, 1024], "padneg": [1, 2304],
        "l0_wq": [1024, 1024], "l0_wk": [1024, 1024], "l0_wv": [1024, 1024],
        "l0_wo": [1024, 1024], "l0_brow": [4, 1024],
        "mod_w1": [1024, 4096], "mod_w2": [4096, 1024], "mod_b1c": [128, 32],
        "mod_b2c": [128, 8], "mod_wr8": [8, 128],
        "l1_wq": [1024, 256], "l1_wk": [1024, 256], "l1_wv": [1024, 256],
        "l1_wo": [256, 1024],
        "f_sIT": [1024, 64], "f_sI": [64, 1024], "f_w1t": [1024, 512],
        "f_w1b": [1024, 512], "f_b1": [1, 512], "f_w2": [512, 64], "f_b2": [1, 64],
        "f_tau": [1, 1], "f_wihT": [1024, 3072], "f_whhT": [1024, 3072],
        "f_bih": [1, 3072], "f_bhh": [1, 3072], "f_wq": [1024, 256], "f_bq": [1, 256],
        "f_wk": [1024, 256], "f_bk": [1, 256], "f_wv": [1024, 1024], "f_bv": [1, 1024],
        "f_lng8": [8, 128], "f_lnb8": [8, 128], "f_wvp": [1024, 1024],
        "f_bvp": [1, 1024], "f_wo": [1024, 1024], "f_bo": [1, 1024],
    }
    for nm, shp in specs.items():
        dr[nm] = nc.dram_tensor(nm, shp, F32, kind="ExternalInput").ap()
    out = nc.dram_tensor("xout", [2048, 1024], F32, kind="ExternalOutput").ap()
    dbg = nc.dram_tensor("dbg", [128, 8192], F32, kind="ExternalOutput").ap() if DBG else None
    xs = {}
    if fused:
        for nm in ("x1", "x2", "x3"):
            xs[nm] = nc.dram_tensor(nm, [2048, 1024], F32, kind="Internal").ap()
    else:
        if blocks[0] != "l0a":
            xs["xin"] = nc.dram_tensor("xin", [2048, 1024], F32, kind="ExternalInput").ap()
    cio = {
        "sc_in": nc.dram_tensor("sc_in", [1, 2048], F32, kind="Internal").ap(),
        "sc_out": nc.dram_tensor("sc_out", [4, 2048], F32, kind="Internal").ap(),
        "ag_in": nc.dram_tensor("ag_in", [1024, 2048], BF16, kind="Internal").ap(),
        "ag_out": nc.dram_tensor("ag_out", [4096, 2048], BF16, kind="Internal").ap(),
        "rs_in": nc.dram_tensor("rs_in", [8192, 1024], BF16, kind="Internal").ap(),
        "rs_out": nc.dram_tensor("rs_out", [2048, 1024], BF16, kind="Internal").ap(),
        "ar_in": nc.dram_tensor("ar_in", [64, 1028], F32, kind="Internal").ap(),
        "ar_out": nc.dram_tensor("ar_out", [64, 1028], F32, kind="Internal").ap(),
        "o2_dram": nc.dram_tensor("o2_dram", [2048, 1024], BF16, kind="Internal").ap(),
    }
    with ExitStack() as st:
        p = P(nc, st)
        block = st.enter_context(nc.Block())
        C = setup_consts(p, st, nc, dr)
        chain = {"l0a": ("x0h", "x1"), "l0f": ("x1", "x2"),
                 "l1a": ("x2", "x3"), "l1f": ("x3", "out")}
        for i, bname in enumerate(blocks):
            srcn, dstn = chain[bname]
            if fused:
                src = dr["x0h"] if srcn == "x0h" else xs[srcn]
                dst = out if (i == len(blocks) - 1) else xs[dstn]
            else:
                src = dr["x0h"] if bname == "l0a" else xs["xin"]
                dst = out
            if bname == "l0a":
                phase_l0a(p, C, nc, dr, src, dst, dbg=dbg)
            elif bname == "l0f":
                phase_l0f(p, C, nc, dr, src, dst, cio)
            elif bname == "l1a":
                phase_l1a(p, C, nc, dr, src, dst, cio)
            elif bname == "l1f":
                phase_l1f(p, C, nc, dr, src, dst, cio)
        p.barrier()
        p.emit(block)
    mybir.codegen_inst_isa_subclasses(nc)
    return nc


def _run(in_maps, blocks, fused):
    if fused:
        nc = _build(blocks, True)
        res = run_bass_kernel_spmd(nc, in_maps, core_ids=list(range(8)))
        return [np.asarray(r["xout"]) for r in res.results]
    cur = None
    for bname in blocks:
        nc = _build((bname,), False)
        maps = []
        for c in range(8):
            m = dict(in_maps[c])
            if bname != "l0a":
                m["xin"] = cur[c]
            maps.append(m)
        res = run_bass_kernel_spmd(nc, maps, core_ids=list(range(8)))
        cur = [np.asarray(r["xout"]) for r in res.results]
    return cur


def kernel(**inputs):
    in_maps = _prep(inputs)
    shards = _run(in_maps, ALL_BLOCKS, FUSED)
    out = np.zeros((NB, SEQ, DM), np.float32)
    for c in range(8):
        b, j = c // 4, c % 4
        out[b, j * TOK:(j + 1) * TOK] = shards[c]
    return out


# version 08
# speedup vs baseline: 1.7839x; 1.0077x over previous
import numpy as np
import ml_dtypes
from contextlib import ExitStack

import concourse.bass as bass
import concourse.mybir as mybir
from concourse.bass_isa import ReduceOp
from concourse.bass_utils import run_bass_kernel_spmd

F32 = mybir.dt.float32
BF16 = mybir.dt.bfloat16
AF = mybir.ActivationFunctionType
ALU = mybir.AluOpType
AX = mybir.AxisListType

NB, SEQ, DM = 2, 8192, 1024
TOK = 2048
NT = 16
NTH = 18
GROUPS = [[0, 1, 2, 3], [4, 5, 6, 7]]
FUSED = True
DBG = False
ALL_BLOCKS = ("l0a", "l0f", "l1a", "l1f")

ENGS = ("tensor", "vector", "scalar", "gpsimd", "sync")
_OWN = {"tensor": "pe", "vector": "dve", "scalar": "act", "gpsimd": "pool", "sync": None}


class P:
    def __init__(self, nc, st):
        self.nc = nc
        self.ops = {e: [] for e in ENGS}
        self.sems = {}
        self.cnt = {}
        for s in ("pe", "dve", "act", "pool", "qs", "qg", "cc"):
            self.sems[s] = st.enter_context(nc.semaphore(s))
            self.cnt[s] = 0

    def _op(self, e, fn, waits, sig):
        self.ops[e].append((fn, tuple(waits or ()), sig))
        if sig:
            self.cnt[sig[0]] += sig[1]
            return (sig[0], self.cnt[sig[0]])
        return None

    def t(self, fn, waits=()):  return self._op("tensor", fn, waits, ("pe", 1))
    def v(self, fn, waits=()):  return self._op("vector", fn, waits, ("dve", 1))
    def s(self, fn, waits=()):  return self._op("scalar", fn, waits, ("act", 1))
    def g(self, fn, waits=()):  return self._op("gpsimd", fn, waits, ("pool", 1))
    def ds(self, fn, waits=()): return self._op("sync", fn, waits, ("qs", 16))
    def dg(self, fn, waits=()): return self._op("gpsimd", fn, waits, ("qg", 16))
    def ccop(self, fn, waits=()): return self._op("gpsimd", fn, waits, ("cc", 1))

    def tok(self, *sems):
        return tuple((s, self.cnt[s]) for s in sems)

    def barrier(self):
        snap = tuple((s, self.cnt[s]) for s in self.cnt)
        for e in ENGS:
            self._op(e, None, snap, None)

    def emit(self, block):
        def mk(e):
            own = _OWN[e]
            serialize = e in ("vector", "scalar", "gpsimd")
            def run(eng):
                waited = {}
                cum = {}
                for fn, waits, sig in self.ops[e]:
                    allw = list(waits)
                    if serialize and sig and sig[0] == own and cum.get(own, 0) > 0:
                        allw.append((own, cum[own]))
                    for (s, v) in allw:
                        if v <= 0:
                            continue
                        if waited.get(s, -1) >= v:
                            continue
                        eng.wait_ge(self.sems[s], v)
                        waited[s] = v
                    if fn is not None:
                        inst = fn(eng)
                        if sig:
                            inst.then_inc(self.sems[sig[0]], sig[1])
                    if sig:
                        cum[sig[0]] = cum.get(sig[0], 0) + sig[1]
            return run
        block.tensor(mk("tensor"))
        block.vector(mk("vector"))
        block.scalar(mk("scalar"))
        block.gpsimd(mk("gpsimd"))
        block.sync(mk("sync"))


def _win_masks():
    ms = np.zeros((6, 128, 512), np.float32)
    k = np.arange(128)[:, None]
    q = np.arange(512)[None, :]
    for kt in range(6):
        ms[kt] = (np.abs(k - q + 128 * (kt - 1)) <= 128).astype(np.float32)
    return ms.reshape(768, 512)


def _causal_masks():
    ms = np.zeros((4, 128, 512), np.float32)
    k = np.arange(128)[:, None]
    q = np.arange(512)[None, :]
    for d in range(4):
        ms[d] = ((k + 128 * d) <= q).astype(np.float32)
    return ms.reshape(512, 512)


def _prep(inp):
    inp = {k: np.asarray(v) for k, v in inp.items()}
    x = inp["L0_attn_x"].astype(np.float32, copy=False)
    C = np.ascontiguousarray
    shared = {
        "eye": np.eye(128, dtype=np.float32),
        "l0_masks": _win_masks(),
        "l1_masks": _causal_masks(),
        "l0_wq": C(inp["L0_attn_wq"]), "l0_wk": C(inp["L0_attn_wk"]),
        "l0_wv": C(inp["L0_attn_wv"]), "l0_wo": C(inp["L0_attn_wo"]),
        "l0_brow": C(np.stack([inp["L0_attn_bq"], inp["L0_attn_bk"],
                               inp["L0_attn_bv"], inp["L0_attn_bo"]])),
        "mod_w1": C(inp["L0_ffn_W1"]), "mod_w2": C(inp["L0_ffn_W2"]),
        "mod_b1c": C(np.asarray(inp["L0_ffn_b1"]).reshape(32, 128).T), "mod_b2c": C(np.asarray(inp["L0_ffn_b2"]).reshape(8, 128).T),
        "mod_wr8": C(inp["L0_ffn_w_router"].reshape(8, 128)),
        "f_sIT": C(inp["L1_ffn_slot_init"].T), "f_sI": C(inp["L1_ffn_slot_init"]),
        "f_w1t": C(inp["L1_ffn_w1"][0:1024]), "f_w1b": C(inp["L1_ffn_w1"][1024:2048]),
        "f_b1": C(inp["L1_ffn_b1"][None, :]), "f_w2": C(inp["L1_ffn_w2"]),
        "f_b2": C(inp["L1_ffn_b2"][None, :]), "f_tau": C(inp["L1_ffn_tau"].reshape(1, 1)),
        "f_wihT": C(inp["L1_ffn_Wih"].T), "f_whhT": C(inp["L1_ffn_Whh"].T),
        "f_bih": C(inp["L1_ffn_bih"][None, :]), "f_bhh": C(inp["L1_ffn_bhh"][None, :]),
        "f_wq": C(inp["L1_ffn_wq"]), "f_bq": C(inp["L1_ffn_bq"][None, :]),
        "f_wk": C(inp["L1_ffn_wk"]), "f_bk": C(inp["L1_ffn_bk"][None, :]),
        "f_wv": C(inp["L1_ffn_wv"]), "f_bv": C(inp["L1_ffn_bv"][None, :]),
        "f_lng8": C(inp["L1_ffn_ln_g"].reshape(8, 128)),
        "f_lnb8": C(inp["L1_ffn_ln_b"].reshape(8, 128)),
        "f_wvp": C(inp["L1_ffn_wvp"]), "f_bvp": C(inp["L1_ffn_bvp"][None, :]),
        "f_wo": C(inp["L1_ffn_wo"]), "f_bo": C(inp["L1_ffn_bo"][None, :]),
    }
    qp, kp, vp, op = (inp["L1_attn_q_proj"], inp["L1_attn_k_proj"],
                      inp["L1_attn_v_proj"], inp["L1_attn_o_proj"])
    in_maps = []
    for c in range(8):
        b, j = c // 4, c % 4
        xh = np.zeros((2304, DM), np.float32)
        xh[128:2176] = x[b, j * TOK:(j + 1) * TOK]
        if j > 0:
            xh[0:128] = x[b, j * TOK - 128:j * TOK]
        if j < 3:
            xh[2176:2304] = x[b, (j + 1) * TOK:(j + 1) * TOK + 128]
        padneg = np.zeros((1, 2304), np.float32)
        if j == 0:
            padneg[0, 0:128] = -1.0
        if j == 3:
            padneg[0, 2176:2304] = -1.0
        m = dict(shared)
        m["x0h"] = xh
        m["padneg"] = padneg
        hs = slice(256 * j, 256 * j + 256)
        m["l1_wq"] = C(qp[hs, :].T)
        m["l1_wk"] = C(kp[hs, :].T)
        m["l1_wv"] = C(vp[hs, :].T)
        m["l1_wo"] = C(op[:, hs].T)
        in_maps.append(m)
    return in_maps


def setup_consts(p, st, nc, dr):
    C = {}
    C["eyef"] = st.enter_context(nc.sbuf_tensor("eyef", [128, 128], F32))
    C["eyeb"] = st.enter_context(nc.sbuf_tensor("eyeb", [128, 128], BF16))
    C["onesb"] = st.enter_context(nc.sbuf_tensor("onesb", [128, 2304], BF16))
    C["onesf"] = st.enter_context(nc.sbuf_tensor("onesf", [128, 128], F32))
    C["tiny"] = st.enter_context(nc.sbuf_tensor("tiny", [128, 64], F32))
    C["stats"] = st.enter_context(nc.sbuf_tensor("stats", [128, 14], F32))
    C["mrs"] = st.enter_context(nc.sbuf_tensor("mrs", [128, 4 * NTH], F32))
    p.ds(lambda e: e.dma_start(C["eyef"][:], dr["eye"]))
    p.dg(lambda e: e.dma_start(C["eyeb"][:], dr["eye"]))
    p.v(lambda e: e.memset(C["onesb"][:], 1.0))
    p.v(lambda e: e.memset(C["onesf"][:], 1.0))
    p.barrier()
    return C


def tile_ln(p, C, xin_ap, t, u_dst, wait=()):
    st_, mrs, tmp = C["stats"], C["mrs"], C["tiny"]
    tk = p.v(lambda e: e.bn_stats(st_[:, 0:6], xin_ap[:, 0:512]), tuple(wait) + p.tok("act"))
    tk = p.v(lambda e: e.bn_stats(st_[:, 6:12], xin_ap[:, 512:1024]), [tk])
    tk = p.v(lambda e: e.bn_aggr(mrs[:, 4 * t:4 * t + 2], st_[:, 0:12]), [tk])
    tk = p.v(lambda e, t=t: e.tensor_scalar_add(tmp[:, 0:1], mrs[:, 4 * t + 1:4 * t + 2], 1e-5),
             [tk])
    tk = p.s(lambda e: e.activation(tmp[:, 1:2], tmp[:, 0:1], AF.Ln), [tk])
    tk = p.s(lambda e, t=t: e.activation(mrs[:, 4 * t + 2:4 * t + 3], tmp[:, 1:2], AF.Exp,
                                         scale=-0.5), [tk])
    tk = p.v(lambda e, t=t: e.tensor_tensor(tmp[:, 2:3], mrs[:, 4 * t:4 * t + 1],
                                            mrs[:, 4 * t + 2:4 * t + 3], ALU.mult), [tk])
    tk = p.v(lambda e, t=t: e.tensor_scalar_mul(mrs[:, 4 * t + 3:4 * t + 4], tmp[:, 2:3], -1.0),
             [tk])
    return p.s(lambda e, t=t, xin_ap=xin_ap, u_dst=u_dst: e.activation(
        u_dst, xin_ap, AF.Identity, bias=mrs[:, 4 * t + 3:4 * t + 4],
        scale=mrs[:, 4 * t + 2:4 * t + 3]), [tk])


def ln_resid_store(p, C, src_kind, src0, src1, xin_ap, yb, x_dst_tile, extra_wait=()):
    st_, tmp = C["stats"], C["tiny"]
    if src_kind == "psum2":
        a0, a1 = src0, src1
    else:
        a0, a1 = src0[:, 0:512], src0[:, 512:1024]
    p.v(lambda e, a0=a0: e.bn_stats(st_[:, 0:6], a0), tuple(extra_wait) + p.tok("act"))
    p.v(lambda e, a1=a1: e.bn_stats(st_[:, 6:12], a1))
    p.v(lambda e: e.bn_aggr(tmp[:, 8:10], st_[:, 0:12]))
    p.v(lambda e: e.tensor_scalar_add(tmp[:, 0:1], tmp[:, 9:10], 1e-5))
    p.s(lambda e: e.activation(tmp[:, 1:2], tmp[:, 0:1], AF.Ln), p.tok("dve"))
    p.s(lambda e: e.activation(tmp[:, 2:3], tmp[:, 1:2], AF.Exp, scale=-0.5))
    p.v(lambda e: e.tensor_tensor(tmp[:, 3:4], tmp[:, 8:9], tmp[:, 2:3], ALU.mult), p.tok("act"))
    p.v(lambda e: e.tensor_scalar_mul(tmp[:, 4:5], tmp[:, 3:4], -1.0))
    p.s(lambda e, a0=a0, yb=yb: e.activation(yb[:, 0:512], a0, AF.Identity,
                                             bias=tmp[:, 4:5], scale=tmp[:, 2:3]), p.tok("dve", "qs"))
    lntok = p.s(lambda e, a1=a1, yb=yb: e.activation(yb[:, 512:1024], a1, AF.Identity,
                                                     bias=tmp[:, 4:5], scale=tmp[:, 2:3]))
    p.v(lambda e, yb=yb, xin_ap=xin_ap: e.tensor_add(yb[:], yb[:], xin_ap), [lntok])
    return p.ds(lambda e, x_dst_tile=x_dst_tile, yb=yb: e.dma_start(x_dst_tile, yb[:]),
                p.tok("dve"))


def ln_resid_pipe(p, C, nc, st, prefix, n_tiles, make_src, x_src, x_dst, nslot=3):
    stt = st.enter_context(nc.sbuf_tensor(prefix + "_pst", [128, 12 * nslot], F32))
    tmp = st.enter_context(nc.sbuf_tensor(prefix + "_ptp", [128, 8 * nslot], F32))
    xts = st.enter_context(nc.sbuf_tensor(prefix + "_pxt", [128, 1024 * nslot], F32))
    ybs = st.enter_context(nc.sbuf_tensor(prefix + "_pyb", [128, 1024 * nslot], F32))
    slot_free = {}
    for t in range(n_tiles):
        s = t % nslot
        wfree = [slot_free[s]] if s in slot_free else []
        xin = xts[:, 1024 * s:1024 * (s + 1)]
        ybx = ybs[:, 1024 * s:1024 * (s + 1)]
        xdt = p.ds(lambda e, t=t, xin=xin: e.dma_start(
            xin, x_src[128 * t:128 * (t + 1), :]), wfree)
        src, stok = make_src(t, s, xin, xdt, wfree)
        v = p.v(lambda e, s=s, src=src: e.bn_stats(
            stt[:, 12 * s:12 * s + 6], src[:, 0:512]), [stok])
        v = p.v(lambda e, s=s, src=src: e.bn_stats(
            stt[:, 12 * s + 6:12 * s + 12], src[:, 512:1024]), [v])
        v = p.v(lambda e, s=s: e.bn_aggr(tmp[:, 8 * s + 4:8 * s + 6],
                                         stt[:, 12 * s:12 * s + 12]), [v])
        v = p.v(lambda e, s=s: e.tensor_scalar_add(
            tmp[:, 8 * s:8 * s + 1], tmp[:, 8 * s + 5:8 * s + 6], 1e-5), [v])
        a = p.s(lambda e, s=s: e.activation(tmp[:, 8 * s + 1:8 * s + 2],
                                            tmp[:, 8 * s:8 * s + 1], AF.Ln), [v])
        a = p.s(lambda e, s=s: e.activation(tmp[:, 8 * s + 2:8 * s + 3],
                                            tmp[:, 8 * s + 1:8 * s + 2], AF.Exp,
                                            scale=-0.5), [a])
        v = p.v(lambda e, s=s: e.tensor_tensor(
            tmp[:, 8 * s + 3:8 * s + 4], tmp[:, 8 * s + 4:8 * s + 5],
            tmp[:, 8 * s + 2:8 * s + 3], ALU.mult), [a])
        v = p.v(lambda e, s=s: e.tensor_scalar_mul(
            tmp[:, 8 * s + 6:8 * s + 7], tmp[:, 8 * s + 3:8 * s + 4], -1.0), [v])
        a = p.s(lambda e, s=s, src=src, ybx=ybx: e.activation(
            ybx, src, AF.Identity,
            bias=tmp[:, 8 * s + 6:8 * s + 7], scale=tmp[:, 8 * s + 2:8 * s + 3]), [v])
        v = p.v(lambda e, ybx=ybx, xin=xin: e.tensor_add(ybx, ybx, xin), [a, xdt])
        slot_free[s] = p.dg(lambda e, t=t, ybx=ybx: e.dma_start(
            x_dst[128 * t:128 * (t + 1), :], ybx), [v])


def norm_bound(p, C, src_view, ncols, sq, psbanks, outcol, crow, nch=8):
    groups = []
    off = 0
    while off < ncols:
        groups.append((off, min(2048, ncols - off)))
        off += 2048
    gi = 0
    for (goff, gw) in groups:
        nwin = (gw + 511) // 512
        for c in range(nch):
            p.s(lambda e, c=c, goff=goff, gw=gw, src_view=src_view, sq=sq: e.activation(
                sq[:, 0:gw], src_view[:, c, goff:goff + gw], AF.Square), p.tok("pe"))
            for n in range(nwin):
                w = min(512, gw - 512 * n)
                p.t(lambda e, c=c, n=n, w=w, sq=sq, bk=psbanks[n], nch=nch: e.matmul(
                    bk[0:1, 0:w], C["onesf"][:, 0:1], sq[:, 512 * n:512 * n + w],
                    start=(c == 0), stop=(c == nch - 1)),
                    p.tok("act", "dve") if n == 0 else ())
        for n in range(nwin):
            w = min(512, gw - 512 * n)
            p.v(lambda e, n=n, w=w, gi=gi, bk=psbanks[n]: e.reduce_max(
                C["tiny"][0:1, gi + n:gi + n + 1], bk[0:1, 0:w], axis=AX.X), p.tok("pe"))
        gi += nwin
    p.v(lambda e, gi=gi, outcol=outcol, crow=crow: e.reduce_max(
        crow[0:1, outcol:outcol + 1], C["tiny"][0:1, 0:gi], axis=AX.X))


def bias_col_from(p, C, crow, src_col, dst_col_ap, psbank, scale):
    p.t(lambda e: e.matmul(psbank[:, 0:1], C["onesf"][0:1, :], crow[0:1, src_col:src_col + 1],
                           start=True, stop=True), p.tok("dve"))
    return p.v(lambda e: e.tensor_copy(dst_col_ap, psbank[:, 0:1]), p.tok("pe"))


def phase_l0a(p, C, nc, dr, x_src, x_dst, dbg=None):
    with ExitStack() as st:
        sb = lambda n, shp, dt: st.enter_context(nc.sbuf_tensor(n, shp, dt))
        ps = lambda n, shp, dt=F32: st.enter_context(nc.psum_tensor(n, shp, dt))
        stA = ExitStack()
        sba = lambda n, shp, dt: stA.enter_context(nc.sbuf_tensor(n, shp, dt))
        brow = sb("a_brow", [128, 4 * 1024], BF16)
        padZ = sb("a_padZ", [128, 2304], BF16)
        QT = sb("a_QT", [128, 8 * 2048], BF16)
        KT = sb("a_KT", [128, 8 * 2304], BF16)
        VP = sb("a_VP", [128, 18 * 1040], BF16)
        xt = sb("a_xt", [128, 2 * 1024], F32)
        ub = sb("a_ub", [128, 2 * 1024], BF16)
        uT = sb("a_uT", [128, 2 * 1024], BF16)
        crow = sb("a_crow", [1, 16], F32)
        bias_c = sb("a_biasc", [128, 1], F32)
        rsumZ = sb("a_rsumZ", [128, 2 * 512], BF16)
        rsumf = sb("a_rsumf", [1, 2 * 512], F32)
        bcastS = sb("a_bcastS", [64, 2 * 512], BF16)
        wq = sba("a_wq", [128, 8 * 1024], BF16)
        wk = sba("a_wk", [128, 8 * 1024], BF16)
        wv = sba("a_wv", [128, 8 * 1024], BF16)
        stP = ExitStack()
        pbt = stP.enter_context(nc.psum_tensor("a_pbt", [128, 2 * 1024], BF16))
        psA = stP.enter_context(nc.psum_tensor("a_psA", [128, 512], F32))
        psB = stP.enter_context(nc.psum_tensor("a_psB", [128, 512], F32))
        psS = stP.enter_context(nc.psum_tensor("a_psS", [128, 2 * 512], F32))

        QTv = QT.rearrange("p (c n) -> p c n", n=2048)
        KTv = KT.rearrange("p (c n) -> p c n", n=2304)
        VPv = VP.rearrange("p (tt h n) -> p tt h n", h=16, n=65)

        for wsb, nm in ((wq, "l0_wq"), (wk, "l0_wk"), (wv, "l0_wv")):
            for c in range(8):
                p.dg(lambda e, wsb=wsb, nm=nm, c=c: e.dma_start(
                    wsb[:, 1024 * c:1024 * (c + 1)], dr[nm][128 * c:128 * (c + 1), :]))
        mz1 = p.v(lambda e: e.memset(brow[:], 0.0))
        mz2 = p.v(lambda e: e.memset(padZ[:], 0.0))
        p.dg(lambda e: e.dma_start(brow[0:1, :], dr["l0_brow"].rearrange("a b -> (a b)").unsqueeze(0)),
             [mz1])
        p.dg(lambda e: e.dma_start(padZ[0:1, :], dr["padneg"]), [mz2])
        p.v(lambda e: e.memset(VP[:], 1.0))
        p.v(lambda e: e.memset(rsumZ[:], 0.0))
        p.barrier()

        dma_toks = {}
        last_use_ub = {0: None, 1: None}
        last_evT = {0: None, 1: None}
        for t in (0, 1):
            b = t % 2
            dma_toks[t] = p.ds(lambda e, t=t, b=b: e.dma_start(
                xt[:, 1024 * b:1024 * (b + 1)], x_src[128 * t:128 * (t + 1), :]))

        def project_fmaj(bank, wsb, uTt, brw, f4, dst3, t, padfix, wait=()):
            grpw = list(p.tok("act")) + list(wait)
            first = True
            for fc in range(4):
                f = f4 + fc
                last_grp = (fc == 3)
                for c in range(8):
                    p.t(lambda e, f=f, c=c, fc=fc, bank=bank, wsb=wsb, uTt=uTt: e.matmul(
                        bank[:, 128 * fc:128 * (fc + 1)],
                        wsb[:, 1024 * c + 128 * f:1024 * c + 128 * (f + 1)],
                        uTt[:, 128 * c:128 * (c + 1)],
                        start=(c == 0 and fc == 0), stop=False,
                        skip_group_check=True), grpw if first else ())
                    first = False
                p.t(lambda e, f=f, fc=fc, bank=bank, brw=brw, last_grp=last_grp: e.matmul(
                    bank[:, 128 * fc:128 * (fc + 1)],
                    brw[:, 128 * f:128 * (f + 1)], C["onesb"][:, 0:128],
                    start=False, stop=(last_grp and not padfix),
                    skip_group_check=True))
                if padfix:
                    p.t(lambda e, f=f, fc=fc, bank=bank, brw=brw, t=t, last_grp=last_grp: e.matmul(
                        bank[:, 128 * fc:128 * (fc + 1)],
                        brw[:, 128 * f:128 * (f + 1)],
                        padZ[:, 128 * t:128 * (t + 1)],
                        start=False, stop=last_grp, skip_group_check=True))
            p.s(lambda e, bank=bank, dst3=dst3: e.activation(
                dst3, bank.rearrange("p (c n) -> p c n", n=128), AF.Identity), p.tok("pe"))

        evtoks = {}

        def do_proj(t):
            b = t % 2
            uTt = uT[:, 1024 * b:1024 * (b + 1)]
            evtok = evtoks[t]
            pad = t in (0, 17)
            if 1 <= t <= 16:
                tq = t - 1
                project_fmaj(psA, wq, uTt, brow[:, 0:1024], 0,
                             QTv[:, 0:4, 128 * tq:128 * (tq + 1)], t, False, wait=[evtok])
                project_fmaj(psB, wq, uTt, brow[:, 0:1024], 4,
                             QTv[:, 4:8, 128 * tq:128 * (tq + 1)], t, False)
            project_fmaj(psA, wk, uTt, brow[:, 1024:2048], 0,
                         KTv[:, 0:4, 128 * t:128 * (t + 1)], t, pad, wait=[evtok])
            project_fmaj(psB, wk, uTt, brow[:, 1024:2048], 4,
                         KTv[:, 4:8, 128 * t:128 * (t + 1)], t, pad)
            for half, bank in ((0, psA), (1, psB)):
                vw = list(p.tok("act")) + [evtok]
                for c in range(8):
                    p.t(lambda e, c=c, half=half, bank=bank, uTt=uTt: e.matmul(
                        bank[:], uTt[:, 128 * c:128 * (c + 1)],
                        wv[:, 1024 * c + 512 * half:1024 * c + 512 * (half + 1)],
                        start=(c == 0), stop=False), vw if c == 0 else ())
                p.t(lambda e, half=half, bank=bank, pad=pad: e.matmul(
                    bank[:], C["onesb"][:, 0:128],
                    brow[:, 2048 + 512 * half:2048 + 512 * (half + 1)],
                    start=False, stop=not pad))
                if pad:
                    p.t(lambda e, half=half, bank=bank, t=t: e.matmul(
                        bank[:], padZ[:, 128 * t:128 * (t + 1)],
                        brow[:, 2048 + 512 * half:2048 + 512 * (half + 1)],
                        start=False, stop=True))
                p.s(lambda e, half=half, bank=bank, t=t: e.activation(
                    VPv[:, t, 8 * half:8 * (half + 1), 0:64],
                    bank.rearrange("p (h n) -> p h n", n=64), AF.Identity), p.tok("pe"))

        for t in range(NTH):
            b = t % 2
            xin = xt[:, 1024 * b:1024 * (b + 1)]
            uin = ub[:, 1024 * b:1024 * (b + 1)]
            uTt = uT[:, 1024 * b:1024 * (b + 1)]
            w = [dma_toks[t]]
            if last_use_ub[b]:
                w.append(last_use_ub[b])
            ntok = tile_ln(p, C, xin, t, uin, wait=w)
            if t + 2 < NTH:
                dma_toks[t + 2] = p.ds(lambda e, t=t, b=b: e.dma_start(
                    xt[:, 1024 * b:1024 * (b + 1)],
                    x_src[128 * (t + 2):128 * (t + 3), :]), [ntok])
            trw = [ntok]
            if last_evT[b]:
                trw.append(last_evT[b])
            for c in range(8):
                p.t(lambda e, c=c, b=b, uin=uin: e.transpose(
                    pbt[:, 1024 * b + 128 * c:1024 * b + 128 * (c + 1)],
                    uin[:, 128 * c:128 * (c + 1)], C["eyeb"][:]), trw if c == 0 else ())
            last_use_ub[b] = p.tok("pe")[0]
            evw = list(p.tok("pe"))
            evtoks[t] = p.v(lambda e, uTt=uTt, b=b: e.tensor_copy(
                uTt, pbt[:, 1024 * b:1024 * (b + 1)]), evw)
            last_evT[b] = evtoks[t]
            if t >= 1:
                do_proj(t - 1)
        do_proj(NTH - 1)
        p.barrier()
        stA.close()
        wo = sb("a_wo", [128, 8 * 1024], BF16)
        masks = sb("a_masks", [128, 6 * 512], BF16)
        for c in range(8):
            p.dg(lambda e, c=c: e.dma_start(
                wo[:, 1024 * c:1024 * (c + 1)], dr["l0_wo"][128 * c:128 * (c + 1), :]))
        for i in range(6):
            p.dg(lambda e, i=i: e.dma_start(masks[:, 512 * i:512 * (i + 1)],
                                            dr["l0_masks"][128 * i:128 * (i + 1), :]))
        p.barrier()
        sq = sb("a_sq", [128, 2048], F32)
        Pb = sb("a_Pb", [128, 2 * 1024], BF16)
        attn = sb("a_attn", [128, 8 * 512], BF16)
        yb = sb("a_yb", [128, 1024], F32)
        xr = sb("a_xr", [128, 1024], F32)

        norm_bound(p, C, QTv, 2048, sq, [psA, psB, psS[:, 0:512], psS[:, 512:1024]], 8, crow)
        norm_bound(p, C, KTv, 2304, sq, [psA, psB, psS[:, 0:512], psS[:, 512:1024]], 9, crow)
        p.v(lambda e: e.tensor_tensor(crow[0:1, 10:11], crow[0:1, 8:9], crow[0:1, 9:10], ALU.mult))
        p.s(lambda e: e.activation(crow[0:1, 11:12], crow[0:1, 10:11], AF.Ln), p.tok("dve"))
        p.s(lambda e: e.activation(crow[0:1, 12:13], crow[0:1, 11:12], AF.Exp, scale=0.5))
        p.v(lambda e: e.tensor_scalar_mul(crow[0:1, 13:14], crow[0:1, 12:13], -0.125), p.tok("act"))
        bias_col_from(p, C, crow, 13, bias_c[:, 0:1], psA, None)
        p.barrier()
        stP.close()
        psS4 = st.enter_context(nc.psum_tensor("a_psS4", [128, 2048], F32))
        psAcc2 = st.enter_context(nc.psum_tensor("a_psAcc2", [128, 1024], F32))
        psBb = st.enter_context(nc.psum_tensor("a_psBb", [128, 512], F32))

        qk_tok = {}
        exp_tok = {}
        av_tok = {}
        h_done = {}
        gg_off = [0]
        oproj_tok = [None]
        for qt in range(4):
            items = [(h, kt) for h in range(16) for kt in range(6)]
            groups = [items[i:i + 2] for i in range(0, len(items), 2)]
            G = len(groups)
            pending = []

            def issue_qk(g, qt=qt, gg_off=gg_off):
                gg = gg_off[0] + g
                for i, (h, kt) in enumerate(groups[g]):
                    hp, hb = h % 2, h // 2
                    w = []
                    if i == 0 and gg >= 2:
                        w.append(exp_tok[gg - 2])
                    if i == 0 and g <= 1 and oproj_tok[0] is not None:
                        w.append(oproj_tok[0])
                    qk_tok[gg] = p.t(lambda e, kt=kt, qt=qt, gg=gg, i=i, hp=hp, hb=hb: e.matmul(
                        psS4[:, 1024 * (gg % 2) + 512 * i:1024 * (gg % 2) + 512 * (i + 1)],
                        KT[64 * hp:64 * (hp + 1),
                           2304 * hb + 512 * qt + 128 * kt:2304 * hb + 512 * qt + 128 * (kt + 1)],
                        QT[64 * hp:64 * (hp + 1),
                           2048 * hb + 512 * qt:2048 * hb + 512 * (qt + 1)],
                        start=True, stop=True), w)

            def issue_exp(g, gg_off=gg_off):
                gg = gg_off[0] + g
                w = [qk_tok[gg]]
                if gg >= 2:
                    w.append(av_tok[gg - 2])
                tok = p.s(lambda e, gg=gg: e.activation(
                    Pb[:, 1024 * (gg % 2):1024 * (gg % 2) + 1024],
                    psS4[:, 1024 * (gg % 2):1024 * (gg % 2) + 1024],
                    AF.Exp, scale=0.125, bias=bias_c[:, 0:1]), w)
                kt0 = groups[g][0][1]
                tok = p.v(lambda e, gg=gg, kt0=kt0: e.tensor_tensor(
                    Pb[:, 1024 * (gg % 2):1024 * (gg % 2) + 1024],
                    Pb[:, 1024 * (gg % 2):1024 * (gg % 2) + 1024],
                    masks[:, 512 * kt0:512 * kt0 + 1024], ALU.mult), [tok])
                exp_tok[gg] = tok

            def issue_av(g, it, qt=qt, gg_off=gg_off):
                gg = gg_off[0] + g
                for i, (h, kt) in enumerate(groups[g]):
                    w = [exp_tok[gg]] if i == 0 else []
                    if kt == 0 and (qt, h - 2) in h_done:
                        w.append(h_done[(qt, h - 2)])
                    if kt == 0 and h <= 1 and (qt - 1, 14 + h % 2) in h_done:
                        w.append(h_done[(qt - 1, 14 + h % 2)])
                    av_tok[gg] = p.t(lambda e, kt=kt, qt=qt, gg=gg, i=i, h=h: e.matmul(
                        psAcc2[0:65, 512 * (h % 2):512 * (h % 2) + 512],
                        VP[:, 1040 * (4 * qt + kt) + 65 * h:1040 * (4 * qt + kt) + 65 * h + 65],
                        Pb[:, 1024 * (gg % 2) + 512 * i:1024 * (gg % 2) + 512 * (i + 1)],
                        start=(kt == 0), stop=(kt == 5),
                        skip_group_check=True), w)
                    if kt == 5:
                        rt = p.v(lambda e, h=h: e.reciprocal(
                            rsumf[0:1, 512 * (h % 2):512 * (h % 2) + 512],
                            psAcc2[64:65, 512 * (h % 2):512 * (h % 2) + 512]),
                            [av_tok[gg]])
                        rt = p.v(lambda e, h=h: e.tensor_copy(
                            rsumZ[0:1, 512 * (h % 2):512 * (h % 2) + 512],
                            rsumf[0:1, 512 * (h % 2):512 * (h % 2) + 512]), [rt])
                        pending.append((h, rt, it))

            def flush_tail(it, qt=qt):
                while pending and pending[0][2] <= it - 2:
                    h, rt, _ = pending.pop(0)
                    bw = [rt]
                    if (qt, h - 1) in h_done:
                        bw.append(h_done[(qt, h - 1)])
                    elif h == 0 and (qt - 1, 15) in h_done:
                        bw.append(h_done[(qt - 1, 15)])
                    bt = p.t(lambda e, h=h: e.matmul(
                        psBb[0:64, :], C["onesb"][:, 0:64],
                        rsumZ[:, 512 * (h % 2):512 * (h % 2) + 512],
                        start=True, stop=True), bw)
                    bt = p.v(lambda e, h=h: e.tensor_copy(
                        bcastS[:, 512 * (h % 2):512 * (h % 2) + 512],
                        psBb[0:64, :]), [bt])
                    h_done[(qt, h)] = p.v(lambda e, h=h, qt=qt: e.tensor_tensor(
                        attn[64 * (h % 2):64 * (h % 2) + 64,
                             512 * (h // 2):512 * (h // 2) + 512],
                        psAcc2[0:64, 512 * (h % 2):512 * (h % 2) + 512],
                        bcastS[:, 512 * (h % 2):512 * (h % 2) + 512],
                        ALU.mult), [bt])

            for g in range(G):
                issue_qk(g)
                issue_exp(g)
                if g >= 1:
                    issue_av(g - 1, g)
                flush_tail(g)
            issue_av(G - 1, G)
            flush_tail(G + 2)
            gg_off[0] += G

            prev_store = None
            attn_w = [h_done[(qt, 14)], h_done[(qt, 15)]]
            for ts in range(4):
                tglob = 4 * qt + ts
                mw = attn_w + list(p.tok("act"))
                for nh in range(2):
                    for pr in range(8):
                        p.t(lambda e, ts=ts, nh=nh, pr=pr: e.matmul(
                            psS4[:, 512 * nh:512 * (nh + 1)],
                            attn[:, 512 * pr + 128 * ts:512 * pr + 128 * (ts + 1)],
                            wo[:, 1024 * pr + 512 * nh:1024 * pr + 512 * (nh + 1)],
                            start=(pr == 0), stop=False), mw if (nh == 0 and pr == 0) else ())
                    p.t(lambda e, nh=nh: e.matmul(
                        psS4[:, 512 * nh:512 * (nh + 1)], C["onesb"][:, 0:128],
                        brow[:, 3072 + 512 * nh:3072 + 512 * (nh + 1)], start=False, stop=True))
                xw = list(p.tok("dve"))
                if prev_store:
                    xw.append(prev_store)
                xrtok = p.ds(lambda e, tglob=tglob: e.dma_start(
                    xr[:], x_src[128 * (tglob + 1):128 * (tglob + 2), :]), xw)
                prev_store = ln_resid_store(
                    p, C, "psum2", psS4[:, 0:512], psS4[:, 512:1024], xr[:], yb,
                    x_dst[128 * tglob:128 * (tglob + 1), :],
                    extra_wait=list(p.tok("pe")) + [xrtok])
            oproj_tok[0] = prev_store
        p.barrier()


def phase_l0f(p, C, nc, dr, x_src, x_dst, cio):
    o2d = cio["o2_dram"]
    with ExitStack() as st:
        sb = lambda n, shp, dt: st.enter_context(nc.sbuf_tensor(n, shp, dt))
        ps = lambda n, shp, dt=F32: st.enter_context(nc.psum_tensor(n, shp, dt))
        stA = ExitStack()
        sba = lambda n, shp, dt: stA.enter_context(nc.sbuf_tensor(n, shp, dt))
        xt = sb("f_xt", [128, 2 * 1024], F32)
        ub = sb("f_ub", [128, 2 * 1024], BF16)
        scfull = sb("f_scfull", [128, 64], F32)
        scloc = sb("f_scloc", [128, 16], F32)
        bis = sb("f_bis", [128, 8], F32)
        gates = sb("f_gates", [128, 16], F32)
        o2tok = sb("f_o2tok", [128, 2 * 1024], BF16)
        W1 = sba("f_W1", [128, 8 * 4096], BF16)
        W2 = sba("f_W2", [128, 32 * 1024], BF16)
        b1c = sba("f_b1c", [128, 32], F32)
        b2c = sba("f_b2c", [128, 8], F32)
        wr8 = sba("f_wr8", [8, 128], F32)
        wrc = sba("f_wrc", [128, 8], BF16)
        u2T = sba("f_u2T", [128, 8 * 2048], BF16)
        scrow = sba("f_scrow", [1, 2048], F32)
        scl16 = sba("f_scl16", [16, 128], F32)
        hbuf = sba("f_hb", [128, 2 * 256], BF16)
        o2f = sba("f_o2f", [128, 256], BF16)
        pbt = ps("f_pbt", [128, 1024], BF16)
        psH = ps("f_psH", [128, 2 * 512])
        psO0 = ps("f_psO0", [128, 2 * 256])
        psO1 = ps("f_psO1", [128, 2 * 256])
        psO2 = ps("f_psO2", [128, 2 * 256])
        psO3 = ps("f_psO3", [128, 2 * 256])
        psC = ps("f_psC", [128, 512])
        psOm = [psO0[:, 0:256], psO0[:, 256:512], psO1[:, 0:256], psO1[:, 256:512],
                psO2[:, 0:256], psO2[:, 256:512], psO3[:, 0:256], psO3[:, 256:512]]

        for c in range(8):
            p.dg(lambda e, c=c: e.dma_start(W1[:, 4096 * c:4096 * (c + 1)],
                                            dr["mod_w1"][128 * c:128 * (c + 1), :]))
        for c in range(32):
            p.dg(lambda e, c=c: e.dma_start(W2[:, 1024 * c:1024 * (c + 1)],
                                            dr["mod_w2"][128 * c:128 * (c + 1), :]))
        p.dg(lambda e: e.dma_start(b1c[:], dr["mod_b1c"]))
        p.dg(lambda e: e.dma_start(b2c[:], dr["mod_b2c"]))
        p.ds(lambda e: e.dma_start(wr8[:], dr["mod_wr8"]))
        p.barrier()
        p.t(lambda e: e.matmul(psC[:, 0:8], wr8[:], C["eyef"][0:8, 0:8], start=True, stop=True))
        p.v(lambda e: e.tensor_copy(wrc[:], psC[:, 0:8]), p.tok("pe"))
        p.barrier()

        dma_toks = {}
        last_use_ub = {0: None, 1: None}
        last_evT = {0: None, 1: None}
        for t in (0, 1):
            b = t % 2
            dma_toks[t] = p.ds(lambda e, t=t, b=b: e.dma_start(
                xt[:, 1024 * b:1024 * (b + 1)], x_src[128 * t:128 * (t + 1), :]))
        for t in range(NT):
            b = t % 2
            xin = xt[:, 1024 * b:1024 * (b + 1)]
            uin = ub[:, 1024 * b:1024 * (b + 1)]
            w = [dma_toks[t]]
            if last_use_ub[b]:
                w.append(last_use_ub[b])
            ntok = tile_ln(p, C, xin, t, uin, wait=w)
            if t + 2 < NT:
                dma_toks[t + 2] = p.ds(lambda e, t=t, b=b: e.dma_start(
                    xt[:, 1024 * b:1024 * (b + 1)],
                    x_src[128 * (t + 2):128 * (t + 3), :]), [ntok])
            trw = [ntok]
            if last_evT[0]:
                trw.append(last_evT[0])
            for c in range(8):
                p.t(lambda e, c=c, uin=uin: e.transpose(
                    pbt[:, 128 * c:128 * (c + 1)],
                    uin[:, 128 * c:128 * (c + 1)], C["eyeb"][:]), trw if c == 0 else ())
            last_use_ub[b] = p.tok("pe")[0]
            last_evT[0] = p.v(lambda e, t=t: e.tensor_copy(
                u2T.rearrange("p (c n) -> p c n", n=2048)[:, :, 128 * t:128 * (t + 1)],
                pbt[:, 0:1024].rearrange("p (c n) -> p c n", n=128)),
                p.tok("pe"))
        p.barrier()

        for n in range(4):
            for c in range(8):
                p.t(lambda e, n=n, c=c: e.matmul(
                    psC[0:1, 0:512], wrc[:, c:c + 1],
                    u2T[:, 2048 * c + 512 * n:2048 * c + 512 * (n + 1)],
                    start=(c == 0), stop=(c == 7)), p.tok("dve") if c == 0 else ())
            p.v(lambda e, n=n: e.tensor_copy(scrow[0:1, 512 * n:512 * (n + 1)], psC[0:1, 0:512]),
                p.tok("pe"))
        p.ds(lambda e: e.dma_start(cio["sc_in"], scrow[0:1, :]), p.tok("dve"))
        p.ccop(lambda e: e.collective_compute(
            "AllGather", ALU.bypass, GROUPS, ins=[cio["sc_in"].opt()],
            outs=[cio["sc_out"].opt()]), p.tok("qs"))
        p.ds(lambda e: e.dma_start(
            scfull[:], cio["sc_out"].rearrange("a (c g) -> (a c) g", g=64)), p.tok("cc"))
        p.ds(lambda e: e.dma_start(
            scl16[:], cio["sc_in"].rearrange("a (b c) -> (a b) c", c=128)), p.tok("cc"))
        p.t(lambda e: e.transpose(psC[:, 0:16], scl16[:], C["eyef"][0:16, 0:16]), p.tok("qs"))
        p.v(lambda e: e.tensor_copy(scloc[:], psC[:, 0:16]), p.tok("pe"))
        p.barrier()

        K_SEL = 4096.0
        N_BIS = 42
        p.v(lambda e: e.memset(bis[:, 0:1], -10000.0))
        p.v(lambda e: e.memset(bis[:, 1:2], 10000.0))
        p.v(lambda e: e.memset(bis[:, 3:4], 3.0e38))

        def bis_pre(it):
            p.v(lambda e: e.tensor_add(bis[:, 6:7], bis[:, 0:1], bis[:, 1:2]))
            p.v(lambda e: e.tensor_scalar_mul(bis[:, 2:3], bis[:, 6:7], 0.5))
            p.v(lambda e: e.tensor_single_scalar(C["tiny"][:], scfull[:], bis[:, 2:3],
                                                 ALU.is_gt))
            return p.v(lambda e: e.reduce_sum(bis[:, 6:7], C["tiny"][:], axis=AX.X))

        def bis_step(it, red_tok):
            cnt = p.t(lambda e: e.matmul(psC[:, 0:1], C["onesf"][:], bis[:, 6:7],
                                         start=True, stop=True), [red_tok])
            p.v(lambda e: e.tensor_single_scalar(bis[:, 4:5], psC[:, 0:1], K_SEL,
                                                 ALU.is_gt), [cnt])
            p.v(lambda e: e.tensor_single_scalar(bis[:, 5:6], psC[:, 0:1], K_SEL,
                                                 ALU.is_equal))
            p.v(lambda e: e.tensor_sub(bis[:, 7:8], bis[:, 2:3], bis[:, 3:4]))
            p.v(lambda e: e.scalar_tensor_tensor(bis[:, 3:4], bis[:, 7:8], bis[:, 5:6],
                                                 bis[:, 3:4], ALU.mult, ALU.add))
            p.v(lambda e: e.tensor_sub(bis[:, 7:8], bis[:, 2:3], bis[:, 0:1]))
            p.v(lambda e: e.scalar_tensor_tensor(bis[:, 0:1], bis[:, 7:8], bis[:, 4:5],
                                                 bis[:, 0:1], ALU.mult, ALU.add))
            p.v(lambda e: e.tensor_scalar(bis[:, 5:6], bis[:, 4:5], -1.0, 1.0,
                                          ALU.mult, ALU.add))
            p.v(lambda e: e.tensor_sub(bis[:, 7:8], bis[:, 2:3], bis[:, 1:2]))
            p.v(lambda e: e.scalar_tensor_tensor(bis[:, 1:2], bis[:, 7:8], bis[:, 5:6],
                                                 bis[:, 1:2], ALU.mult, ALU.add))
            if it + 1 < N_BIS:
                return bis_pre(it + 1)
            return None

        lastgelu = {0: None, 1: None}
        w2done = {0: None, 1: None}
        lasto2 = None
        last_etok = None
        seq = [(tq, f) for tq in range(8) for f in range(32)]

        def issue_w1(tq, f):
            hb = f % 2
            hw = [lastgelu[hb]] if lastgelu[hb] else []
            first = True
            qk_last = None
            for c in range(8):
                qk_last = p.t(lambda e, tq=tq, f=f, c=c, hb=hb: e.matmul(
                    psH[:, 512 * hb:512 * hb + 256],
                    W1[:, 4096 * c + 128 * f:4096 * c + 128 * (f + 1)],
                    u2T[:, 2048 * c + 256 * tq:2048 * c + 256 * (tq + 1)],
                    start=(c == 0), stop=(c == 7), skip_group_check=True),
                    hw if first else ())
                first = False
            gw = [qk_last]
            if w2done[hb]:
                gw.append(w2done[hb])
            lastgelu[hb] = p.s(lambda e, f=f, hb=hb: e.activation(
                hbuf[:, 256 * hb:256 * (hb + 1)], psH[:, 512 * hb:512 * hb + 256],
                AF.Gelu_apprx_tanh, bias=b1c[:, f:f + 1]), gw)

        def issue_w2(tq, f):
            nonlocal lasto2, last_etok
            hb = f % 2
            w2w = [lastgelu[hb]]
            if f == 0 and last_etok is not None:
                w2w.append(last_etok)
            for m in range(8):
                p.t(lambda e, f=f, m=m, hb=hb: e.matmul(
                    psOm[m], W2[:, 1024 * f + 128 * m:1024 * f + 128 * (m + 1)],
                    hbuf[:, 256 * hb:256 * (hb + 1)],
                    start=(f == 0 and m % 2 == 0), stop=(f == 31 and m % 2 == 1),
                    skip_group_check=True),
                    w2w if m == 0 else ())
            w2done[hb] = p.tok("pe")[0]
            if f != 31:
                return
            w2fin = w2done[hb]
            for m in range(8):
                evw = [w2fin]
                if lasto2:
                    evw.append(lasto2)
                etok = p.v(lambda e, m=m: e.tensor_scalar(
                    o2f[:], psOm[m], b2c[:, m:m + 1], None, ALU.add), evw)
                last_etok = etok
                for half in range(2):
                    p.t(lambda e, half=half: e.transpose(
                        pbt[:, 128 * half:128 * (half + 1)],
                        o2f[:, 128 * half:128 * (half + 1)], C["eyeb"][:]),
                        [etok] if half == 0 else ())
                lasto2 = p.v(lambda e, m=m: e.tensor_copy(
                    o2tok.rearrange("p (tt n) -> p tt n", n=1024)[:, 0:2,
                                                                 128 * m:128 * (m + 1)],
                    pbt[:, 0:256].rearrange("p (b n) -> p b n", n=128)),
                    list(p.tok("pe")) + list(p.tok("qs")))
            for tt in range(2):
                p.ds(lambda e, tq=tq, tt=tt: e.dma_start(
                    o2d[128 * (2 * tq + tt):128 * (2 * tq + tt + 1), :],
                    o2tok[:, 1024 * tt:1024 * (tt + 1)]), p.tok("dve"))

        red_tok = bis_pre(0)
        bis_it = 0
        for idx, (tq, f) in enumerate(seq):
            issue_w1(tq, f)
            if idx >= 1:
                issue_w2(*seq[idx - 1])
            if idx >= 1 and bis_it < N_BIS:
                red_tok = bis_step(bis_it, red_tok)
                bis_it += 1
        issue_w2(*seq[-1])
        p.v(lambda e: e.tensor_single_scalar(bis[:, 4:5], bis[:, 3:4], 1.0e38, ALU.is_lt))
        p.v(lambda e: e.tensor_sub(bis[:, 7:8], bis[:, 3:4], bis[:, 2:3]))
        p.v(lambda e: e.scalar_tensor_tensor(bis[:, 6:7], bis[:, 7:8], bis[:, 4:5],
                                             bis[:, 2:3], ALU.mult, ALU.add))
        p.v(lambda e: e.tensor_copy(bis[:, 3:4], bis[:, 6:7]))
        p.v(lambda e: e.tensor_single_scalar(C["tiny"][:, 16:32], scloc[:], bis[:, 3:4],
                                             ALU.is_gt))
        bis_tok = p.tok("dve")
        p.barrier()
        stA.close()

        p.s(lambda e: e.activation(gates[:], scloc[:], AF.Sigmoid), bis_tok)
        gtok = p.v(lambda e: e.tensor_tensor(gates[:], gates[:], C["tiny"][:, 16:32],
                                             ALU.mult), p.tok("act"))

        ub3 = st.enter_context(nc.sbuf_tensor("f_ub3", [128, 3 * 1024], BF16))
        o23 = st.enter_context(nc.sbuf_tensor("f_o23", [128, 3 * 1024], BF16))
        cb3 = st.enter_context(nc.sbuf_tensor("f_cb3", [128, 3 * 1024], F32))

        def mk_src_l0f(t, s, xin, xdt, wfree):
            utok = p.s(lambda e, t=t, s=s, xin=xin: e.activation(
                ub3[:, 1024 * s:1024 * (s + 1)], xin, AF.Identity,
                bias=C["mrs"][:, 4 * t + 3:4 * t + 4],
                scale=C["mrs"][:, 4 * t + 2:4 * t + 3]), [xdt])
            otok = p.ds(lambda e, t=t, s=s: e.dma_start(
                o23[:, 1024 * s:1024 * (s + 1)], o2d[128 * t:128 * (t + 1), :]), wfree)
            ctok = p.v(lambda e, t=t, s=s: e.scalar_tensor_tensor(
                cb3[:, 1024 * s:1024 * (s + 1)], o23[:, 1024 * s:1024 * (s + 1)],
                gates[:, t:t + 1], ub3[:, 1024 * s:1024 * (s + 1)],
                ALU.mult, ALU.add), [utok, otok, gtok])
            return cb3[:, 1024 * s:1024 * (s + 1)], ctok

        ln_resid_pipe(p, C, nc, st, "f", NT, mk_src_l0f, x_src, x_dst)
        p.barrier()


def phase_l1a(p, C, nc, dr, x_src, x_dst, cio):
    with ExitStack() as st:
        sb = lambda n, shp, dt: st.enter_context(nc.sbuf_tensor(n, shp, dt))
        ps = lambda n, shp, dt=F32: st.enter_context(nc.psum_tensor(n, shp, dt))
        masks = sb("c_masks", [128, 4 * 512], BF16)
        xt = sb("c_xt", [128, 2 * 1024], F32)
        wqp = sb("c_wqp", [128, 8 * 128], BF16)
        wkp = sb("c_wkp", [128, 8 * 128], BF16)
        wvpp = sb("c_wvp", [128, 8 * 128], BF16)
        woT = sb("c_woT", [128, 2 * 1024], BF16)
        QTp = sb("c_QTp", [128, 8192], BF16)
        KT0 = sb("c_KT0", [128, 8192], BF16)
        KT1 = sb("c_KT1", [128, 8192], BF16)
        VPp = sb("c_VPp", [128, 64 * 130], BF16)
        AT0 = sb("c_AT0", [128, 8192], BF16)
        AT1 = sb("c_AT1", [128, 8192], BF16)
        sq = sb("c_sq", [128, 2048], F32)
        crow = sb("c_crow", [1, 16], F32)
        bias_c = sb("c_biasc", [128, 1], F32)
        Pb = sb("c_Pb", [128, 2 * 1024], BF16)
        rsumZ = sb("c_rsumZ", [128, 2 * 512], BF16)
        rsumf = sb("c_rsumf", [1, 2 * 512], F32)
        bcastS = sb("c_bcastS", [64, 2 * 512], BF16)
        ytile = sb("c_ytile", [128, 2 * 1024], BF16)
        stP = ExitStack()
        pbt = stP.enter_context(nc.psum_tensor("c_pbt", [128, 2 * 1024], BF16))
        ub = stP.enter_context(nc.sbuf_tensor("c_ub", [128, 2 * 1024], BF16))
        u3T = stP.enter_context(nc.sbuf_tensor("c_u3T", [128, 8 * 2048], BF16))

        for i in range(4):
            p.dg(lambda e, i=i: e.dma_start(masks[:, 512 * i:512 * (i + 1)],
                                            dr["l1_masks"][128 * i:128 * (i + 1), :]))
        for c in range(2):
            p.dg(lambda e, c=c: e.dma_start(woT[:, 1024 * c:1024 * (c + 1)],
                                            dr["l1_wo"][128 * c:128 * (c + 1), :]))
        p.v(lambda e: e.memset(VPp[:], 1.0))
        p.v(lambda e: e.memset(KT0[64:128, :], 0.0))
        p.v(lambda e: e.memset(KT1[0:64, :], 0.0))
        p.v(lambda e: e.memset(rsumZ[:], 0.0))
        p.barrier()

        dma_toks = {}
        last_use_ub = {0: None, 1: None}
        last_evT = {0: None, 1: None}
        for t in (0, 1):
            b = t % 2
            dma_toks[t] = p.ds(lambda e, t=t, b=b: e.dma_start(
                xt[:, 1024 * b:1024 * (b + 1)], x_src[128 * t:128 * (t + 1), :]))
        for t in range(NT):
            b = t % 2
            xin = xt[:, 1024 * b:1024 * (b + 1)]
            uin = ub[:, 1024 * b:1024 * (b + 1)]
            w = [dma_toks[t]]
            if last_use_ub[b]:
                w.append(last_use_ub[b])
            ntok = tile_ln(p, C, xin, t, uin, wait=w)
            if t + 2 < NT:
                dma_toks[t + 2] = p.ds(lambda e, t=t, b=b: e.dma_start(
                    xt[:, 1024 * b:1024 * (b + 1)],
                    x_src[128 * (t + 2):128 * (t + 3), :]), [ntok])
            trw = [ntok]
            if last_evT[b]:
                trw.append(last_evT[b])
            for c in range(8):
                p.t(lambda e, c=c, b=b, uin=uin: e.transpose(
                    pbt[:, 1024 * b + 128 * c:1024 * b + 128 * (c + 1)],
                    uin[:, 128 * c:128 * (c + 1)], C["eyeb"][:]), trw if c == 0 else ())
            last_use_ub[b] = p.tok("pe")[0]
            last_evT[b] = p.v(lambda e, t=t, b=b: e.tensor_copy(
                u3T.rearrange("p (c n) -> p c n", n=2048)[:, :, 128 * t:128 * (t + 1)],
                pbt[:, 1024 * b:1024 * (b + 1)].rearrange("p (c n) -> p c n", n=128)),
                p.tok("pe"))
        for c in range(8):
            p.ds(lambda e, c=c: e.dma_start(
                cio["ag_in"][128 * c:128 * (c + 1), :], u3T[:, 2048 * c:2048 * (c + 1)]),
                p.tok("dve"))
        p.ccop(lambda e: e.collective_compute(
            "AllGather", ALU.bypass, GROUPS, ins=[cio["ag_in"].opt()],
            outs=[cio["ag_out"].opt()]), p.tok("qs"))
        p.barrier()
        stP.close()
        xb = st.enter_context(nc.sbuf_tensor("c_xb", [128, 8 * 2048], BF16))
        psS4 = st.enter_context(nc.psum_tensor("c_psS4", [128, 2048], F32))
        psAcc2 = st.enter_context(nc.psum_tensor("c_psAcc2", [128, 1024], F32))
        psB = st.enter_context(nc.psum_tensor("c_psB", [128, 512], F32))
        psV = st.enter_context(nc.psum_tensor("c_psV", [128, 256], F32))

        wsafe = [None]
        for hp in range(2):
            AT = AT0 if hp == 0 else AT1
            wdma = [wsafe[0]] if wsafe[0] is not None else []
            for wsb, nm in ((wqp, "l1_wq"), (wkp, "l1_wk"), (wvpp, "l1_wv")):
                for c in range(8):
                    p.dg(lambda e, wsb=wsb, nm=nm, c=c, hp=hp: e.dma_start(
                        wsb[:, 128 * c:128 * (c + 1)],
                        dr[nm][128 * c:128 * (c + 1), 128 * hp:128 * (hp + 1)]),
                        wdma if (wsb is wqp and c == 0) else ())
            sr_tok = {}

            def load_sr(sr, w):
                r, th = divmod(sr, 2)
                b = sr % 2
                for c in range(8):
                    p.dg(lambda e, r=r, th=th, b=b, c=c: e.dma_start(
                        xb[:, 8192 * b + 1024 * c:8192 * b + 1024 * (c + 1)],
                        cio["ag_out"][1024 * r + 128 * c:1024 * r + 128 * (c + 1),
                                      1024 * th:1024 * (th + 1)]),
                        w if c == 0 else ())
                sr_tok[sr] = p.tok("qg")[0]

            load_sr(0, [])
            load_sr(1, [])
            for sr in range(8):
                r, th = divmod(sr, 2)
                b = sr % 2
                ldt = [sr_tok[sr]]
                for dst, wsb in (("q", wqp), ("k", wkp)):
                    for n2 in range(2):
                        gw = list(ldt) + list(p.tok("act"))
                        for c in range(8):
                            p.t(lambda e, wsb=wsb, b=b, n2=n2, c=c: e.matmul(
                                psS4[:, 0:512], wsb[:, 128 * c:128 * (c + 1)],
                                xb[:, 8192 * b + 1024 * c + 512 * n2:
                                   8192 * b + 1024 * c + 512 * (n2 + 1)],
                                start=(c == 0), stop=(c == 7)), gw if c == 0 else ())
                        seg = 2048 * r + 1024 * th + 512 * n2
                        if dst == "q":
                            p.s(lambda e, seg=seg: e.activation(
                                QTp[:, seg:seg + 512],
                                psS4[:, 0:512], AF.Identity), p.tok("pe"))
                        else:
                            p.s(lambda e, seg=seg: e.activation(
                                KT0[0:64, seg:seg + 512],
                                psS4[0:64, 0:512], AF.Identity), p.tok("pe"))
                            p.s(lambda e, seg=seg: e.activation(
                                KT1[64:128, seg:seg + 512],
                                psS4[64:128, 0:512], AF.Identity))
                for ts2 in range(8):
                    tglob = 16 * r + 8 * th + ts2
                    bank = psV[:, 0:128] if ts2 % 2 == 0 else psB[:, 0:128]
                    vw = list(ldt) + list(p.tok("act"))
                    for c in range(8):
                        p.t(lambda e, ts2=ts2, b=b, c=c, bank=bank: e.matmul(
                            bank,
                            xb[:, 8192 * b + 1024 * c + 128 * ts2:
                               8192 * b + 1024 * c + 128 * (ts2 + 1)],
                            wvpp[:, 128 * c:128 * (c + 1)],
                            start=(c == 0), stop=(c == 7)), vw if c == 0 else ())
                    p.s(lambda e, tglob=tglob, bank=bank: e.activation(
                        VPp.rearrange("p (tt h n) -> p tt h n", h=2, n=65)[:, tglob, :, 0:64],
                        bank.rearrange("p (h n) -> p h n", n=64), AF.Identity),
                        p.tok("pe"))
                if sr + 2 < 8:
                    load_sr(sr + 2, list(p.tok("pe")))
            wsafe[0] = p.tok("pe")[0]
            p.barrier()
            QTv = QTp.rearrange("p (c n) -> p c n", c=1)
            norm_bound(p, C, QTv, 8192, sq,
                       [psS4[:, 0:512], psS4[:, 512:1024], psS4[:, 1024:1536],
                        psS4[:, 1536:2048]], 8, crow, nch=1)
            norm_bound(p, C, KT0.rearrange("p (c n) -> p c n", c=1), 8192, sq,
                       [psS4[:, 0:512], psS4[:, 512:1024], psS4[:, 1024:1536],
                        psS4[:, 1536:2048]], 9, crow, nch=1)
            norm_bound(p, C, KT1.rearrange("p (c n) -> p c n", c=1), 8192, sq,
                       [psS4[:, 0:512], psS4[:, 512:1024], psS4[:, 1024:1536],
                        psS4[:, 1536:2048]], 14, crow, nch=1)
            p.v(lambda e: e.tensor_tensor(crow[0:1, 9:10], crow[0:1, 9:10],
                                          crow[0:1, 14:15], ALU.max))
            p.v(lambda e: e.tensor_tensor(crow[0:1, 10:11], crow[0:1, 8:9], crow[0:1, 9:10], ALU.mult))
            p.s(lambda e: e.activation(crow[0:1, 11:12], crow[0:1, 10:11], AF.Ln), p.tok("dve"))
            p.s(lambda e: e.activation(crow[0:1, 12:13], crow[0:1, 11:12], AF.Exp, scale=0.5))
            p.v(lambda e: e.tensor_scalar_mul(crow[0:1, 13:14], crow[0:1, 12:13], -0.125),
                p.tok("act"))
            bias_col_from(p, C, crow, 13, bias_c[:, 0:1], psB, None)
            p.barrier()
            for h2 in range(2):
                items = []
                for qt in range(16):
                    for kt in range(4 * qt + 4):
                        items.append((qt, kt))
                groups = [items[i:i + 2] for i in range(0, len(items), 2)]
                G = len(groups)
                qk_tok = {}
                exp_tok = {}
                av_tok = {}
                qt_done = {}

                KTsel = KT0 if h2 == 0 else KT1

                def issue_qk(g, KTsel=KTsel):
                    for i, (qt, kt) in enumerate(groups[g]):
                        w = []
                        if i == 0 and g >= 2:
                            w.append(exp_tok[g - 2])
                        qk_tok[g] = p.t(lambda e, kt=kt, qt=qt, g=g, i=i, KTsel=KTsel: e.matmul(
                            psS4[:, 1024 * (g % 2) + 512 * i:1024 * (g % 2) + 512 * (i + 1)],
                            KTsel[:, 128 * kt:128 * (kt + 1)],
                            QTp[:, 512 * qt:512 * (qt + 1)],
                            start=True, stop=True), w)

                def issue_exp(g):
                    w = [qk_tok[g]]
                    if g >= 2:
                        w.append(av_tok[g - 2])
                    tok = p.s(lambda e, g=g: e.activation(
                        Pb[:, 1024 * (g % 2):1024 * (g % 2) + 1024],
                        psS4[:, 1024 * (g % 2):1024 * (g % 2) + 1024],
                        AF.Exp, scale=0.125, bias=bias_c[:, 0:1]), w)
                    qt0, kt0 = groups[g][0]
                    dlt0 = kt0 - 4 * qt0
                    if dlt0 >= 0:
                        tok = p.v(lambda e, g=g, dlt0=dlt0: e.tensor_tensor(
                            Pb[:, 1024 * (g % 2):1024 * (g % 2) + 1024],
                            Pb[:, 1024 * (g % 2):1024 * (g % 2) + 1024],
                            masks[:, 512 * dlt0:512 * dlt0 + 1024], ALU.mult), [tok])
                    exp_tok[g] = tok

                pending = []

                def issue_av(g, it):
                    for i, (qt, kt) in enumerate(groups[g]):
                        nkt = 4 * qt + 4
                        w = [exp_tok[g]] if i == 0 else []
                        if kt == 0 and qt >= 2:
                            w.append(qt_done[qt - 2])
                        av_tok[g] = p.t(lambda e, h2=h2, kt=kt, qt=qt, g=g, i=i, nkt=nkt: e.matmul(
                            psAcc2[0:65, 512 * (qt % 2):512 * (qt % 2) + 512],
                            VPp[:, 130 * kt + 65 * h2:130 * kt + 65 * h2 + 65],
                            Pb[:, 1024 * (g % 2) + 512 * i:1024 * (g % 2) + 512 * (i + 1)],
                            start=(kt == 0), stop=(kt == nkt - 1),
                            skip_group_check=True), w)
                        if kt == nkt - 1:
                            rt = p.v(lambda e, qt=qt: e.reciprocal(
                                rsumf[0:1, 512 * (qt % 2):512 * (qt % 2) + 512],
                                psAcc2[64:65, 512 * (qt % 2):512 * (qt % 2) + 512]),
                                [av_tok[g]])
                            rt = p.v(lambda e, qt=qt: e.tensor_copy(
                                rsumZ[0:1, 512 * (qt % 2):512 * (qt % 2) + 512],
                                rsumf[0:1, 512 * (qt % 2):512 * (qt % 2) + 512]), [rt])
                            pending.append((qt, rt, it))

                def flush_tail(it, h2=h2, AT=AT):
                    while pending and pending[0][2] <= it - 2:
                        qt, rt, _ = pending.pop(0)
                        bw = [rt]
                        if qt >= 1:
                            bw.append(qt_done[qt - 1])
                        bt = p.t(lambda e, qt=qt: e.matmul(
                            psB[0:64, :], C["onesb"][:, 0:64],
                            rsumZ[:, 512 * (qt % 2):512 * (qt % 2) + 512],
                            start=True, stop=True), bw)
                        bt = p.v(lambda e, qt=qt: e.tensor_copy(
                            bcastS[:, 512 * (qt % 2):512 * (qt % 2) + 512],
                            psB[0:64, :]), [bt])
                        qt_done[qt] = p.v(lambda e, h2=h2, qt=qt, AT=AT: e.tensor_tensor(
                            AT[64 * h2:64 * (h2 + 1), 512 * qt:512 * (qt + 1)],
                            psAcc2[0:64, 512 * (qt % 2):512 * (qt % 2) + 512],
                            bcastS[:, 512 * (qt % 2):512 * (qt % 2) + 512],
                            ALU.mult), [bt])

                for g in range(G):
                    issue_qk(g)
                    issue_exp(g)
                    if g >= 1:
                        issue_av(g - 1, g)
                    flush_tail(g)
                issue_av(G - 1, G)
                flush_tail(G + 2)
            p.barrier()
        lastst = {0: None, 1: None}
        evtok = {0: None, 1: None}
        for ts in range(64):
            pb = ts % 2
            mw = [evtok[pb]] if evtok[pb] else list(p.tok("dve", "act"))
            for nh in range(2):
                for pr, AT in ((0, AT0), (1, AT1)):
                    p.t(lambda e, ts=ts, nh=nh, pr=pr, AT=AT, pb=pb: e.matmul(
                        psS4[:, 1024 * pb + 512 * nh:1024 * pb + 512 * (nh + 1)],
                        AT[:, 128 * ts:128 * (ts + 1)],
                        woT[:, 1024 * pr + 512 * nh:1024 * pr + 512 * (nh + 1)],
                        start=(pr == 0), stop=(pr == 1)), mw if (nh == 0 and pr == 0) else ())
            evw = list(p.tok("pe"))
            if lastst[pb]:
                evw.append(lastst[pb])
            p.s(lambda e, pb=pb: e.activation(
                ytile[:, 1024 * pb:1024 * pb + 512],
                psS4[:, 1024 * pb:1024 * pb + 512], AF.Identity), evw)
            a2 = p.s(lambda e, pb=pb: e.activation(
                ytile[:, 1024 * pb + 512:1024 * pb + 1024],
                psS4[:, 1024 * pb + 512:1024 * pb + 1024], AF.Identity))
            evtok[pb] = a2
            lastst[pb] = p.ds(lambda e, ts=ts, pb=pb: e.dma_start(
                cio["rs_in"][128 * ts:128 * (ts + 1), :],
                ytile[:, 1024 * pb:1024 * pb + 1024]), [a2])
        p.ccop(lambda e: e.collective_compute(
            "ReduceScatter", ALU.add, GROUPS, ins=[cio["rs_in"].opt()],
            outs=[cio["rs_out"].opt()]), p.tok("qs"))
        p.barrier()
        cc_tok = [p.tok("cc")[0]]
        ybf3 = st.enter_context(nc.sbuf_tensor("c_ybf3", [128, 3 * 1024], BF16))

        def mk_src_l1a(t, s, xin, xdt, wfree):
            ydt = p.ds(lambda e, t=t, s=s: e.dma_start(
                ybf3[:, 1024 * s:1024 * (s + 1)],
                cio["rs_out"][128 * t:128 * (t + 1), :]), wfree + cc_tok)
            return ybf3[:, 1024 * s:1024 * (s + 1)], ydt

        ln_resid_pipe(p, C, nc, st, "c", NT, mk_src_l1a, x_src, x_dst, nslot=2)
        p.barrier()


def phase_l1f(p, C, nc, dr, x_src, x_dst, cio):
    with ExitStack() as st:
        sb = lambda n, shp, dt: st.enter_context(nc.sbuf_tensor(n, shp, dt))
        ps = lambda n, shp, dt=F32: st.enter_context(nc.psum_tensor(n, shp, dt))
        xt = sb("d_xt", [128, 2 * 1024], F32)
        uf = sb("d_uf", [128, 1024], F32)
        uhi = sb("d_uhi", [128, 1024], BF16)
        ulo = sb("d_ulo", [128, 1024], BF16)
        alpha = sb("d_alpha", [128, 16 * 64], BF16)
        alphaT = sb("d_alphaT", [128, 2048], BF16)
        mcol = sb("d_mcol", [128, 8], BF16)
        taus = sb("d_taus", [128, 4], F32)
        b1p = sb("d_b1p", [128, 512], BF16)
        sIT = sb("d_sIT", [128, 8 * 64], BF16)
        sITf = sb("d_sITf", [128, 8 * 64], F32)
        sIb = sb("d_sI", [64, 1024], BF16)
        si = sb("d_si", [64, 1028], F32)
        sid_b = sb("d_sidb", [128, 1024], BF16)
        siT = sb("d_siT", [128, 8 * 64], BF16)
        SnT = sb("d_SnT", [128, 8 * 64], F32)
        SnTb = sb("d_SnTb", [128, 8 * 64], BF16)
        qT = sb("d_qT", [128, 2 * 64], BF16)
        kT = sb("d_kT", [128, 2 * 64], BF16)
        scb = sb("d_scb", [128, 64], BF16)
        scq = sb("d_scq", [64, 64], F32)
        attnSb = sb("d_attnSb", [128, 64], BF16)
        attnTb = sb("d_attnTb", [128, 64], BF16)
        vs = sb("d_vs", [128, 1024], BF16)
        ySlT = sb("d_ySlT", [128, 8 * 64], F32)
        ySl = sb("d_ySl", [64, 1024], F32)
        yhat = sb("d_yhat", [128, 1024], BF16)
        yhT = sb("d_yhT", [128, 8 * 64], BF16)
        sv = sb("d_sv", [128, 1024], BF16)
        yb = sb("d_yb", [128, 1024], F32)
        pbt = ps("d_pbt", [128, 2 * 1024], BF16)
        pbf = ps("d_pbf", [128, 2 * 512])
        psHM = ps("d_psHM", [128, 512])
        psX = ps("d_psX", [128, 64])
        psSI = ps("d_psSI", [128, 2 * 512])
        psH = psHM
        psL = psX
        psM = psHM
        psD = psX

        for t_ in (alphaT, sid_b, scb, attnSb, attnTb, vs, yhat, sv):
            p.v(lambda e, t_=t_: e.memset(t_[64:128, :], 0.0))
        p.dg(lambda e: e.dma_start(sIb[:], dr["f_sI"]))
        for c in range(8):
            p.dg(lambda e, c=c: e.dma_start(sIT[:, 64 * c:64 * (c + 1)],
                                            dr["f_sIT"][128 * c:128 * (c + 1), :]))
            p.ds(lambda e, c=c: e.dma_start(sITf[:, 64 * c:64 * (c + 1)],
                                            dr["f_sIT"][128 * c:128 * (c + 1), :]))
        p.ds(lambda e: e.dma_start(taus[0:1, 0:1], dr["f_tau"]))
        p.barrier()
        p.s(lambda e: e.activation(taus[0:1, 1:2], taus[0:1, 0:1], AF.Abs))
        p.v(lambda e: e.tensor_scalar_add(taus[0:1, 2:3], taus[0:1, 1:2], 0.1), p.tok("act"))
        p.v(lambda e: e.reciprocal(taus[0:1, 3:4], taus[0:1, 2:3]))
        p.t(lambda e: e.matmul(pbf[:, 0:1], C["onesf"][0:1, :], taus[0:1, 3:4],
                               start=True, stop=True), p.tok("dve"))
        p.v(lambda e: e.tensor_copy(taus[:, 0:1], pbf[:, 0:1]), p.tok("pe"))
        for c in range(8):
            p.t(lambda e, c=c: e.matmul(psM[:, 0:1], sIb[:, 128 * c:128 * (c + 1)],
                                        C["onesb"][0:64, 0:1], start=True, stop=True),
                p.tok("act"))
            p.s(lambda e, c=c: e.activation(mcol[:, c:c + 1], psM[:, 0:1], AF.Identity,
                                            scale=1.0 / 64.0), p.tok("pe"))
        p.barrier()

        stA = ExitStack()
        sba = lambda n, shp, dt: stA.enter_context(nc.sbuf_tensor(n, shp, dt))
        w1hi = sba("d_w1hi", [128, 8 * 512], BF16)
        w1lo = sba("d_w1lo", [128, 8 * 512], BF16)
        w1f = sba("d_w1f", [128, 512], F32)
        w1bot = sba("d_w1bot", [128, 8 * 512], BF16)
        w2f = sba("d_w2f", [128, 4 * 64], F32)
        hT = sba("d_hT", [128, 4 * 2048], F32)
        b1raw = sba("d_b1raw", [1, 512], BF16)
        b2r = sba("d_b2r", [128, 64], F32)
        logit = sba("d_logit", [128, 64], F32)
        ebuf = sba("d_ebuf", [128, 64], F32)
        mbuf = sba("d_mbuf", [128, 64], F32)
        mx8 = sba("d_mx8", [128, 8], F32)
        for c in range(8):
            p.dg(lambda e, c=c: e.dma_start(w1bot[:, 512 * c:512 * (c + 1)],
                                            dr["f_w1b"][128 * c:128 * (c + 1), :]))
        p.dg(lambda e: e.dma_start(b1raw[:], dr["f_b1"]))
        mzb = p.v(lambda e: e.memset(b2r[:], 0.0))
        p.ds(lambda e: e.dma_start(b2r[0:1, :], dr["f_b2"]), [mzb])
        for c in range(4):
            p.ds(lambda e, c=c: e.dma_start(w2f[:, 64 * c:64 * (c + 1)],
                                            dr["f_w2"][128 * c:128 * (c + 1), :]))
        for c in range(8):
            dt = p.ds(lambda e, c=c: e.dma_start(w1f[:], dr["f_w1t"][128 * c:128 * (c + 1), :]),
                      p.tok("dve"))
            p.v(lambda e, c=c: e.tensor_copy(w1hi[:, 512 * c:512 * (c + 1)], w1f[:]), [dt])
            p.v(lambda e, c=c: e.tensor_tensor(w1lo[:, 512 * c:512 * (c + 1)], w1f[:],
                                               w1hi[:, 512 * c:512 * (c + 1)], ALU.subtract))
        p.barrier()
        for c in range(8):
            p.t(lambda e, c=c: e.matmul(psM[0:1, 0:512], mcol[:, c:c + 1],
                                        w1bot[:, 512 * c:512 * (c + 1)],
                                        start=(c == 0), stop=False))
        p.t(lambda e: e.matmul(psM[0:1, 0:512], C["onesb"][0:1, 0:1], b1raw[0:1, :],
                               start=False, stop=True))
        mzp = p.v(lambda e: e.memset(b1p[:], 0.0))
        p.s(lambda e: e.activation(b1p[0:1, :], psM[0:1, 0:512], AF.Identity),
            list(p.tok("pe")) + [mzp])
        p.barrier()

        stA2 = ExitStack()
        uThi = stA2.enter_context(nc.sbuf_tensor("d_uThi", [128, 8 * 2048], BF16))
        uTlo = stA2.enter_context(nc.sbuf_tensor("d_uTlo", [128, 8 * 2048], BF16))
        dma_toks = {}
        last_lo = None
        last_hi_tr = None
        for t in (0, 1):
            b = t % 2
            dma_toks[t] = p.ds(lambda e, t=t, b=b: e.dma_start(
                xt[:, 1024 * b:1024 * (b + 1)], x_src[128 * t:128 * (t + 1), :]))
        for t in range(NT):
            b = t % 2
            xin = xt[:, 1024 * b:1024 * (b + 1)]
            w = [dma_toks[t]]
            if last_lo:
                w.append(last_lo)
            ntok = tile_ln(p, C, xin, t, uf[:], wait=w)
            if t + 2 < NT:
                dma_toks[t + 2] = p.ds(lambda e, t=t, b=b: e.dma_start(
                    xt[:, 1024 * b:1024 * (b + 1)],
                    x_src[128 * (t + 2):128 * (t + 3), :]), [ntok])
            hitok = p.v(lambda e: e.tensor_copy(uhi[:], uf[:]),
                        [ntok] + ([last_hi_tr] if last_hi_tr else []))
            lotok = p.v(lambda e: e.tensor_tensor(ulo[:], uf[:], uhi[:], ALU.subtract))
            for c in range(8):
                p.t(lambda e, c=c: e.transpose(pbt[:, 128 * c:128 * (c + 1)],
                                               uhi[:, 128 * c:128 * (c + 1)], C["eyeb"][:]),
                    [hitok] if c == 0 else ())
            last_hi_tr = p.tok("pe")[0]
            p.v(lambda e, t=t: e.tensor_copy(
                uThi.rearrange("p (c n) -> p c n", n=2048)[:, :, 128 * t:128 * (t + 1)],
                pbt[:, 0:1024].rearrange("p (c n) -> p c n", n=128)), p.tok("pe"))
            for c in range(8):
                p.t(lambda e, c=c: e.transpose(pbt[:, 1024 + 128 * c:1024 + 128 * (c + 1)],
                                               ulo[:, 128 * c:128 * (c + 1)], C["eyeb"][:]),
                    [lotok] if c == 0 else ())
            last_lo = p.v(lambda e, t=t: e.tensor_copy(
                uTlo.rearrange("p (c n) -> p c n", n=2048)[:, :, 128 * t:128 * (t + 1)],
                pbt[:, 1024:2048].rearrange("p (c n) -> p c n", n=128)), p.tok("pe"))
        p.barrier()

        lastg = None
        for fc in range(4):
            for n in range(4):
                hw = [lastg] if lastg else []
                first = True
                for c in range(8):
                    for wsb, usb, st_flag in ((w1hi, uThi, True), (w1lo, uThi, False),
                                              (w1hi, uTlo, False)):
                        p.t(lambda e, fc=fc, n=n, c=c, wsb=wsb, usb=usb, st_flag=st_flag: e.matmul(
                            psH[:], wsb[:, 512 * c + 128 * fc:512 * c + 128 * (fc + 1)],
                            usb[:, 2048 * c + 512 * n:2048 * c + 512 * (n + 1)],
                            start=(c == 0 and st_flag), stop=False), hw if first else ())
                        first = False
                p.t(lambda e, fc=fc: e.matmul(
                    psH[:], b1p[:, 128 * fc:128 * (fc + 1)], C["onesb"][:, 0:512],
                    start=False, stop=True))
                lastg = p.s(lambda e, fc=fc, n=n: e.activation(
                    hT[:, 2048 * fc + 512 * n:2048 * fc + 512 * (n + 1)], psH[:], AF.Gelu),
                    p.tok("pe"))
        p.barrier()
        lastal = None
        for t in range(NT):
            lw = list(p.tok("act", "dve"))
            for fc in range(4):
                p.t(lambda e, t=t, fc=fc: e.matmul(
                    psL[:], hT[:, 2048 * fc + 128 * t:2048 * fc + 128 * (t + 1)],
                    w2f[:, 64 * fc:64 * (fc + 1)],
                    start=(fc == 0), stop=False), lw if fc == 0 else ())
            p.t(lambda e: e.matmul(psL[:], C["onesf"][:, 0:128], b2r[:],
                                   start=False, stop=True))
            ltok = p.s(lambda e: e.activation(logit[:], psL[:], AF.Identity,
                                              scale=taus[:, 0:1]), p.tok("pe", "dve"))
            p.v(lambda e: e.max(out=mx8[:], in_=logit[:]), [ltok])
            p.v(lambda e: e.tensor_scalar_mul(C["tiny"][:, 0:1], mx8[:, 0:1], -1.0))
            etok = p.s(lambda e: e.activation(ebuf[:], logit[:], AF.Exp,
                                              bias=C["tiny"][:, 0:1]), p.tok("dve"))
            p.v(lambda e: e.tensor_single_scalar(mbuf[:], logit[:], mx8[:, 7:8], ALU.is_ge),
                [etok])
            p.v(lambda e: e.tensor_tensor(ebuf[:], ebuf[:], mbuf[:], ALU.mult))
            p.v(lambda e: e.reduce_sum(C["tiny"][:, 1:2], ebuf[:], axis=AX.X))
            p.v(lambda e: e.reciprocal(C["tiny"][:, 2:3], C["tiny"][:, 1:2]))
            altok = p.v(lambda e, t=t: e.tensor_scalar(
                alpha[:, 64 * t:64 * (t + 1)], ebuf[:], C["tiny"][:, 2:3], None, ALU.mult))
            trw = [altok]
            if lastal:
                trw.append(lastal)
            p.t(lambda e, t=t: e.transpose(pbt[0:64, 0:128], alpha[:, 64 * t:64 * (t + 1)],
                                           C["eyeb"][:]), trw)
            lastal = p.v(lambda e, t=t: e.tensor_copy(
                alphaT[0:64, 128 * t:128 * (t + 1)], pbt[0:64, 0:128]), p.tok("pe"))
        p.barrier()
        stA2.close()
        stA.close()

        p.v(lambda e: e.memset(si[:], 0.0))
        lastu = None
        for t in range(NT):
            b = t % 2
            w = list(p.tok("dve"))
            dtok = p.ds(lambda e, t=t, b=b: e.dma_start(
                xt[:, 1024 * b:1024 * (b + 1)], x_src[128 * t:128 * (t + 1), :]), w)
            xin = xt[:, 1024 * b:1024 * (b + 1)]
            uw = [dtok]
            if lastu:
                uw.append(lastu)
            utok = p.s(lambda e, xin=xin, t=t: e.activation(
                uhi[:], xin, AF.Identity, bias=C["mrs"][:, 4 * t + 3:4 * t + 4],
                scale=C["mrs"][:, 4 * t + 2:4 * t + 3]), uw)
            for half in range(2):
                p.t(lambda e, t=t, half=half: e.matmul(
                    psSI[0:64, 512 * half:512 * (half + 1)],
                    alpha[:, 64 * t:64 * (t + 1)],
                    uhi[:, 512 * half:512 * (half + 1)],
                    start=(t == 0), stop=(t == NT - 1), skip_group_check=True),
                    [utok] if half == 0 else ())
            p.t(lambda e, t=t: e.matmul(
                psD[0:64, 0:1], alpha[:, 64 * t:64 * (t + 1)], C["onesb"][:, 0:1],
                start=(t == 0), stop=(t == NT - 1), skip_group_check=True))
            lastu = p.tok("pe")[0]
        p.s(lambda e: e.activation(si[0:64, 0:512], psSI[0:64, 0:512], AF.Identity), p.tok("pe"))
        p.s(lambda e: e.activation(si[0:64, 512:1024], psSI[0:64, 512:1024], AF.Identity))
        p.s(lambda e: e.activation(si[0:64, 1024:1025], psD[0:64, 0:1], AF.Identity))
        p.ds(lambda e: e.dma_start(cio["ar_in"], si[0:64, :]), p.tok("act"))
        p.ccop(lambda e: e.collective_compute(
            "AllReduce", ALU.add, GROUPS, ins=[cio["ar_in"].opt()],
            outs=[cio["ar_out"].opt()]), p.tok("qs"))
        p.ds(lambda e: e.dma_start(si[0:64, :], cio["ar_out"]), p.tok("cc"))
        p.barrier()

        p.v(lambda e: e.tensor_scalar_add(si[0:64, 1024:1025], si[0:64, 1024:1025], 1e-8))
        p.v(lambda e: e.reciprocal(si[0:64, 1025:1026], si[0:64, 1024:1025]))
        p.v(lambda e: e.tensor_scalar(sid_b[0:64, :], si[0:64, 0:1024],
                                      si[0:64, 1025:1026], None, ALU.mult))
        for c in range(8):
            p.t(lambda e, c=c: e.transpose(pbt[:, 128 * c:128 * (c + 1)],
                                           sid_b[:, 128 * c:128 * (c + 1)],
                                           C["eyeb"][:]),
                p.tok("dve") if c == 0 else ())
        p.v(lambda e: e.tensor_copy(
            siT.rearrange("p (c n) -> p c n", n=64),
            pbt[:, 0:1024].rearrange("p (c n) -> p c n", n=128)[:, :, 0:64]), p.tok("pe"))
        giT = sb("d_giT", [128, 24 * 64], F32)
        ghT = sb("d_ghT", [128, 24 * 64], F32)
        rz = sb("d_rz", [128, 16 * 64], F32)
        gn = sb("d_gn", [128, 8 * 64], F32)
        stC1 = ExitStack()
        wih = stC1.enter_context(nc.sbuf_tensor("d_wih", [128, 8 * 3072], BF16))
        bihr = stC1.enter_context(nc.sbuf_tensor("d_bihr", [128, 3072], BF16))
        whh = stC1.enter_context(nc.sbuf_tensor("d_whh", [128, 8 * 3072], BF16))
        bhhr = stC1.enter_context(nc.sbuf_tensor("d_bhhr", [128, 3072], BF16))
        mzi = p.v(lambda e: e.memset(bihr[:], 0.0))
        for c in range(8):
            p.dg(lambda e, c=c: e.dma_start(wih[:, 3072 * c:3072 * (c + 1)],
                                            dr["f_wihT"][128 * c:128 * (c + 1), :]))
        p.dg(lambda e: e.dma_start(bihr[0:1, :], dr["f_bih"]), [mzi])
        p.barrier()
        mzh = p.v(lambda e: e.memset(bhhr[:], 0.0))
        for c in range(8):
            p.dg(lambda e, c=c: e.dma_start(whh[:, 3072 * c:3072 * (c + 1)],
                                            dr["f_whhT"][128 * c:128 * (c + 1), :]))
        p.dg(lambda e: e.dma_start(bhhr[0:1, :], dr["f_bhh"]), [mzh])
        whh_tok = p.tok("qg")
        for g in range(24):
            gw = list(p.tok("dve", "act"))
            for c in range(8):
                p.t(lambda e, g=g, c=c: e.matmul(
                    psM[:, 0:64], wih[:, 3072 * c + 128 * g:3072 * c + 128 * (g + 1)],
                    siT[:, 64 * c:64 * (c + 1)], start=(c == 0), stop=False),
                    gw if c == 0 else ())
            p.t(lambda e, g=g: e.matmul(psM[:, 0:64], bihr[:, 128 * g:128 * (g + 1)],
                                        C["onesb"][:, 0:64], start=False, stop=True))
            p.s(lambda e, g=g: e.activation(giT[:, 64 * g:64 * (g + 1)], psM[:, 0:64],
                                            AF.Identity), p.tok("pe"))
        for g in range(24):
            gw = list(p.tok("act")) + (list(whh_tok) if g == 0 else [])
            for c in range(8):
                p.t(lambda e, g=g, c=c: e.matmul(
                    psM[:, 0:64], whh[:, 3072 * c + 128 * g:3072 * c + 128 * (g + 1)],
                    sIT[:, 64 * c:64 * (c + 1)], start=(c == 0), stop=False),
                    gw if c == 0 else ())
            p.t(lambda e, g=g: e.matmul(psM[:, 0:64], bhhr[:, 128 * g:128 * (g + 1)],
                                        C["onesb"][:, 0:64], start=False, stop=True))
            p.s(lambda e, g=g: e.activation(ghT[:, 64 * g:64 * (g + 1)], psM[:, 0:64],
                                            AF.Identity), p.tok("pe"))
        p.barrier()
        stC1.close()
        stC3 = ExitStack()
        sbc = lambda n, shp, dt: stC3.enter_context(nc.sbuf_tensor(n, shp, dt))
        wqs = sbc("d_wqs", [128, 8 * 256], BF16)
        wks = sbc("d_wks", [128, 8 * 256], BF16)
        wvs = sbc("d_wvs", [128, 8 * 1024], BF16)
        bqr = sbc("d_bqr", [128, 256], BF16)
        bkr = sbc("d_bkr", [128, 256], BF16)
        bvr = sbc("d_bvr", [128, 1024], BF16)
        gwvp = sbc("d_gwvp", [128, 8 * 1024], BF16)
        wvpf = sbc("d_wvpf", [128, 1024], F32)
        lngc = sbc("d_lngc", [128, 8], F32)
        lnbc = sbc("d_lnbc", [128, 8], F32)
        lng8 = sbc("d_lng8", [8, 128], F32)
        lnb8 = sbc("d_lnb8", [8, 128], F32)
        cr1 = sbc("d_cr1", [1, 1024], F32)
        crb = sbc("d_crb", [128, 1024], BF16)
        bvpr = sbc("d_bvpr", [128, 1024], F32)
        for c in range(8):
            p.dg(lambda e, c=c: e.dma_start(wqs[:, 256 * c:256 * (c + 1)],
                                            dr["f_wq"][128 * c:128 * (c + 1), :]))
            p.dg(lambda e, c=c: e.dma_start(wks[:, 256 * c:256 * (c + 1)],
                                            dr["f_wk"][128 * c:128 * (c + 1), :]))
            p.dg(lambda e, c=c: e.dma_start(wvs[:, 1024 * c:1024 * (c + 1)],
                                            dr["f_wv"][128 * c:128 * (c + 1), :]))
        mzc = p.v(lambda e: e.memset(bqr[:], 0.0))
        mzc = p.v(lambda e: e.memset(bkr[:], 0.0), [mzc])
        mzc = p.v(lambda e: e.memset(bvr[:], 0.0), [mzc])
        mzc = p.v(lambda e: e.memset(crb[:], 0.0), [mzc])
        mzc = p.v(lambda e: e.memset(bvpr[:], 0.0), [mzc])
        p.dg(lambda e: e.dma_start(bqr[0:1, :], dr["f_bq"]), [mzc])
        p.dg(lambda e: e.dma_start(bkr[0:1, :], dr["f_bk"]))
        p.dg(lambda e: e.dma_start(bvr[0:1, :], dr["f_bv"]))
        p.ds(lambda e: e.dma_start(lng8[:], dr["f_lng8"]))
        p.ds(lambda e: e.dma_start(lnb8[:], dr["f_lnb8"]))
        p.ds(lambda e: e.dma_start(bvpr[0:1, :], dr["f_bvp"]), [mzc])
        p.barrier()
        p.v(lambda e: e.tensor_add(rz[:, 0:1024], giT[:, 0:1024], ghT[:, 0:1024]))
        p.s(lambda e: e.activation(rz[:, 0:1024], rz[:, 0:1024], AF.Sigmoid), p.tok("dve"))
        p.v(lambda e: e.tensor_tensor(gn[:], rz[:, 0:512], ghT[:, 1024:1536], ALU.mult),
            p.tok("act"))
        p.v(lambda e: e.tensor_add(gn[:], gn[:], giT[:, 1024:1536]))
        p.s(lambda e: e.activation(gn[:], gn[:], AF.Tanh), p.tok("dve"))
        p.v(lambda e: e.tensor_tensor(SnT[:], sITf[:], gn[:], ALU.subtract), p.tok("act"))
        p.v(lambda e: e.tensor_tensor(SnT[:], SnT[:], rz[:, 512:1024], ALU.mult))
        p.v(lambda e: e.tensor_add(SnT[:], SnT[:], gn[:]))
        p.v(lambda e: e.tensor_copy(SnTb[:], SnT[:]))
        for dst, wsb, bsb in ((qT, wqs, bqr), (kT, wks, bkr)):
            for f2 in range(2):
                gw = list(p.tok("dve", "act"))
                for c in range(8):
                    p.t(lambda e, f2=f2, c=c, wsb=wsb: e.matmul(
                        psM[:, 0:64], wsb[:, 256 * c + 128 * f2:256 * c + 128 * (f2 + 1)],
                        SnTb[:, 64 * c:64 * (c + 1)], start=(c == 0), stop=False),
                        gw if c == 0 else ())
                p.t(lambda e, f2=f2, bsb=bsb: e.matmul(
                    psM[:, 0:64], bsb[:, 128 * f2:128 * (f2 + 1)], C["onesb"][:, 0:64],
                    start=False, stop=True))
                p.s(lambda e, f2=f2, dst=dst: e.activation(
                    dst[:, 64 * f2:64 * (f2 + 1)], psM[:, 0:64], AF.Identity), p.tok("pe"))
        saw = list(p.tok("act"))
        for f2 in range(2):
            p.t(lambda e, f2=f2: e.matmul(
                psM[0:64, 0:64], kT[:, 64 * f2:64 * (f2 + 1)], qT[:, 64 * f2:64 * (f2 + 1)],
                start=(f2 == 0), stop=(f2 == 1)), saw if f2 == 0 else ())
        p.s(lambda e: e.activation(scb[0:64, :], psM[0:64, 0:64], AF.Identity), p.tok("pe"))
        p.t(lambda e: e.transpose(pbt[0:64, 0:128], scb[:], C["eyeb"][:]), p.tok("act"))
        p.v(lambda e: e.tensor_copy(scq[:], pbt[0:64, 0:64]), p.tok("pe"))
        p.v(lambda e: e.reduce_max(C["tiny"][0:64, 0:1], scq[:], axis=AX.X, negate=True))
        p.v(lambda e: e.tensor_scalar_mul(C["tiny"][0:64, 1:2], C["tiny"][0:64, 0:1], 0.0625))
        p.s(lambda e: e.activation(scq[:], scq[:], AF.Exp, scale=0.0625,
                                   bias=C["tiny"][0:64, 1:2]), p.tok("dve"))
        p.v(lambda e: e.reduce_sum(C["tiny"][0:64, 2:3], scq[:], axis=AX.X), p.tok("act"))
        p.v(lambda e: e.reciprocal(C["tiny"][0:64, 3:4], C["tiny"][0:64, 2:3]))
        p.v(lambda e: e.tensor_scalar(attnSb[0:64, :], scq[:], C["tiny"][0:64, 3:4],
                                      None, ALU.mult))
        p.t(lambda e: e.transpose(pbt[0:64, 0:128], attnSb[:], C["eyeb"][:]),
            p.tok("dve"))
        p.v(lambda e: e.tensor_copy(attnTb[0:64, :], pbt[0:64, 0:64]), p.tok("pe"))
        for n in range(2):
            gw = list(p.tok("dve", "act"))
            for c in range(8):
                p.t(lambda e, n=n, c=c: e.matmul(
                    psM[0:64, 0:512], SnTb[:, 64 * c:64 * (c + 1)],
                    wvs[:, 1024 * c + 512 * n:1024 * c + 512 * (n + 1)],
                    start=(c == 0), stop=False), gw if c == 0 else ())
            p.t(lambda e, n=n: e.matmul(psM[0:64, 0:512], C["onesb"][:, 0:64],
                                        bvr[:, 512 * n:512 * (n + 1)],
                                        start=False, stop=True))
            p.s(lambda e, n=n: e.activation(vs[0:64, 512 * n:512 * (n + 1)],
                                            psM[0:64, 0:512], AF.Identity), p.tok("pe"))
        for c in range(8):
            p.t(lambda e, c=c: e.matmul(
                psM[:, 0:64], vs[:, 128 * c:128 * (c + 1)], attnTb[:, 0:64],
                start=True, stop=True), p.tok("act", "dve"))
            p.v(lambda e, c=c: e.tensor_tensor(ySlT[:, 64 * c:64 * (c + 1)],
                                               SnT[:, 64 * c:64 * (c + 1)], psM[:, 0:64],
                                               ALU.add), p.tok("pe"))
        for c in range(8):
            bk = c % 2
            tw = list(p.tok("dve"))
            p.t(lambda e, c=c, bk=bk: e.transpose(
                pbf[0:64, 512 * bk:512 * bk + 128], ySlT[:, 64 * c:64 * (c + 1)],
                C["eyef"][:]), tw)
            p.v(lambda e, c=c, bk=bk: e.tensor_copy(
                ySl[:, 128 * c:128 * (c + 1)], pbf[0:64, 512 * bk:512 * bk + 128]), p.tok("pe"))
        st_, tmp = C["stats"], C["tiny"]
        p.v(lambda e: e.bn_stats(st_[0:64, 0:6], ySl[:, 0:512]), p.tok("act"))
        p.v(lambda e: e.bn_stats(st_[0:64, 6:12], ySl[:, 512:1024]))
        p.v(lambda e: e.bn_aggr(tmp[0:64, 8:10], st_[0:64, 0:12]))
        p.v(lambda e: e.tensor_scalar_add(tmp[0:64, 0:1], tmp[0:64, 9:10], 1e-5))
        p.s(lambda e: e.activation(tmp[0:64, 1:2], tmp[0:64, 0:1], AF.Ln), p.tok("dve"))
        p.s(lambda e: e.activation(tmp[0:64, 2:3], tmp[0:64, 1:2], AF.Exp, scale=-0.5))
        p.v(lambda e: e.tensor_tensor(tmp[0:64, 3:4], tmp[0:64, 8:9], tmp[0:64, 2:3], ALU.mult),
            p.tok("act"))
        p.v(lambda e: e.tensor_scalar_mul(tmp[0:64, 4:5], tmp[0:64, 3:4], -1.0))
        p.s(lambda e: e.activation(yhat[0:64, :], ySl[:], AF.Identity, bias=tmp[0:64, 4:5],
                                   scale=tmp[0:64, 2:3]), p.tok("dve"))
        p.t(lambda e: e.matmul(pbf[:, 0:8], lng8[:], C["eyef"][0:8, 0:8], start=True, stop=True),
            p.tok("dve"))
        p.v(lambda e: e.tensor_copy(lngc[:], pbf[:, 0:8]), p.tok("pe"))
        p.t(lambda e: e.matmul(pbf[:, 8:16], lnb8[:], C["eyef"][0:8, 0:8], start=True, stop=True),
            p.tok("dve"))
        p.v(lambda e: e.tensor_copy(lnbc[:], pbf[:, 8:16]), p.tok("pe"))
        for c in range(8):
            dt = p.ds(lambda e, c=c: e.dma_start(wvpf[:], dr["f_wvp"][128 * c:128 * (c + 1), :]),
                      p.tok("dve", "pe"))
            p.v(lambda e, c=c: e.tensor_scalar(gwvp[:, 1024 * c:1024 * (c + 1)], wvpf[:],
                                               lngc[:, c:c + 1], None, ALU.mult), [dt])
            for half in range(2):
                p.t(lambda e, c=c, half=half: e.matmul(
                    pbf[0:1, 512 * half:512 * half + 512], lnbc[:, c:c + 1],
                    wvpf[:, 512 * half:512 * (half + 1)],
                    start=(c == 0), stop=False, skip_group_check=True),
                    p.tok("dve") if half == 0 else ())
        for half in range(2):
            p.t(lambda e, half=half: e.matmul(
                pbf[0:1, 512 * half:512 * half + 512], C["onesf"][:, 0:1],
                bvpr[:, 512 * half:512 * (half + 1)], start=False, stop=True,
                skip_group_check=True))
        p.v(lambda e: e.tensor_copy(cr1[0:1, 0:512], pbf[0:1, 0:512]), p.tok("pe"))
        p.v(lambda e: e.tensor_copy(cr1[0:1, 512:1024], pbf[0:1, 512:1024]))
        p.v(lambda e: e.tensor_copy(crb[0:1, :], cr1[:]))
        for c in range(8):
            p.t(lambda e, c=c: e.transpose(pbt[:, 128 * c:128 * (c + 1)],
                                           yhat[:, 128 * c:128 * (c + 1)],
                                           C["eyeb"][:]),
                p.tok("act", "dve") if c == 0 else ())
        p.v(lambda e: e.tensor_copy(
            yhT.rearrange("p (c n) -> p c n", n=64),
            pbt[:, 0:1024].rearrange("p (c n) -> p c n", n=128)[:, :, 0:64]), p.tok("pe"))
        for n in range(2):
            gw = list(p.tok("dve", "act"))
            for c in range(8):
                p.t(lambda e, n=n, c=c: e.matmul(
                    psM[0:64, 0:512], yhT[:, 64 * c:64 * (c + 1)],
                    gwvp[:, 1024 * c + 512 * n:1024 * c + 512 * (n + 1)],
                    start=(c == 0), stop=False), gw if c == 0 else ())
            p.t(lambda e, n=n: e.matmul(psM[0:64, 0:512], C["onesb"][:, 0:64],
                                        crb[:, 512 * n:512 * (n + 1)],
                                        start=False, stop=True))
            p.s(lambda e, n=n: e.activation(sv[0:64, 512 * n:512 * (n + 1)],
                                            psM[0:64, 0:512], AF.Identity), p.tok("pe"))
        p.barrier()
        stC3.close()

        stD = ExitStack()
        outT = stD.enter_context(nc.sbuf_tensor("d_outT", [128, 8 * 2048], BF16))
        wo2 = stD.enter_context(nc.sbuf_tensor("d_wo2", [128, 8 * 1024], BF16))
        bor = stD.enter_context(nc.sbuf_tensor("d_bor", [128, 1024], BF16))
        mzo = p.v(lambda e: e.memset(bor[:], 0.0))
        for c in range(8):
            p.dg(lambda e, c=c: e.dma_start(wo2[:, 1024 * c:1024 * (c + 1)],
                                            dr["f_wo"][128 * c:128 * (c + 1), :]))
        p.dg(lambda e: e.dma_start(bor[0:1, :], dr["f_bo"]), [mzo])
        p.barrier()
        for c in range(8):
            for n in range(4):
                w = list(p.tok("act"))
                p.t(lambda e, c=c, n=n: e.matmul(
                    psSI[:, 0:512], sv[:, 128 * c:128 * (c + 1)],
                    alphaT[:, 512 * n:512 * (n + 1)], start=True, stop=True), w)
                p.s(lambda e, c=c, n=n: e.activation(
                    outT[:, 2048 * c + 512 * n:2048 * c + 512 * (n + 1)], psSI[:, 0:512],
                    AF.Identity), p.tok("pe"))
        p.barrier()
        for t in range(NT):
            b = t % 2
            mw = list(p.tok("act", "dve"))
            for half in range(2):
                for c in range(8):
                    p.t(lambda e, t=t, half=half, c=c: e.matmul(
                        psSI[:, 512 * half:512 * (half + 1)],
                        outT[:, 2048 * c + 128 * t:2048 * c + 128 * (t + 1)],
                        wo2[:, 1024 * c + 512 * half:1024 * c + 512 * (half + 1)],
                        start=(c == 0), stop=False), mw if (half == 0 and c == 0) else ())
                p.t(lambda e, half=half: e.matmul(
                    psSI[:, 512 * half:512 * (half + 1)], C["onesb"][:, 0:128],
                    bor[:, 512 * half:512 * (half + 1)], start=False, stop=True))
            xdt = p.ds(lambda e, t=t, b=b: e.dma_start(
                xt[:, 1024 * b:1024 * (b + 1)], x_src[128 * t:128 * (t + 1), :]),
                p.tok("dve"))
            xin = xt[:, 1024 * b:1024 * (b + 1)]
            ln_resid_store(p, C, "psum2", psSI[:, 0:512], psSI[:, 512:1024], xin, yb,
                           x_dst[128 * t:128 * (t + 1), :],
                           extra_wait=list(p.tok("pe")) + [xdt])
        p.barrier()
        stD.close()


def _build(blocks, fused):
    nc = bass.Bass("TRN2", target_bir_lowering=False, detect_race_conditions=False)
    dr = {}
    specs = {
        "eye": [128, 128], "l0_masks": [768, 512], "l1_masks": [512, 512],
        "x0h": [2304, 1024], "padneg": [1, 2304],
        "l0_wq": [1024, 1024], "l0_wk": [1024, 1024], "l0_wv": [1024, 1024],
        "l0_wo": [1024, 1024], "l0_brow": [4, 1024],
        "mod_w1": [1024, 4096], "mod_w2": [4096, 1024], "mod_b1c": [128, 32],
        "mod_b2c": [128, 8], "mod_wr8": [8, 128],
        "l1_wq": [1024, 256], "l1_wk": [1024, 256], "l1_wv": [1024, 256],
        "l1_wo": [256, 1024],
        "f_sIT": [1024, 64], "f_sI": [64, 1024], "f_w1t": [1024, 512],
        "f_w1b": [1024, 512], "f_b1": [1, 512], "f_w2": [512, 64], "f_b2": [1, 64],
        "f_tau": [1, 1], "f_wihT": [1024, 3072], "f_whhT": [1024, 3072],
        "f_bih": [1, 3072], "f_bhh": [1, 3072], "f_wq": [1024, 256], "f_bq": [1, 256],
        "f_wk": [1024, 256], "f_bk": [1, 256], "f_wv": [1024, 1024], "f_bv": [1, 1024],
        "f_lng8": [8, 128], "f_lnb8": [8, 128], "f_wvp": [1024, 1024],
        "f_bvp": [1, 1024], "f_wo": [1024, 1024], "f_bo": [1, 1024],
    }
    for nm, shp in specs.items():
        dr[nm] = nc.dram_tensor(nm, shp, F32, kind="ExternalInput").ap()
    out = nc.dram_tensor("xout", [2048, 1024], F32, kind="ExternalOutput").ap()
    dbg = nc.dram_tensor("dbg", [128, 8192], F32, kind="ExternalOutput").ap() if DBG else None
    xs = {}
    if fused:
        for nm in ("x1", "x2", "x3"):
            xs[nm] = nc.dram_tensor(nm, [2048, 1024], F32, kind="Internal").ap()
    else:
        if blocks[0] != "l0a":
            xs["xin"] = nc.dram_tensor("xin", [2048, 1024], F32, kind="ExternalInput").ap()
    cio = {
        "sc_in": nc.dram_tensor("sc_in", [1, 2048], F32, kind="Internal").ap(),
        "sc_out": nc.dram_tensor("sc_out", [4, 2048], F32, kind="Internal").ap(),
        "ag_in": nc.dram_tensor("ag_in", [1024, 2048], BF16, kind="Internal").ap(),
        "ag_out": nc.dram_tensor("ag_out", [4096, 2048], BF16, kind="Internal").ap(),
        "rs_in": nc.dram_tensor("rs_in", [8192, 1024], BF16, kind="Internal").ap(),
        "rs_out": nc.dram_tensor("rs_out", [2048, 1024], BF16, kind="Internal").ap(),
        "ar_in": nc.dram_tensor("ar_in", [64, 1028], F32, kind="Internal").ap(),
        "ar_out": nc.dram_tensor("ar_out", [64, 1028], F32, kind="Internal").ap(),
        "o2_dram": nc.dram_tensor("o2_dram", [2048, 1024], BF16, kind="Internal").ap(),
    }
    with ExitStack() as st:
        p = P(nc, st)
        block = st.enter_context(nc.Block())
        C = setup_consts(p, st, nc, dr)
        chain = {"l0a": ("x0h", "x1"), "l0f": ("x1", "x2"),
                 "l1a": ("x2", "x3"), "l1f": ("x3", "out")}
        for i, bname in enumerate(blocks):
            srcn, dstn = chain[bname]
            if fused:
                src = dr["x0h"] if srcn == "x0h" else xs[srcn]
                dst = out if (i == len(blocks) - 1) else xs[dstn]
            else:
                src = dr["x0h"] if bname == "l0a" else xs["xin"]
                dst = out
            if bname == "l0a":
                phase_l0a(p, C, nc, dr, src, dst, dbg=dbg)
            elif bname == "l0f":
                phase_l0f(p, C, nc, dr, src, dst, cio)
            elif bname == "l1a":
                phase_l1a(p, C, nc, dr, src, dst, cio)
            elif bname == "l1f":
                phase_l1f(p, C, nc, dr, src, dst, cio)
        p.barrier()
        p.emit(block)
    mybir.codegen_inst_isa_subclasses(nc)
    return nc


def _run(in_maps, blocks, fused):
    if fused:
        nc = _build(blocks, True)
        res = run_bass_kernel_spmd(nc, in_maps, core_ids=list(range(8)))
        return [np.asarray(r["xout"]) for r in res.results]
    cur = None
    for bname in blocks:
        nc = _build((bname,), False)
        maps = []
        for c in range(8):
            m = dict(in_maps[c])
            if bname != "l0a":
                m["xin"] = cur[c]
            maps.append(m)
        res = run_bass_kernel_spmd(nc, maps, core_ids=list(range(8)))
        cur = [np.asarray(r["xout"]) for r in res.results]
    return cur


def kernel(**inputs):
    in_maps = _prep(inputs)
    shards = _run(in_maps, ALL_BLOCKS, FUSED)
    out = np.zeros((NB, SEQ, DM), np.float32)
    for c in range(8):
        b, j = c // 4, c % 4
        out[b, j * TOK:(j + 1) * TOK] = shards[c]
    return out
